# version 14
# baseline (speedup 1.0000x reference)
import numpy as np
import ml_dtypes
from contextlib import ExitStack

import concourse.bass as bass
import concourse.bacc as bacc
import concourse.mybir as mybir
from concourse.bass import IndirectOffsetOnAxis
from concourse.bass_utils import run_bass_kernel_spmd
from concourse import library_config

F32 = mybir.dt.float32
BF16 = mybir.dt.bfloat16
I16 = mybir.dt.int16
I32 = mybir.dt.int32
U32 = mybir.dt.uint32
AF = mybir.ActivationFunctionType
OP = mybir.AluOpType
AX = mybir.AxisListType
BF = ml_dtypes.bfloat16


class Cfg:
    def __init__(self, n_cores=8, segs=8):
        self.n_cores = n_cores
        self.segs = segs
        self.SEG = 512
        self.D = 1024
        self.DC = 8
        self.H = 16
        self.E = 64
        self.CAP = 512
        self.I = 704
        self.IC = 6
        self.SI = 1408
        self.SIC = 11
        self.Tl = segs * self.SEG
        self.tiles = self.Tl // 128
        self.T = self.Tl * n_cores
        self.EPC = self.E // n_cores
        self.nseg_tot = segs * n_cores
        self.NCH = 4
        self.EPCH = self.EPC // self.NCH


class Sem:
    def __init__(self, name, producer=None):
        self.name, self.h, self.n, self.producer = name, None, 0, producer


class Stream:
    def __init__(self, b, name, nsem=4):
        self.sems = [b.sem(f"{name}{i}") for i in range(nsem)]
        self.i = 0

    def next(self):
        s = self.sems[self.i % len(self.sems)]
        self.i += 1
        return s


class Builder:
    ENGINES = ("sync", "scalar", "vector", "tensor", "gpsimd")

    def __init__(self, nc):
        self.nc = nc
        self.steps = []
        self.sems = []
        self.res = {}
        self.gwait = None
        self.eng_sem = {e: self.sem("c_" + e, producer=e) for e in self.ENGINES}

    def sem(self, name, producer=None):
        s = Sem(name, producer)
        self.sems.append(s)
        return s

    def _dep(self, name):
        return self.res.setdefault(name, {"w": None, "r": []})

    def do(self, engine, fn, reads=(), writes=(), wait=(), inc=(), dma=None,
           coll=None):
        waits = {}

        def addw(sv):
            if sv is None:
                return
            s, v = sv
            waits[s] = max(waits.get(s, 0), v)

        for r in reads:
            addw(self._dep(r)["w"])
        for w in writes:
            d = self._dep(w)
            addw(d["w"])
            for sv in d["r"]:
                addw(sv)
        for sv in wait:
            addw(sv)
        addw(self.gwait)

        if coll is not None:
            s = coll
            amt = 1
        elif dma is None:
            s = self.eng_sem[engine]
            amt = 1
        else:
            s = dma.next() if isinstance(dma, Stream) else dma
            amt = 16
            if s.n > 0:
                waits[s] = max(waits.get(s, 0), s.n)
        s.n += amt
        comp = (s, s.n)
        incs = [(s, amt)]
        for (x, a) in inc:
            x.n += a
            incs.append((x, a))

        for w in writes:
            d = self._dep(w)
            d["w"] = comp
            d["r"] = []
        for r in reads:
            self._dep(r)["r"].append(comp)

        self.steps.append((engine, fn,
                           sorted(waits.items(), key=lambda kv: kv[0].name), incs))
        return comp

    def fence(self, exclude=()):
        allr = [k for k in self.res.keys() if k not in exclude]
        h = self.do("vector", lambda eng: None, reads=allr)
        self.gwait = h

    def emit(self, stack):
        for s in self.sems:
            s.h = stack.enter_context(self.nc.semaphore(s.name))
        per = {e: [] for e in self.ENGINES}
        for st in self.steps:
            per[st[0]].append(st)
        with self.nc.Block() as block:
            def mk(engname):
                def body(eng):
                    seen = {}
                    for (_, fn, waits, incs) in per[engname]:
                        for (s, v) in waits:
                            if seen.get(s, 0) >= v:
                                continue
                            eng.wait_ge(s.h, v)
                            seen[s] = v
                        instr = fn(eng)
                        for (s, amt) in incs:
                            if instr is None:
                                eng.sem_inc(s.h, amt)
                            else:
                                instr = instr.then_inc(s.h, amt)
                return body
            for e in self.ENGINES:
                if per[e]:
                    getattr(block, e)(mk(e))


def build_program(c: Cfg, stack: ExitStack, stop_after="D", debug_dump=False):
    nc = bacc.Bacc("TRN2")
    b = Builder(nc)
    _PHASES = ["A1", "AG", "A2a", "Q7", "A2b", "C", "D"]
    _stop = _PHASES.index(stop_after)

    def _cut(phase):
        return _PHASES.index(phase) > _stop

    def din(name, shape, dt):
        return nc.dram_tensor(name, shape, dt, kind="ExternalInput").ap()

    def dint(name, shape, dt):
        return nc.dram_tensor(name, shape, dt).ap()

    x_in = din("x", [c.Tl, c.D], F32)
    out = nc.dram_tensor("out", [c.Tl + 128, c.D], F32, kind="ExternalOutput").ap()
    attw_in = [din(w, [c.D, c.D], F32) for w in ("Wq", "Wk", "Wv", "Wo")]
    bqkv_in = din("bqkv", [128, 3 * c.DC], F32)
    bo_in = din("bo_row", [1, c.D], F32)
    ws1T_in = din("ws1T", [c.D, c.SI], F32)
    ws3T_in = din("ws3T", [c.D, c.SI], F32)
    ws2T_in = din("ws2T", [c.SI, c.D], F32)
    gwT_in = din("gwT", [c.D, c.E], F32)
    gb_in = din("gb_bcast", [128, c.E], F32)
    w1T_in = din("w1T", [c.EPC, c.D, c.I], F32)
    w3T_in = din("w3T", [c.EPC, c.D, c.I], F32)
    w2T_in = din("w2T", [c.EPC, c.I, c.D], F32)
    Lup_in = din("Lup", [128, 132], BF16)
    idf_in = din("idf32", [128, 128], F32)
    idb_in = din("idbf", [128, 128], BF16)
    ssum_in = din("Ssum", [128, c.DC * 128], BF16)
    sbsel_in = din("Sb", [128, c.DC * 128], BF16)
    ones1_in = din("ones1", [1, 128], F32)
    segsel_in = din("segsel", [128, c.segs * c.segs], BF16)
    ids16_in = din("ids16", [c.E, c.Tl], I16)
    slothi_in = din("slothi", [c.E, c.CAP], BF16)
    slotlo_in = din("slotlo", [c.E, c.CAP], BF16)
    ustr_in = din("Ustrict", [c.E, c.E], BF16)
    iota_in = din("iota", [1, 512], F32)
    maskm_in = din("maskmat", [c.nseg_tot, c.segs], F32)
    rowb_in = din("rowbase", [128, 2], F32)
    myexp_in = din("myexp", [c.EPC, 1], I32)

    ln1_d = dint("ln1_d", [c.Tl, c.D], BF16)
    x1_d = dint("x1_d", [c.Tl, c.D], F32)
    agin_ln2 = dint("agin_ln2", [c.Tl, c.D], BF16)
    ln2_full = dint("ln2_full", [c.n_cores * c.Tl, c.D], BF16)
    agin_cnt = dint("agin_cnt", [c.segs, c.E], F32)
    cnt_full = dint("cnt_full", [c.nseg_tot, c.E], F32)
    arin_tab = dint("arin_tab", [c.E, 3 * c.CAP], F32)
    tab_full = dint("tab_full", [c.E, 3 * c.CAP], F32)
    agin_yb = dint("agin_yb", [c.EPC * c.CAP, c.D], BF16)
    eidx_d = dint("eidx_d", [c.EPC, c.CAP], I16)
    yb_full = dint("yb_full", [c.E * c.CAP, c.D], BF16)
    ybc = [yb_full[k * c.n_cores * c.EPCH * c.CAP:(k + 1) * c.n_cores * c.EPCH * c.CAP, :]
           for k in range(c.NCH)]
    KLISTS = 7
    klist_d = dint("klist_d", [2 * KLISTS, c.Tl], I16)

    S = {}
    CMS = {}

    def sbuf(name, shape, dt, scope=None):
        cm = nc.sbuf_tensor("sb_" + name, shape, dt)
        if scope is None:
            S[name] = stack.enter_context(cm)
        else:
            S[name] = cm.__enter__()
            CMS.setdefault(scope, []).append(cm)
        return S[name]

    def free_scope(scope):
        for cm in reversed(CMS.pop(scope, [])):
            cm.__exit__(None, None, None)

    sbuf("bqkv", [128, 3 * c.DC], F32)
    sbuf("bo_row", [1, c.D], F32)
    sbuf("gb", [128, c.E], F32)
    sbuf("Lup", [128, 132], BF16)
    sbuf("idf32", [128, 128], F32)
    sbuf("idbf", [128, 128], BF16)
    sbuf("ones1", [1, 128], F32)
    sbuf("segsel", [128, c.segs * c.segs], BF16)
    sbuf("slothi", [c.E, c.CAP], BF16)
    sbuf("slotlo", [c.E, c.CAP], BF16)
    sbuf("Ustrict", [c.E, c.E], BF16)
    sbuf("iota", [1, 512], F32)
    sbuf("maskm", [c.nseg_tot, c.segs], F32)
    sbuf("rowb", [128, 2], F32)
    sbuf("myexp", [c.EPC, 1], I32)
    sbuf("wstage", [128, c.D], F32)
    sbuf("stat", [128, 16], F32)
    sbuf("x1t", [128, 2 * c.D], F32)
    sbuf("gstat", [128, 4], F32)
    sbuf("sbt", [128, c.E], F32)
    sbuf("top8", [128, 8], F32)
    sbuf("ln2fmseg", [128, 2 * c.DC * c.SEG], BF16)
    sbuf("tilec", [c.E, c.tiles], F32)
    sbuf("segcnt", [c.segs, c.E], F32)
    sbuf("cntf", [c.nseg_tot, c.E], F32)
    sbuf("segoff", [c.E, c.segs], F32)
    sbuf("totoff", [c.E, c.tiles], F32)
    sbuf("tabf", [c.E, 3 * c.CAP], F32)
    sbuf("idsG", [c.E, c.CAP], I16)
    sbuf("wG", [c.E, c.CAP], BF16)
    sbuf("dl2", [128, 2 * KLISTS * (c.Tl // 16)], I16)

    P = {}
    for n_, sh in (("g0", [128, 512]), ("g1", [128, 512]), ("g2", [128, 512]),
                   ("g3", [128, 512]), ("psB", [128, c.SEG]), ("t0", [c.E, 132]),
                   ("tsm", [128, 128]), ("psC", [c.segs, c.E])):
        P[n_] = stack.enter_context(nc.psum_tensor(n_, sh, F32))

    ld = Stream(b, "ld", 12)
    stw = Stream(b, "st", 8)
    pda = b.sem("pda")
    pdb = Stream(b, "pdb", 8)
    pdc = Stream(b, "pdc", 4)
    pdd = Stream(b, "pdd", 4)
    cc1, cc2, cc3 = b.sem("cc1"), b.sem("cc2"), b.sem("cc3")
    cc4s = [b.sem(f"cc4{k}") for k in range(c.NCH)]
    CST = c.CAP
    RG = [list(range(c.n_cores))]

    for nm, o, i in (("bqkv", S["bqkv"], bqkv_in), ("bo_row", S["bo_row"], bo_in),
                     ("gb", S["gb"], gb_in), ("Lup", S["Lup"], Lup_in),
                     ("idf32", S["idf32"], idf_in), ("idbf", S["idbf"], idb_in),
                     ("ones1", S["ones1"], ones1_in), ("segsel", S["segsel"], segsel_in),
                     ("slothi", S["slothi"], slothi_in),
                     ("slotlo", S["slotlo"], slotlo_in),
                     ("Ustrict", S["Ustrict"], ustr_in),
                     ("iota", S["iota"], iota_in),
                     ("maskm", S["maskm"], maskm_in), ("rowb", S["rowb"], rowb_in),
                     ("myexp", S["myexp"], myexp_in)):
        b.do("sync", (lambda o_, i_: (lambda eng: eng.dma_start(out=o_[:, :], in_=i_[:, :])))(o, i),
             writes=[nm], dma=ld)

    sbuf("Ssum", [128, c.DC * 128], BF16, scope="A1")
    sbuf("Sb", [128, c.DC * 128], BF16, scope="A1")
    sbuf("attW", [128, 4 * c.DC * c.D], BF16, scope="A1")
    for nm, o_, i_ in (("Ssum", S["Ssum"], ssum_in), ("Sb", S["Sb"], sbsel_in)):
        b.do("sync", (lambda o2, i2: (lambda eng: eng.dma_start(out=o2[:, :], in_=i2[:, :])))(o_, i_),
             writes=[nm], dma=ld)
    sbuf("xt2", [128, 2 * c.D], F32, scope="A1")
    sbuf("sqdump", [128, c.D], BF16, scope="A1")
    sbuf("lnt", [128, 2 * c.D], BF16, scope="A1")
    sbuf("ln1fm", [128, 2 * c.DC * c.SEG], BF16, scope="A1")
    sbuf("Qfm", [128, c.DC * c.SEG], BF16, scope="A1")
    sbuf("Kfm", [128, c.DC * c.SEG], BF16, scope="A1")
    sbuf("Vfm", [128, c.DC * c.SEG], BF16, scope="A1")
    sbuf("Pch", [128, 2 * c.SEG], BF16, scope="A1")
    sbuf("praw", [128, c.SEG], F32, scope="A1")
    sbuf("probs", [128, c.SEG], BF16, scope="A1")
    sbuf("pstat", [128, 4], F32, scope="A1")

    for wi in range(4):
        for kc in range(c.DC):
            b.do("sync",
                 (lambda wi_, kc_: (lambda eng: eng.dma_start(
                     out=S["wstage"][:, 0:c.D],
                     in_=attw_in[wi_][kc_ * 128:(kc_ + 1) * 128, :])))(wi, kc),
                 writes=["wstage"], dma=ld)
            b.do("vector",
                 (lambda wi_, kc_: (lambda eng: eng.tensor_copy(
                     out=S["attW"][:, (wi_ * 8 + kc_) * c.D:(wi_ * 8 + kc_ + 1) * c.D],
                     in_=S["wstage"][:, 0:c.D])))(wi, kc),
                 reads=["wstage"], writes=[f"attW{wi}_{kc}"])

    def ln_tile(xt_ap, out_ap, so, out_res, in_res):
        st_ = S["stat"]
        b.do("scalar",
             (lambda xt_, so_: (lambda eng: eng.activation(
                 out=S["sqdump"][:, :], in_=xt_, func=AF.Square,
                 accum_out=st_[:, so_ * 8 + 1: so_ * 8 + 2])))(xt_ap, so),
             reads=list(in_res), writes=[f"stat{so}b", "sqdump"])
        b.do("vector",
             (lambda xt_, so_: (lambda eng: eng.reduce_sum(
                 out=st_[:, so_ * 8 + 0: so_ * 8 + 1], in_=xt_, axis=AX.X)))(xt_ap, so),
             reads=list(in_res), writes=[f"stat{so}a"])
        b.do("vector",
             (lambda so_: (lambda eng: eng.tensor_scalar_mul(
                 st_[:, so_ * 8 + 2: so_ * 8 + 3], st_[:, so_ * 8 + 0: so_ * 8 + 1],
                 1.0 / c.D)))(so),
             reads=[f"stat{so}a"], writes=[f"stat{so}m"])
        b.do("vector",
             (lambda so_: (lambda eng: eng.tensor_mul(
                 st_[:, so_ * 8 + 3: so_ * 8 + 4], st_[:, so_ * 8 + 2: so_ * 8 + 3],
                 st_[:, so_ * 8 + 2: so_ * 8 + 3])))(so),
             reads=[f"stat{so}m"], writes=[f"stat{so}q"])
        b.do("vector",
             (lambda so_: (lambda eng: eng.scalar_tensor_tensor(
                 out=st_[:, so_ * 8 + 4: so_ * 8 + 5], in0=st_[:, so_ * 8 + 1: so_ * 8 + 2],
                 scalar=1.0 / c.D, in1=st_[:, so_ * 8 + 3: so_ * 8 + 4],
                 op0=OP.mult, op1=OP.subtract)))(so),
             reads=[f"stat{so}b", f"stat{so}q"], writes=[f"stat{so}v"])
        b.do("vector",
             (lambda so_: (lambda eng: eng.tensor_scalar_add(
                 st_[:, so_ * 8 + 4: so_ * 8 + 5], st_[:, so_ * 8 + 4: so_ * 8 + 5],
                 1e-5)))(so),
             reads=[f"stat{so}v"], writes=[f"stat{so}v"])
        b.do("scalar",
             (lambda so_: (lambda eng: eng.activation(
                 out=st_[:, so_ * 8 + 6: so_ * 8 + 7], in_=st_[:, so_ * 8 + 4: so_ * 8 + 5],
                 func=AF.Sqrt)))(so),
             reads=[f"stat{so}v"], writes=[f"stat{so}s"])
        b.do("vector",
             (lambda so_: (lambda eng: eng.reciprocal(
                 out=st_[:, so_ * 8 + 5: so_ * 8 + 6],
                 in_=st_[:, so_ * 8 + 6: so_ * 8 + 7])))(so),
             reads=[f"stat{so}s"], writes=[f"stat{so}r"])
        b.do("vector",
             (lambda xt_, so_, o_: (lambda eng: eng.tensor_scalar(
                 out=o_, in0=xt_,
                 scalar1=st_[:, so_ * 8 + 2: so_ * 8 + 3],
                 scalar2=st_[:, so_ * 8 + 5: so_ * 8 + 6],
                 op0=OP.subtract, op1=OP.mult)))(xt_ap, so, out_ap),
             reads=list(in_res) + [f"stat{so}m", f"stat{so}r"], writes=[out_res])

    for ls in range(c.segs):
        q = ls % 2
        for tt in range(4):
            hh = tt % 2
            b.do("sync",
                 (lambda ls_, tt_, h_: (lambda eng: eng.dma_start(
                     out=S["xt2"][:, h_ * c.D:(h_ + 1) * c.D],
                     in_=x_in[ls_ * c.SEG + tt_ * 128: ls_ * c.SEG + (tt_ + 1) * 128, :])))(ls, tt, hh),
                 writes=[f"xt2{hh}"], dma=ld)
            xt = S["xt2"][:, hh * c.D:(hh + 1) * c.D]
            ln_tile(xt, S["lnt"][:, hh * c.D:(hh + 1) * c.D], hh, f"lnt{hh}", [f"xt2{hh}"])
            b.do("scalar",
                 (lambda ls_, tt_, h_: (lambda eng: eng.dma_start(
                     out=ln1_d[ls_ * c.SEG + tt_ * 128: ls_ * c.SEG + (tt_ + 1) * 128, :],
                     in_=S["lnt"][:, h_ * c.D:(h_ + 1) * c.D])))(ls, tt, hh),
                 reads=[f"lnt{hh}"], writes=[f"ln1_d{ls}_{tt}"], dma=stw)
        for kc in range(c.DC):
            b.do("sync",
                 (lambda ls_, kc_, q_: (lambda eng: eng.dma_start(
                     out=S["ln1fm"][:, (q_ * c.DC + kc_) * c.SEG:(q_ * c.DC + kc_ + 1) * c.SEG],
                     in_=ln1_d[ls_ * c.SEG:(ls_ + 1) * c.SEG, kc_ * 128:(kc_ + 1) * 128],
                     transpose=True)))(ls, kc, q),
                 reads=[f"ln1_d{ls}_{t2}" for t2 in range(4)],
                 writes=[f"ln1fm{q}_{kc}"], dma=ld)

        for pi, dst in enumerate(("Qfm", "Kfm", "Vfm")):
            for oc in range(c.DC):
                bank = f"g{(pi * c.DC + oc) % 2}"

                def qkv_group(eng, pi_=pi, oc_=oc, q_=q, bank_=bank):
                    ii = None
                    for kc in range(c.DC):
                        ii = eng.matmul(
                            P[bank_][:, :],
                            S["attW"][:, (pi_ * 8 + kc) * c.D + oc_ * 128:
                                      (pi_ * 8 + kc) * c.D + (oc_ + 1) * 128],
                            S["ln1fm"][:, (q_ * c.DC + kc) * c.SEG:(q_ * c.DC + kc + 1) * c.SEG],
                            start=(kc == 0), stop=(kc == c.DC - 1))
                    return ii

                b.do("tensor", qkv_group,
                     reads=[f"ln1fm{q}_{kc}" for kc in range(c.DC)] +
                           [f"attW{pi}_{kc}" for kc in range(c.DC)],
                     writes=[bank])
                b.do("scalar",
                     (lambda pi_, oc_, dst_, bank_: (lambda eng: eng.activation(
                         out=S[dst_][:, oc_ * c.SEG:(oc_ + 1) * c.SEG], in_=P[bank_][:, :],
                         func=AF.Identity,
                         bias=S["bqkv"][:, pi_ * 8 + oc_: pi_ * 8 + oc_ + 1],
                         scale=1.0)))(pi, oc, dst, bank),
                     reads=[bank, "bqkv"], writes=[f"{dst}_{oc}"])

        for kc in range(c.DC):
            pq = kc % 2
            b.do("vector",
                 (lambda kc_, pq_: (lambda eng: eng.tensor_mul(
                     S["Pch"][:, pq_ * c.SEG:(pq_ + 1) * c.SEG],
                     S["Qfm"][:, kc_ * c.SEG:(kc_ + 1) * c.SEG],
                     S["Kfm"][:, kc_ * c.SEG:(kc_ + 1) * c.SEG])))(kc, pq),
                 reads=[f"Qfm_{kc}", f"Kfm_{kc}"], writes=[f"Pch{pq}"])
            b.do("tensor",
                 (lambda kc_, pq_: (lambda eng: eng.matmul(
                     P["psB"][:, :], S["Ssum"][:, kc_ * 128:(kc_ + 1) * 128],
                     S["Pch"][:, pq_ * c.SEG:(pq_ + 1) * c.SEG],
                     start=(kc_ == 0), stop=(kc_ == c.DC - 1))))(kc, pq),
                 reads=[f"Pch{pq}", "Ssum"], writes=["psB"])

        b.do("vector", lambda eng: eng.memset(S["probs"][:, :], 0.0), writes=["probs"])
        for j in range(4):
            rows = slice(32 * j, 32 * j + 4)
            cols = slice(j, c.SEG, 4)
            b.do("vector",
                 (lambda r_, co_: (lambda eng: eng.tensor_reduce(
                     out=S["pstat"][r_, 0:1], in_=P["psB"][r_, co_],
                     axis=AX.X, op=OP.max, negate=True)))(rows, cols),
                 reads=["psB"], writes=[f"pstat{j}a"])
            b.do("scalar",
                 (lambda r_, co_: (lambda eng: eng.activation(
                     out=S["praw"][r_, co_], in_=P["psB"][r_, co_],
                     func=AF.Exp, bias=S["pstat"][r_, 0:1], scale=1.0,
                     accum_out=S["pstat"][r_, 1:2])))(rows, cols),
                 reads=["psB", f"pstat{j}a"], writes=[f"praw{j}", f"pstat{j}b"])
            b.do("vector",
                 (lambda r_: (lambda eng: eng.reciprocal(
                     out=S["pstat"][r_, 2:3], in_=S["pstat"][r_, 1:2])))(rows),
                 reads=[f"pstat{j}b"], writes=[f"pstat{j}c"])
            b.do("vector",
                 (lambda r_, co_: (lambda eng: eng.tensor_scalar_mul(
                     S["probs"][r_, co_], S["praw"][r_, co_],
                     S["pstat"][r_, 2:3])))(rows, cols),
                 reads=[f"praw{j}", f"pstat{j}c", "probs"], writes=[f"probsw{j}"])

        for kc in range(c.DC):
            bank = f"g{2 + kc % 2}"
            b.do("tensor",
                 (lambda kc_, bank_: (lambda eng: eng.matmul(
                     P[bank_][:, :], S["Sb"][:, kc_ * 128:(kc_ + 1) * 128],
                     S["probs"][:, :], start=True, stop=True)))(kc, bank),
                 reads=["probs", "Sb"] + [f"probsw{j}" for j in range(4)],
                 writes=[bank])
            b.do("vector",
                 (lambda kc_, bank_: (lambda eng: eng.tensor_mul(
                     S["Qfm"][:, kc_ * c.SEG:(kc_ + 1) * c.SEG], P[bank_][:, :],
                     S["Vfm"][:, kc_ * c.SEG:(kc_ + 1) * c.SEG])))(kc, bank),
                 reads=[bank, f"Vfm_{kc}", f"Qfm_{kc}"], writes=[f"attnfm_{kc}"])

        for tt in range(4):
            hh = tt % 2
            b.do("sync",
                 (lambda ls_, tt_, h_: (lambda eng: eng.dma_start(
                     out=S["xt2"][:, h_ * c.D:(h_ + 1) * c.D],
                     in_=x_in[ls_ * c.SEG + tt_ * 128: ls_ * c.SEG + (tt_ + 1) * 128, :])))(ls, tt, hh),
                 writes=[f"xt2{hh}"], dma=ld)
            for nh in range(2):
                bank = f"g{nh}"

                def oproj_group(eng, tt_=tt, nh_=nh, bank_=bank):
                    for kc in range(c.DC):
                        eng.matmul(
                            P[bank_][:, :],
                            S["Qfm"][:, kc * c.SEG + tt_ * 128: kc * c.SEG + (tt_ + 1) * 128],
                            S["attW"][:, (3 * 8 + kc) * c.D + nh_ * 512:
                                      (3 * 8 + kc) * c.D + (nh_ + 1) * 512],
                            start=(kc == 0), stop=False)
                    return eng.matmul(P[bank_][:, :], S["ones1"][:, :],
                                      S["bo_row"][:, nh_ * 512:(nh_ + 1) * 512],
                                      start=False, stop=True)

                b.do("tensor", oproj_group,
                     reads=[f"attnfm_{kc}" for kc in range(c.DC)] +
                           [f"Qfm_{kc}" for kc in range(c.DC)] +
                           [f"attW3_{kc}" for kc in range(c.DC)] + ["ones1", "bo_row"],
                     writes=[bank])
                b.do("vector",
                     (lambda bank_, tt_, nh_, h_: (lambda eng: eng.tensor_add(
                         S["x1t"][:, h_ * c.D + nh_ * 512: h_ * c.D + (nh_ + 1) * 512],
                         P[bank_][:, :],
                         S["xt2"][:, h_ * c.D + nh_ * 512: h_ * c.D + (nh_ + 1) * 512])))(bank, tt, nh, hh),
                     reads=[bank, f"xt2{hh}"], writes=[f"x1t{hh}_{nh}"])
            b.do("sync",
                 (lambda ls_, tt_, h_: (lambda eng: eng.dma_start(
                     out=x1_d[ls_ * c.SEG + tt_ * 128: ls_ * c.SEG + (tt_ + 1) * 128, :],
                     in_=S["x1t"][:, h_ * c.D:(h_ + 1) * c.D])))(ls, tt, hh),
                 reads=[f"x1t{hh}_0", f"x1t{hh}_1"], writes=[f"x1_d{ls}_{tt}"], dma=ld)
            xt = S["x1t"][:, hh * c.D:(hh + 1) * c.D]
            ln_tile(xt, S["lnt"][:, hh * c.D:(hh + 1) * c.D], hh,
                    f"lnt{hh}", [f"x1t{hh}_0", f"x1t{hh}_1"])
            b.do("scalar",
                 (lambda ls_, tt_, h_: (lambda eng: eng.dma_start(
                     out=agin_ln2[ls_ * c.SEG + tt_ * 128: ls_ * c.SEG + (tt_ + 1) * 128, :],
                     in_=S["lnt"][:, h_ * c.D:(h_ + 1) * c.D])))(ls, tt, hh),
                 reads=[f"lnt{hh}"], writes=[f"agin{ls}_{tt}"], dma=stw)

    all_agin = [f"agin{ls}_{tt}" for ls in range(c.segs) for tt in range(4)]

    if _cut("AG"):
        b.fence()
        b.do("gpsimd", lambda eng: None, wait=[b.gwait])
        b.do("sync", lambda eng: None, wait=[b.gwait])
        b.emit(stack)
        nc.compile()
        return nc
    b.do("gpsimd",
         lambda eng: eng.collective_compute(
             "AllGather", OP.bypass, replica_groups=RG,
             ins=[agin_ln2[:, :]], outs=[ln2_full[:, :]]),
         reads=all_agin, writes=["ln2_full"], coll=cc1)

    if _cut("A2a"):
        b.fence()
        b.do("gpsimd", lambda eng: None, wait=[b.gwait])
        b.do("sync", lambda eng: None, wait=[b.gwait])
        b.emit(stack)
        nc.compile()
        return nc
    b.fence(exclude=["ln2_full"])
    free_scope("A1")

    sbuf("gwT", [128, c.DC * c.E], BF16, scope="R")
    sbuf("sc_t", [128, 2 * c.E], F32, scope="R")
    sbuf("M_t", [128, 2 * c.E], BF16, scope="R")
    QW = c.Tl // 8
    sbuf("MT", [c.E, c.Tl], BF16, scope="R")
    sbuf("scoresT", [c.E, c.Tl], BF16, scope="R")
    sbuf("rank16", [c.E, c.Tl], I16, scope="R")
    sbuf("scrF", [c.E, QW], F32, scope="R")
    sbuf("scrI", [c.E, QW], I32, scope="R")
    sbuf("maskh", [c.E, QW], F32, scope="R")
    sbuf("idx16", [c.E, c.Tl], I16, scope="R")
    sbuf("idx16b", [c.E, c.Tl], I16, scope="R")
    sbuf("ids16", [c.E, c.Tl], I16, scope="R")
    sbuf("kT", [c.E, c.Tl], BF16, scope="R")
    sbuf("kG", [c.E, c.CAP], BF16, scope="R")

    b.do("sync", lambda eng: eng.dma_start(out=S["ids16"][:, :], in_=ids16_in[:, :]),
         writes=["ids16"], dma=ld)
    b.do("sync",
         lambda eng: eng.dma_start(
             out=S["wstage"][:, 0:c.DC * c.E].rearrange("p (k e) -> p k e", k=c.DC),
             in_=gwT_in.rearrange("(k p) e -> p k e", p=128)),
         writes=["wstage"], dma=ld)
    b.do("vector",
         lambda eng: eng.tensor_copy(out=S["gwT"][:, :], in_=S["wstage"][:, 0:c.DC * c.E]),
         reads=["wstage"], writes=["gwT"])

    for ls in range(c.segs):
        q = ls % 2
        for kc in range(c.DC):
            b.do("sync",
                 (lambda ls_, kc_, q_: (lambda eng: eng.dma_start(
                     out=S["ln2fmseg"][:, (q_ * c.DC + kc_) * c.SEG:
                                       (q_ * c.DC + kc_ + 1) * c.SEG],
                     in_=agin_ln2[ls_ * c.SEG:(ls_ + 1) * c.SEG, kc_ * 128:(kc_ + 1) * 128],
                     transpose=True)))(ls, kc, q),
                 reads=[f"agin{ls}_{t2}" for t2 in range(4)],
                 writes=[f"l2s{q}_{kc}"], dma=ld)
        for tt in range(4):
            ti = 4 * ls + tt
            tp = ti % 2
            bank = f"g{tp}"

            def gate_group(eng, tt_=tt, q_=q, bank_=bank):
                ii = None
                for kc in range(c.DC):
                    ii = eng.matmul(
                        P[bank_][:, 0:c.E],
                        S["ln2fmseg"][:, (q_ * c.DC + kc) * c.SEG + tt_ * 128:
                                      (q_ * c.DC + kc) * c.SEG + (tt_ + 1) * 128],
                        S["gwT"][:, kc * c.E:(kc + 1) * c.E],
                        start=(kc == 0), stop=(kc == c.DC - 1))
                return ii

            b.do("tensor", gate_group,
                 reads=[f"l2s{q}_{kc}" for kc in range(c.DC)] + ["gwT"], writes=[bank])
            b.do("vector",
                 (lambda bank_: (lambda eng: eng.tensor_reduce(
                     out=S["gstat"][:, 0:1], in_=P[bank_][:, 0:c.E], axis=AX.X,
                     op=OP.max, negate=True)))(bank),
                 reads=[bank], writes=["gstat0"])
            b.do("scalar",
                 (lambda bank_, tp_: (lambda eng: eng.activation(
                     out=S["sc_t"][:, tp_ * c.E:(tp_ + 1) * c.E], in_=P[bank_][:, 0:c.E],
                     func=AF.Exp, bias=S["gstat"][:, 0:1], scale=1.0,
                     accum_out=S["gstat"][:, 1:2])))(bank, tp),
                 reads=[bank, "gstat0"], writes=[f"scraw{tp}", "gstat1"])
            b.do("vector",
                 lambda eng: eng.reciprocal(out=S["gstat"][:, 2:3], in_=S["gstat"][:, 1:2]),
                 reads=["gstat1"], writes=["gstat2"])
            b.do("vector",
                 (lambda tp_: (lambda eng: eng.tensor_scalar_mul(
                     S["sc_t"][:, tp_ * c.E:(tp_ + 1) * c.E],
                     S["sc_t"][:, tp_ * c.E:(tp_ + 1) * c.E], S["gstat"][:, 2:3])))(tp),
                 reads=[f"scraw{tp}", "gstat2"], writes=[f"sct{tp}"])
            b.do("vector",
                 (lambda tp_: (lambda eng: eng.tensor_add(
                     S["sbt"][:, :], S["sc_t"][:, tp_ * c.E:(tp_ + 1) * c.E],
                     S["gb"][:, :])))(tp),
                 reads=[f"sct{tp}", "gb"], writes=["sbt"])
            b.do("vector", lambda eng: eng.max(out=S["top8"][:, :], in_=S["sbt"][:, :]),
                 reads=["sbt"], writes=["top8"])
            b.do("vector",
                 (lambda tp_: (lambda eng: eng.tensor_scalar(
                     out=S["M_t"][:, tp_ * c.E:(tp_ + 1) * c.E], in0=S["sbt"][:, :],
                     scalar1=S["top8"][:, 5:6], scalar2=None, op0=OP.is_ge)))(tp),
                 reads=["sbt", "top8"], writes=[f"Mt{tp}"])
            b.do("tensor",
                 (lambda tp_: (lambda eng: eng.transpose(
                     P["tsm"][0:c.E, 0:128], S["sc_t"][:, tp_ * c.E:(tp_ + 1) * c.E],
                     S["idf32"][:, :])))(tp),
                 reads=[f"sct{tp}", "idf32"], writes=["tsm"])
            b.do("vector",
                 (lambda ti_: (lambda eng: eng.tensor_copy(
                     out=S["scoresT"][:, ti_ * 128:(ti_ + 1) * 128],
                     in_=P["tsm"][0:c.E, 0:128])))(ti),
                 reads=["tsm"], writes=[f"scT{ti}"])
            b.do("tensor",
                 (lambda tp_: (lambda eng: eng.transpose(
                     P["tsm"][0:c.E, 0:64].bitcast(BF16),
                     S["M_t"][:, tp_ * c.E:(tp_ + 1) * c.E],
                     S["idbf"][:, :])))(tp),
                 reads=[f"Mt{tp}", "idbf"], writes=["tsm"])
            b.do("vector",
                 (lambda ti_: (lambda eng: eng.tensor_copy(
                     out=S["MT"][:, ti_ * 128:(ti_ + 1) * 128],
                     in_=P["tsm"][0:c.E, 0:64].bitcast(BF16))))(ti),
                 reads=["tsm"], writes=[f"MTc{ti}"])
            b.do("tensor",
                 (lambda ti_: (lambda eng: eng.matmul(
                     P["tsm"][0:c.E, 0:128], S["Ustrict"][:, :],
                     S["MT"][:, ti_ * 128:(ti_ + 1) * 128],
                     start=True, stop=True)))(ti),
                 reads=[f"MTc{ti}", "Ustrict"], writes=["tsm"])
            b.do("vector",
                 (lambda ti_: (lambda eng: eng.tensor_scalar_add(
                     S["kT"][:, ti_ * 128:(ti_ + 1) * 128],
                     P["tsm"][0:c.E, 0:128], 1.0)))(ti),
                 reads=["tsm"], writes=[f"kTc{ti}"])
            b.do("tensor",
                 (lambda tp_: (lambda eng: eng.matmul(
                     P["t0"][:, 0:129], S["M_t"][:, tp_ * c.E:(tp_ + 1) * c.E],
                     S["Lup"][:, 0:129], start=True, stop=True)))(tp),
                 reads=[f"Mt{tp}", "Lup"], writes=["t0"])
            b.do("vector",
                 (lambda ti_: (lambda eng: eng.tensor_copy(
                     out=S["rank16"][:, ti_ * 128:(ti_ + 1) * 128],
                     in_=P["t0"][:, 0:128])))(ti),
                 reads=["t0"], writes=["rkF"])
            b.do("vector",
                 (lambda ti_: (lambda eng: eng.tensor_copy(
                     out=S["tilec"][:, ti_:ti_ + 1], in_=P["t0"][:, 128:129])))(ti),
                 reads=["t0"], writes=[f"tilec{ti}"])
            b.do("tensor",
                 (lambda tp_, ls_, fst, lst: (lambda eng: eng.matmul(
                     P["psC"][:, :], S["segsel"][:, ls_ * c.segs:(ls_ + 1) * c.segs],
                     S["M_t"][:, tp_ * c.E:(tp_ + 1) * c.E],
                     start=fst, stop=lst)))(tp, ls, ti == 0, ti == c.tiles - 1),
                 reads=[f"Mt{tp}", "segsel"], writes=["psC"])

    b.do("vector", lambda eng: eng.tensor_copy(out=S["segcnt"][:, :], in_=P["psC"][:, :]),
         reads=["psC"], writes=["segcnt"])
    b.do("sync", lambda eng: eng.dma_start(out=agin_cnt[:, :], in_=S["segcnt"][:, :]),
         reads=["segcnt"], writes=["agin_cnt"], dma=ld)
    b.do("gpsimd",
         lambda eng: eng.collective_compute(
             "AllGather", OP.bypass, replica_groups=RG,
             ins=[agin_cnt[:, :]], outs=[cnt_full[:, :]]),
         reads=["agin_cnt"], writes=["cnt_full"], coll=cc2)
    sbuf("zW", [128, 3 * 11264], BF16, scope="Z")
    sbuf("zh", [128, c.SIC * c.SEG], BF16, scope="Z")
    sbuf("silt", [128, c.SEG], F32, scope="Z")

    b.do("gpsimd",
         lambda eng: eng.dma_start(
             out=S["zW"][:, 0:c.DC * c.SI].rearrange("p (k i) -> p k i", k=c.DC),
             in_=ws1T_in.rearrange("(k p) i -> p k i", p=128)),
         writes=[f"zW0_{kc}" for kc in range(c.DC)], dma=pdb)
    b.do("gpsimd",
         lambda eng: eng.dma_start(
             out=S["zW"][:, 11264:11264 + c.DC * c.SI].rearrange(
                 "p (k i) -> p k i", k=c.DC),
             in_=ws3T_in.rearrange("(k p) i -> p k i", p=128)),
         writes=[f"zW1_{kc}" for kc in range(c.DC)], dma=pdb)
    b.do("gpsimd",
         lambda eng: eng.dma_start(
             out=S["zW"][:, 22528:22528 + c.SIC * c.D].rearrange(
                 "p (k i) -> p k i", k=c.SIC),
             in_=ws2T_in.rearrange("(k p) i -> p k i", p=128)),
         writes=[f"zW2_{ic}" for ic in range(c.SIC)], dma=pdb)

    def z_seg(ls):
        q = ls % 2
        for kc in range(c.DC):
            b.do("sync",
                 (lambda ls_, kc_, q_: (lambda eng: eng.dma_start(
                     out=S["ln2fmseg"][:, (q_ * c.DC + kc_) * c.SEG:
                                       (q_ * c.DC + kc_ + 1) * c.SEG],
                     in_=agin_ln2[ls_ * c.SEG:(ls_ + 1) * c.SEG, kc_ * 128:(kc_ + 1) * 128],
                     transpose=True)))(ls, kc, q),
                 reads=[f"agin{ls}_{t2}" for t2 in range(4)],
                 writes=[f"l2s{q}_{kc}"], dma=ld)
        for ic in range(c.SIC):
            for zi in range(2):
                bank = f"g{zi}"

                def z_group(eng, ic_=ic, zi_=zi, q_=q, bank_=bank):
                    ii = None
                    for kc in range(c.DC):
                        ii = eng.matmul(
                            P[bank_][:, :],
                            S["zW"][:, zi_ * 11264 + kc * c.SI + ic_ * 128:
                                    zi_ * 11264 + kc * c.SI + (ic_ + 1) * 128],
                            S["ln2fmseg"][:, (q_ * c.DC + kc) * c.SEG:
                                          (q_ * c.DC + kc + 1) * c.SEG],
                            start=(kc == 0), stop=(kc == c.DC - 1))
                    return ii

                b.do("tensor", z_group,
                     reads=[f"zW{zi}_{kc}" for kc in range(c.DC)] +
                           [f"l2s{q}_{kc}" for kc in range(c.DC)],
                     writes=[bank])
            b.do("scalar",
                 lambda eng: eng.activation(out=S["silt"][:, :], in_=P["g0"][:, :], func=AF.Silu),
                 reads=["g0"], writes=["silt0"])
            b.do("vector",
                 (lambda ic_: (lambda eng: eng.tensor_mul(
                     S["zh"][:, ic_ * c.SEG:(ic_ + 1) * c.SEG], S["silt"][:, :],
                     P["g1"][:, :])))(ic),
                 reads=["silt0", "g1"], writes=[f"zh{ic}"])
        for tt in range(4):
            hh = tt % 2
            b.do("sync",
                 (lambda ls_, tt_, h_: (lambda eng: eng.dma_start(
                     out=S["x1t"][:, h_ * c.D:(h_ + 1) * c.D],
                     in_=x1_d[ls_ * c.SEG + tt_ * 128: ls_ * c.SEG + (tt_ + 1) * 128, :])))(ls, tt, hh),
                 reads=[f"x1_d{ls}_{tt}"], writes=[f"x1t{hh}_0", f"x1t{hh}_1"], dma=ld)
            for nh in range(2):
                bank = f"g{2 + nh}"

                def z2_group(eng, tt_=tt, nh_=nh, bank_=bank):
                    ii = None
                    for ic in range(c.SIC):
                        ii = eng.matmul(
                            P[bank_][:, :],
                            S["zh"][:, ic * c.SEG + tt_ * 128: ic * c.SEG + (tt_ + 1) * 128],
                            S["zW"][:, 22528 + ic * c.D + nh_ * 512:
                                    22528 + ic * c.D + (nh_ + 1) * 512],
                            start=(ic == 0), stop=(ic == c.SIC - 1))
                    return ii

                b.do("tensor", z2_group,
                     reads=[f"zh{ic}" for ic in range(c.SIC)] +
                           [f"zW2_{ic}" for ic in range(c.SIC)],
                     writes=[bank])
                b.do("vector",
                     (lambda bank_, nh_, h_: (lambda eng: eng.tensor_add(
                         S["x1t"][:, h_ * c.D + nh_ * 512: h_ * c.D + (nh_ + 1) * 512],
                         P[bank_][:, :],
                         S["x1t"][:, h_ * c.D + nh_ * 512: h_ * c.D + (nh_ + 1) * 512])))(bank, nh, hh),
                     reads=[bank, f"x1t{hh}_0", f"x1t{hh}_1"], writes=[f"x1t{hh}_{nh}"])
            b.do("scalar",
                 (lambda ls_, tt_, h_: (lambda eng: eng.dma_start(
                     out=out[ls_ * c.SEG + tt_ * 128: ls_ * c.SEG + (tt_ + 1) * 128, :],
                     in_=S["x1t"][:, h_ * c.D:(h_ + 1) * c.D])))(ls, tt, hh),
                 reads=[f"x1t{hh}_0", f"x1t{hh}_1"], writes=[f"out{ls}_{tt}"], dma=stw)

    zsplit = max(1, c.segs - 3)
    for ls in range(zsplit):
        z_seg(ls)

    b.do("sync", lambda eng: eng.dma_start(out=S["cntf"][:, :], in_=cnt_full[:, :]),
         reads=["cnt_full"], writes=["cntf"], dma=ld)

    b.do("tensor",
         lambda eng: eng.matmul(P["tsm"][0:c.E, 0:c.segs], S["cntf"][:, :],
                                S["maskm"][:, :], start=True, stop=True),
         reads=["cntf", "maskm"], writes=["tsm"])
    b.do("vector",
         lambda eng: eng.tensor_copy(out=S["segoff"][:, :], in_=P["tsm"][0:c.E, 0:c.segs]),
         reads=["tsm"], writes=["segoff"])
    for ls in range(c.segs):
        b.do("vector",
             (lambda ls_: (lambda eng: eng.tensor_copy(
                 out=S["totoff"][:, 4 * ls_:4 * ls_ + 1], in_=S["segoff"][:, ls_:ls_ + 1])))(ls),
             reads=["segoff"], writes=[f"totoff{4 * ls}"])
        for j in range(3):
            b.do("vector",
                 (lambda ls_, j_: (lambda eng: eng.tensor_add(
                     S["totoff"][:, 4 * ls_ + j_ + 1:4 * ls_ + j_ + 2],
                     S["totoff"][:, 4 * ls_ + j_:4 * ls_ + j_ + 1],
                     S["tilec"][:, 4 * ls_ + j_:4 * ls_ + j_ + 1])))(ls, j),
                 reads=[f"totoff{4 * ls + j}", f"tilec{4 * ls + j}"],
                 writes=[f"totoff{4 * ls + j + 1}"])
    def V(fn, reads, writes):
        b.do("vector", fn, reads=reads, writes=writes)

    TPQ = QW // 128
    for h4 in range(c.Tl // QW):
        sl = slice(h4 * QW, (h4 + 1) * QW)
        V((lambda sl_: (lambda eng: eng.tensor_copy(
            out=S["scrF"][:, :], in_=S["rank16"][:, sl_])))(sl),
          ["rkF"], ["scrF"])
        for tl in range(TPQ):
            ti = h4 * TPQ + tl
            V((lambda tl_, ti_: (lambda eng: eng.tensor_scalar_add(
                S["scrF"][:, tl_ * 128:(tl_ + 1) * 128],
                S["scrF"][:, tl_ * 128:(tl_ + 1) * 128],
                S["totoff"][:, ti_:ti_ + 1])))(tl, ti),
              ["scrF", f"totoff{ti}"], ["scrF"])
        V(lambda eng: eng.tensor_copy(out=S["scrI"][:, :], in_=S["scrF"][:, :]),
          ["scrF"], ["scrI"])
        V(lambda eng: eng.tensor_scalar(out=S["maskh"][:, :], in0=S["scrF"][:, :],
                                        scalar1=float(c.CAP), scalar2=None,
                                        op0=OP.is_lt),
          ["scrF"], ["maskH"])
        V((lambda sl_: (lambda eng: eng.tensor_mul(
            S["maskh"][:, :], S["maskh"][:, :], S["MT"][:, sl_])))(sl),
          ["maskH"] + [f"MTc{h4 * TPQ + tl}" for tl in range(TPQ)], ["maskH"])
        V(lambda eng: eng.tensor_scalar_add(S["scrF"][:, :], S["scrF"][:, :], 1.0),
          ["scrF"], ["scrF"])
        V(lambda eng: eng.tensor_mul(S["scrF"][:, :], S["scrF"][:, :], S["maskh"][:, :]),
          ["scrF", "maskH"], ["scrF"])
        V((lambda sl_: (lambda eng: eng.tensor_scalar_add(
            S["idx16"][:, sl_], S["scrF"][:, :], -1.0)))(sl),
          ["scrF"], ["idx16F"])
        V(lambda eng: eng.tensor_copy(out=S["scrF"][:, :], in_=S["scrI"][:, :]),
          ["scrI"], ["scrF"])
        V(lambda eng: eng.tensor_scalar(out=S["scrI"][:, :], in0=S["scrI"][:, :],
                                        scalar1=15, scalar2=None, op0=OP.bitwise_and),
          ["scrI"], ["scrI"])
        V(lambda eng: eng.tensor_copy(out=S["maskh"][:, :], in_=S["scrI"][:, :]),
          ["scrI"], ["maskH"])
        V(lambda eng: eng.tensor_sub(S["scrF"][:, :], S["scrF"][:, :], S["maskh"][:, :]),
          ["scrF", "maskH"], ["scrF"])
        V(lambda eng: eng.tensor_scalar_mul(S["scrF"][:, :], S["scrF"][:, :], 0.0625),
          ["scrF"], ["scrF"])
        V(lambda eng: eng.scalar_tensor_tensor(
            out=S["scrF"][:, :], in0=S["maskh"][:, :], scalar=32.0,
            in1=S["scrF"][:, :], op0=OP.mult, op1=OP.add),
          ["scrF", "maskH"], ["scrF"])
        V((lambda sl_: (lambda eng: eng.tensor_copy(
            out=S["maskh"][:, :], in_=S["idx16"][:, sl_])))(sl),
          ["idx16F"], ["maskH"])
        V(lambda eng: eng.tensor_scalar(
            out=S["maskh"][:, :], in0=S["maskh"][:, :],
            scalar1=0.0, scalar2=None, op0=OP.is_ge),
          ["maskH"], ["maskH"])
        V(lambda eng: eng.tensor_scalar_add(S["scrF"][:, :], S["scrF"][:, :], 1.0),
          ["scrF"], ["scrF"])
        V(lambda eng: eng.tensor_mul(S["scrF"][:, :], S["scrF"][:, :], S["maskh"][:, :]),
          ["scrF", "maskH"], ["scrF"])
        V((lambda sl_: (lambda eng: eng.tensor_scalar_add(
            S["idx16b"][:, sl_], S["scrF"][:, :], -1.0)))(sl),
          ["scrF"], ["idx16bF"])
    ix_all = ["idx16F"]
    sct_all = [f"scT{ti}" for ti in range(c.tiles)]

    b.do("gpsimd",
         lambda eng: eng.local_scatter(S["idsG"][:, :], S["ids16"][:, :], S["idx16b"][:, :],
                                       c.E, c.CAP, c.Tl),
         reads=["ids16", "idx16bF"], writes=["idsG"])
    b.do("gpsimd",
         lambda eng: eng.local_scatter(S["wG"][:, :], S["scoresT"][:, :], S["idx16"][:, :],
                                       c.E, c.CAP, c.Tl),
         reads=ix_all + sct_all, writes=["wG"])
    b.do("gpsimd",
         lambda eng: eng.local_scatter(S["kG"][:, :], S["kT"][:, :], S["idx16b"][:, :],
                                       c.E, c.CAP, c.Tl),
         reads=["idx16bF"] + [f"kTc{ti}" for ti in range(c.tiles)], writes=["kG"])
    b.do("vector", lambda eng: eng.tensor_copy(out=S["tabf"][:, 0:CST], in_=S["idsG"][:, :]),
         reads=["idsG"], writes=["tabf_a"])
    b.do("vector", lambda eng: eng.tensor_copy(out=S["tabf"][:, CST:2 * CST], in_=S["wG"][:, :]),
         reads=["wG"], writes=["tabf_b"])
    b.do("vector", lambda eng: eng.tensor_copy(out=S["tabf"][:, 2 * CST:3 * CST], in_=S["kG"][:, :]),
         reads=["kG"], writes=["tabf_c"])
    b.do("sync", lambda eng: eng.dma_start(out=arin_tab[:, :], in_=S["tabf"][:, :]),
         reads=["tabf_a", "tabf_b", "tabf_c"], writes=["arin_tab"], dma=ld)
    b.do("gpsimd",
         lambda eng: eng.collective_compute(
             "AllReduce", OP.add, replica_groups=RG,
             ins=[arin_tab[:, :]], outs=[tab_full[:, :]]),
         reads=["arin_tab"], writes=["tab_full"], coll=cc3)

    for ls in range(zsplit, c.segs):
        z_seg(ls)
    out_all = [f"out{ls}_{tt}" for ls in range(c.segs) for tt in range(4)]

    b.fence(exclude=["ln2_full", "tab_full", "cnt_full"])
    free_scope("Z")
    free_scope("R")

    if _cut("C"):
        b.fence()
        b.do("gpsimd", lambda eng: None, wait=[b.gwait])
        b.do("sync", lambda eng: None, wait=[b.gwait])
        b.emit(stack)
        nc.compile()
        return nc
    EWP = 2 * c.DC * c.I + c.IC * c.D
    sbuf("expW", [128, 2 * EWP], BF16, scope="C")
    sbuf("xbfm", [128, 2 * c.DC * c.CAP], BF16, scope="C")
    sbuf("hh", [128, c.IC * c.CAP], BF16, scope="C")
    sbuf("ybt", [128, 2 * c.D], BF16, scope="C")
    sbuf("siltC", [128, c.CAP], F32, scope="C")
    sbuf("myidw", [c.EPC, 3 * c.CAP], F32, scope="C")
    sbuf("myids16", [c.EPC, c.CAP], I16, scope="C")
    sbuf("wTslot", [128, 4 * c.EPC], F32, scope="C")
    sbuf("eidx", [128, c.EPC * (c.CAP // 16)], I16, scope="C")

    b.do("gpsimd",
         lambda eng: eng.indirect_dma_start(
             S["myidw"][:, :], None, tab_full[:, :],
             IndirectOffsetOnAxis(S["myexp"][:, :], 0)),
         reads=["tab_full", "myexp"], writes=["myidw"], dma=pda)
    for sb4 in range(4):
        b.do("tensor",
             (lambda s4: (lambda eng: eng.transpose(
                 P["tsm"][0:128, 0:c.EPC],
                 S["myidw"][:, CST + s4 * 128: CST + (s4 + 1) * 128],
                 S["idf32"][0:c.EPC, 0:c.EPC])))(sb4),
             reads=["myidw", "idf32"], writes=["tsm"])
        b.do("vector",
             (lambda s4: (lambda eng: eng.tensor_copy(
                 out=S["wTslot"][:, s4 * c.EPC:(s4 + 1) * c.EPC],
                 in_=P["tsm"][0:128, 0:c.EPC])))(sb4),
             reads=["tsm"], writes=[f"wTslot{sb4}"])
    b.do("vector",
         lambda eng: eng.tensor_copy(out=S["myids16"][:, :], in_=S["myidw"][:, 0:CST]),
         reads=["myidw"], writes=["myids16"])
    b.do("scalar",
         lambda eng: eng.dma_start(out=eidx_d[:, :], in_=S["myids16"][:, :]),
         reads=["myids16"], writes=["eidx_d"], dma=stw)
    b.do("vector", lambda eng: eng.memset(S["eidx"][:, :], 0), writes=["eidx_z"])
    b.do("sync",
         lambda eng: eng.dma_start(
             out=S["eidx"][0:16, :].rearrange("p (e s) -> p e s", e=c.EPC),
             in_=eidx_d.rearrange("e (p s) -> p e s", p=16)),
         reads=["eidx_d", "eidx_z"], writes=["eidx_a"], dma=ld)
    b.do("sync",
         lambda eng: eng.dma_start(
             out=S["eidx"][16:32, :].rearrange("p (e s) -> p e s", e=c.EPC),
             in_=eidx_d.rearrange("e (p s) -> p e s", p=16)),
         reads=["eidx_d", "eidx_z"], writes=["eidx_b"], dma=ld)

    for j in range(c.EPC):
        par = j % 2
        eb = par * EWP
        b.do("gpsimd",
             (lambda j_, par_: (lambda eng: eng.dma_gather(
                 S["xbfm"][:, par_ * c.DC * c.CAP:(par_ + 1) * c.DC * c.CAP].rearrange(
                     "p (k t) -> p k t", k=c.DC),
                 ln2_full[:, :],
                 S["eidx"][:, j_ * (c.CAP // 16):(j_ + 1) * (c.CAP // 16)],
                 c.CAP, c.CAP, c.D, transpose=True)))(j, par),
             reads=["eidx_a", "eidx_b", "ln2_full"], writes=[f"xbfm{par}"], dma=pdb)
        b.do("gpsimd",
             (lambda j_, eb_: (lambda eng: eng.dma_start(
                 out=S["expW"][:, eb_: eb_ + c.DC * c.I].rearrange(
                     "p (k i) -> p k i", k=c.DC),
                 in_=w1T_in[j_].rearrange("(k p) i -> p k i", p=128))))(j, eb),
             writes=[f"expWa{par}"], dma=pdb)
        b.do("gpsimd",
             (lambda j_, eb_: (lambda eng: eng.dma_start(
                 out=S["expW"][:, eb_ + c.DC * c.I: eb_ + 2 * c.DC * c.I].rearrange(
                     "p (k i) -> p k i", k=c.DC),
                 in_=w3T_in[j_].rearrange("(k p) i -> p k i", p=128))))(j, eb),
             writes=[f"expWb{par}"], dma=pdb)
        nfull = c.I // 128
        b.do("gpsimd",
             (lambda j_, eb_, nf: (lambda eng: eng.dma_start(
                 out=S["expW"][:, eb_ + 2 * c.DC * c.I:
                               eb_ + 2 * c.DC * c.I + nf * c.D].rearrange(
                     "p (k i) -> p k i", k=nf),
                 in_=w2T_in[j_][0:nf * 128, :].rearrange("(k p) i -> p k i", p=128))))(j, eb, nfull),
             writes=[f"expWc{par}"], dma=pdb)
        b.do("gpsimd",
             (lambda j_, eb_, nf: (lambda eng: eng.dma_start(
                 out=S["expW"][0:c.I - nf * 128,
                               eb_ + 2 * c.DC * c.I + nf * c.D:
                               eb_ + 2 * c.DC * c.I + (nf + 1) * c.D],
                 in_=w2T_in[j_][nf * 128:, :])))(j, eb, nfull),
             writes=[f"expWc2{par}"], dma=pdb)

        for ic in range(c.IC):
            icw = 128 if ic < c.IC - 1 else c.I - 128 * (c.IC - 1)
            for zi in range(2):
                bank = f"g{zi}"

                def h_group(eng, ic_=ic, zi_=zi, bank_=bank, par_=par, icw_=icw, eb_=eb):
                    ii = None
                    for kc in range(c.DC):
                        ii = eng.matmul(
                            P[bank_][0:icw_, :],
                            S["expW"][:, eb_ + zi_ * c.DC * c.I + kc * c.I + ic_ * 128:
                                      eb_ + zi_ * c.DC * c.I + kc * c.I + ic_ * 128 + icw_],
                            S["xbfm"][:, (par_ * c.DC + kc) * c.CAP:
                                      (par_ * c.DC + kc + 1) * c.CAP],
                            start=(kc == 0), stop=(kc == c.DC - 1))
                    return ii

                b.do("tensor", h_group,
                     reads=[f"expWa{par}", f"expWb{par}", f"xbfm{par}"],
                     writes=[bank])
            b.do("scalar",
                 (lambda icw_: (lambda eng: eng.activation(
                     out=S["siltC"][0:icw_, :], in_=P["g0"][0:icw_, :], func=AF.Silu)))(icw),
                 reads=["g0"], writes=["siltC0"])
            b.do("vector",
                 (lambda ic_, icw_: (lambda eng: eng.tensor_mul(
                     S["hh"][0:icw_, ic_ * c.CAP:(ic_ + 1) * c.CAP],
                     S["siltC"][0:icw_, :], P["g1"][0:icw_, :])))(ic, icw),
                 reads=["siltC0", "g1"], writes=[f"hhc{ic}"])

        for sb4 in range(4):
            yh = sb4 % 2
            for nh in range(2):
                bank = f"g{2 + nh}"

                def y_group(eng, sb4_=sb4, nh_=nh, bank_=bank, eb_=eb):
                    ii = None
                    for ic in range(c.IC):
                        icw = 128 if ic < c.IC - 1 else c.I - 128 * (c.IC - 1)
                        ii = eng.matmul(
                            P[bank_][:, :],
                            S["hh"][0:icw, ic * c.CAP + sb4_ * 128: ic * c.CAP + (sb4_ + 1) * 128],
                            S["expW"][0:icw, eb_ + 2 * c.DC * c.I + ic * c.D + nh_ * 512:
                                      eb_ + 2 * c.DC * c.I + ic * c.D + (nh_ + 1) * 512],
                            start=(ic == 0), stop=(ic == c.IC - 1))
                    return ii

                b.do("tensor", y_group,
                     reads=[f"hhc{ic}" for ic in range(c.IC)] +
                           [f"expWc{par}", f"expWc2{par}"],
                     writes=[bank])
                b.do("vector",
                     (lambda bank_, sb4_, nh_, j_, yh_: (lambda eng: eng.tensor_scalar_mul(
                         S["ybt"][:, yh_ * c.D + nh_ * 512: yh_ * c.D + (nh_ + 1) * 512],
                         P[bank_][:, :],
                         S["wTslot"][:, sb4_ * c.EPC + j_: sb4_ * c.EPC + j_ + 1])))(bank, sb4, nh, j, yh),
                     reads=[bank, f"wTslot{sb4}"], writes=[f"ybt{yh}_{nh}"])
            b.do("scalar",
                 (lambda j_, sb4_, yh_: (lambda eng: eng.dma_start(
                     out=agin_yb[j_ * c.CAP + sb4_ * 128: j_ * c.CAP + (sb4_ + 1) * 128, :],
                     in_=S["ybt"][:, yh_ * c.D:(yh_ + 1) * c.D])))(j, sb4, yh),
                 reads=[f"ybt{yh}_0", f"ybt{yh}_1"], writes=[f"agyb{j}_{sb4}"], dma=stw)

    b.do("gpsimd",
         lambda eng: eng.collective_compute(
             "AllGather", OP.bypass, replica_groups=RG,
             ins=[agin_yb[:, :]], outs=[yb_full[:, :]]),
         reads=[f"agyb{jj}_{s4}" for jj in range(c.EPC) for s4 in range(4)],
         writes=[f"ybc{k}" for k in range(c.NCH)], coll=cc4s[0])


    sbuf("hiG", [c.E, c.Tl], BF16, scope="C")
    sbuf("loG", [c.E, c.Tl], BF16, scope="C")
    sbuf("minef", [c.E, c.CAP], F32, scope="C")
    sbuf("mtmp", [c.E, c.CAP], F32, scope="C")
    sbuf("mtmp2", [c.E, c.CAP], F32, scope="C")
    sbuf("mtmp3", [c.E, c.CAP], F32, scope="C")
    sbuf("mtmpI", [c.E, c.CAP], I32, scope="C")
    sbuf("rankD", [c.E, c.CAP], I16, scope="C")
    sbuf("mq", [c.E, c.CAP], F32, scope="C")
    sbuf("mq2", [c.E, c.CAP], F32, scope="C")
    sbuf("his", [1, 512], F32, scope="C")
    sbuf("los", [1, 512], F32, scope="C")
    sbuf("vld", [1, 512], F32, scope="C")
    sbuf("klg", [1, 512], I16, scope="C")
    sbuf("klt", [1, 512], I16, scope="C")

    b.do("sync", lambda eng: eng.dma_start(out=S["tabf"][:, :], in_=tab_full[:, :]),
         reads=["tab_full"], writes=["tabf_a", "tabf_b", "tabf_c"], dma=ld)
    b.do("vector",
         lambda eng: eng.tensor_scalar(out=S["minef"][:, :], in0=S["tabf"][:, 0:CST],
                                       scalar1=S["rowb"][0:c.E, 0:1], scalar2=None,
                                       op0=OP.is_ge),
         reads=["tabf_a", "rowb"], writes=["minef"])
    b.do("vector",
         lambda eng: eng.tensor_scalar(out=S["mtmp"][:, :], in0=S["tabf"][:, 0:CST],
                                       scalar1=S["rowb"][0:c.E, 1:2], scalar2=None,
                                       op0=OP.is_lt),
         reads=["tabf_a", "rowb"], writes=["mtmp"])
    b.do("vector",
         lambda eng: eng.tensor_mul(S["minef"][:, :], S["minef"][:, :], S["mtmp"][:, :]),
         reads=["mtmp", "minef"], writes=["minef"])
    b.do("vector",
         lambda eng: eng.tensor_scalar(out=S["mtmp2"][:, :], in0=S["tabf"][:, 0:CST],
                                       scalar1=S["rowb"][0:c.E, 0:1], scalar2=None,
                                       op0=OP.subtract),
         reads=["tabf_a", "rowb"], writes=["mtmp2"])
    QL = c.Tl // 16
    b.do("vector",
         lambda eng: eng.tensor_copy(out=S["mtmpI"][:, :], in_=S["mtmp2"][:, :]),
         reads=["mtmp2"], writes=["mtmpI"])
    b.do("vector",
         lambda eng: eng.tensor_scalar(out=S["mtmpI"][:, :], in0=S["mtmpI"][:, :],
                                       scalar1=15, scalar2=None, op0=OP.bitwise_and),
         reads=["mtmpI"], writes=["mtmpI"])
    b.do("vector",
         lambda eng: eng.tensor_copy(out=S["mtmp"][:, :], in_=S["mtmpI"][:, :]),
         reads=["mtmpI"], writes=["mtmp"])
    b.do("vector",
         lambda eng: eng.tensor_sub(S["mtmp2"][:, :], S["mtmp2"][:, :], S["mtmp"][:, :]),
         reads=["mtmp2", "mtmp"], writes=["mtmp2"])
    b.do("vector",
         lambda eng: eng.tensor_scalar_mul(S["mtmp2"][:, :], S["mtmp2"][:, :], 0.0625),
         reads=["mtmp2"], writes=["mtmp2"])
    b.do("vector",
         lambda eng: eng.scalar_tensor_tensor(
             out=S["mtmp2"][:, :], in0=S["mtmp"][:, :], scalar=float(QL),
             in1=S["mtmp2"][:, :], op0=OP.mult, op1=OP.add),
         reads=["mtmp", "mtmp2"], writes=["mtmp2"])
    b.do("vector",
         lambda eng: eng.tensor_scalar_add(S["mtmp2"][:, :], S["mtmp2"][:, :], 1.0),
         reads=["mtmp2"], writes=["mtmp2"])
    b.do("vector", lambda eng: eng.memset(S["dl2"][:, :], 0), writes=["dl2z"])
    for kk in range(KLISTS):
        b.do("vector",
             (lambda k_: (lambda eng: eng.tensor_scalar(
                 out=S["mtmp3"][:, :], in0=S["tabf"][:, 2 * CST:3 * CST],
                 scalar1=float(k_ + 1), scalar2=None, op0=OP.is_equal)))(kk),
             reads=["tabf_c"], writes=["mtmp3"])
        b.do("vector",
             lambda eng: eng.tensor_mul(S["mtmp3"][:, :], S["mtmp3"][:, :], S["minef"][:, :]),
             reads=["mtmp3", "minef"], writes=["mtmp3"])
        b.do("vector",
             lambda eng: eng.tensor_mul(S["mtmp"][:, :], S["mtmp2"][:, :], S["mtmp3"][:, :]),
             reads=["mtmp2", "mtmp3"], writes=["mtmp"])
        for q4 in range(c.Tl // 1024):
            b.do("vector",
                 (lambda q_: (lambda eng: eng.tensor_scalar(
                     out=S["mtmp3"][:, :], in0=S["mtmp"][:, :],
                     scalar1=float(q_ * 1024), scalar2=None, op0=OP.subtract)))(q4),
                 reads=["mtmp"], writes=["mtmp3"])
            b.do("vector",
                 lambda eng: eng.tensor_scalar(out=S["mq"][:, :], in0=S["mtmp3"][:, :],
                                               scalar1=0.5, scalar2=None, op0=OP.is_ge),
                 reads=["mtmp3"], writes=["mq"])
            b.do("vector",
                 lambda eng: eng.tensor_scalar(out=S["mq2"][:, :], in0=S["mtmp3"][:, :],
                                               scalar1=1024.5, scalar2=None, op0=OP.is_le),
                 reads=["mtmp3"], writes=["mq2"])
            b.do("vector",
                 lambda eng: eng.tensor_mul(S["mq"][:, :], S["mq"][:, :], S["mq2"][:, :]),
                 reads=["mq", "mq2"], writes=["mq"])
            b.do("vector",
                 lambda eng: eng.tensor_mul(S["mtmp3"][:, :], S["mtmp3"][:, :], S["mq"][:, :]),
                 reads=["mtmp3", "mq"], writes=["mtmp3"])
            b.do("vector",
                 lambda eng: eng.tensor_scalar_add(S["rankD"][:, :], S["mtmp3"][:, :], -1.0),
                 reads=["mtmp3"], writes=["rankD"])
            b.do("gpsimd",
                 (lambda q_: (lambda eng: eng.local_scatter(
                     S["hiG"][:, q_ * 1024:(q_ + 1) * 1024], S["slothi"][:, :],
                     S["rankD"][:, :], c.E, 1024, c.CAP)))(q4),
                 reads=["slothi", "rankD"], writes=["hiG"])
            b.do("gpsimd",
                 (lambda q_: (lambda eng: eng.local_scatter(
                     S["loG"][:, q_ * 1024:(q_ + 1) * 1024], S["slotlo"][:, :],
                     S["rankD"][:, :], c.E, 1024, c.CAP)))(q4),
                 reads=["slotlo", "rankD"], writes=["loG"])
        for nb in range(c.Tl // 512):
            for src, dst, rs in (("hiG", "his", "hs"), ("loG", "los", "ls2")):
                b.do("tensor",
                     (lambda s_, nb_: (lambda eng: eng.matmul(
                         P["psB"][0:1, :], S["Lup"][0:c.E, 128:129],
                         S[s_][:, nb_ * 512:(nb_ + 1) * 512],
                         start=True, stop=True)))(src, nb),
                     reads=[src, "Lup"], writes=["psB"])
                b.do("vector",
                     (lambda d_, rs_: (lambda eng: eng.tensor_copy(
                         out=S[d_][:, :], in_=P["psB"][0:1, :])))(dst, rs),
                     reads=["psB"], writes=[rs])
            b.do("vector",
                 lambda eng: eng.tensor_scalar(out=S["vld"][:, :], in0=S["his"][:, :],
                                               scalar1=0.5, scalar2=None, op0=OP.is_ge),
                 reads=["hs"], writes=["vld"])
            b.do("vector",
                 lambda eng: eng.tensor_scalar(out=S["his"][:, :], in0=S["his"][:, :],
                                               scalar1=-1.0, scalar2=256.0,
                                               op0=OP.add, op1=OP.mult),
                 reads=["hs"], writes=["hs"])
            b.do("vector",
                 lambda eng: eng.tensor_add(S["his"][:, :], S["his"][:, :], S["los"][:, :]),
                 reads=["hs", "ls2"], writes=["hs"])
            b.do("vector",
                 lambda eng: eng.tensor_mul(S["his"][:, :], S["his"][:, :], S["vld"][:, :]),
                 reads=["hs", "vld"], writes=["hs"])
            b.do("vector",
                 lambda eng: eng.tensor_copy(out=S["klg"][:, :], in_=S["his"][:, :]),
                 reads=["hs"], writes=["klg"])
            b.do("vector",
                 (lambda nb_: (lambda eng: eng.tensor_scalar_add(
                     S["los"][:, :], S["iota"][:, :],
                     float(nb_ * (512 // QL) - c.Tl))))(nb),
                 reads=["iota"], writes=["ls2"])
            b.do("vector",
                 lambda eng: eng.tensor_mul(S["los"][:, :], S["los"][:, :], S["vld"][:, :]),
                 reads=["ls2", "vld"], writes=["ls2"])
            b.do("vector",
                 lambda eng: eng.tensor_scalar_add(S["los"][:, :], S["los"][:, :],
                                                   float(c.Tl)),
                 reads=["ls2"], writes=["ls2"])
            b.do("vector",
                 lambda eng: eng.tensor_copy(out=S["klt"][:, :], in_=S["los"][:, :]),
                 reads=["ls2"], writes=["klt"])
            b.do("sync",
                 (lambda k_, nb_: (lambda eng: eng.dma_start(
                     out=klist_d[2 * k_:2 * k_ + 1, nb_ * 512:(nb_ + 1) * 512],
                     in_=S["klg"][:, :])))(kk, nb),
                 reads=["klg"], writes=[f"kld{2 * kk}"], dma=ld)
            b.do("sync",
                 (lambda k_, nb_: (lambda eng: eng.dma_start(
                     out=klist_d[2 * k_ + 1:2 * k_ + 2, nb_ * 512:(nb_ + 1) * 512],
                     in_=S["klt"][:, :])))(kk, nb),
                 reads=["klt"], writes=[f"kld{2 * kk + 1}"], dma=ld)
    for r in range(2 * KLISTS):
        for pp in (0, 16):
            b.do("sync",
                 (lambda r_, pp_: (lambda eng: eng.dma_start(
                     out=S["dl2"][pp_:pp_ + 16, r_ * QL:(r_ + 1) * QL],
                     in_=klist_d[r_:r_ + 1, :].rearrange("a (p s) -> p (a s)", p=16))))(r, pp),
                 reads=[f"kld{r}", "dl2z"], writes=[f"dl2r{r}_{pp}"], dma=ld)
    dl2_all = [f"dl2r{r}_{pp}" for r in range(2 * KLISTS) for pp in (0, 16)]

    b.fence(exclude=["ln2_full", "tab_full", "cnt_full"] +
            [f"ybc{k}" for k in range(c.NCH)])
    free_scope("C")

    if _cut("D"):
        b.fence()
        b.do("gpsimd", lambda eng: None, wait=[b.gwait])
        b.do("sync", lambda eng: None, wait=[b.gwait])
        b.emit(stack)
        nc.compile()
        return nc
    sbuf("gvb", [128, 2 * 8 * c.D], BF16, scope="D")
    sbuf("gvf", [128, 2 * 8 * c.D], F32, scope="D")
    sbuf("zrow", [128, c.D], F32, scope="D")

    b.do("vector", lambda eng: eng.memset(S["gvb"][:, :], 0), writes=["gvb0", "gvb1"])
    b.do("vector", lambda eng: eng.memset(S["zrow"][:, :], 0.0), writes=["zrow"])
    b.do("sync",
         lambda eng: eng.dma_start(out=out[c.Tl:c.Tl + 128, :], in_=S["zrow"][:, :]),
         reads=["zrow"], writes=["outPad"], dma=ld)

    NSUB = c.Tl // 1024
    ybc_all = [f"ybc{k}" for k in range(c.NCH)]
    for kk in range(KLISTS):
        for sub in range(NSUB):
            i2 = kk * NSUB + sub
            par = i2 % 2
            gvb = S["gvb"][:, par * 8 * c.D:(par + 1) * 8 * c.D].rearrange(
                "p (a d) -> p a d", d=c.D)
            gvf = S["gvf"][:, par * 8 * c.D:(par + 1) * 8 * c.D].rearrange(
                "p (a d) -> p a d", d=c.D)
            b.do("gpsimd",
                 (lambda gvb_=gvb, k_=kk, s_=sub: (lambda eng: eng.dma_gather(
                     gvb_, yb_full[:, :],
                     S["dl2"][:, 2 * k_ * QL + s_ * 64: 2 * k_ * QL + (s_ + 1) * 64],
                     1024, 1024, c.D)))(),
                 reads=ybc_all + dl2_all, writes=[f"gvb{par}"], dma=pdc)
            b.do("vector",
                 (lambda par_: (lambda eng: eng.tensor_copy(
                     out=S["gvf"][:, par_ * 8 * c.D:(par_ + 1) * 8 * c.D],
                     in_=S["gvb"][:, par_ * 8 * c.D:(par_ + 1) * 8 * c.D])))(par),
                 reads=[f"gvb{par}"], writes=[f"gvf{par}"])
            prevr = (out_all if kk == 0 else
                     [f"oR{kk - 1}_{s2}" for s2 in range(NSUB)]) + ["outPad"]
            b.do("gpsimd",
                 (lambda gvf_=gvf, k_=kk, s_=sub: (lambda eng: eng.dma_scatter_add(
                     out[:, :], gvf_,
                     S["dl2"][:, (2 * k_ + 1) * QL + s_ * 64:
                              (2 * k_ + 1) * QL + (s_ + 1) * 64],
                     1024, 1024, c.D)))(),
                 reads=[f"gvf{par}"] + dl2_all + prevr,
                 writes=[f"oR{kk}_{sub}"], dma=pdd)
    out_all = out_all + [f"oR{kk}_{s2}" for kk in range(KLISTS) for s2 in range(NSUB)]

    if debug_dump:
        dbg_specs = [("dbg_tab", tab_full, [c.E, 3 * c.CAP], F32, ["tab_full"]),
                     ("dbg_ln2", agin_ln2, [c.Tl, c.D], BF16, all_agin),
                     ("dbg_kl", klist_d, [2 * KLISTS, c.Tl], I16,
                      [f"kld{r}" for r in range(2 * KLISTS)]),
                     ("dbg_x1", x1_d, [c.Tl, c.D], F32,
                      [f"x1_d{ls}_{tt}" for ls in range(c.segs) for tt in range(4)])]
        for nm, srcap, shp, dt, rds in dbg_specs:
            od = nc.dram_tensor(nm, shp, dt, kind="ExternalOutput").ap()
            b.do("sync",
                 (lambda od_, s_: (lambda eng: eng.dma_start(out=od_[:, :], in_=s_[:, :])))(od, srcap),
                 reads=rds + out_all, writes=[nm], dma=ld)
        b.do("sync", lambda eng: None,
             reads=out_all + [d[0] for d in dbg_specs])
    b.do("gpsimd", lambda eng: None, reads=out_all)
    b.do("sync", lambda eng: None, reads=out_all)
    b.emit(stack)
    nc.compile()
    return nc


def _host_constants(c: Cfg, core):
    o = {}
    Lup = np.zeros((128, 132), np.float32)
    for t in range(128):
        Lup[:t, t] = 1.0
    Lup[:, 128] = 1.0
    o["Lup"] = Lup.astype(BF)
    o["idf32"] = np.eye(128, dtype=np.float32)
    o["idbf"] = np.eye(128, dtype=np.float32).astype(BF)
    ss = np.zeros((128, c.DC * 128), np.float32)
    sb = np.zeros((128, c.DC * 128), np.float32)
    for kc in range(c.DC):
        for hloc in range(2):
            hglob = 2 * kc + hloc
            rc = (hglob % 4) * 32 + hglob // 4
            ss[hloc * 64:(hloc + 1) * 64, kc * 128 + rc] = 0.125
            sb[rc, kc * 128 + hloc * 64: kc * 128 + (hloc + 1) * 64] = 1.0
    o["Ssum"] = ss.astype(BF)
    o["Sb"] = sb.astype(BF)
    o["ones1"] = np.ones((1, 128), np.float32)
    segsel = np.zeros((128, c.segs * c.segs), np.float32)
    for ls in range(c.segs):
        segsel[:, ls * c.segs + ls] = 1.0
    o["segsel"] = segsel.astype(BF)
    ids = core * c.Tl + np.arange(c.Tl, dtype=np.int64)
    o["ids16"] = np.broadcast_to(ids[None, :], (c.E, c.Tl)).astype(np.int16).copy()
    col = np.arange(c.CAP, dtype=np.int64)
    siginv = (col % 32) * 16 + col // 32
    ee = np.arange(c.E, dtype=np.int64)
    ebase = (ee // c.EPC) * (c.EPC * c.CAP) + (ee % c.EPC) * c.CAP
    slotr = ebase[:, None] + siginv[None, :]
    o["slothi"] = (slotr // 256 + 1).astype(BF)
    o["slotlo"] = (slotr % 256).astype(BF)
    u = np.zeros((c.E, c.E), np.float32)
    for e2 in range(c.E):
        u[:e2, e2] = 1.0
    o["Ustrict"] = u.astype(BF)
    QL = c.Tl // 16
    q = np.arange(512, dtype=np.int64)
    o["iota"] = ((q % QL) * 16 + q // QL).astype(np.float32).reshape(1, 512)
    mm = np.zeros((c.nseg_tot, c.segs), np.float32)
    for r in range(c.nseg_tot):
        cc, ls2 = divmod(r, c.segs)
        nglob = cc + c.n_cores * ls2
        for ls in range(c.segs):
            if nglob < core + c.n_cores * ls:
                mm[r, ls] = 1.0
    o["maskmat"] = mm
    rb = np.zeros((128, 2), np.float32)
    rb[:, 0] = core * c.Tl
    rb[:, 1] = (core + 1) * c.Tl
    o["rowbase"] = rb
    o["myexp"] = (core * c.EPC + np.arange(c.EPC, dtype=np.int32)).reshape(c.EPC, 1)
    return o


def host_inputs(c: Cfg, inp):
    x = np.ascontiguousarray(np.asarray(inp["L0_attn_x"], np.float32).reshape(c.T, c.D))
    f32 = lambda a: np.ascontiguousarray(np.asarray(a, np.float32))
    Wq, Wk = f32(inp["L0_attn_Wq"]), f32(inp["L0_attn_Wk"])
    Wv, Wo = f32(inp["L0_attn_Wv"]), f32(inp["L0_attn_Wo"])
    bqkv = np.concatenate(
        [np.asarray(inp[k], np.float32).reshape(c.DC, 128).T
         for k in ("L0_attn_bq", "L0_attn_bk", "L0_attn_bv")], axis=1)
    bqkv = np.ascontiguousarray(bqkv)
    bo_row = np.ascontiguousarray(np.asarray(inp["L0_attn_bo"], np.float32).reshape(1, c.D))
    ws1T = np.ascontiguousarray(np.asarray(inp["L0_ffn_ws1"], np.float32).T)
    ws3T = np.ascontiguousarray(np.asarray(inp["L0_ffn_ws3"], np.float32).T)
    ws2T = np.ascontiguousarray(np.asarray(inp["L0_ffn_ws2"], np.float32).T)
    gwT = np.ascontiguousarray(np.asarray(inp["L0_ffn_gate_w"], np.float32).T)
    gb_b = np.ascontiguousarray(np.broadcast_to(
        np.asarray(inp["L0_ffn_gate_b"], np.float32)[None, :], (128, c.E)))
    w1T = np.ascontiguousarray(np.asarray(inp["L0_ffn_w1"], np.float32).transpose(0, 2, 1))
    w3T = np.ascontiguousarray(np.asarray(inp["L0_ffn_w3"], np.float32).transpose(0, 2, 1))
    w2T = np.ascontiguousarray(np.asarray(inp["L0_ffn_w2"], np.float32).transpose(0, 2, 1))

    in_maps = []
    for core in range(c.n_cores):
        segs = [core + c.n_cores * ls for ls in range(c.segs)]
        xs = np.ascontiguousarray(
            np.concatenate([x[n * c.SEG:(n + 1) * c.SEG] for n in segs], axis=0))
        m = {"x": xs, "Wq": Wq, "Wk": Wk, "Wv": Wv, "Wo": Wo,
             "bqkv": bqkv, "bo_row": bo_row,
             "ws1T": ws1T, "ws3T": ws3T, "ws2T": ws2T,
             "gwT": gwT, "gb_bcast": gb_b,
             "w1T": w1T[core * c.EPC:(core + 1) * c.EPC],
             "w3T": w3T[core * c.EPC:(core + 1) * c.EPC],
             "w2T": w2T[core * c.EPC:(core + 1) * c.EPC]}
        m.update(_host_constants(c, core))
        in_maps.append(m)
    return in_maps


def assemble_out(c: Cfg, results, shape):
    outf = np.empty((c.T, c.D), np.float32)
    for core in range(c.n_cores):
        oc = results[core]["out"]
        for ls in range(c.segs):
            n = core + c.n_cores * ls
            outf[n * c.SEG:(n + 1) * c.SEG] = oc[ls * c.SEG:(ls + 1) * c.SEG]
    return outf.reshape(shape)


_CACHED = {}


def kernel(L0_attn_x, L0_attn_Wq, L0_attn_bq, L0_attn_Wk, L0_attn_bk,
           L0_attn_Wv, L0_attn_bv, L0_attn_Wo, L0_attn_bo,
           L0_ffn_gate_w, L0_ffn_gate_b, L0_ffn_w1, L0_ffn_w2, L0_ffn_w3,
           L0_ffn_ws1, L0_ffn_ws2, L0_ffn_ws3):
    c = Cfg(n_cores=8, segs=8)
    inp = dict(L0_attn_x=L0_attn_x, L0_attn_Wq=L0_attn_Wq, L0_attn_bq=L0_attn_bq,
               L0_attn_Wk=L0_attn_Wk, L0_attn_bk=L0_attn_bk, L0_attn_Wv=L0_attn_Wv,
               L0_attn_bv=L0_attn_bv, L0_attn_Wo=L0_attn_Wo, L0_attn_bo=L0_attn_bo,
               L0_ffn_gate_w=L0_ffn_gate_w, L0_ffn_gate_b=L0_ffn_gate_b,
               L0_ffn_w1=L0_ffn_w1, L0_ffn_w2=L0_ffn_w2, L0_ffn_w3=L0_ffn_w3,
               L0_ffn_ws1=L0_ffn_ws1, L0_ffn_ws2=L0_ffn_ws2, L0_ffn_ws3=L0_ffn_ws3)
    key = (c.n_cores, c.segs)
    if key not in _CACHED:
        stack = ExitStack()
        nc = build_program(c, stack)
        _CACHED[key] = (nc, stack)
    nc = _CACHED[key][0]
    in_maps = host_inputs(c, inp)
    res = run_bass_kernel_spmd(nc, in_maps, core_ids=list(range(c.n_cores)))
    return assemble_out(c, res.results, np.asarray(L0_attn_x).shape)


# version 15
# speedup vs baseline: 1.5874x; 1.5874x over previous
import numpy as np
import ml_dtypes
from contextlib import ExitStack

import concourse.bass as bass
import concourse.bacc as bacc
import concourse.mybir as mybir
from concourse.bass import IndirectOffsetOnAxis
from concourse.bass_utils import run_bass_kernel_spmd
from concourse import library_config

F32 = mybir.dt.float32
BF16 = mybir.dt.bfloat16
I16 = mybir.dt.int16
I32 = mybir.dt.int32
U32 = mybir.dt.uint32
AF = mybir.ActivationFunctionType
OP = mybir.AluOpType
AX = mybir.AxisListType
BF = ml_dtypes.bfloat16


class Cfg:
    def __init__(self, n_cores=8, segs=8):
        self.n_cores = n_cores
        self.segs = segs
        self.SEG = 512
        self.D = 1024
        self.DC = 8
        self.H = 16
        self.E = 64
        self.CAP = 512
        self.I = 704
        self.IC = 6
        self.SI = 1408
        self.SIC = 11
        self.Tl = segs * self.SEG
        self.tiles = self.Tl // 128
        self.T = self.Tl * n_cores
        self.EPC = self.E // n_cores
        self.nseg_tot = segs * n_cores
        self.NCH = 4
        self.EPCH = self.EPC // self.NCH


class Sem:
    def __init__(self, name, producer=None):
        self.name, self.h, self.n, self.producer = name, None, 0, producer


class Stream:
    def __init__(self, b, name, nsem=4):
        self.sems = [b.sem(f"{name}{i}") for i in range(nsem)]
        self.i = 0

    def next(self):
        s = self.sems[self.i % len(self.sems)]
        self.i += 1
        return s


class Builder:
    ENGINES = ("sync", "scalar", "vector", "tensor", "gpsimd")

    def __init__(self, nc):
        self.nc = nc
        self.steps = []
        self.sems = []
        self.res = {}
        self.gwait = None
        self.eng_sem = {e: self.sem("c_" + e, producer=e) for e in self.ENGINES}

    def sem(self, name, producer=None):
        s = Sem(name, producer)
        self.sems.append(s)
        return s

    def _dep(self, name):
        return self.res.setdefault(name, {"w": None, "r": []})

    def do(self, engine, fn, reads=(), writes=(), wait=(), inc=(), dma=None,
           coll=None):
        waits = {}

        def addw(sv):
            if sv is None:
                return
            s, v = sv
            waits[s] = max(waits.get(s, 0), v)

        for r in reads:
            addw(self._dep(r)["w"])
        for w in writes:
            d = self._dep(w)
            addw(d["w"])
            for sv in d["r"]:
                addw(sv)
        for sv in wait:
            addw(sv)
        addw(self.gwait)

        if coll is not None:
            s = coll
            amt = 1
        elif dma is None:
            s = self.eng_sem[engine]
            amt = 1
        else:
            s = dma.next() if isinstance(dma, Stream) else dma
            amt = 16
            if s.n > 0:
                waits[s] = max(waits.get(s, 0), s.n)
        s.n += amt
        comp = (s, s.n)
        incs = [(s, amt)]
        for (x, a) in inc:
            x.n += a
            incs.append((x, a))

        for w in writes:
            d = self._dep(w)
            d["w"] = comp
            d["r"] = []
        for r in reads:
            self._dep(r)["r"].append(comp)

        self.steps.append((engine, fn,
                           sorted(waits.items(), key=lambda kv: kv[0].name), incs))
        return comp

    def fence(self, exclude=()):
        allr = [k for k in self.res.keys() if k not in exclude]
        h = self.do("vector", lambda eng: None, reads=allr)
        self.gwait = h

    def emit(self, stack):
        for s in self.sems:
            s.h = stack.enter_context(self.nc.semaphore(s.name))
        per = {e: [] for e in self.ENGINES}
        for st in self.steps:
            per[st[0]].append(st)
        with self.nc.Block() as block:
            def mk(engname):
                def body(eng):
                    seen = {}
                    for (_, fn, waits, incs) in per[engname]:
                        for (s, v) in waits:
                            if seen.get(s, 0) >= v:
                                continue
                            eng.wait_ge(s.h, v)
                            seen[s] = v
                        instr = fn(eng)
                        for (s, amt) in incs:
                            if instr is None:
                                eng.sem_inc(s.h, amt)
                            else:
                                instr = instr.then_inc(s.h, amt)
                return body
            for e in self.ENGINES:
                if per[e]:
                    getattr(block, e)(mk(e))


def build_program(c: Cfg, stack: ExitStack, stop_after="D", debug_dump=False):
    nc = bacc.Bacc("TRN2")
    b = Builder(nc)
    _PHASES = ["A1", "AG", "A2a", "Q7", "A2b", "C", "D"]
    _stop = _PHASES.index(stop_after)

    def _cut(phase):
        return _PHASES.index(phase) > _stop

    def din(name, shape, dt):
        return nc.dram_tensor(name, shape, dt, kind="ExternalInput").ap()

    def dint(name, shape, dt):
        return nc.dram_tensor(name, shape, dt).ap()

    x_in = din("x", [c.Tl, c.D], F32)
    out = nc.dram_tensor("out", [c.Tl + 128, c.D], F32, kind="ExternalOutput").ap()
    attw_in = [din(w, [c.D, c.D], F32) for w in ("Wq", "Wk", "Wv", "Wo")]
    bqkv_in = din("bqkv", [128, 3 * c.DC], F32)
    bo_in = din("bo_row", [1, c.D], F32)
    ws1T_in = din("ws1T", [c.D, c.SI], F32)
    ws3T_in = din("ws3T", [c.D, c.SI], F32)
    ws2T_in = din("ws2T", [c.SI, c.D], F32)
    gwT_in = din("gwT", [c.D, c.E], F32)
    gb_in = din("gb_bcast", [128, c.E], F32)
    w1T_in = din("w1T", [c.EPC, c.D, c.I], F32)
    w3T_in = din("w3T", [c.EPC, c.D, c.I], F32)
    w2T_in = din("w2T", [c.EPC, c.I, c.D], F32)
    Lup_in = din("Lup", [128, 132], BF16)
    idf_in = din("idf32", [128, 128], F32)
    idb_in = din("idbf", [128, 128], BF16)
    ssum_in = din("Ssum", [128, c.DC * 128], BF16)
    sbsel_in = din("Sb", [128, c.DC * 128], BF16)
    ones1_in = din("ones1", [1, 128], F32)
    segsel_in = din("segsel", [128, c.segs * c.segs], BF16)
    ids16_in = din("ids16", [c.E, c.Tl], I16)
    sloti_in = din("sloti", [c.E, c.CAP], I16)
    maskm_in = din("maskmat", [c.nseg_tot, c.segs], F32)
    rowb_in = din("rowbase", [128, 2], F32)
    myexp_in = din("myexp", [c.EPC, 1], I32)

    ln1_d = dint("ln1_d", [c.Tl, c.D], BF16)
    x1_d = dint("x1_d", [c.Tl, c.D], F32)
    agin_ln2 = dint("agin_ln2", [c.Tl, c.D], BF16)
    ln2_full = dint("ln2_full", [c.n_cores * c.Tl, c.D], BF16)
    agin_cnt = dint("agin_cnt", [c.segs, c.E], F32)
    cnt_full = dint("cnt_full", [c.nseg_tot, c.E], F32)
    arin_tab = dint("arin_tab", [c.E, 2 * c.CAP], F32)
    tab_full = dint("tab_full", [c.E, 2 * c.CAP], F32)
    agin_yb = dint("agin_yb", [c.EPC * c.CAP, c.D], BF16)
    eidx_d = dint("eidx_d", [c.EPC, c.CAP], I16)
    lists_d = dint("lists_d", [c.E, 2 * c.CAP], I16)
    cnts_d = dint("cnts_d", [1, c.E], F32)
    ybc = [dint(f"ybc{k}", [c.n_cores * c.EPCH * c.CAP, c.D], BF16)
           for k in range(c.NCH)]

    S = {}
    CMS = {}

    def sbuf(name, shape, dt, scope=None):
        cm = nc.sbuf_tensor("sb_" + name, shape, dt)
        if scope is None:
            S[name] = stack.enter_context(cm)
        else:
            S[name] = cm.__enter__()
            CMS.setdefault(scope, []).append(cm)
        return S[name]

    def free_scope(scope):
        for cm in reversed(CMS.pop(scope, [])):
            cm.__exit__(None, None, None)

    sbuf("bqkv", [128, 3 * c.DC], F32)
    sbuf("bo_row", [1, c.D], F32)
    sbuf("gb", [128, c.E], F32)
    sbuf("Lup", [128, 132], BF16)
    sbuf("idf32", [128, 128], F32)
    sbuf("idbf", [128, 128], BF16)
    sbuf("Ssum", [128, c.DC * 128], BF16)
    sbuf("Sb", [128, c.DC * 128], BF16)
    sbuf("ones1", [1, 128], F32)
    sbuf("segsel", [128, c.segs * c.segs], BF16)
    sbuf("sloti", [c.E, c.CAP], I16)
    sbuf("maskm", [c.nseg_tot, c.segs], F32)
    sbuf("rowb", [128, 2], F32)
    sbuf("myexp", [c.EPC, 1], I32)
    sbuf("wstage", [128, c.SI], F32)
    sbuf("stat", [128, 16], F32)
    sbuf("x1t", [128, 2 * c.D], F32)
    sbuf("gstat", [128, 4], F32)
    sbuf("sbt", [128, c.E], F32)
    sbuf("top8", [128, 8], F32)
    sbuf("ln2fmseg", [128, 2 * c.DC * c.SEG], BF16)
    sbuf("tilec", [c.E, c.tiles], F32)
    sbuf("segcnt", [c.segs, c.E], F32)
    sbuf("cntf", [c.nseg_tot, c.E], F32)
    sbuf("segoff", [c.E, c.segs], F32)
    sbuf("totoff", [c.E, c.tiles], F32)
    sbuf("tabf", [c.E, 2 * c.CAP], F32)
    sbuf("idsG", [c.E, c.CAP], I16)
    sbuf("wG", [c.E, c.CAP], BF16)
    nblk = c.E // 16
    sbuf("minef", [c.E, c.CAP], F32)
    sbuf("mtmp", [c.E, c.CAP], F32)
    sbuf("mtmp2", [c.E, c.CAP], F32)
    sbuf("mtmp3", [c.E, c.CAP], F32)
    sbuf("mtmpI", [c.E, c.CAP], I32)
    sbuf("ltok", [c.E, c.CAP], I16)
    sbuf("onescol", [c.E, 1], F32)
    sbuf("rankD", [c.E, c.CAP], I16)
    sbuf("listSS", [c.E, c.CAP], I16)
    sbuf("listTT", [c.E, c.CAP], I16)
    sbuf("dlist", [128, 2 * c.E * (c.CAP // 16)], I16)
    sbuf("cntrow", [c.E, 1], F32)
    sbuf("nfs", [1, c.E], U32)
    sbuf("nfsf", [1, c.E], F32)
    sbuf("t128", [c.E, 128], F32)

    P = {}
    for n_, sh in (("g0", [128, 512]), ("g1", [128, 512]), ("g2", [128, 512]),
                   ("g3", [128, 512]), ("psB", [128, c.SEG]), ("t0", [c.E, 132]),
                   ("tsm", [128, 128]), ("psC", [c.segs, c.E])):
        P[n_] = stack.enter_context(nc.psum_tensor(n_, sh, F32))

    ld = Stream(b, "ld", 12)
    stw = Stream(b, "st", 8)
    pda = b.sem("pda")
    pdb = Stream(b, "pdb", 8)
    pdc = Stream(b, "pdc", 4)
    pdd = Stream(b, "pdd", 4)
    cc1, cc2, cc3 = b.sem("cc1"), b.sem("cc2"), b.sem("cc3")
    cc4s = [b.sem(f"cc4{k}") for k in range(c.NCH)]
    CST = c.CAP
    RG = [list(range(c.n_cores))]

    for nm, o, i in (("bqkv", S["bqkv"], bqkv_in), ("bo_row", S["bo_row"], bo_in),
                     ("gb", S["gb"], gb_in), ("Lup", S["Lup"], Lup_in),
                     ("idf32", S["idf32"], idf_in), ("idbf", S["idbf"], idb_in),
                     ("Ssum", S["Ssum"], ssum_in), ("Sb", S["Sb"], sbsel_in),
                     ("ones1", S["ones1"], ones1_in), ("segsel", S["segsel"], segsel_in),
                     ("sloti", S["sloti"], sloti_in),
                     ("maskm", S["maskm"], maskm_in), ("rowb", S["rowb"], rowb_in),
                     ("myexp", S["myexp"], myexp_in)):
        b.do("sync", (lambda o_, i_: (lambda eng: eng.dma_start(out=o_[:, :], in_=i_[:, :])))(o, i),
             writes=[nm], dma=ld)

    sbuf("attW", [128, 4 * c.DC * c.D], BF16, scope="A1")
    sbuf("xt2", [128, 2 * c.D], F32, scope="A1")
    sbuf("sqdump", [128, c.D], BF16, scope="A1")
    sbuf("lnt", [128, 2 * c.D], BF16, scope="A1")
    sbuf("ln1fm", [128, 2 * c.DC * c.SEG], BF16, scope="A1")
    sbuf("Qfm", [128, c.DC * c.SEG], BF16, scope="A1")
    sbuf("Kfm", [128, c.DC * c.SEG], BF16, scope="A1")
    sbuf("Vfm", [128, c.DC * c.SEG], BF16, scope="A1")
    sbuf("Pch", [128, 2 * c.SEG], BF16, scope="A1")
    sbuf("praw", [128, c.SEG], F32, scope="A1")
    sbuf("probs", [128, c.SEG], BF16, scope="A1")
    sbuf("pstat", [128, 4], F32, scope="A1")

    for wi in range(4):
        for kc in range(c.DC):
            b.do("sync",
                 (lambda wi_, kc_: (lambda eng: eng.dma_start(
                     out=S["wstage"][:, 0:c.D],
                     in_=attw_in[wi_][kc_ * 128:(kc_ + 1) * 128, :])))(wi, kc),
                 writes=["wstage"], dma=ld)
            b.do("vector",
                 (lambda wi_, kc_: (lambda eng: eng.tensor_copy(
                     out=S["attW"][:, (wi_ * 8 + kc_) * c.D:(wi_ * 8 + kc_ + 1) * c.D],
                     in_=S["wstage"][:, 0:c.D])))(wi, kc),
                 reads=["wstage"], writes=[f"attW{wi}_{kc}"])

    def ln_tile(xt_ap, out_ap, so, out_res, in_res):
        st_ = S["stat"]
        b.do("scalar",
             (lambda xt_, so_: (lambda eng: eng.activation(
                 out=S["sqdump"][:, :], in_=xt_, func=AF.Square,
                 accum_out=st_[:, so_ * 8 + 1: so_ * 8 + 2])))(xt_ap, so),
             reads=list(in_res), writes=[f"stat{so}b", "sqdump"])
        b.do("vector",
             (lambda xt_, so_: (lambda eng: eng.reduce_sum(
                 out=st_[:, so_ * 8 + 0: so_ * 8 + 1], in_=xt_, axis=AX.X)))(xt_ap, so),
             reads=list(in_res), writes=[f"stat{so}a"])
        b.do("vector",
             (lambda so_: (lambda eng: eng.tensor_scalar_mul(
                 st_[:, so_ * 8 + 2: so_ * 8 + 3], st_[:, so_ * 8 + 0: so_ * 8 + 1],
                 1.0 / c.D)))(so),
             reads=[f"stat{so}a"], writes=[f"stat{so}m"])
        b.do("vector",
             (lambda so_: (lambda eng: eng.tensor_mul(
                 st_[:, so_ * 8 + 3: so_ * 8 + 4], st_[:, so_ * 8 + 2: so_ * 8 + 3],
                 st_[:, so_ * 8 + 2: so_ * 8 + 3])))(so),
             reads=[f"stat{so}m"], writes=[f"stat{so}q"])
        b.do("vector",
             (lambda so_: (lambda eng: eng.scalar_tensor_tensor(
                 out=st_[:, so_ * 8 + 4: so_ * 8 + 5], in0=st_[:, so_ * 8 + 1: so_ * 8 + 2],
                 scalar=1.0 / c.D, in1=st_[:, so_ * 8 + 3: so_ * 8 + 4],
                 op0=OP.mult, op1=OP.subtract)))(so),
             reads=[f"stat{so}b", f"stat{so}q"], writes=[f"stat{so}v"])
        b.do("vector",
             (lambda so_: (lambda eng: eng.tensor_scalar_add(
                 st_[:, so_ * 8 + 4: so_ * 8 + 5], st_[:, so_ * 8 + 4: so_ * 8 + 5],
                 1e-5)))(so),
             reads=[f"stat{so}v"], writes=[f"stat{so}v"])
        b.do("scalar",
             (lambda so_: (lambda eng: eng.activation(
                 out=st_[:, so_ * 8 + 6: so_ * 8 + 7], in_=st_[:, so_ * 8 + 4: so_ * 8 + 5],
                 func=AF.Sqrt)))(so),
             reads=[f"stat{so}v"], writes=[f"stat{so}s"])
        b.do("vector",
             (lambda so_: (lambda eng: eng.reciprocal(
                 out=st_[:, so_ * 8 + 5: so_ * 8 + 6],
                 in_=st_[:, so_ * 8 + 6: so_ * 8 + 7])))(so),
             reads=[f"stat{so}s"], writes=[f"stat{so}r"])
        b.do("vector",
             (lambda xt_, so_, o_: (lambda eng: eng.tensor_scalar(
                 out=o_, in0=xt_,
                 scalar1=st_[:, so_ * 8 + 2: so_ * 8 + 3],
                 scalar2=st_[:, so_ * 8 + 5: so_ * 8 + 6],
                 op0=OP.subtract, op1=OP.mult)))(xt_ap, so, out_ap),
             reads=list(in_res) + [f"stat{so}m", f"stat{so}r"], writes=[out_res])

    for ls in range(c.segs):
        q = ls % 2
        for tt in range(4):
            hh = tt % 2
            b.do("sync",
                 (lambda ls_, tt_, h_: (lambda eng: eng.dma_start(
                     out=S["xt2"][:, h_ * c.D:(h_ + 1) * c.D],
                     in_=x_in[ls_ * c.SEG + tt_ * 128: ls_ * c.SEG + (tt_ + 1) * 128, :])))(ls, tt, hh),
                 writes=[f"xt2{hh}"], dma=ld)
            xt = S["xt2"][:, hh * c.D:(hh + 1) * c.D]
            ln_tile(xt, S["lnt"][:, hh * c.D:(hh + 1) * c.D], hh, f"lnt{hh}", [f"xt2{hh}"])
            b.do("scalar",
                 (lambda ls_, tt_, h_: (lambda eng: eng.dma_start(
                     out=ln1_d[ls_ * c.SEG + tt_ * 128: ls_ * c.SEG + (tt_ + 1) * 128, :],
                     in_=S["lnt"][:, h_ * c.D:(h_ + 1) * c.D])))(ls, tt, hh),
                 reads=[f"lnt{hh}"], writes=[f"ln1_d{ls}_{tt}"], dma=stw)
        for kc in range(c.DC):
            b.do("sync",
                 (lambda ls_, kc_, q_: (lambda eng: eng.dma_start(
                     out=S["ln1fm"][:, (q_ * c.DC + kc_) * c.SEG:(q_ * c.DC + kc_ + 1) * c.SEG],
                     in_=ln1_d[ls_ * c.SEG:(ls_ + 1) * c.SEG, kc_ * 128:(kc_ + 1) * 128],
                     transpose=True)))(ls, kc, q),
                 reads=[f"ln1_d{ls}_{t2}" for t2 in range(4)],
                 writes=[f"ln1fm{q}_{kc}"], dma=ld)

        for pi, dst in enumerate(("Qfm", "Kfm", "Vfm")):
            for oc in range(c.DC):
                bank = f"g{(pi * c.DC + oc) % 2}"

                def qkv_group(eng, pi_=pi, oc_=oc, q_=q, bank_=bank):
                    ii = None
                    for kc in range(c.DC):
                        ii = eng.matmul(
                            P[bank_][:, :],
                            S["attW"][:, (pi_ * 8 + kc) * c.D + oc_ * 128:
                                      (pi_ * 8 + kc) * c.D + (oc_ + 1) * 128],
                            S["ln1fm"][:, (q_ * c.DC + kc) * c.SEG:(q_ * c.DC + kc + 1) * c.SEG],
                            start=(kc == 0), stop=(kc == c.DC - 1))
                    return ii

                b.do("tensor", qkv_group,
                     reads=[f"ln1fm{q}_{kc}" for kc in range(c.DC)] +
                           [f"attW{pi}_{kc}" for kc in range(c.DC)],
                     writes=[bank])
                b.do("scalar",
                     (lambda pi_, oc_, dst_, bank_: (lambda eng: eng.activation(
                         out=S[dst_][:, oc_ * c.SEG:(oc_ + 1) * c.SEG], in_=P[bank_][:, :],
                         func=AF.Identity,
                         bias=S["bqkv"][:, pi_ * 8 + oc_: pi_ * 8 + oc_ + 1],
                         scale=1.0)))(pi, oc, dst, bank),
                     reads=[bank, "bqkv"], writes=[f"{dst}_{oc}"])

        for kc in range(c.DC):
            pq = kc % 2
            b.do("vector",
                 (lambda kc_, pq_: (lambda eng: eng.tensor_mul(
                     S["Pch"][:, pq_ * c.SEG:(pq_ + 1) * c.SEG],
                     S["Qfm"][:, kc_ * c.SEG:(kc_ + 1) * c.SEG],
                     S["Kfm"][:, kc_ * c.SEG:(kc_ + 1) * c.SEG])))(kc, pq),
                 reads=[f"Qfm_{kc}", f"Kfm_{kc}"], writes=[f"Pch{pq}"])
            b.do("tensor",
                 (lambda kc_, pq_: (lambda eng: eng.matmul(
                     P["psB"][:, :], S["Ssum"][:, kc_ * 128:(kc_ + 1) * 128],
                     S["Pch"][:, pq_ * c.SEG:(pq_ + 1) * c.SEG],
                     start=(kc_ == 0), stop=(kc_ == c.DC - 1))))(kc, pq),
                 reads=[f"Pch{pq}", "Ssum"], writes=["psB"])

        b.do("vector", lambda eng: eng.memset(S["probs"][:, :], 0.0), writes=["probs"])
        for j in range(4):
            rows = slice(32 * j, 32 * j + 4)
            cols = slice(j, c.SEG, 4)
            b.do("vector",
                 (lambda r_, co_: (lambda eng: eng.tensor_reduce(
                     out=S["pstat"][r_, 0:1], in_=P["psB"][r_, co_],
                     axis=AX.X, op=OP.max, negate=True)))(rows, cols),
                 reads=["psB"], writes=[f"pstat{j}a"])
            b.do("scalar",
                 (lambda r_, co_: (lambda eng: eng.activation(
                     out=S["praw"][r_, co_], in_=P["psB"][r_, co_],
                     func=AF.Exp, bias=S["pstat"][r_, 0:1], scale=1.0,
                     accum_out=S["pstat"][r_, 1:2])))(rows, cols),
                 reads=["psB", f"pstat{j}a"], writes=[f"praw{j}", f"pstat{j}b"])
            b.do("vector",
                 (lambda r_: (lambda eng: eng.reciprocal(
                     out=S["pstat"][r_, 2:3], in_=S["pstat"][r_, 1:2])))(rows),
                 reads=[f"pstat{j}b"], writes=[f"pstat{j}c"])
            b.do("vector",
                 (lambda r_, co_: (lambda eng: eng.tensor_scalar_mul(
                     S["probs"][r_, co_], S["praw"][r_, co_],
                     S["pstat"][r_, 2:3])))(rows, cols),
                 reads=[f"praw{j}", f"pstat{j}c", "probs"], writes=[f"probsw{j}"])

        for kc in range(c.DC):
            bank = f"g{2 + kc % 2}"
            b.do("tensor",
                 (lambda kc_, bank_: (lambda eng: eng.matmul(
                     P[bank_][:, :], S["Sb"][:, kc_ * 128:(kc_ + 1) * 128],
                     S["probs"][:, :], start=True, stop=True)))(kc, bank),
                 reads=["probs", "Sb"] + [f"probsw{j}" for j in range(4)],
                 writes=[bank])
            b.do("vector",
                 (lambda kc_, bank_: (lambda eng: eng.tensor_mul(
                     S["Qfm"][:, kc_ * c.SEG:(kc_ + 1) * c.SEG], P[bank_][:, :],
                     S["Vfm"][:, kc_ * c.SEG:(kc_ + 1) * c.SEG])))(kc, bank),
                 reads=[bank, f"Vfm_{kc}", f"Qfm_{kc}"], writes=[f"attnfm_{kc}"])

        for tt in range(4):
            hh = tt % 2
            b.do("sync",
                 (lambda ls_, tt_, h_: (lambda eng: eng.dma_start(
                     out=S["xt2"][:, h_ * c.D:(h_ + 1) * c.D],
                     in_=x_in[ls_ * c.SEG + tt_ * 128: ls_ * c.SEG + (tt_ + 1) * 128, :])))(ls, tt, hh),
                 writes=[f"xt2{hh}"], dma=ld)
            for nh in range(2):
                bank = f"g{nh}"

                def oproj_group(eng, tt_=tt, nh_=nh, bank_=bank):
                    for kc in range(c.DC):
                        eng.matmul(
                            P[bank_][:, :],
                            S["Qfm"][:, kc * c.SEG + tt_ * 128: kc * c.SEG + (tt_ + 1) * 128],
                            S["attW"][:, (3 * 8 + kc) * c.D + nh_ * 512:
                                      (3 * 8 + kc) * c.D + (nh_ + 1) * 512],
                            start=(kc == 0), stop=False)
                    return eng.matmul(P[bank_][:, :], S["ones1"][:, :],
                                      S["bo_row"][:, nh_ * 512:(nh_ + 1) * 512],
                                      start=False, stop=True)

                b.do("tensor", oproj_group,
                     reads=[f"attnfm_{kc}" for kc in range(c.DC)] +
                           [f"Qfm_{kc}" for kc in range(c.DC)] +
                           [f"attW3_{kc}" for kc in range(c.DC)] + ["ones1", "bo_row"],
                     writes=[bank])
                b.do("vector",
                     (lambda bank_, tt_, nh_, h_: (lambda eng: eng.tensor_add(
                         S["x1t"][:, h_ * c.D + nh_ * 512: h_ * c.D + (nh_ + 1) * 512],
                         P[bank_][:, :],
                         S["xt2"][:, h_ * c.D + nh_ * 512: h_ * c.D + (nh_ + 1) * 512])))(bank, tt, nh, hh),
                     reads=[bank, f"xt2{hh}"], writes=[f"x1t{hh}_{nh}"])
            b.do("sync",
                 (lambda ls_, tt_, h_: (lambda eng: eng.dma_start(
                     out=x1_d[ls_ * c.SEG + tt_ * 128: ls_ * c.SEG + (tt_ + 1) * 128, :],
                     in_=S["x1t"][:, h_ * c.D:(h_ + 1) * c.D])))(ls, tt, hh),
                 reads=[f"x1t{hh}_0", f"x1t{hh}_1"], writes=[f"x1_d{ls}_{tt}"], dma=ld)
            xt = S["x1t"][:, hh * c.D:(hh + 1) * c.D]
            ln_tile(xt, S["lnt"][:, hh * c.D:(hh + 1) * c.D], hh,
                    f"lnt{hh}", [f"x1t{hh}_0", f"x1t{hh}_1"])
            b.do("scalar",
                 (lambda ls_, tt_, h_: (lambda eng: eng.dma_start(
                     out=agin_ln2[ls_ * c.SEG + tt_ * 128: ls_ * c.SEG + (tt_ + 1) * 128, :],
                     in_=S["lnt"][:, h_ * c.D:(h_ + 1) * c.D])))(ls, tt, hh),
                 reads=[f"lnt{hh}"], writes=[f"agin{ls}_{tt}"], dma=stw)

    all_agin = [f"agin{ls}_{tt}" for ls in range(c.segs) for tt in range(4)]

    if _cut("AG"):
        b.fence()
        b.do("gpsimd", lambda eng: None, wait=[b.gwait])
        b.do("sync", lambda eng: None, wait=[b.gwait])
        b.emit(stack)
        nc.compile()
        return nc
    b.do("gpsimd",
         lambda eng: eng.collective_compute(
             "AllGather", OP.bypass, replica_groups=RG,
             ins=[agin_ln2[:, :]], outs=[ln2_full[:, :]]),
         reads=all_agin, writes=["ln2_full"], coll=cc1)

    if _cut("A2a"):
        b.fence()
        b.do("gpsimd", lambda eng: None, wait=[b.gwait])
        b.do("sync", lambda eng: None, wait=[b.gwait])
        b.emit(stack)
        nc.compile()
        return nc
    b.fence(exclude=["ln2_full"])
    free_scope("A1")

    sbuf("gwT", [128, c.DC * c.E], BF16, scope="R")
    sbuf("sc_t", [128, 2 * c.E], F32, scope="R")
    sbuf("M_t", [128, 2 * c.E], BF16, scope="R")
    QW = c.Tl // 8
    sbuf("MT", [c.E, c.Tl], BF16, scope="R")
    sbuf("scoresT", [c.E, c.Tl], BF16, scope="R")
    sbuf("rank16", [c.E, c.Tl], I16, scope="R")
    sbuf("scrF", [c.E, QW], F32, scope="R")
    sbuf("scrI", [c.E, QW], I32, scope="R")
    sbuf("maskh", [c.E, QW], F32, scope="R")
    sbuf("idx16", [c.E, c.Tl], I16, scope="R")
    sbuf("idx16b", [c.E, c.Tl], I16, scope="R")
    sbuf("ids16", [c.E, c.Tl], I16, scope="R")

    b.do("sync", lambda eng: eng.dma_start(out=S["ids16"][:, :], in_=ids16_in[:, :]),
         writes=["ids16"], dma=ld)
    b.do("sync",
         lambda eng: eng.dma_start(
             out=S["wstage"][:, 0:c.DC * c.E].rearrange("p (k e) -> p k e", k=c.DC),
             in_=gwT_in.rearrange("(k p) e -> p k e", p=128)),
         writes=["wstage"], dma=ld)
    b.do("vector",
         lambda eng: eng.tensor_copy(out=S["gwT"][:, :], in_=S["wstage"][:, 0:c.DC * c.E]),
         reads=["wstage"], writes=["gwT"])

    sbuf("zW", [128, 3 * 11264], BF16, scope="Z")
    sbuf("zh", [128, c.SIC * c.SEG], BF16, scope="Z")
    sbuf("silt", [128, c.SEG], F32, scope="Z")

    b.do("gpsimd",
         lambda eng: eng.dma_start(
             out=S["zW"][:, 0:c.DC * c.SI].rearrange("p (k i) -> p k i", k=c.DC),
             in_=ws1T_in.rearrange("(k p) i -> p k i", p=128)),
         writes=[f"zW0_{kc}" for kc in range(c.DC)], dma=pdb)
    b.do("gpsimd",
         lambda eng: eng.dma_start(
             out=S["zW"][:, 11264:11264 + c.DC * c.SI].rearrange(
                 "p (k i) -> p k i", k=c.DC),
             in_=ws3T_in.rearrange("(k p) i -> p k i", p=128)),
         writes=[f"zW1_{kc}" for kc in range(c.DC)], dma=pdb)
    b.do("gpsimd",
         lambda eng: eng.dma_start(
             out=S["zW"][:, 22528:22528 + c.SIC * c.D].rearrange(
                 "p (k i) -> p k i", k=c.SIC),
             in_=ws2T_in.rearrange("(k p) i -> p k i", p=128)),
         writes=[f"zW2_{ic}" for ic in range(c.SIC)], dma=pdb)


    for ls in range(c.segs):
        q = ls % 2
        for kc in range(c.DC):
            b.do("sync",
                 (lambda ls_, kc_, q_: (lambda eng: eng.dma_start(
                     out=S["ln2fmseg"][:, (q_ * c.DC + kc_) * c.SEG:
                                       (q_ * c.DC + kc_ + 1) * c.SEG],
                     in_=agin_ln2[ls_ * c.SEG:(ls_ + 1) * c.SEG, kc_ * 128:(kc_ + 1) * 128],
                     transpose=True)))(ls, kc, q),
                 reads=[f"agin{ls}_{t2}" for t2 in range(4)],
                 writes=[f"l2s{q}_{kc}"], dma=ld)
        for tt in range(4):
            ti = 4 * ls + tt
            tp = ti % 2
            bank = f"g{tp}"

            def gate_group(eng, tt_=tt, q_=q, bank_=bank):
                ii = None
                for kc in range(c.DC):
                    ii = eng.matmul(
                        P[bank_][:, 0:c.E],
                        S["ln2fmseg"][:, (q_ * c.DC + kc) * c.SEG + tt_ * 128:
                                      (q_ * c.DC + kc) * c.SEG + (tt_ + 1) * 128],
                        S["gwT"][:, kc * c.E:(kc + 1) * c.E],
                        start=(kc == 0), stop=(kc == c.DC - 1))
                return ii

            b.do("tensor", gate_group,
                 reads=[f"l2s{q}_{kc}" for kc in range(c.DC)] + ["gwT"], writes=[bank])
            b.do("vector",
                 (lambda bank_: (lambda eng: eng.tensor_reduce(
                     out=S["gstat"][:, 0:1], in_=P[bank_][:, 0:c.E], axis=AX.X,
                     op=OP.max, negate=True)))(bank),
                 reads=[bank], writes=["gstat0"])
            b.do("scalar",
                 (lambda bank_, tp_: (lambda eng: eng.activation(
                     out=S["sc_t"][:, tp_ * c.E:(tp_ + 1) * c.E], in_=P[bank_][:, 0:c.E],
                     func=AF.Exp, bias=S["gstat"][:, 0:1], scale=1.0,
                     accum_out=S["gstat"][:, 1:2])))(bank, tp),
                 reads=[bank, "gstat0"], writes=[f"scraw{tp}", "gstat1"])
            b.do("vector",
                 lambda eng: eng.reciprocal(out=S["gstat"][:, 2:3], in_=S["gstat"][:, 1:2]),
                 reads=["gstat1"], writes=["gstat2"])
            b.do("vector",
                 (lambda tp_: (lambda eng: eng.tensor_scalar_mul(
                     S["sc_t"][:, tp_ * c.E:(tp_ + 1) * c.E],
                     S["sc_t"][:, tp_ * c.E:(tp_ + 1) * c.E], S["gstat"][:, 2:3])))(tp),
                 reads=[f"scraw{tp}", "gstat2"], writes=[f"sct{tp}"])
            b.do("vector",
                 (lambda tp_: (lambda eng: eng.tensor_add(
                     S["sbt"][:, :], S["sc_t"][:, tp_ * c.E:(tp_ + 1) * c.E],
                     S["gb"][:, :])))(tp),
                 reads=[f"sct{tp}", "gb"], writes=["sbt"])
            b.do("vector", lambda eng: eng.max(out=S["top8"][:, :], in_=S["sbt"][:, :]),
                 reads=["sbt"], writes=["top8"])
            b.do("vector",
                 (lambda tp_: (lambda eng: eng.tensor_scalar(
                     out=S["M_t"][:, tp_ * c.E:(tp_ + 1) * c.E], in0=S["sbt"][:, :],
                     scalar1=S["top8"][:, 5:6], scalar2=None, op0=OP.is_ge)))(tp),
                 reads=["sbt", "top8"], writes=[f"Mt{tp}"])
            b.do("tensor",
                 (lambda tp_: (lambda eng: eng.transpose(
                     P["tsm"][0:c.E, 0:128], S["sc_t"][:, tp_ * c.E:(tp_ + 1) * c.E],
                     S["idf32"][:, :])))(tp),
                 reads=[f"sct{tp}", "idf32"], writes=["tsm"])
            b.do("vector",
                 (lambda ti_: (lambda eng: eng.tensor_copy(
                     out=S["scoresT"][:, ti_ * 128:(ti_ + 1) * 128],
                     in_=P["tsm"][0:c.E, 0:128])))(ti),
                 reads=["tsm"], writes=[f"scT{ti}"])
            b.do("tensor",
                 (lambda tp_: (lambda eng: eng.transpose(
                     P["tsm"][0:c.E, 0:64].bitcast(BF16),
                     S["M_t"][:, tp_ * c.E:(tp_ + 1) * c.E],
                     S["idbf"][:, :])))(tp),
                 reads=[f"Mt{tp}", "idbf"], writes=["tsm"])
            b.do("vector",
                 (lambda ti_: (lambda eng: eng.tensor_copy(
                     out=S["MT"][:, ti_ * 128:(ti_ + 1) * 128],
                     in_=P["tsm"][0:c.E, 0:64].bitcast(BF16))))(ti),
                 reads=["tsm"], writes=[f"MTc{ti}"])
            b.do("tensor",
                 (lambda tp_: (lambda eng: eng.matmul(
                     P["t0"][:, 0:129], S["M_t"][:, tp_ * c.E:(tp_ + 1) * c.E],
                     S["Lup"][:, 0:129], start=True, stop=True)))(tp),
                 reads=[f"Mt{tp}", "Lup"], writes=["t0"])
            b.do("vector",
                 (lambda ti_: (lambda eng: eng.tensor_copy(
                     out=S["rank16"][:, ti_ * 128:(ti_ + 1) * 128],
                     in_=P["t0"][:, 0:128])))(ti),
                 reads=["t0"], writes=["rkF"])
            b.do("vector",
                 (lambda ti_: (lambda eng: eng.tensor_copy(
                     out=S["tilec"][:, ti_:ti_ + 1], in_=P["t0"][:, 128:129])))(ti),
                 reads=["t0"], writes=[f"tilec{ti}"])
            b.do("tensor",
                 (lambda tp_, ls_, fst, lst: (lambda eng: eng.matmul(
                     P["psC"][:, :], S["segsel"][:, ls_ * c.segs:(ls_ + 1) * c.segs],
                     S["M_t"][:, tp_ * c.E:(tp_ + 1) * c.E],
                     start=fst, stop=lst)))(tp, ls, ti == 0, ti == c.tiles - 1),
                 reads=[f"Mt{tp}", "segsel"], writes=["psC"])

    b.do("vector", lambda eng: eng.tensor_copy(out=S["segcnt"][:, :], in_=P["psC"][:, :]),
         reads=["psC"], writes=["segcnt"])
    b.do("sync", lambda eng: eng.dma_start(out=agin_cnt[:, :], in_=S["segcnt"][:, :]),
         reads=["segcnt"], writes=["agin_cnt"], dma=ld)
    b.do("gpsimd",
         lambda eng: eng.collective_compute(
             "AllGather", OP.bypass, replica_groups=RG,
             ins=[agin_cnt[:, :]], outs=[cnt_full[:, :]]),
         reads=["agin_cnt"], writes=["cnt_full"], coll=cc2)
    def z_seg(ls):
        q = ls % 2
        for kc in range(c.DC):
            b.do("sync",
                 (lambda ls_, kc_, q_: (lambda eng: eng.dma_start(
                     out=S["ln2fmseg"][:, (q_ * c.DC + kc_) * c.SEG:
                                       (q_ * c.DC + kc_ + 1) * c.SEG],
                     in_=agin_ln2[ls_ * c.SEG:(ls_ + 1) * c.SEG, kc_ * 128:(kc_ + 1) * 128],
                     transpose=True)))(ls, kc, q),
                 reads=[f"agin{ls}_{t2}" for t2 in range(4)],
                 writes=[f"l2s{q}_{kc}"], dma=ld)
        for ic in range(c.SIC):
            for zi in range(2):
                bank = f"g{zi}"

                def z_group(eng, ic_=ic, zi_=zi, q_=q, bank_=bank):
                    ii = None
                    for kc in range(c.DC):
                        ii = eng.matmul(
                            P[bank_][:, :],
                            S["zW"][:, zi_ * 11264 + kc * c.SI + ic_ * 128:
                                    zi_ * 11264 + kc * c.SI + (ic_ + 1) * 128],
                            S["ln2fmseg"][:, (q_ * c.DC + kc) * c.SEG:
                                          (q_ * c.DC + kc + 1) * c.SEG],
                            start=(kc == 0), stop=(kc == c.DC - 1))
                    return ii

                b.do("tensor", z_group,
                     reads=[f"zW{zi}_{kc}" for kc in range(c.DC)] +
                           [f"l2s{q}_{kc}" for kc in range(c.DC)],
                     writes=[bank])
            b.do("scalar",
                 lambda eng: eng.activation(out=S["silt"][:, :], in_=P["g0"][:, :], func=AF.Silu),
                 reads=["g0"], writes=["silt0"])
            b.do("vector",
                 (lambda ic_: (lambda eng: eng.tensor_mul(
                     S["zh"][:, ic_ * c.SEG:(ic_ + 1) * c.SEG], S["silt"][:, :],
                     P["g1"][:, :])))(ic),
                 reads=["silt0", "g1"], writes=[f"zh{ic}"])
        for tt in range(4):
            hh = tt % 2
            b.do("sync",
                 (lambda ls_, tt_, h_: (lambda eng: eng.dma_start(
                     out=S["x1t"][:, h_ * c.D:(h_ + 1) * c.D],
                     in_=x1_d[ls_ * c.SEG + tt_ * 128: ls_ * c.SEG + (tt_ + 1) * 128, :])))(ls, tt, hh),
                 reads=[f"x1_d{ls}_{tt}"], writes=[f"x1t{hh}_0", f"x1t{hh}_1"], dma=ld)
            for nh in range(2):
                bank = f"g{2 + nh}"

                def z2_group(eng, tt_=tt, nh_=nh, bank_=bank):
                    ii = None
                    for ic in range(c.SIC):
                        ii = eng.matmul(
                            P[bank_][:, :],
                            S["zh"][:, ic * c.SEG + tt_ * 128: ic * c.SEG + (tt_ + 1) * 128],
                            S["zW"][:, 22528 + ic * c.D + nh_ * 512:
                                    22528 + ic * c.D + (nh_ + 1) * 512],
                            start=(ic == 0), stop=(ic == c.SIC - 1))
                    return ii

                b.do("tensor", z2_group,
                     reads=[f"zh{ic}" for ic in range(c.SIC)] +
                           [f"zW2_{ic}" for ic in range(c.SIC)],
                     writes=[bank])
                b.do("vector",
                     (lambda bank_, nh_, h_: (lambda eng: eng.tensor_add(
                         S["x1t"][:, h_ * c.D + nh_ * 512: h_ * c.D + (nh_ + 1) * 512],
                         P[bank_][:, :],
                         S["x1t"][:, h_ * c.D + nh_ * 512: h_ * c.D + (nh_ + 1) * 512])))(bank, nh, hh),
                     reads=[bank, f"x1t{hh}_0", f"x1t{hh}_1"], writes=[f"x1t{hh}_{nh}"])
            b.do("scalar",
                 (lambda ls_, tt_, h_: (lambda eng: eng.dma_start(
                     out=out[ls_ * c.SEG + tt_ * 128: ls_ * c.SEG + (tt_ + 1) * 128, :],
                     in_=S["x1t"][:, h_ * c.D:(h_ + 1) * c.D])))(ls, tt, hh),
                 reads=[f"x1t{hh}_0", f"x1t{hh}_1"], writes=[f"out{ls}_{tt}"], dma=stw)

    zsplit = max(1, c.segs - 3)
    for ls in range(zsplit):
        z_seg(ls)

    b.do("sync", lambda eng: eng.dma_start(out=S["cntf"][:, :], in_=cnt_full[:, :]),
         reads=["cnt_full"], writes=["cntf"], dma=ld)

    b.do("tensor",
         lambda eng: eng.matmul(P["tsm"][0:c.E, 0:c.segs], S["cntf"][:, :],
                                S["maskm"][:, :], start=True, stop=True),
         reads=["cntf", "maskm"], writes=["tsm"])
    b.do("vector",
         lambda eng: eng.tensor_copy(out=S["segoff"][:, :], in_=P["tsm"][0:c.E, 0:c.segs]),
         reads=["tsm"], writes=["segoff"])
    for ls in range(c.segs):
        b.do("vector",
             (lambda ls_: (lambda eng: eng.tensor_copy(
                 out=S["totoff"][:, 4 * ls_:4 * ls_ + 1], in_=S["segoff"][:, ls_:ls_ + 1])))(ls),
             reads=["segoff"], writes=[f"totoff{4 * ls}"])
        for j in range(3):
            b.do("vector",
                 (lambda ls_, j_: (lambda eng: eng.tensor_add(
                     S["totoff"][:, 4 * ls_ + j_ + 1:4 * ls_ + j_ + 2],
                     S["totoff"][:, 4 * ls_ + j_:4 * ls_ + j_ + 1],
                     S["tilec"][:, 4 * ls_ + j_:4 * ls_ + j_ + 1])))(ls, j),
                 reads=[f"totoff{4 * ls + j}", f"tilec{4 * ls + j}"],
                 writes=[f"totoff{4 * ls + j + 1}"])
    def V(fn, reads, writes):
        b.do("vector", fn, reads=reads, writes=writes)

    TPQ = QW // 128
    for h4 in range(c.Tl // QW):
        sl = slice(h4 * QW, (h4 + 1) * QW)
        V((lambda sl_: (lambda eng: eng.tensor_copy(
            out=S["scrF"][:, :], in_=S["rank16"][:, sl_])))(sl),
          ["rkF"], ["scrF"])
        for tl in range(TPQ):
            ti = h4 * TPQ + tl
            V((lambda tl_, ti_: (lambda eng: eng.tensor_scalar_add(
                S["scrF"][:, tl_ * 128:(tl_ + 1) * 128],
                S["scrF"][:, tl_ * 128:(tl_ + 1) * 128],
                S["totoff"][:, ti_:ti_ + 1])))(tl, ti),
              ["scrF", f"totoff{ti}"], ["scrF"])
        V(lambda eng: eng.tensor_copy(out=S["scrI"][:, :], in_=S["scrF"][:, :]),
          ["scrF"], ["scrI"])
        V(lambda eng: eng.tensor_scalar(out=S["maskh"][:, :], in0=S["scrF"][:, :],
                                        scalar1=float(c.CAP), scalar2=None,
                                        op0=OP.is_lt),
          ["scrF"], ["maskH"])
        V((lambda sl_: (lambda eng: eng.tensor_mul(
            S["maskh"][:, :], S["maskh"][:, :], S["MT"][:, sl_])))(sl),
          ["maskH"] + [f"MTc{h4 * TPQ + tl}" for tl in range(TPQ)], ["maskH"])
        V(lambda eng: eng.tensor_scalar_add(S["scrF"][:, :], S["scrF"][:, :], 1.0),
          ["scrF"], ["scrF"])
        V(lambda eng: eng.tensor_mul(S["scrF"][:, :], S["scrF"][:, :], S["maskh"][:, :]),
          ["scrF", "maskH"], ["scrF"])
        V((lambda sl_: (lambda eng: eng.tensor_scalar_add(
            S["idx16"][:, sl_], S["scrF"][:, :], -1.0)))(sl),
          ["scrF"], ["idx16F"])
        V(lambda eng: eng.tensor_copy(out=S["scrF"][:, :], in_=S["scrI"][:, :]),
          ["scrI"], ["scrF"])
        V(lambda eng: eng.tensor_scalar(out=S["scrI"][:, :], in0=S["scrI"][:, :],
                                        scalar1=15, scalar2=None, op0=OP.bitwise_and),
          ["scrI"], ["scrI"])
        V(lambda eng: eng.tensor_copy(out=S["maskh"][:, :], in_=S["scrI"][:, :]),
          ["scrI"], ["maskH"])
        V(lambda eng: eng.tensor_sub(S["scrF"][:, :], S["scrF"][:, :], S["maskh"][:, :]),
          ["scrF", "maskH"], ["scrF"])
        V(lambda eng: eng.tensor_scalar_mul(S["scrF"][:, :], S["scrF"][:, :], 0.0625),
          ["scrF"], ["scrF"])
        V(lambda eng: eng.scalar_tensor_tensor(
            out=S["scrF"][:, :], in0=S["maskh"][:, :], scalar=32.0,
            in1=S["scrF"][:, :], op0=OP.mult, op1=OP.add),
          ["scrF", "maskH"], ["scrF"])
        V((lambda sl_: (lambda eng: eng.tensor_copy(
            out=S["maskh"][:, :], in_=S["idx16"][:, sl_])))(sl),
          ["idx16F"], ["maskH"])
        V(lambda eng: eng.tensor_scalar(
            out=S["maskh"][:, :], in0=S["maskh"][:, :],
            scalar1=0.0, scalar2=None, op0=OP.is_ge),
          ["maskH"], ["maskH"])
        V(lambda eng: eng.tensor_scalar_add(S["scrF"][:, :], S["scrF"][:, :], 1.0),
          ["scrF"], ["scrF"])
        V(lambda eng: eng.tensor_mul(S["scrF"][:, :], S["scrF"][:, :], S["maskh"][:, :]),
          ["scrF", "maskH"], ["scrF"])
        V((lambda sl_: (lambda eng: eng.tensor_scalar_add(
            S["idx16b"][:, sl_], S["scrF"][:, :], -1.0)))(sl),
          ["scrF"], ["idx16bF"])
    ix_all = ["idx16F"]
    sct_all = [f"scT{ti}" for ti in range(c.tiles)]

    b.do("gpsimd",
         lambda eng: eng.local_scatter(S["idsG"][:, :], S["ids16"][:, :], S["idx16b"][:, :],
                                       c.E, c.CAP, c.Tl),
         reads=["ids16", "idx16bF"], writes=["idsG"])
    b.do("gpsimd",
         lambda eng: eng.local_scatter(S["wG"][:, :], S["scoresT"][:, :], S["idx16"][:, :],
                                       c.E, c.CAP, c.Tl),
         reads=ix_all + sct_all, writes=["wG"])
    b.do("vector", lambda eng: eng.tensor_copy(out=S["tabf"][:, 0:CST], in_=S["idsG"][:, :]),
         reads=["idsG"], writes=["tabf_a"])
    b.do("vector", lambda eng: eng.tensor_copy(out=S["tabf"][:, CST:2 * CST], in_=S["wG"][:, :]),
         reads=["wG"], writes=["tabf_b"])
    b.do("sync", lambda eng: eng.dma_start(out=arin_tab[:, :], in_=S["tabf"][:, :]),
         reads=["tabf_a", "tabf_b"], writes=["arin_tab"], dma=ld)
    b.do("gpsimd",
         lambda eng: eng.collective_compute(
             "AllReduce", OP.add, replica_groups=RG,
             ins=[arin_tab[:, :]], outs=[tab_full[:, :]]),
         reads=["arin_tab"], writes=["tab_full"], coll=cc3)

    for ls in range(zsplit, c.segs):
        z_seg(ls)
    out_all = [f"out{ls}_{tt}" for ls in range(c.segs) for tt in range(4)]

    b.fence(exclude=["ln2_full", "tab_full", "cnt_full"])
    free_scope("Z")
    free_scope("R")

    if _cut("C"):
        b.fence()
        b.do("gpsimd", lambda eng: None, wait=[b.gwait])
        b.do("sync", lambda eng: None, wait=[b.gwait])
        b.emit(stack)
        nc.compile()
        return nc
    b.do("sync", lambda eng: eng.dma_start(out=S["tabf"][:, :], in_=tab_full[:, :]),
         reads=["tab_full"], writes=["tabf_a", "tabf_b"], dma=ld)
    b.do("vector",
         lambda eng: eng.tensor_scalar(out=S["minef"][:, :], in0=S["tabf"][:, 0:CST],
                                       scalar1=S["rowb"][0:c.E, 0:1], scalar2=None,
                                       op0=OP.is_ge),
         reads=["tabf_a", "rowb"], writes=["minef"])
    b.do("vector",
         lambda eng: eng.tensor_scalar(out=S["mtmp"][:, :], in0=S["tabf"][:, 0:CST],
                                       scalar1=S["rowb"][0:c.E, 1:2], scalar2=None,
                                       op0=OP.is_lt),
         reads=["tabf_a", "rowb"], writes=["mtmp"])
    b.do("vector",
         lambda eng: eng.tensor_mul(S["minef"][:, :], S["minef"][:, :], S["mtmp"][:, :]),
         reads=["mtmp", "minef"], writes=["minef"])
    b.do("vector",
         lambda eng: eng.tensor_scalar(out=S["mtmp"][:, :], in0=S["tabf"][:, CST:2 * CST],
                                       scalar1=0.0, scalar2=None, op0=OP.is_gt),
         reads=["tabf_b"], writes=["mtmp"])
    b.do("vector",
         lambda eng: eng.tensor_mul(S["minef"][:, :], S["minef"][:, :], S["mtmp"][:, :]),
         reads=["mtmp", "minef"], writes=["minef"])
    b.do("vector",
         lambda eng: eng.tensor_scalar(out=S["mtmp"][:, :], in0=S["tabf"][:, 0:CST],
                                       scalar1=S["rowb"][0:c.E, 0:1], scalar2=None,
                                       op0=OP.subtract),
         reads=["tabf_a", "rowb"], writes=["mtmp"])
    b.do("vector",
         lambda eng: eng.tensor_scalar(out=S["mtmp"][:, :], in0=S["mtmp"][:, :],
                                       scalar1=-1.0, scalar2=-1.0, op0=OP.mult,
                                       op1=OP.add),
         reads=["mtmp"], writes=["mtmp"])
    b.do("vector", lambda eng: eng.tensor_copy(out=S["ltok"][:, :], in_=S["mtmp"][:, :]),
         reads=["mtmp"], writes=["ltok"])
    b.do("vector", lambda eng: eng.memset(S["onescol"][:, :], 1.0), writes=["onescol"])
    b.do("vector",
         lambda eng: eng.tensor_tensor_scan(
             out=S["mtmp"][:, :], data0=S["onescol"][:, :].to_broadcast([c.E, c.CAP]),
             data1=S["minef"][:, :], initial=0.0, op0=OP.mult, op1=OP.add),
         reads=["onescol", "minef"], writes=["mtmp"])
    b.do("vector",
         lambda eng: eng.tensor_copy(out=S["cntrow"][:, :], in_=S["mtmp"][:, CST - 1:CST]),
         reads=["mtmp"], writes=["cntrow"])
    b.do("vector",
         lambda eng: eng.tensor_sub(S["mtmp"][:, :], S["mtmp"][:, :], S["minef"][:, :]),
         reads=["mtmp", "minef"], writes=["mtmp"])
    b.do("vector",
         lambda eng: eng.tensor_copy(out=S["mtmpI"][:, :], in_=S["mtmp"][:, :]),
         reads=["mtmp"], writes=["mtmpI"])
    b.do("vector",
         lambda eng: eng.tensor_scalar(out=S["mtmpI"][:, :], in0=S["mtmpI"][:, :],
                                       scalar1=15, scalar2=None, op0=OP.bitwise_and),
         reads=["mtmpI"], writes=["mtmpI"])
    b.do("vector",
         lambda eng: eng.tensor_copy(out=S["mtmp3"][:, :], in_=S["mtmpI"][:, :]),
         reads=["mtmpI"], writes=["mtmp3"])
    b.do("vector",
         lambda eng: eng.tensor_sub(S["mtmp2"][:, :], S["mtmp"][:, :], S["mtmp3"][:, :]),
         reads=["mtmp", "mtmp3"], writes=["mtmp2"])
    b.do("vector",
         lambda eng: eng.tensor_scalar_mul(S["mtmp2"][:, :], S["mtmp2"][:, :], 0.0625),
         reads=["mtmp2"], writes=["mtmp2"])
    b.do("vector",
         lambda eng: eng.scalar_tensor_tensor(out=S["mtmp2"][:, :], in0=S["mtmp3"][:, :],
                                              scalar=32.0, in1=S["mtmp2"][:, :],
                                              op0=OP.mult, op1=OP.add),
         reads=["mtmp3", "mtmp2"], writes=["mtmp2"])
    b.do("vector",
         lambda eng: eng.scalar_tensor_tensor(out=S["mtmp2"][:, :], in0=S["mtmp2"][:, :],
                                              scalar=1.0, in1=S["minef"][:, :],
                                              op0=OP.add, op1=OP.mult),
         reads=["mtmp2", "minef"], writes=["mtmp2"])
    b.do("vector",
         lambda eng: eng.tensor_scalar_add(S["mtmp2"][:, :], S["mtmp2"][:, :], -1.0),
         reads=["mtmp2"], writes=["mtmp2"])
    b.do("vector", lambda eng: eng.tensor_copy(out=S["rankD"][:, :], in_=S["mtmp2"][:, :]),
         reads=["mtmp2"], writes=["rankD"])
    b.do("gpsimd",
         lambda eng: eng.local_scatter(S["listSS"][:, :], S["sloti"][:, :],
                                       S["rankD"][:, :], c.E, c.CAP, c.CAP),
         reads=["sloti", "rankD"], writes=["listSS"])
    b.do("gpsimd",
         lambda eng: eng.local_scatter(S["listTT"][:, :], S["ltok"][:, :],
                                       S["rankD"][:, :], c.E, c.CAP, c.CAP),
         reads=["ltok", "rankD"], writes=["listTT"])
    b.do("vector",
         lambda eng: eng.tensor_scalar(out=S["listSS"][:, :], in0=S["listSS"][:, :],
                                       scalar1=-1.0, scalar2=-1.0, op0=OP.mult, op1=OP.add),
         reads=["listSS"], writes=["listSS"])
    b.do("vector",
         lambda eng: eng.tensor_scalar(out=S["listTT"][:, :], in0=S["listTT"][:, :],
                                       scalar1=-1.0, scalar2=-1.0, op0=OP.mult, op1=OP.add),
         reads=["listTT"], writes=["listTT"])
    b.do("vector",
         lambda eng: eng.tensor_scalar_max(S["listSS"][:, 0:1], S["listSS"][:, 0:1], 0.0),
         reads=["listSS"], writes=["listSS"])
    b.do("vector",
         lambda eng: eng.tensor_copy(out=S["t128"][:, 0:1], in_=S["listTT"][:, 0:1]),
         reads=["listTT"], writes=["t128d0"])
    b.do("vector",
         lambda eng: eng.tensor_scalar(out=S["t128"][:, 1:2], in0=S["t128"][:, 0:1],
                                       scalar1=0.0, scalar2=None, op0=OP.is_lt),
         reads=["t128d0"], writes=["t128d1"])
    b.do("vector",
         lambda eng: eng.scalar_tensor_tensor(out=S["t128"][:, 0:1], in0=S["t128"][:, 1:2],
                                              scalar=float(c.Tl + 1), in1=S["t128"][:, 0:1],
                                              op0=OP.mult, op1=OP.add),
         reads=["t128d1", "t128d0"], writes=["t128d0"])
    b.do("vector",
         lambda eng: eng.tensor_copy(out=S["listTT"][:, 0:1], in_=S["t128"][:, 0:1]),
         reads=["t128d0"], writes=["listTT"])
    b.do("sync", lambda eng: eng.dma_start(out=cnts_d[0:1, :].rearrange("a e -> e a"),
                                           in_=S["cntrow"][:, :]),
         reads=["cntrow"], writes=["cnts_d"], dma=ld)
    b.do("sync", lambda eng: eng.dma_start(out=S["nfsf"][:, :], in_=cnts_d[:, :]),
         reads=["cnts_d"], writes=["nfsf"], dma=ld)
    b.do("vector", lambda eng: eng.tensor_copy(out=S["nfs"][:, :], in_=S["nfsf"][:, :]),
         reads=["nfsf"], writes=["nfs"])
    b.do("sync", lambda eng: eng.dma_start(out=lists_d[:, 0:c.CAP], in_=S["listSS"][:, :]),
         reads=["listSS"], writes=["lists_d1"], dma=ld)
    b.do("sync", lambda eng: eng.dma_start(out=lists_d[:, c.CAP:], in_=S["listTT"][:, :]),
         reads=["listTT"], writes=["lists_d2"], dma=ld)
    b.do("vector", lambda eng: eng.memset(S["dlist"][:, :], 0), writes=["dlist_z"])
    for half, res in ((0, "lists_d1"), (1, "lists_d2")):
        for pp in (0, 16):
            b.do("sync",
                 (lambda half_, pp_: (lambda eng: eng.dma_start(
                     out=S["dlist"][pp_:pp_ + 16,
                                    half_ * c.E * 32:(half_ + 1) * c.E * 32].rearrange(
                         "p (e s) -> p e s", e=c.E),
                     in_=lists_d[:, half_ * c.CAP:(half_ + 1) * c.CAP].rearrange(
                         "e (p s) -> p e s", p=16))))(half, pp),
                 reads=[res, "dlist_z"], writes=[f"dlist{half}_{pp}"], dma=ld)
    dlist_all = [f"dlist{h}_{p}" for h in (0, 1) for p in (0, 16)]

    EWP = 2 * c.DC * c.I + c.IC * c.D
    sbuf("expW", [128, 2 * EWP], BF16, scope="C")
    sbuf("xbfm", [128, 2 * c.DC * c.CAP], BF16, scope="C")
    sbuf("hh", [128, c.IC * c.CAP], BF16, scope="C")
    sbuf("ybt", [128, 2 * c.D], BF16, scope="C")
    sbuf("siltC", [128, c.CAP], F32, scope="C")
    sbuf("myidw", [c.EPC, 2 * c.CAP], F32, scope="C")
    sbuf("myids16", [c.EPC, c.CAP], I16, scope="C")
    sbuf("wTslot", [128, 4 * c.EPC], F32, scope="C")
    sbuf("eidx", [128, c.EPC * (c.CAP // 16)], I16, scope="C")

    b.do("gpsimd",
         lambda eng: eng.indirect_dma_start(
             S["myidw"][:, :], None, tab_full[:, :],
             IndirectOffsetOnAxis(S["myexp"][:, :], 0)),
         reads=["tab_full", "myexp"], writes=["myidw"], dma=pda)
    for sb4 in range(4):
        b.do("tensor",
             (lambda s4: (lambda eng: eng.transpose(
                 P["tsm"][0:128, 0:c.EPC],
                 S["myidw"][:, CST + s4 * 128: CST + (s4 + 1) * 128],
                 S["idf32"][0:c.EPC, 0:c.EPC])))(sb4),
             reads=["myidw", "idf32"], writes=["tsm"])
        b.do("vector",
             (lambda s4: (lambda eng: eng.tensor_copy(
                 out=S["wTslot"][:, s4 * c.EPC:(s4 + 1) * c.EPC],
                 in_=P["tsm"][0:128, 0:c.EPC])))(sb4),
             reads=["tsm"], writes=[f"wTslot{sb4}"])
    b.do("vector",
         lambda eng: eng.tensor_copy(out=S["myids16"][:, :], in_=S["myidw"][:, 0:CST]),
         reads=["myidw"], writes=["myids16"])
    b.do("scalar",
         lambda eng: eng.dma_start(out=eidx_d[:, :], in_=S["myids16"][:, :]),
         reads=["myids16"], writes=["eidx_d"], dma=stw)
    b.do("vector", lambda eng: eng.memset(S["eidx"][:, :], 0), writes=["eidx_z"])
    b.do("sync",
         lambda eng: eng.dma_start(
             out=S["eidx"][0:16, :].rearrange("p (e s) -> p e s", e=c.EPC),
             in_=eidx_d.rearrange("e (p s) -> p e s", p=16)),
         reads=["eidx_d", "eidx_z"], writes=["eidx_a"], dma=ld)
    b.do("sync",
         lambda eng: eng.dma_start(
             out=S["eidx"][16:32, :].rearrange("p (e s) -> p e s", e=c.EPC),
             in_=eidx_d.rearrange("e (p s) -> p e s", p=16)),
         reads=["eidx_d", "eidx_z"], writes=["eidx_b"], dma=ld)

    for j in range(c.EPC):
        par = j % 2
        eb = par * EWP
        b.do("gpsimd",
             (lambda j_, par_: (lambda eng: eng.dma_gather(
                 S["xbfm"][:, par_ * c.DC * c.CAP:(par_ + 1) * c.DC * c.CAP].rearrange(
                     "p (k t) -> p k t", k=c.DC),
                 ln2_full[:, :],
                 S["eidx"][:, j_ * (c.CAP // 16):(j_ + 1) * (c.CAP // 16)],
                 c.CAP, c.CAP, c.D, transpose=True)))(j, par),
             reads=["eidx_a", "eidx_b", "ln2_full"], writes=[f"xbfm{par}"], dma=pdb)
        b.do("gpsimd",
             (lambda j_, eb_: (lambda eng: eng.dma_start(
                 out=S["expW"][:, eb_: eb_ + c.DC * c.I].rearrange(
                     "p (k i) -> p k i", k=c.DC),
                 in_=w1T_in[j_].rearrange("(k p) i -> p k i", p=128))))(j, eb),
             writes=[f"expWa{par}"], dma=pdb)
        b.do("gpsimd",
             (lambda j_, eb_: (lambda eng: eng.dma_start(
                 out=S["expW"][:, eb_ + c.DC * c.I: eb_ + 2 * c.DC * c.I].rearrange(
                     "p (k i) -> p k i", k=c.DC),
                 in_=w3T_in[j_].rearrange("(k p) i -> p k i", p=128))))(j, eb),
             writes=[f"expWb{par}"], dma=pdb)
        nfull = c.I // 128
        b.do("gpsimd",
             (lambda j_, eb_, nf: (lambda eng: eng.dma_start(
                 out=S["expW"][:, eb_ + 2 * c.DC * c.I:
                               eb_ + 2 * c.DC * c.I + nf * c.D].rearrange(
                     "p (k i) -> p k i", k=nf),
                 in_=w2T_in[j_][0:nf * 128, :].rearrange("(k p) i -> p k i", p=128))))(j, eb, nfull),
             writes=[f"expWc{par}"], dma=pdb)
        b.do("gpsimd",
             (lambda j_, eb_, nf: (lambda eng: eng.dma_start(
                 out=S["expW"][0:c.I - nf * 128,
                               eb_ + 2 * c.DC * c.I + nf * c.D:
                               eb_ + 2 * c.DC * c.I + (nf + 1) * c.D],
                 in_=w2T_in[j_][nf * 128:, :])))(j, eb, nfull),
             writes=[f"expWc2{par}"], dma=pdb)

        for ic in range(c.IC):
            icw = 128 if ic < c.IC - 1 else c.I - 128 * (c.IC - 1)
            for zi in range(2):
                bank = f"g{zi}"

                def h_group(eng, ic_=ic, zi_=zi, bank_=bank, par_=par, icw_=icw, eb_=eb):
                    ii = None
                    for kc in range(c.DC):
                        ii = eng.matmul(
                            P[bank_][0:icw_, :],
                            S["expW"][:, eb_ + zi_ * c.DC * c.I + kc * c.I + ic_ * 128:
                                      eb_ + zi_ * c.DC * c.I + kc * c.I + ic_ * 128 + icw_],
                            S["xbfm"][:, (par_ * c.DC + kc) * c.CAP:
                                      (par_ * c.DC + kc + 1) * c.CAP],
                            start=(kc == 0), stop=(kc == c.DC - 1))
                    return ii

                b.do("tensor", h_group,
                     reads=[f"expWa{par}", f"expWb{par}", f"xbfm{par}"],
                     writes=[bank])
            b.do("scalar",
                 (lambda icw_: (lambda eng: eng.activation(
                     out=S["siltC"][0:icw_, :], in_=P["g0"][0:icw_, :], func=AF.Silu)))(icw),
                 reads=["g0"], writes=["siltC0"])
            b.do("vector",
                 (lambda ic_, icw_: (lambda eng: eng.tensor_mul(
                     S["hh"][0:icw_, ic_ * c.CAP:(ic_ + 1) * c.CAP],
                     S["siltC"][0:icw_, :], P["g1"][0:icw_, :])))(ic, icw),
                 reads=["siltC0", "g1"], writes=[f"hhc{ic}"])

        for sb4 in range(4):
            yh = sb4 % 2
            for nh in range(2):
                bank = f"g{2 + nh}"

                def y_group(eng, sb4_=sb4, nh_=nh, bank_=bank, eb_=eb):
                    ii = None
                    for ic in range(c.IC):
                        icw = 128 if ic < c.IC - 1 else c.I - 128 * (c.IC - 1)
                        ii = eng.matmul(
                            P[bank_][:, :],
                            S["hh"][0:icw, ic * c.CAP + sb4_ * 128: ic * c.CAP + (sb4_ + 1) * 128],
                            S["expW"][0:icw, eb_ + 2 * c.DC * c.I + ic * c.D + nh_ * 512:
                                      eb_ + 2 * c.DC * c.I + ic * c.D + (nh_ + 1) * 512],
                            start=(ic == 0), stop=(ic == c.IC - 1))
                    return ii

                b.do("tensor", y_group,
                     reads=[f"hhc{ic}" for ic in range(c.IC)] +
                           [f"expWc{par}", f"expWc2{par}"],
                     writes=[bank])
                b.do("vector",
                     (lambda bank_, sb4_, nh_, j_, yh_: (lambda eng: eng.tensor_scalar_mul(
                         S["ybt"][:, yh_ * c.D + nh_ * 512: yh_ * c.D + (nh_ + 1) * 512],
                         P[bank_][:, :],
                         S["wTslot"][:, sb4_ * c.EPC + j_: sb4_ * c.EPC + j_ + 1])))(bank, sb4, nh, j, yh),
                     reads=[bank, f"wTslot{sb4}"], writes=[f"ybt{yh}_{nh}"])
            b.do("scalar",
                 (lambda j_, sb4_, yh_: (lambda eng: eng.dma_start(
                     out=agin_yb[j_ * c.CAP + sb4_ * 128: j_ * c.CAP + (sb4_ + 1) * 128, :],
                     in_=S["ybt"][:, yh_ * c.D:(yh_ + 1) * c.D])))(j, sb4, yh),
                 reads=[f"ybt{yh}_0", f"ybt{yh}_1"], writes=[f"agyb{j}_{sb4}"], dma=stw)

        if (j + 1) % c.EPCH == 0:
            k4 = j // c.EPCH
            b.do("gpsimd",
                 (lambda k_: (lambda eng: eng.collective_compute(
                     "AllGather", OP.bypass, replica_groups=RG,
                     ins=[agin_yb[k_ * c.EPCH * c.CAP:(k_ + 1) * c.EPCH * c.CAP, :]],
                     outs=[ybc[k_][:, :]])))(k4),
                 reads=[f"agyb{jj}_{s4}"
                        for jj in range(k4 * c.EPCH, (k4 + 1) * c.EPCH)
                        for s4 in range(4)],
                 writes=[f"ybc{k4}"], coll=cc4s[k4])

    b.fence(exclude=["ln2_full", "tab_full", "cnt_full"] +
            [f"ybc{k}" for k in range(c.NCH)])
    free_scope("C")

    if _cut("D"):
        b.fence()
        b.do("gpsimd", lambda eng: None, wait=[b.gwait])
        b.do("sync", lambda eng: None, wait=[b.gwait])
        b.emit(stack)
        nc.compile()
        return nc
    sbuf("gvb", [128, 4 * 4 * c.D], BF16, scope="D")
    sbuf("gvf", [128, 4 * 4 * c.D], F32, scope="D")
    sbuf("zrow", [128, c.D], F32, scope="D")

    b.do("vector", lambda eng: eng.memset(S["gvb"][:, :], 0),
         writes=[f"gvb{p}" for p in range(4)])
    b.do("vector", lambda eng: eng.memset(S["zrow"][:, :], 0.0), writes=["zrow"])
    b.do("sync",
         lambda eng: eng.dma_start(out=out[c.Tl:c.Tl + 128, :], in_=S["zrow"][:, :]),
         reads=["zrow"], writes=["outPad"], dma=ld)

    def get_nkreg(eng):
        r = getattr(eng, "_nkreg", None)
        if r is None:
            r = eng.alloc_register("nkreg")
            eng._nkreg = r
        return r

    ei = 0
    for k4 in range(c.NCH):
        chunk = [cc_ * c.EPC + k4 * c.EPCH + jj
                 for cc_ in range(c.n_cores) for jj in range(c.EPCH)]
        for e in chunk:
            par = ei % 4
            ei += 1
            gvb = S["gvb"][:, par * 4 * c.D:(par + 1) * 4 * c.D].rearrange(
                "p (a d) -> p a d", d=c.D)
            gvf = S["gvf"][:, par * 4 * c.D:(par + 1) * 4 * c.D].rearrange(
                "p (a d) -> p a d", d=c.D)

            def d_gather(eng, e_=e, gvb_=gvb, k_=k4):
                r = get_nkreg(eng)
                eng.reg_load(r, S["nfs"][0:1, e_:e_ + 1])
                eng.reg_alu(r, r, 1, OP.max)
                return eng.dma_gather(
                    gvb_, ybc[k_][:, :],
                    S["dlist"][:, e_ * 32:(e_ + 1) * 32],
                    c.CAP, r, c.D)

            b.do("gpsimd", d_gather,
                 reads=[f"ybc{k4}", "nfs"] + dlist_all, writes=[f"gvb{par}"], dma=pdc)
            b.do("vector",
                 (lambda par_: (lambda eng: eng.tensor_copy(
                     out=S["gvf"][:, par_ * 4 * c.D:(par_ + 1) * 4 * c.D],
                     in_=S["gvb"][:, par_ * 4 * c.D:(par_ + 1) * 4 * c.D])))(par),
                 reads=[f"gvb{par}"], writes=[f"gvf{par}"])

            def d_scatter(eng, e_=e, gvf_=gvf):
                r = get_nkreg(eng)
                eng.reg_load(r, S["nfs"][0:1, e_:e_ + 1])
                eng.reg_alu(r, r, 1, OP.max)
                return eng.dma_scatter_add(
                    out[:, :], gvf_,
                    S["dlist"][:, (c.E + e_) * 32:(c.E + e_ + 1) * 32],
                    c.CAP, r, c.D)

            b.do("gpsimd", d_scatter,
                 reads=["nfs", f"gvf{par}", "outPad"] + dlist_all + out_all,
                 writes=["outRMW"], dma=pdd)

    if debug_dump:
        dbg_specs = [("dbg_tab", tab_full, [c.E, 2 * c.CAP], F32, ["tab_full"]),
                     ("dbg_ln2", agin_ln2, [c.Tl, c.D], BF16, all_agin),
                     ("dbg_x1", x1_d, [c.Tl, c.D], F32,
                      [f"x1_d{ls}_{tt}" for ls in range(c.segs) for tt in range(4)]),
                     ("dbg_lists", lists_d, [c.E, 2 * c.CAP], I16,
                      ["lists_d1", "lists_d2"]),
                     ("dbg_cnt", cnts_d, [1, c.E], F32, ["cnts_d"])]
        for nm, srcap, shp, dt, rds in dbg_specs:
            od = nc.dram_tensor(nm, shp, dt, kind="ExternalOutput").ap()
            b.do("sync",
                 (lambda od_, s_: (lambda eng: eng.dma_start(out=od_[:, :], in_=s_[:, :])))(od, srcap),
                 reads=rds + ["outRMW"], writes=[nm], dma=ld)
        b.do("sync", lambda eng: None,
             reads=["outRMW"] + out_all + [d[0] for d in dbg_specs])
    b.do("gpsimd", lambda eng: None, reads=["outRMW", "outPad"] + out_all)
    b.do("sync", lambda eng: None, reads=["outRMW", "outPad"] + out_all)
    b.emit(stack)
    nc.compile()
    return nc


def _host_constants(c: Cfg, core):
    o = {}
    Lup = np.zeros((128, 132), np.float32)
    for t in range(128):
        Lup[:t, t] = 1.0
    Lup[:, 128] = 1.0
    o["Lup"] = Lup.astype(BF)
    o["idf32"] = np.eye(128, dtype=np.float32)
    o["idbf"] = np.eye(128, dtype=np.float32).astype(BF)
    ss = np.zeros((128, c.DC * 128), np.float32)
    sb = np.zeros((128, c.DC * 128), np.float32)
    for kc in range(c.DC):
        for hloc in range(2):
            hglob = 2 * kc + hloc
            rc = (hglob % 4) * 32 + hglob // 4
            ss[hloc * 64:(hloc + 1) * 64, kc * 128 + rc] = 0.125
            sb[rc, kc * 128 + hloc * 64: kc * 128 + (hloc + 1) * 64] = 1.0
    o["Ssum"] = ss.astype(BF)
    o["Sb"] = sb.astype(BF)
    o["ones1"] = np.ones((1, 128), np.float32)
    segsel = np.zeros((128, c.segs * c.segs), np.float32)
    for ls in range(c.segs):
        segsel[:, ls * c.segs + ls] = 1.0
    o["segsel"] = segsel.astype(BF)
    ids = core * c.Tl + np.arange(c.Tl, dtype=np.int64)
    o["ids16"] = np.broadcast_to(ids[None, :], (c.E, c.Tl)).astype(np.int16).copy()
    col = np.arange(c.CAP, dtype=np.int64)
    siginv = (col % 32) * 16 + col // 32
    ee = np.arange(c.E, dtype=np.int64)
    ebase = (ee // c.EPC) * (c.EPCH * c.CAP) + ((ee % c.EPC) % c.EPCH) * c.CAP
    sloti = ebase[:, None] + siginv[None, :]
    o["sloti"] = (-(sloti + 1)).astype(np.int16)
    mm = np.zeros((c.nseg_tot, c.segs), np.float32)
    for r in range(c.nseg_tot):
        cc, ls2 = divmod(r, c.segs)
        nglob = cc + c.n_cores * ls2
        for ls in range(c.segs):
            if nglob < core + c.n_cores * ls:
                mm[r, ls] = 1.0
    o["maskmat"] = mm
    rb = np.zeros((128, 2), np.float32)
    rb[:, 0] = core * c.Tl
    rb[:, 1] = (core + 1) * c.Tl
    o["rowbase"] = rb
    o["myexp"] = (core * c.EPC + np.arange(c.EPC, dtype=np.int32)).reshape(c.EPC, 1)
    return o


def host_inputs(c: Cfg, inp):
    x = np.ascontiguousarray(np.asarray(inp["L0_attn_x"], np.float32).reshape(c.T, c.D))
    f32 = lambda a: np.ascontiguousarray(np.asarray(a, np.float32))
    Wq, Wk = f32(inp["L0_attn_Wq"]), f32(inp["L0_attn_Wk"])
    Wv, Wo = f32(inp["L0_attn_Wv"]), f32(inp["L0_attn_Wo"])
    bqkv = np.concatenate(
        [np.asarray(inp[k], np.float32).reshape(c.DC, 128).T
         for k in ("L0_attn_bq", "L0_attn_bk", "L0_attn_bv")], axis=1)
    bqkv = np.ascontiguousarray(bqkv)
    bo_row = np.ascontiguousarray(np.asarray(inp["L0_attn_bo"], np.float32).reshape(1, c.D))
    ws1T = np.ascontiguousarray(np.asarray(inp["L0_ffn_ws1"], np.float32).T)
    ws3T = np.ascontiguousarray(np.asarray(inp["L0_ffn_ws3"], np.float32).T)
    ws2T = np.ascontiguousarray(np.asarray(inp["L0_ffn_ws2"], np.float32).T)
    gwT = np.ascontiguousarray(np.asarray(inp["L0_ffn_gate_w"], np.float32).T)
    gb_b = np.ascontiguousarray(np.broadcast_to(
        np.asarray(inp["L0_ffn_gate_b"], np.float32)[None, :], (128, c.E)))
    w1T = np.ascontiguousarray(np.asarray(inp["L0_ffn_w1"], np.float32).transpose(0, 2, 1))
    w3T = np.ascontiguousarray(np.asarray(inp["L0_ffn_w3"], np.float32).transpose(0, 2, 1))
    w2T = np.ascontiguousarray(np.asarray(inp["L0_ffn_w2"], np.float32).transpose(0, 2, 1))

    in_maps = []
    for core in range(c.n_cores):
        segs = [core + c.n_cores * ls for ls in range(c.segs)]
        xs = np.ascontiguousarray(
            np.concatenate([x[n * c.SEG:(n + 1) * c.SEG] for n in segs], axis=0))
        m = {"x": xs, "Wq": Wq, "Wk": Wk, "Wv": Wv, "Wo": Wo,
             "bqkv": bqkv, "bo_row": bo_row,
             "ws1T": ws1T, "ws3T": ws3T, "ws2T": ws2T,
             "gwT": gwT, "gb_bcast": gb_b,
             "w1T": w1T[core * c.EPC:(core + 1) * c.EPC],
             "w3T": w3T[core * c.EPC:(core + 1) * c.EPC],
             "w2T": w2T[core * c.EPC:(core + 1) * c.EPC]}
        m.update(_host_constants(c, core))
        in_maps.append(m)
    return in_maps


def assemble_out(c: Cfg, results, shape):
    outf = np.empty((c.T, c.D), np.float32)
    for core in range(c.n_cores):
        oc = results[core]["out"]
        for ls in range(c.segs):
            n = core + c.n_cores * ls
            outf[n * c.SEG:(n + 1) * c.SEG] = oc[ls * c.SEG:(ls + 1) * c.SEG]
    return outf.reshape(shape)


_CACHED = {}


def kernel(L0_attn_x, L0_attn_Wq, L0_attn_bq, L0_attn_Wk, L0_attn_bk,
           L0_attn_Wv, L0_attn_bv, L0_attn_Wo, L0_attn_bo,
           L0_ffn_gate_w, L0_ffn_gate_b, L0_ffn_w1, L0_ffn_w2, L0_ffn_w3,
           L0_ffn_ws1, L0_ffn_ws2, L0_ffn_ws3):
    c = Cfg(n_cores=8, segs=8)
    inp = dict(L0_attn_x=L0_attn_x, L0_attn_Wq=L0_attn_Wq, L0_attn_bq=L0_attn_bq,
               L0_attn_Wk=L0_attn_Wk, L0_attn_bk=L0_attn_bk, L0_attn_Wv=L0_attn_Wv,
               L0_attn_bv=L0_attn_bv, L0_attn_Wo=L0_attn_Wo, L0_attn_bo=L0_attn_bo,
               L0_ffn_gate_w=L0_ffn_gate_w, L0_ffn_gate_b=L0_ffn_gate_b,
               L0_ffn_w1=L0_ffn_w1, L0_ffn_w2=L0_ffn_w2, L0_ffn_w3=L0_ffn_w3,
               L0_ffn_ws1=L0_ffn_ws1, L0_ffn_ws2=L0_ffn_ws2, L0_ffn_ws3=L0_ffn_ws3)
    key = (c.n_cores, c.segs)
    if key not in _CACHED:
        stack = ExitStack()
        nc = build_program(c, stack)
        _CACHED[key] = (nc, stack)
    nc = _CACHED[key][0]
    in_maps = host_inputs(c, inp)
    res = run_bass_kernel_spmd(nc, in_maps, core_ids=list(range(c.n_cores)))
    return assemble_out(c, res.results, np.asarray(L0_attn_x).shape)


# version 16
# speedup vs baseline: 1.5909x; 1.0022x over previous
import numpy as np
import ml_dtypes
from contextlib import ExitStack

import concourse.bass as bass
import concourse.bacc as bacc
import concourse.mybir as mybir
from concourse.bass import IndirectOffsetOnAxis
from concourse.bass_utils import run_bass_kernel_spmd
from concourse import library_config

F32 = mybir.dt.float32
BF16 = mybir.dt.bfloat16
I16 = mybir.dt.int16
I32 = mybir.dt.int32
U32 = mybir.dt.uint32
AF = mybir.ActivationFunctionType
OP = mybir.AluOpType
AX = mybir.AxisListType
BF = ml_dtypes.bfloat16


class Cfg:
    def __init__(self, n_cores=8, segs=8):
        self.n_cores = n_cores
        self.segs = segs
        self.SEG = 512
        self.D = 1024
        self.DC = 8
        self.H = 16
        self.E = 64
        self.CAP = 512
        self.I = 704
        self.IC = 6
        self.SI = 1408
        self.SIC = 11
        self.Tl = segs * self.SEG
        self.tiles = self.Tl // 128
        self.T = self.Tl * n_cores
        self.EPC = self.E // n_cores
        self.nseg_tot = segs * n_cores
        self.NCH = 2
        self.EPCH = self.EPC // self.NCH


class Sem:
    def __init__(self, name, producer=None):
        self.name, self.h, self.n, self.producer = name, None, 0, producer


class Stream:
    def __init__(self, b, name, nsem=4):
        self.sems = [b.sem(f"{name}{i}") for i in range(nsem)]
        self.i = 0

    def next(self):
        s = self.sems[self.i % len(self.sems)]
        self.i += 1
        return s


class Builder:
    ENGINES = ("sync", "scalar", "vector", "tensor", "gpsimd")

    def __init__(self, nc):
        self.nc = nc
        self.steps = []
        self.sems = []
        self.res = {}
        self.gwait = None
        self.eng_sem = {e: self.sem("c_" + e, producer=e) for e in self.ENGINES}

    def sem(self, name, producer=None):
        s = Sem(name, producer)
        self.sems.append(s)
        return s

    def _dep(self, name):
        return self.res.setdefault(name, {"w": None, "r": []})

    def do(self, engine, fn, reads=(), writes=(), wait=(), inc=(), dma=None,
           coll=None):
        waits = {}

        def addw(sv):
            if sv is None:
                return
            s, v = sv
            waits[s] = max(waits.get(s, 0), v)

        for r in reads:
            addw(self._dep(r)["w"])
        for w in writes:
            d = self._dep(w)
            addw(d["w"])
            for sv in d["r"]:
                addw(sv)
        for sv in wait:
            addw(sv)
        addw(self.gwait)

        if coll is not None:
            s = coll
            amt = 1
        elif dma is None:
            s = self.eng_sem[engine]
            amt = 1
        else:
            s = dma.next() if isinstance(dma, Stream) else dma
            amt = 16
            if s.n > 0:
                waits[s] = max(waits.get(s, 0), s.n)
        s.n += amt
        comp = (s, s.n)
        incs = [(s, amt)]
        for (x, a) in inc:
            x.n += a
            incs.append((x, a))

        for w in writes:
            d = self._dep(w)
            d["w"] = comp
            d["r"] = []
        for r in reads:
            self._dep(r)["r"].append(comp)

        self.steps.append((engine, fn,
                           sorted(waits.items(), key=lambda kv: kv[0].name), incs))
        return comp

    def fence(self, exclude=()):
        allr = [k for k in self.res.keys() if k not in exclude]
        h = self.do("vector", lambda eng: None, reads=allr)
        self.gwait = h

    def emit(self, stack):
        for s in self.sems:
            s.h = stack.enter_context(self.nc.semaphore(s.name))
        per = {e: [] for e in self.ENGINES}
        for st in self.steps:
            per[st[0]].append(st)
        with self.nc.Block() as block:
            def mk(engname):
                def body(eng):
                    seen = {}
                    for (_, fn, waits, incs) in per[engname]:
                        for (s, v) in waits:
                            if seen.get(s, 0) >= v:
                                continue
                            eng.wait_ge(s.h, v)
                            seen[s] = v
                        instr = fn(eng)
                        for (s, amt) in incs:
                            if instr is None:
                                eng.sem_inc(s.h, amt)
                            else:
                                instr = instr.then_inc(s.h, amt)
                return body
            for e in self.ENGINES:
                if per[e]:
                    getattr(block, e)(mk(e))


def build_program(c: Cfg, stack: ExitStack, stop_after="D", debug_dump=False):
    nc = bacc.Bacc("TRN2")
    b = Builder(nc)
    _PHASES = ["A1", "AG", "A2a", "Q7", "A2b", "C", "D"]
    _stop = _PHASES.index(stop_after)

    def _cut(phase):
        return _PHASES.index(phase) > _stop

    def din(name, shape, dt):
        return nc.dram_tensor(name, shape, dt, kind="ExternalInput").ap()

    def dint(name, shape, dt):
        return nc.dram_tensor(name, shape, dt).ap()

    x_in = din("x", [c.Tl, c.D], F32)
    out = nc.dram_tensor("out", [c.Tl + 128, c.D], F32, kind="ExternalOutput").ap()
    attw_in = [din(w, [c.D, c.D], F32) for w in ("Wq", "Wk", "Wv", "Wo")]
    bqkv_in = din("bqkv", [128, 3 * c.DC], F32)
    bo_in = din("bo_row", [1, c.D], F32)
    ws1T_in = din("ws1T", [c.D, c.SI], F32)
    ws3T_in = din("ws3T", [c.D, c.SI], F32)
    ws2T_in = din("ws2T", [c.SI, c.D], F32)
    gwT_in = din("gwT", [c.D, c.E], F32)
    gb_in = din("gb_bcast", [128, c.E], F32)
    w1T_in = din("w1T", [c.EPC, c.D, c.I], F32)
    w3T_in = din("w3T", [c.EPC, c.D, c.I], F32)
    w2T_in = din("w2T", [c.EPC, c.I, c.D], F32)
    Lup_in = din("Lup", [128, 132], BF16)
    idf_in = din("idf32", [128, 128], F32)
    idb_in = din("idbf", [128, 128], BF16)
    ssum_in = din("Ssum", [128, c.DC * 128], BF16)
    sbsel_in = din("Sb", [128, c.DC * 128], BF16)
    ones1_in = din("ones1", [1, 128], F32)
    segsel_in = din("segsel", [128, c.segs * c.segs], BF16)
    ids16_in = din("ids16", [c.E, c.Tl], I16)
    sloti_in = din("sloti", [c.E, c.CAP], I16)
    maskm_in = din("maskmat", [c.nseg_tot, c.segs], F32)
    rowb_in = din("rowbase", [128, 2], F32)
    myexp_in = din("myexp", [c.EPC, 1], I32)

    ln1_d = dint("ln1_d", [c.Tl, c.D], BF16)
    x1_d = dint("x1_d", [c.Tl, c.D], F32)
    agin_ln2 = dint("agin_ln2", [c.Tl, c.D], BF16)
    ln2_full = dint("ln2_full", [c.n_cores * c.Tl, c.D], BF16)
    agin_cnt = dint("agin_cnt", [c.segs, c.E], F32)
    cnt_full = dint("cnt_full", [c.nseg_tot, c.E], F32)
    arin_tab = dint("arin_tab", [c.E, 2 * c.CAP], F32)
    tab_full = dint("tab_full", [c.E, 2 * c.CAP], F32)
    agin_yb = dint("agin_yb", [c.EPC * c.CAP, c.D], BF16)
    eidx_d = dint("eidx_d", [c.EPC, c.CAP], I16)
    lists_d = dint("lists_d", [c.E, 2 * c.CAP], I16)
    cnts_d = dint("cnts_d", [1, c.E], F32)
    ybc = [dint(f"ybc{k}", [c.n_cores * c.EPCH * c.CAP, c.D], BF16)
           for k in range(c.NCH)]

    S = {}
    CMS = {}

    def sbuf(name, shape, dt, scope=None):
        cm = nc.sbuf_tensor("sb_" + name, shape, dt)
        if scope is None:
            S[name] = stack.enter_context(cm)
        else:
            S[name] = cm.__enter__()
            CMS.setdefault(scope, []).append(cm)
        return S[name]

    def free_scope(scope):
        for cm in reversed(CMS.pop(scope, [])):
            cm.__exit__(None, None, None)

    sbuf("bqkv", [128, 3 * c.DC], F32)
    sbuf("bo_row", [1, c.D], F32)
    sbuf("gb", [128, c.E], F32)
    sbuf("Lup", [128, 132], BF16)
    sbuf("idf32", [128, 128], F32)
    sbuf("idbf", [128, 128], BF16)
    sbuf("Ssum", [128, c.DC * 128], BF16)
    sbuf("Sb", [128, c.DC * 128], BF16)
    sbuf("ones1", [1, 128], F32)
    sbuf("segsel", [128, c.segs * c.segs], BF16)
    sbuf("sloti", [c.E, c.CAP], I16)
    sbuf("maskm", [c.nseg_tot, c.segs], F32)
    sbuf("rowb", [128, 2], F32)
    sbuf("myexp", [c.EPC, 1], I32)
    sbuf("wstage", [128, c.SI], F32)
    sbuf("stat", [128, 16], F32)
    sbuf("x1t", [128, 2 * c.D], F32)
    sbuf("gstat", [128, 4], F32)
    sbuf("sbt", [128, c.E], F32)
    sbuf("top8", [128, 8], F32)
    sbuf("ln2fmseg", [128, 2 * c.DC * c.SEG], BF16)
    sbuf("tilec", [c.E, c.tiles], F32)
    sbuf("segcnt", [c.segs, c.E], F32)
    sbuf("cntf", [c.nseg_tot, c.E], F32)
    sbuf("segoff", [c.E, c.segs], F32)
    sbuf("totoff", [c.E, c.tiles], F32)
    sbuf("tabf", [c.E, 2 * c.CAP], F32)
    sbuf("idsG", [c.E, c.CAP], I16)
    sbuf("wG", [c.E, c.CAP], BF16)
    nblk = c.E // 16
    sbuf("minef", [c.E, c.CAP], F32)
    sbuf("mtmp", [c.E, c.CAP], F32)
    sbuf("mtmp2", [c.E, c.CAP], F32)
    sbuf("mtmp3", [c.E, c.CAP], F32)
    sbuf("mtmpI", [c.E, c.CAP], I32)
    sbuf("ltok", [c.E, c.CAP], I16)
    sbuf("onescol", [c.E, 1], F32)
    sbuf("rankD", [c.E, c.CAP], I16)
    sbuf("listSS", [c.E, c.CAP], I16)
    sbuf("listTT", [c.E, c.CAP], I16)
    sbuf("dlist", [128, 2 * c.E * (c.CAP // 16)], I16)
    sbuf("cntrow", [c.E, 1], F32)
    sbuf("nfs", [1, c.E], U32)
    sbuf("nfsf", [1, c.E], F32)
    sbuf("t128", [c.E, 128], F32)

    P = {}
    for n_, sh in (("g0", [128, 512]), ("g1", [128, 512]), ("g2", [128, 512]),
                   ("g3", [128, 512]), ("psB", [128, c.SEG]), ("t0", [c.E, 132]),
                   ("tsm", [128, 128]), ("psC", [c.segs, c.E])):
        P[n_] = stack.enter_context(nc.psum_tensor(n_, sh, F32))

    ld = Stream(b, "ld", 12)
    stw = Stream(b, "st", 8)
    pda = b.sem("pda")
    pdb = Stream(b, "pdb", 8)
    pdc = Stream(b, "pdc", 4)
    pdd = Stream(b, "pdd", 4)
    cc1, cc2, cc3 = b.sem("cc1"), b.sem("cc2"), b.sem("cc3")
    cc4s = [b.sem(f"cc4{k}") for k in range(c.NCH)]
    CST = c.CAP
    RG = [list(range(c.n_cores))]

    for nm, o, i in (("bqkv", S["bqkv"], bqkv_in), ("bo_row", S["bo_row"], bo_in),
                     ("gb", S["gb"], gb_in), ("Lup", S["Lup"], Lup_in),
                     ("idf32", S["idf32"], idf_in), ("idbf", S["idbf"], idb_in),
                     ("Ssum", S["Ssum"], ssum_in), ("Sb", S["Sb"], sbsel_in),
                     ("ones1", S["ones1"], ones1_in), ("segsel", S["segsel"], segsel_in),
                     ("sloti", S["sloti"], sloti_in),
                     ("maskm", S["maskm"], maskm_in), ("rowb", S["rowb"], rowb_in),
                     ("myexp", S["myexp"], myexp_in)):
        b.do("sync", (lambda o_, i_: (lambda eng: eng.dma_start(out=o_[:, :], in_=i_[:, :])))(o, i),
             writes=[nm], dma=ld)

    sbuf("attW", [128, 4 * c.DC * c.D], BF16, scope="A1")
    sbuf("xt2", [128, 2 * c.D], F32, scope="A1")
    sbuf("sqdump", [128, c.D], BF16, scope="A1")
    sbuf("lnt", [128, 2 * c.D], BF16, scope="A1")
    sbuf("ln1fm", [128, 2 * c.DC * c.SEG], BF16, scope="A1")
    sbuf("Qfm", [128, c.DC * c.SEG], BF16, scope="A1")
    sbuf("Kfm", [128, c.DC * c.SEG], BF16, scope="A1")
    sbuf("Vfm", [128, c.DC * c.SEG], BF16, scope="A1")
    sbuf("Pch", [128, 2 * c.SEG], BF16, scope="A1")
    sbuf("praw", [128, c.SEG], F32, scope="A1")
    sbuf("probs", [128, c.SEG], BF16, scope="A1")
    sbuf("pstat", [128, 4], F32, scope="A1")

    for wi in range(4):
        for kc in range(c.DC):
            b.do("sync",
                 (lambda wi_, kc_: (lambda eng: eng.dma_start(
                     out=S["wstage"][:, 0:c.D],
                     in_=attw_in[wi_][kc_ * 128:(kc_ + 1) * 128, :])))(wi, kc),
                 writes=["wstage"], dma=ld)
            b.do("vector",
                 (lambda wi_, kc_: (lambda eng: eng.tensor_copy(
                     out=S["attW"][:, (wi_ * 8 + kc_) * c.D:(wi_ * 8 + kc_ + 1) * c.D],
                     in_=S["wstage"][:, 0:c.D])))(wi, kc),
                 reads=["wstage"], writes=[f"attW{wi}_{kc}"])

    def ln_tile(xt_ap, out_ap, so, out_res, in_res):
        st_ = S["stat"]
        b.do("scalar",
             (lambda xt_, so_: (lambda eng: eng.activation(
                 out=S["sqdump"][:, :], in_=xt_, func=AF.Square,
                 accum_out=st_[:, so_ * 8 + 1: so_ * 8 + 2])))(xt_ap, so),
             reads=list(in_res), writes=[f"stat{so}b", "sqdump"])
        b.do("vector",
             (lambda xt_, so_: (lambda eng: eng.reduce_sum(
                 out=st_[:, so_ * 8 + 0: so_ * 8 + 1], in_=xt_, axis=AX.X)))(xt_ap, so),
             reads=list(in_res), writes=[f"stat{so}a"])
        b.do("vector",
             (lambda so_: (lambda eng: eng.tensor_scalar_mul(
                 st_[:, so_ * 8 + 2: so_ * 8 + 3], st_[:, so_ * 8 + 0: so_ * 8 + 1],
                 1.0 / c.D)))(so),
             reads=[f"stat{so}a"], writes=[f"stat{so}m"])
        b.do("vector",
             (lambda so_: (lambda eng: eng.tensor_mul(
                 st_[:, so_ * 8 + 3: so_ * 8 + 4], st_[:, so_ * 8 + 2: so_ * 8 + 3],
                 st_[:, so_ * 8 + 2: so_ * 8 + 3])))(so),
             reads=[f"stat{so}m"], writes=[f"stat{so}q"])
        b.do("vector",
             (lambda so_: (lambda eng: eng.scalar_tensor_tensor(
                 out=st_[:, so_ * 8 + 4: so_ * 8 + 5], in0=st_[:, so_ * 8 + 1: so_ * 8 + 2],
                 scalar=1.0 / c.D, in1=st_[:, so_ * 8 + 3: so_ * 8 + 4],
                 op0=OP.mult, op1=OP.subtract)))(so),
             reads=[f"stat{so}b", f"stat{so}q"], writes=[f"stat{so}v"])
        b.do("vector",
             (lambda so_: (lambda eng: eng.tensor_scalar_add(
                 st_[:, so_ * 8 + 4: so_ * 8 + 5], st_[:, so_ * 8 + 4: so_ * 8 + 5],
                 1e-5)))(so),
             reads=[f"stat{so}v"], writes=[f"stat{so}v"])
        b.do("scalar",
             (lambda so_: (lambda eng: eng.activation(
                 out=st_[:, so_ * 8 + 6: so_ * 8 + 7], in_=st_[:, so_ * 8 + 4: so_ * 8 + 5],
                 func=AF.Sqrt)))(so),
             reads=[f"stat{so}v"], writes=[f"stat{so}s"])
        b.do("vector",
             (lambda so_: (lambda eng: eng.reciprocal(
                 out=st_[:, so_ * 8 + 5: so_ * 8 + 6],
                 in_=st_[:, so_ * 8 + 6: so_ * 8 + 7])))(so),
             reads=[f"stat{so}s"], writes=[f"stat{so}r"])
        b.do("vector",
             (lambda xt_, so_, o_: (lambda eng: eng.tensor_scalar(
                 out=o_, in0=xt_,
                 scalar1=st_[:, so_ * 8 + 2: so_ * 8 + 3],
                 scalar2=st_[:, so_ * 8 + 5: so_ * 8 + 6],
                 op0=OP.subtract, op1=OP.mult)))(xt_ap, so, out_ap),
             reads=list(in_res) + [f"stat{so}m", f"stat{so}r"], writes=[out_res])

    for ls in range(c.segs):
        q = ls % 2
        for tt in range(4):
            hh = tt % 2
            b.do("sync",
                 (lambda ls_, tt_, h_: (lambda eng: eng.dma_start(
                     out=S["xt2"][:, h_ * c.D:(h_ + 1) * c.D],
                     in_=x_in[ls_ * c.SEG + tt_ * 128: ls_ * c.SEG + (tt_ + 1) * 128, :])))(ls, tt, hh),
                 writes=[f"xt2{hh}"], dma=ld)
            xt = S["xt2"][:, hh * c.D:(hh + 1) * c.D]
            ln_tile(xt, S["lnt"][:, hh * c.D:(hh + 1) * c.D], hh, f"lnt{hh}", [f"xt2{hh}"])
            b.do("scalar",
                 (lambda ls_, tt_, h_: (lambda eng: eng.dma_start(
                     out=ln1_d[ls_ * c.SEG + tt_ * 128: ls_ * c.SEG + (tt_ + 1) * 128, :],
                     in_=S["lnt"][:, h_ * c.D:(h_ + 1) * c.D])))(ls, tt, hh),
                 reads=[f"lnt{hh}"], writes=[f"ln1_d{ls}_{tt}"], dma=stw)
        for kc in range(c.DC):
            b.do("sync",
                 (lambda ls_, kc_, q_: (lambda eng: eng.dma_start(
                     out=S["ln1fm"][:, (q_ * c.DC + kc_) * c.SEG:(q_ * c.DC + kc_ + 1) * c.SEG],
                     in_=ln1_d[ls_ * c.SEG:(ls_ + 1) * c.SEG, kc_ * 128:(kc_ + 1) * 128],
                     transpose=True)))(ls, kc, q),
                 reads=[f"ln1_d{ls}_{t2}" for t2 in range(4)],
                 writes=[f"ln1fm{q}_{kc}"], dma=ld)

        for pi, dst in enumerate(("Qfm", "Kfm", "Vfm")):
            for oc in range(c.DC):
                bank = f"g{(pi * c.DC + oc) % 2}"

                def qkv_group(eng, pi_=pi, oc_=oc, q_=q, bank_=bank):
                    ii = None
                    for kc in range(c.DC):
                        ii = eng.matmul(
                            P[bank_][:, :],
                            S["attW"][:, (pi_ * 8 + kc) * c.D + oc_ * 128:
                                      (pi_ * 8 + kc) * c.D + (oc_ + 1) * 128],
                            S["ln1fm"][:, (q_ * c.DC + kc) * c.SEG:(q_ * c.DC + kc + 1) * c.SEG],
                            start=(kc == 0), stop=(kc == c.DC - 1))
                    return ii

                b.do("tensor", qkv_group,
                     reads=[f"ln1fm{q}_{kc}" for kc in range(c.DC)] +
                           [f"attW{pi}_{kc}" for kc in range(c.DC)],
                     writes=[bank])
                b.do("scalar",
                     (lambda pi_, oc_, dst_, bank_: (lambda eng: eng.activation(
                         out=S[dst_][:, oc_ * c.SEG:(oc_ + 1) * c.SEG], in_=P[bank_][:, :],
                         func=AF.Identity,
                         bias=S["bqkv"][:, pi_ * 8 + oc_: pi_ * 8 + oc_ + 1],
                         scale=1.0)))(pi, oc, dst, bank),
                     reads=[bank, "bqkv"], writes=[f"{dst}_{oc}"])

        for kc in range(c.DC):
            pq = kc % 2
            b.do("vector",
                 (lambda kc_, pq_: (lambda eng: eng.tensor_mul(
                     S["Pch"][:, pq_ * c.SEG:(pq_ + 1) * c.SEG],
                     S["Qfm"][:, kc_ * c.SEG:(kc_ + 1) * c.SEG],
                     S["Kfm"][:, kc_ * c.SEG:(kc_ + 1) * c.SEG])))(kc, pq),
                 reads=[f"Qfm_{kc}", f"Kfm_{kc}"], writes=[f"Pch{pq}"])
            b.do("tensor",
                 (lambda kc_, pq_: (lambda eng: eng.matmul(
                     P["psB"][:, :], S["Ssum"][:, kc_ * 128:(kc_ + 1) * 128],
                     S["Pch"][:, pq_ * c.SEG:(pq_ + 1) * c.SEG],
                     start=(kc_ == 0), stop=(kc_ == c.DC - 1))))(kc, pq),
                 reads=[f"Pch{pq}", "Ssum"], writes=["psB"])

        b.do("vector", lambda eng: eng.memset(S["probs"][:, :], 0.0), writes=["probs"])
        for j in range(4):
            rows = slice(32 * j, 32 * j + 4)
            cols = slice(j, c.SEG, 4)
            b.do("vector",
                 (lambda r_, co_: (lambda eng: eng.tensor_reduce(
                     out=S["pstat"][r_, 0:1], in_=P["psB"][r_, co_],
                     axis=AX.X, op=OP.max, negate=True)))(rows, cols),
                 reads=["psB"], writes=[f"pstat{j}a"])
            b.do("scalar",
                 (lambda r_, co_: (lambda eng: eng.activation(
                     out=S["praw"][r_, co_], in_=P["psB"][r_, co_],
                     func=AF.Exp, bias=S["pstat"][r_, 0:1], scale=1.0,
                     accum_out=S["pstat"][r_, 1:2])))(rows, cols),
                 reads=["psB", f"pstat{j}a"], writes=[f"praw{j}", f"pstat{j}b"])
            b.do("vector",
                 (lambda r_: (lambda eng: eng.reciprocal(
                     out=S["pstat"][r_, 2:3], in_=S["pstat"][r_, 1:2])))(rows),
                 reads=[f"pstat{j}b"], writes=[f"pstat{j}c"])
            b.do("vector",
                 (lambda r_, co_: (lambda eng: eng.tensor_scalar_mul(
                     S["probs"][r_, co_], S["praw"][r_, co_],
                     S["pstat"][r_, 2:3])))(rows, cols),
                 reads=[f"praw{j}", f"pstat{j}c", "probs"], writes=[f"probsw{j}"])

        for kc in range(c.DC):
            bank = f"g{2 + kc % 2}"
            b.do("tensor",
                 (lambda kc_, bank_: (lambda eng: eng.matmul(
                     P[bank_][:, :], S["Sb"][:, kc_ * 128:(kc_ + 1) * 128],
                     S["probs"][:, :], start=True, stop=True)))(kc, bank),
                 reads=["probs", "Sb"] + [f"probsw{j}" for j in range(4)],
                 writes=[bank])
            b.do("vector",
                 (lambda kc_, bank_: (lambda eng: eng.tensor_mul(
                     S["Qfm"][:, kc_ * c.SEG:(kc_ + 1) * c.SEG], P[bank_][:, :],
                     S["Vfm"][:, kc_ * c.SEG:(kc_ + 1) * c.SEG])))(kc, bank),
                 reads=[bank, f"Vfm_{kc}", f"Qfm_{kc}"], writes=[f"attnfm_{kc}"])

        for tt in range(4):
            hh = tt % 2
            b.do("sync",
                 (lambda ls_, tt_, h_: (lambda eng: eng.dma_start(
                     out=S["xt2"][:, h_ * c.D:(h_ + 1) * c.D],
                     in_=x_in[ls_ * c.SEG + tt_ * 128: ls_ * c.SEG + (tt_ + 1) * 128, :])))(ls, tt, hh),
                 writes=[f"xt2{hh}"], dma=ld)
            for nh in range(2):
                bank = f"g{nh}"

                def oproj_group(eng, tt_=tt, nh_=nh, bank_=bank):
                    for kc in range(c.DC):
                        eng.matmul(
                            P[bank_][:, :],
                            S["Qfm"][:, kc * c.SEG + tt_ * 128: kc * c.SEG + (tt_ + 1) * 128],
                            S["attW"][:, (3 * 8 + kc) * c.D + nh_ * 512:
                                      (3 * 8 + kc) * c.D + (nh_ + 1) * 512],
                            start=(kc == 0), stop=False)
                    return eng.matmul(P[bank_][:, :], S["ones1"][:, :],
                                      S["bo_row"][:, nh_ * 512:(nh_ + 1) * 512],
                                      start=False, stop=True)

                b.do("tensor", oproj_group,
                     reads=[f"attnfm_{kc}" for kc in range(c.DC)] +
                           [f"Qfm_{kc}" for kc in range(c.DC)] +
                           [f"attW3_{kc}" for kc in range(c.DC)] + ["ones1", "bo_row"],
                     writes=[bank])
                b.do("vector",
                     (lambda bank_, tt_, nh_, h_: (lambda eng: eng.tensor_add(
                         S["x1t"][:, h_ * c.D + nh_ * 512: h_ * c.D + (nh_ + 1) * 512],
                         P[bank_][:, :],
                         S["xt2"][:, h_ * c.D + nh_ * 512: h_ * c.D + (nh_ + 1) * 512])))(bank, tt, nh, hh),
                     reads=[bank, f"xt2{hh}"], writes=[f"x1t{hh}_{nh}"])
            b.do("sync",
                 (lambda ls_, tt_, h_: (lambda eng: eng.dma_start(
                     out=x1_d[ls_ * c.SEG + tt_ * 128: ls_ * c.SEG + (tt_ + 1) * 128, :],
                     in_=S["x1t"][:, h_ * c.D:(h_ + 1) * c.D])))(ls, tt, hh),
                 reads=[f"x1t{hh}_0", f"x1t{hh}_1"], writes=[f"x1_d{ls}_{tt}"], dma=ld)
            xt = S["x1t"][:, hh * c.D:(hh + 1) * c.D]
            ln_tile(xt, S["lnt"][:, hh * c.D:(hh + 1) * c.D], hh,
                    f"lnt{hh}", [f"x1t{hh}_0", f"x1t{hh}_1"])
            b.do("scalar",
                 (lambda ls_, tt_, h_: (lambda eng: eng.dma_start(
                     out=agin_ln2[ls_ * c.SEG + tt_ * 128: ls_ * c.SEG + (tt_ + 1) * 128, :],
                     in_=S["lnt"][:, h_ * c.D:(h_ + 1) * c.D])))(ls, tt, hh),
                 reads=[f"lnt{hh}"], writes=[f"agin{ls}_{tt}"], dma=stw)

    all_agin = [f"agin{ls}_{tt}" for ls in range(c.segs) for tt in range(4)]

    if _cut("AG"):
        b.fence()
        b.do("gpsimd", lambda eng: None, wait=[b.gwait])
        b.do("sync", lambda eng: None, wait=[b.gwait])
        b.emit(stack)
        nc.compile()
        return nc
    b.do("gpsimd",
         lambda eng: eng.collective_compute(
             "AllGather", OP.bypass, replica_groups=RG,
             ins=[agin_ln2[:, :]], outs=[ln2_full[:, :]]),
         reads=all_agin, writes=["ln2_full"], coll=cc1)

    if _cut("A2a"):
        b.fence()
        b.do("gpsimd", lambda eng: None, wait=[b.gwait])
        b.do("sync", lambda eng: None, wait=[b.gwait])
        b.emit(stack)
        nc.compile()
        return nc
    b.fence(exclude=["ln2_full"])
    free_scope("A1")

    sbuf("gwT", [128, c.DC * c.E], BF16, scope="R")
    sbuf("sc_t", [128, 2 * c.E], F32, scope="R")
    sbuf("M_t", [128, 2 * c.E], BF16, scope="R")
    QW = c.Tl // 8
    sbuf("MT", [c.E, c.Tl], BF16, scope="R")
    sbuf("scoresT", [c.E, c.Tl], BF16, scope="R")
    sbuf("rank16", [c.E, c.Tl], I16, scope="R")
    sbuf("scrF", [c.E, QW], F32, scope="R")
    sbuf("scrI", [c.E, QW], I32, scope="R")
    sbuf("maskh", [c.E, QW], F32, scope="R")
    sbuf("idx16", [c.E, c.Tl], I16, scope="R")
    sbuf("idx16b", [c.E, c.Tl], I16, scope="R")
    sbuf("ids16", [c.E, c.Tl], I16, scope="R")

    b.do("sync", lambda eng: eng.dma_start(out=S["ids16"][:, :], in_=ids16_in[:, :]),
         writes=["ids16"], dma=ld)
    b.do("sync",
         lambda eng: eng.dma_start(
             out=S["wstage"][:, 0:c.DC * c.E].rearrange("p (k e) -> p k e", k=c.DC),
             in_=gwT_in.rearrange("(k p) e -> p k e", p=128)),
         writes=["wstage"], dma=ld)
    b.do("vector",
         lambda eng: eng.tensor_copy(out=S["gwT"][:, :], in_=S["wstage"][:, 0:c.DC * c.E]),
         reads=["wstage"], writes=["gwT"])

    for ls in range(c.segs):
        q = ls % 2
        for kc in range(c.DC):
            b.do("sync",
                 (lambda ls_, kc_, q_: (lambda eng: eng.dma_start(
                     out=S["ln2fmseg"][:, (q_ * c.DC + kc_) * c.SEG:
                                       (q_ * c.DC + kc_ + 1) * c.SEG],
                     in_=agin_ln2[ls_ * c.SEG:(ls_ + 1) * c.SEG, kc_ * 128:(kc_ + 1) * 128],
                     transpose=True)))(ls, kc, q),
                 reads=[f"agin{ls}_{t2}" for t2 in range(4)],
                 writes=[f"l2s{q}_{kc}"], dma=ld)
        for tt in range(4):
            ti = 4 * ls + tt
            tp = ti % 2
            bank = f"g{tp}"

            def gate_group(eng, tt_=tt, q_=q, bank_=bank):
                ii = None
                for kc in range(c.DC):
                    ii = eng.matmul(
                        P[bank_][:, 0:c.E],
                        S["ln2fmseg"][:, (q_ * c.DC + kc) * c.SEG + tt_ * 128:
                                      (q_ * c.DC + kc) * c.SEG + (tt_ + 1) * 128],
                        S["gwT"][:, kc * c.E:(kc + 1) * c.E],
                        start=(kc == 0), stop=(kc == c.DC - 1))
                return ii

            b.do("tensor", gate_group,
                 reads=[f"l2s{q}_{kc}" for kc in range(c.DC)] + ["gwT"], writes=[bank])
            b.do("vector",
                 (lambda bank_: (lambda eng: eng.tensor_reduce(
                     out=S["gstat"][:, 0:1], in_=P[bank_][:, 0:c.E], axis=AX.X,
                     op=OP.max, negate=True)))(bank),
                 reads=[bank], writes=["gstat0"])
            b.do("scalar",
                 (lambda bank_, tp_: (lambda eng: eng.activation(
                     out=S["sc_t"][:, tp_ * c.E:(tp_ + 1) * c.E], in_=P[bank_][:, 0:c.E],
                     func=AF.Exp, bias=S["gstat"][:, 0:1], scale=1.0,
                     accum_out=S["gstat"][:, 1:2])))(bank, tp),
                 reads=[bank, "gstat0"], writes=[f"scraw{tp}", "gstat1"])
            b.do("vector",
                 lambda eng: eng.reciprocal(out=S["gstat"][:, 2:3], in_=S["gstat"][:, 1:2]),
                 reads=["gstat1"], writes=["gstat2"])
            b.do("vector",
                 (lambda tp_: (lambda eng: eng.tensor_scalar_mul(
                     S["sc_t"][:, tp_ * c.E:(tp_ + 1) * c.E],
                     S["sc_t"][:, tp_ * c.E:(tp_ + 1) * c.E], S["gstat"][:, 2:3])))(tp),
                 reads=[f"scraw{tp}", "gstat2"], writes=[f"sct{tp}"])
            b.do("vector",
                 (lambda tp_: (lambda eng: eng.tensor_add(
                     S["sbt"][:, :], S["sc_t"][:, tp_ * c.E:(tp_ + 1) * c.E],
                     S["gb"][:, :])))(tp),
                 reads=[f"sct{tp}", "gb"], writes=["sbt"])
            b.do("vector", lambda eng: eng.max(out=S["top8"][:, :], in_=S["sbt"][:, :]),
                 reads=["sbt"], writes=["top8"])
            b.do("vector",
                 (lambda tp_: (lambda eng: eng.tensor_scalar(
                     out=S["M_t"][:, tp_ * c.E:(tp_ + 1) * c.E], in0=S["sbt"][:, :],
                     scalar1=S["top8"][:, 5:6], scalar2=None, op0=OP.is_ge)))(tp),
                 reads=["sbt", "top8"], writes=[f"Mt{tp}"])
            b.do("tensor",
                 (lambda tp_: (lambda eng: eng.transpose(
                     P["tsm"][0:c.E, 0:128], S["sc_t"][:, tp_ * c.E:(tp_ + 1) * c.E],
                     S["idf32"][:, :])))(tp),
                 reads=[f"sct{tp}", "idf32"], writes=["tsm"])
            b.do("vector",
                 (lambda ti_: (lambda eng: eng.tensor_copy(
                     out=S["scoresT"][:, ti_ * 128:(ti_ + 1) * 128],
                     in_=P["tsm"][0:c.E, 0:128])))(ti),
                 reads=["tsm"], writes=[f"scT{ti}"])
            b.do("tensor",
                 (lambda tp_: (lambda eng: eng.transpose(
                     P["tsm"][0:c.E, 0:64].bitcast(BF16),
                     S["M_t"][:, tp_ * c.E:(tp_ + 1) * c.E],
                     S["idbf"][:, :])))(tp),
                 reads=[f"Mt{tp}", "idbf"], writes=["tsm"])
            b.do("vector",
                 (lambda ti_: (lambda eng: eng.tensor_copy(
                     out=S["MT"][:, ti_ * 128:(ti_ + 1) * 128],
                     in_=P["tsm"][0:c.E, 0:64].bitcast(BF16))))(ti),
                 reads=["tsm"], writes=[f"MTc{ti}"])
            b.do("tensor",
                 (lambda tp_: (lambda eng: eng.matmul(
                     P["t0"][:, 0:129], S["M_t"][:, tp_ * c.E:(tp_ + 1) * c.E],
                     S["Lup"][:, 0:129], start=True, stop=True)))(tp),
                 reads=[f"Mt{tp}", "Lup"], writes=["t0"])
            b.do("vector",
                 (lambda ti_: (lambda eng: eng.tensor_copy(
                     out=S["rank16"][:, ti_ * 128:(ti_ + 1) * 128],
                     in_=P["t0"][:, 0:128])))(ti),
                 reads=["t0"], writes=["rkF"])
            b.do("vector",
                 (lambda ti_: (lambda eng: eng.tensor_copy(
                     out=S["tilec"][:, ti_:ti_ + 1], in_=P["t0"][:, 128:129])))(ti),
                 reads=["t0"], writes=[f"tilec{ti}"])
            b.do("tensor",
                 (lambda tp_, ls_, fst, lst: (lambda eng: eng.matmul(
                     P["psC"][:, :], S["segsel"][:, ls_ * c.segs:(ls_ + 1) * c.segs],
                     S["M_t"][:, tp_ * c.E:(tp_ + 1) * c.E],
                     start=fst, stop=lst)))(tp, ls, ti == 0, ti == c.tiles - 1),
                 reads=[f"Mt{tp}", "segsel"], writes=["psC"])

    b.do("vector", lambda eng: eng.tensor_copy(out=S["segcnt"][:, :], in_=P["psC"][:, :]),
         reads=["psC"], writes=["segcnt"])
    b.do("sync", lambda eng: eng.dma_start(out=agin_cnt[:, :], in_=S["segcnt"][:, :]),
         reads=["segcnt"], writes=["agin_cnt"], dma=ld)
    b.do("gpsimd",
         lambda eng: eng.collective_compute(
             "AllGather", OP.bypass, replica_groups=RG,
             ins=[agin_cnt[:, :]], outs=[cnt_full[:, :]]),
         reads=["agin_cnt"], writes=["cnt_full"], coll=cc2)
    sbuf("zW", [128, 3 * 11264], BF16, scope="Z")
    sbuf("zh", [128, c.SIC * c.SEG], BF16, scope="Z")
    sbuf("silt", [128, c.SEG], F32, scope="Z")

    b.do("gpsimd",
         lambda eng: eng.dma_start(
             out=S["zW"][:, 0:c.DC * c.SI].rearrange("p (k i) -> p k i", k=c.DC),
             in_=ws1T_in.rearrange("(k p) i -> p k i", p=128)),
         writes=[f"zW0_{kc}" for kc in range(c.DC)], dma=pdb)
    b.do("gpsimd",
         lambda eng: eng.dma_start(
             out=S["zW"][:, 11264:11264 + c.DC * c.SI].rearrange(
                 "p (k i) -> p k i", k=c.DC),
             in_=ws3T_in.rearrange("(k p) i -> p k i", p=128)),
         writes=[f"zW1_{kc}" for kc in range(c.DC)], dma=pdb)
    b.do("gpsimd",
         lambda eng: eng.dma_start(
             out=S["zW"][:, 22528:22528 + c.SIC * c.D].rearrange(
                 "p (k i) -> p k i", k=c.SIC),
             in_=ws2T_in.rearrange("(k p) i -> p k i", p=128)),
         writes=[f"zW2_{ic}" for ic in range(c.SIC)], dma=pdb)

    def z_seg(ls):
        q = ls % 2
        for kc in range(c.DC):
            b.do("sync",
                 (lambda ls_, kc_, q_: (lambda eng: eng.dma_start(
                     out=S["ln2fmseg"][:, (q_ * c.DC + kc_) * c.SEG:
                                       (q_ * c.DC + kc_ + 1) * c.SEG],
                     in_=agin_ln2[ls_ * c.SEG:(ls_ + 1) * c.SEG, kc_ * 128:(kc_ + 1) * 128],
                     transpose=True)))(ls, kc, q),
                 reads=[f"agin{ls}_{t2}" for t2 in range(4)],
                 writes=[f"l2s{q}_{kc}"], dma=ld)
        for ic in range(c.SIC):
            for zi in range(2):
                bank = f"g{zi}"

                def z_group(eng, ic_=ic, zi_=zi, q_=q, bank_=bank):
                    ii = None
                    for kc in range(c.DC):
                        ii = eng.matmul(
                            P[bank_][:, :],
                            S["zW"][:, zi_ * 11264 + kc * c.SI + ic_ * 128:
                                    zi_ * 11264 + kc * c.SI + (ic_ + 1) * 128],
                            S["ln2fmseg"][:, (q_ * c.DC + kc) * c.SEG:
                                          (q_ * c.DC + kc + 1) * c.SEG],
                            start=(kc == 0), stop=(kc == c.DC - 1))
                    return ii

                b.do("tensor", z_group,
                     reads=[f"zW{zi}_{kc}" for kc in range(c.DC)] +
                           [f"l2s{q}_{kc}" for kc in range(c.DC)],
                     writes=[bank])
            b.do("scalar",
                 lambda eng: eng.activation(out=S["silt"][:, :], in_=P["g0"][:, :], func=AF.Silu),
                 reads=["g0"], writes=["silt0"])
            b.do("vector",
                 (lambda ic_: (lambda eng: eng.tensor_mul(
                     S["zh"][:, ic_ * c.SEG:(ic_ + 1) * c.SEG], S["silt"][:, :],
                     P["g1"][:, :])))(ic),
                 reads=["silt0", "g1"], writes=[f"zh{ic}"])
        for tt in range(4):
            hh = tt % 2
            b.do("sync",
                 (lambda ls_, tt_, h_: (lambda eng: eng.dma_start(
                     out=S["x1t"][:, h_ * c.D:(h_ + 1) * c.D],
                     in_=x1_d[ls_ * c.SEG + tt_ * 128: ls_ * c.SEG + (tt_ + 1) * 128, :])))(ls, tt, hh),
                 reads=[f"x1_d{ls}_{tt}"], writes=[f"x1t{hh}_0", f"x1t{hh}_1"], dma=ld)
            for nh in range(2):
                bank = f"g{2 + nh}"

                def z2_group(eng, tt_=tt, nh_=nh, bank_=bank):
                    ii = None
                    for ic in range(c.SIC):
                        ii = eng.matmul(
                            P[bank_][:, :],
                            S["zh"][:, ic * c.SEG + tt_ * 128: ic * c.SEG + (tt_ + 1) * 128],
                            S["zW"][:, 22528 + ic * c.D + nh_ * 512:
                                    22528 + ic * c.D + (nh_ + 1) * 512],
                            start=(ic == 0), stop=(ic == c.SIC - 1))
                    return ii

                b.do("tensor", z2_group,
                     reads=[f"zh{ic}" for ic in range(c.SIC)] +
                           [f"zW2_{ic}" for ic in range(c.SIC)],
                     writes=[bank])
                b.do("vector",
                     (lambda bank_, nh_, h_: (lambda eng: eng.tensor_add(
                         S["x1t"][:, h_ * c.D + nh_ * 512: h_ * c.D + (nh_ + 1) * 512],
                         P[bank_][:, :],
                         S["x1t"][:, h_ * c.D + nh_ * 512: h_ * c.D + (nh_ + 1) * 512])))(bank, nh, hh),
                     reads=[bank, f"x1t{hh}_0", f"x1t{hh}_1"], writes=[f"x1t{hh}_{nh}"])
            b.do("scalar",
                 (lambda ls_, tt_, h_: (lambda eng: eng.dma_start(
                     out=out[ls_ * c.SEG + tt_ * 128: ls_ * c.SEG + (tt_ + 1) * 128, :],
                     in_=S["x1t"][:, h_ * c.D:(h_ + 1) * c.D])))(ls, tt, hh),
                 reads=[f"x1t{hh}_0", f"x1t{hh}_1"], writes=[f"out{ls}_{tt}"], dma=stw)

    zsplit = max(1, c.segs - 3)
    for ls in range(zsplit):
        z_seg(ls)

    b.do("sync", lambda eng: eng.dma_start(out=S["cntf"][:, :], in_=cnt_full[:, :]),
         reads=["cnt_full"], writes=["cntf"], dma=ld)

    b.do("tensor",
         lambda eng: eng.matmul(P["tsm"][0:c.E, 0:c.segs], S["cntf"][:, :],
                                S["maskm"][:, :], start=True, stop=True),
         reads=["cntf", "maskm"], writes=["tsm"])
    b.do("vector",
         lambda eng: eng.tensor_copy(out=S["segoff"][:, :], in_=P["tsm"][0:c.E, 0:c.segs]),
         reads=["tsm"], writes=["segoff"])
    for ls in range(c.segs):
        b.do("vector",
             (lambda ls_: (lambda eng: eng.tensor_copy(
                 out=S["totoff"][:, 4 * ls_:4 * ls_ + 1], in_=S["segoff"][:, ls_:ls_ + 1])))(ls),
             reads=["segoff"], writes=[f"totoff{4 * ls}"])
        for j in range(3):
            b.do("vector",
                 (lambda ls_, j_: (lambda eng: eng.tensor_add(
                     S["totoff"][:, 4 * ls_ + j_ + 1:4 * ls_ + j_ + 2],
                     S["totoff"][:, 4 * ls_ + j_:4 * ls_ + j_ + 1],
                     S["tilec"][:, 4 * ls_ + j_:4 * ls_ + j_ + 1])))(ls, j),
                 reads=[f"totoff{4 * ls + j}", f"tilec{4 * ls + j}"],
                 writes=[f"totoff{4 * ls + j + 1}"])
    def V(fn, reads, writes):
        b.do("vector", fn, reads=reads, writes=writes)

    TPQ = QW // 128
    for h4 in range(c.Tl // QW):
        sl = slice(h4 * QW, (h4 + 1) * QW)
        V((lambda sl_: (lambda eng: eng.tensor_copy(
            out=S["scrF"][:, :], in_=S["rank16"][:, sl_])))(sl),
          ["rkF"], ["scrF"])
        for tl in range(TPQ):
            ti = h4 * TPQ + tl
            V((lambda tl_, ti_: (lambda eng: eng.tensor_scalar_add(
                S["scrF"][:, tl_ * 128:(tl_ + 1) * 128],
                S["scrF"][:, tl_ * 128:(tl_ + 1) * 128],
                S["totoff"][:, ti_:ti_ + 1])))(tl, ti),
              ["scrF", f"totoff{ti}"], ["scrF"])
        V(lambda eng: eng.tensor_copy(out=S["scrI"][:, :], in_=S["scrF"][:, :]),
          ["scrF"], ["scrI"])
        V(lambda eng: eng.tensor_scalar(out=S["maskh"][:, :], in0=S["scrF"][:, :],
                                        scalar1=float(c.CAP), scalar2=None,
                                        op0=OP.is_lt),
          ["scrF"], ["maskH"])
        V((lambda sl_: (lambda eng: eng.tensor_mul(
            S["maskh"][:, :], S["maskh"][:, :], S["MT"][:, sl_])))(sl),
          ["maskH"] + [f"MTc{h4 * TPQ + tl}" for tl in range(TPQ)], ["maskH"])
        V(lambda eng: eng.tensor_scalar_add(S["scrF"][:, :], S["scrF"][:, :], 1.0),
          ["scrF"], ["scrF"])
        V(lambda eng: eng.tensor_mul(S["scrF"][:, :], S["scrF"][:, :], S["maskh"][:, :]),
          ["scrF", "maskH"], ["scrF"])
        V((lambda sl_: (lambda eng: eng.tensor_scalar_add(
            S["idx16"][:, sl_], S["scrF"][:, :], -1.0)))(sl),
          ["scrF"], ["idx16F"])
        V(lambda eng: eng.tensor_copy(out=S["scrF"][:, :], in_=S["scrI"][:, :]),
          ["scrI"], ["scrF"])
        V(lambda eng: eng.tensor_scalar(out=S["scrI"][:, :], in0=S["scrI"][:, :],
                                        scalar1=15, scalar2=None, op0=OP.bitwise_and),
          ["scrI"], ["scrI"])
        V(lambda eng: eng.tensor_copy(out=S["maskh"][:, :], in_=S["scrI"][:, :]),
          ["scrI"], ["maskH"])
        V(lambda eng: eng.tensor_sub(S["scrF"][:, :], S["scrF"][:, :], S["maskh"][:, :]),
          ["scrF", "maskH"], ["scrF"])
        V(lambda eng: eng.tensor_scalar_mul(S["scrF"][:, :], S["scrF"][:, :], 0.0625),
          ["scrF"], ["scrF"])
        V(lambda eng: eng.scalar_tensor_tensor(
            out=S["scrF"][:, :], in0=S["maskh"][:, :], scalar=32.0,
            in1=S["scrF"][:, :], op0=OP.mult, op1=OP.add),
          ["scrF", "maskH"], ["scrF"])
        V((lambda sl_: (lambda eng: eng.tensor_copy(
            out=S["maskh"][:, :], in_=S["idx16"][:, sl_])))(sl),
          ["idx16F"], ["maskH"])
        V(lambda eng: eng.tensor_scalar(
            out=S["maskh"][:, :], in0=S["maskh"][:, :],
            scalar1=0.0, scalar2=None, op0=OP.is_ge),
          ["maskH"], ["maskH"])
        V(lambda eng: eng.tensor_scalar_add(S["scrF"][:, :], S["scrF"][:, :], 1.0),
          ["scrF"], ["scrF"])
        V(lambda eng: eng.tensor_mul(S["scrF"][:, :], S["scrF"][:, :], S["maskh"][:, :]),
          ["scrF", "maskH"], ["scrF"])
        V((lambda sl_: (lambda eng: eng.tensor_scalar_add(
            S["idx16b"][:, sl_], S["scrF"][:, :], -1.0)))(sl),
          ["scrF"], ["idx16bF"])
    ix_all = ["idx16F"]
    sct_all = [f"scT{ti}" for ti in range(c.tiles)]

    b.do("gpsimd",
         lambda eng: eng.local_scatter(S["idsG"][:, :], S["ids16"][:, :], S["idx16b"][:, :],
                                       c.E, c.CAP, c.Tl),
         reads=["ids16", "idx16bF"], writes=["idsG"])
    b.do("gpsimd",
         lambda eng: eng.local_scatter(S["wG"][:, :], S["scoresT"][:, :], S["idx16"][:, :],
                                       c.E, c.CAP, c.Tl),
         reads=ix_all + sct_all, writes=["wG"])
    b.do("vector", lambda eng: eng.tensor_copy(out=S["tabf"][:, 0:CST], in_=S["idsG"][:, :]),
         reads=["idsG"], writes=["tabf_a"])
    b.do("vector", lambda eng: eng.tensor_copy(out=S["tabf"][:, CST:2 * CST], in_=S["wG"][:, :]),
         reads=["wG"], writes=["tabf_b"])
    b.do("sync", lambda eng: eng.dma_start(out=arin_tab[:, :], in_=S["tabf"][:, :]),
         reads=["tabf_a", "tabf_b"], writes=["arin_tab"], dma=ld)
    b.do("gpsimd",
         lambda eng: eng.collective_compute(
             "AllReduce", OP.add, replica_groups=RG,
             ins=[arin_tab[:, :]], outs=[tab_full[:, :]]),
         reads=["arin_tab"], writes=["tab_full"], coll=cc3)

    for ls in range(zsplit, c.segs):
        z_seg(ls)
    out_all = [f"out{ls}_{tt}" for ls in range(c.segs) for tt in range(4)]

    b.fence(exclude=["ln2_full", "tab_full", "cnt_full"])
    free_scope("Z")
    free_scope("R")

    if _cut("C"):
        b.fence()
        b.do("gpsimd", lambda eng: None, wait=[b.gwait])
        b.do("sync", lambda eng: None, wait=[b.gwait])
        b.emit(stack)
        nc.compile()
        return nc
    b.do("sync", lambda eng: eng.dma_start(out=S["tabf"][:, :], in_=tab_full[:, :]),
         reads=["tab_full"], writes=["tabf_a", "tabf_b"], dma=ld)
    b.do("vector",
         lambda eng: eng.tensor_scalar(out=S["minef"][:, :], in0=S["tabf"][:, 0:CST],
                                       scalar1=S["rowb"][0:c.E, 0:1], scalar2=None,
                                       op0=OP.is_ge),
         reads=["tabf_a", "rowb"], writes=["minef"])
    b.do("vector",
         lambda eng: eng.tensor_scalar(out=S["mtmp"][:, :], in0=S["tabf"][:, 0:CST],
                                       scalar1=S["rowb"][0:c.E, 1:2], scalar2=None,
                                       op0=OP.is_lt),
         reads=["tabf_a", "rowb"], writes=["mtmp"])
    b.do("vector",
         lambda eng: eng.tensor_mul(S["minef"][:, :], S["minef"][:, :], S["mtmp"][:, :]),
         reads=["mtmp", "minef"], writes=["minef"])
    b.do("vector",
         lambda eng: eng.tensor_scalar(out=S["mtmp"][:, :], in0=S["tabf"][:, CST:2 * CST],
                                       scalar1=0.0, scalar2=None, op0=OP.is_gt),
         reads=["tabf_b"], writes=["mtmp"])
    b.do("vector",
         lambda eng: eng.tensor_mul(S["minef"][:, :], S["minef"][:, :], S["mtmp"][:, :]),
         reads=["mtmp", "minef"], writes=["minef"])
    b.do("vector",
         lambda eng: eng.tensor_scalar(out=S["mtmp"][:, :], in0=S["tabf"][:, 0:CST],
                                       scalar1=S["rowb"][0:c.E, 0:1], scalar2=None,
                                       op0=OP.subtract),
         reads=["tabf_a", "rowb"], writes=["mtmp"])
    b.do("vector",
         lambda eng: eng.tensor_scalar(out=S["mtmp"][:, :], in0=S["mtmp"][:, :],
                                       scalar1=-1.0, scalar2=-1.0, op0=OP.mult,
                                       op1=OP.add),
         reads=["mtmp"], writes=["mtmp"])
    b.do("vector", lambda eng: eng.tensor_copy(out=S["ltok"][:, :], in_=S["mtmp"][:, :]),
         reads=["mtmp"], writes=["ltok"])
    b.do("vector", lambda eng: eng.memset(S["onescol"][:, :], 1.0), writes=["onescol"])
    b.do("vector",
         lambda eng: eng.tensor_tensor_scan(
             out=S["mtmp"][:, :], data0=S["onescol"][:, :].to_broadcast([c.E, c.CAP]),
             data1=S["minef"][:, :], initial=0.0, op0=OP.mult, op1=OP.add),
         reads=["onescol", "minef"], writes=["mtmp"])
    b.do("vector",
         lambda eng: eng.tensor_copy(out=S["cntrow"][:, :], in_=S["mtmp"][:, CST - 1:CST]),
         reads=["mtmp"], writes=["cntrow"])
    b.do("vector",
         lambda eng: eng.tensor_sub(S["mtmp"][:, :], S["mtmp"][:, :], S["minef"][:, :]),
         reads=["mtmp", "minef"], writes=["mtmp"])
    b.do("vector",
         lambda eng: eng.tensor_copy(out=S["mtmpI"][:, :], in_=S["mtmp"][:, :]),
         reads=["mtmp"], writes=["mtmpI"])
    b.do("vector",
         lambda eng: eng.tensor_scalar(out=S["mtmpI"][:, :], in0=S["mtmpI"][:, :],
                                       scalar1=15, scalar2=None, op0=OP.bitwise_and),
         reads=["mtmpI"], writes=["mtmpI"])
    b.do("vector",
         lambda eng: eng.tensor_copy(out=S["mtmp3"][:, :], in_=S["mtmpI"][:, :]),
         reads=["mtmpI"], writes=["mtmp3"])
    b.do("vector",
         lambda eng: eng.tensor_sub(S["mtmp2"][:, :], S["mtmp"][:, :], S["mtmp3"][:, :]),
         reads=["mtmp", "mtmp3"], writes=["mtmp2"])
    b.do("vector",
         lambda eng: eng.tensor_scalar_mul(S["mtmp2"][:, :], S["mtmp2"][:, :], 0.0625),
         reads=["mtmp2"], writes=["mtmp2"])
    b.do("vector",
         lambda eng: eng.scalar_tensor_tensor(out=S["mtmp2"][:, :], in0=S["mtmp3"][:, :],
                                              scalar=32.0, in1=S["mtmp2"][:, :],
                                              op0=OP.mult, op1=OP.add),
         reads=["mtmp3", "mtmp2"], writes=["mtmp2"])
    b.do("vector",
         lambda eng: eng.scalar_tensor_tensor(out=S["mtmp2"][:, :], in0=S["mtmp2"][:, :],
                                              scalar=1.0, in1=S["minef"][:, :],
                                              op0=OP.add, op1=OP.mult),
         reads=["mtmp2", "minef"], writes=["mtmp2"])
    b.do("vector",
         lambda eng: eng.tensor_scalar_add(S["mtmp2"][:, :], S["mtmp2"][:, :], -1.0),
         reads=["mtmp2"], writes=["mtmp2"])
    b.do("vector", lambda eng: eng.tensor_copy(out=S["rankD"][:, :], in_=S["mtmp2"][:, :]),
         reads=["mtmp2"], writes=["rankD"])
    b.do("gpsimd",
         lambda eng: eng.local_scatter(S["listSS"][:, :], S["sloti"][:, :],
                                       S["rankD"][:, :], c.E, c.CAP, c.CAP),
         reads=["sloti", "rankD"], writes=["listSS"])
    b.do("gpsimd",
         lambda eng: eng.local_scatter(S["listTT"][:, :], S["ltok"][:, :],
                                       S["rankD"][:, :], c.E, c.CAP, c.CAP),
         reads=["ltok", "rankD"], writes=["listTT"])
    b.do("vector",
         lambda eng: eng.tensor_scalar(out=S["listSS"][:, :], in0=S["listSS"][:, :],
                                       scalar1=-1.0, scalar2=-1.0, op0=OP.mult, op1=OP.add),
         reads=["listSS"], writes=["listSS"])
    b.do("vector",
         lambda eng: eng.tensor_scalar(out=S["listTT"][:, :], in0=S["listTT"][:, :],
                                       scalar1=-1.0, scalar2=-1.0, op0=OP.mult, op1=OP.add),
         reads=["listTT"], writes=["listTT"])
    b.do("vector",
         lambda eng: eng.tensor_scalar_max(S["listSS"][:, 0:1], S["listSS"][:, 0:1], 0.0),
         reads=["listSS"], writes=["listSS"])
    b.do("vector",
         lambda eng: eng.tensor_copy(out=S["t128"][:, 0:1], in_=S["listTT"][:, 0:1]),
         reads=["listTT"], writes=["t128d0"])
    b.do("vector",
         lambda eng: eng.tensor_scalar(out=S["t128"][:, 1:2], in0=S["t128"][:, 0:1],
                                       scalar1=0.0, scalar2=None, op0=OP.is_lt),
         reads=["t128d0"], writes=["t128d1"])
    b.do("vector",
         lambda eng: eng.scalar_tensor_tensor(out=S["t128"][:, 0:1], in0=S["t128"][:, 1:2],
                                              scalar=float(c.Tl + 1), in1=S["t128"][:, 0:1],
                                              op0=OP.mult, op1=OP.add),
         reads=["t128d1", "t128d0"], writes=["t128d0"])
    b.do("vector",
         lambda eng: eng.tensor_copy(out=S["listTT"][:, 0:1], in_=S["t128"][:, 0:1]),
         reads=["t128d0"], writes=["listTT"])
    b.do("sync", lambda eng: eng.dma_start(out=cnts_d[0:1, :].rearrange("a e -> e a"),
                                           in_=S["cntrow"][:, :]),
         reads=["cntrow"], writes=["cnts_d"], dma=ld)
    b.do("sync", lambda eng: eng.dma_start(out=S["nfsf"][:, :], in_=cnts_d[:, :]),
         reads=["cnts_d"], writes=["nfsf"], dma=ld)
    b.do("vector", lambda eng: eng.tensor_copy(out=S["nfs"][:, :], in_=S["nfsf"][:, :]),
         reads=["nfsf"], writes=["nfs"])
    b.do("sync", lambda eng: eng.dma_start(out=lists_d[:, 0:c.CAP], in_=S["listSS"][:, :]),
         reads=["listSS"], writes=["lists_d1"], dma=ld)
    b.do("sync", lambda eng: eng.dma_start(out=lists_d[:, c.CAP:], in_=S["listTT"][:, :]),
         reads=["listTT"], writes=["lists_d2"], dma=ld)
    b.do("vector", lambda eng: eng.memset(S["dlist"][:, :], 0), writes=["dlist_z"])
    for half, res in ((0, "lists_d1"), (1, "lists_d2")):
        for pp in (0, 16):
            b.do("sync",
                 (lambda half_, pp_: (lambda eng: eng.dma_start(
                     out=S["dlist"][pp_:pp_ + 16,
                                    half_ * c.E * 32:(half_ + 1) * c.E * 32].rearrange(
                         "p (e s) -> p e s", e=c.E),
                     in_=lists_d[:, half_ * c.CAP:(half_ + 1) * c.CAP].rearrange(
                         "e (p s) -> p e s", p=16))))(half, pp),
                 reads=[res, "dlist_z"], writes=[f"dlist{half}_{pp}"], dma=ld)
    dlist_all = [f"dlist{h}_{p}" for h in (0, 1) for p in (0, 16)]

    EWP = 2 * c.DC * c.I + c.IC * c.D
    sbuf("expW", [128, 2 * EWP], BF16, scope="C")
    sbuf("xbfm", [128, 2 * c.DC * c.CAP], BF16, scope="C")
    sbuf("hh", [128, c.IC * c.CAP], BF16, scope="C")
    sbuf("ybt", [128, 2 * c.D], BF16, scope="C")
    sbuf("siltC", [128, c.CAP], F32, scope="C")
    sbuf("myidw", [c.EPC, 2 * c.CAP], F32, scope="C")
    sbuf("myids16", [c.EPC, c.CAP], I16, scope="C")
    sbuf("wTslot", [128, 4 * c.EPC], F32, scope="C")
    sbuf("eidx", [128, c.EPC * (c.CAP // 16)], I16, scope="C")

    b.do("gpsimd",
         lambda eng: eng.indirect_dma_start(
             S["myidw"][:, :], None, tab_full[:, :],
             IndirectOffsetOnAxis(S["myexp"][:, :], 0)),
         reads=["tab_full", "myexp"], writes=["myidw"], dma=pda)
    for sb4 in range(4):
        b.do("tensor",
             (lambda s4: (lambda eng: eng.transpose(
                 P["tsm"][0:128, 0:c.EPC],
                 S["myidw"][:, CST + s4 * 128: CST + (s4 + 1) * 128],
                 S["idf32"][0:c.EPC, 0:c.EPC])))(sb4),
             reads=["myidw", "idf32"], writes=["tsm"])
        b.do("vector",
             (lambda s4: (lambda eng: eng.tensor_copy(
                 out=S["wTslot"][:, s4 * c.EPC:(s4 + 1) * c.EPC],
                 in_=P["tsm"][0:128, 0:c.EPC])))(sb4),
             reads=["tsm"], writes=[f"wTslot{sb4}"])
    b.do("vector",
         lambda eng: eng.tensor_copy(out=S["myids16"][:, :], in_=S["myidw"][:, 0:CST]),
         reads=["myidw"], writes=["myids16"])
    b.do("scalar",
         lambda eng: eng.dma_start(out=eidx_d[:, :], in_=S["myids16"][:, :]),
         reads=["myids16"], writes=["eidx_d"], dma=stw)
    b.do("vector", lambda eng: eng.memset(S["eidx"][:, :], 0), writes=["eidx_z"])
    b.do("sync",
         lambda eng: eng.dma_start(
             out=S["eidx"][0:16, :].rearrange("p (e s) -> p e s", e=c.EPC),
             in_=eidx_d.rearrange("e (p s) -> p e s", p=16)),
         reads=["eidx_d", "eidx_z"], writes=["eidx_a"], dma=ld)
    b.do("sync",
         lambda eng: eng.dma_start(
             out=S["eidx"][16:32, :].rearrange("p (e s) -> p e s", e=c.EPC),
             in_=eidx_d.rearrange("e (p s) -> p e s", p=16)),
         reads=["eidx_d", "eidx_z"], writes=["eidx_b"], dma=ld)

    for j in range(c.EPC):
        par = j % 2
        eb = par * EWP
        b.do("gpsimd",
             (lambda j_, par_: (lambda eng: eng.dma_gather(
                 S["xbfm"][:, par_ * c.DC * c.CAP:(par_ + 1) * c.DC * c.CAP].rearrange(
                     "p (k t) -> p k t", k=c.DC),
                 ln2_full[:, :],
                 S["eidx"][:, j_ * (c.CAP // 16):(j_ + 1) * (c.CAP // 16)],
                 c.CAP, c.CAP, c.D, transpose=True)))(j, par),
             reads=["eidx_a", "eidx_b", "ln2_full"], writes=[f"xbfm{par}"], dma=pdb)
        b.do("gpsimd",
             (lambda j_, eb_: (lambda eng: eng.dma_start(
                 out=S["expW"][:, eb_: eb_ + c.DC * c.I].rearrange(
                     "p (k i) -> p k i", k=c.DC),
                 in_=w1T_in[j_].rearrange("(k p) i -> p k i", p=128))))(j, eb),
             writes=[f"expWa{par}"], dma=pdb)
        b.do("gpsimd",
             (lambda j_, eb_: (lambda eng: eng.dma_start(
                 out=S["expW"][:, eb_ + c.DC * c.I: eb_ + 2 * c.DC * c.I].rearrange(
                     "p (k i) -> p k i", k=c.DC),
                 in_=w3T_in[j_].rearrange("(k p) i -> p k i", p=128))))(j, eb),
             writes=[f"expWb{par}"], dma=pdb)
        nfull = c.I // 128
        b.do("gpsimd",
             (lambda j_, eb_, nf: (lambda eng: eng.dma_start(
                 out=S["expW"][:, eb_ + 2 * c.DC * c.I:
                               eb_ + 2 * c.DC * c.I + nf * c.D].rearrange(
                     "p (k i) -> p k i", k=nf),
                 in_=w2T_in[j_][0:nf * 128, :].rearrange("(k p) i -> p k i", p=128))))(j, eb, nfull),
             writes=[f"expWc{par}"], dma=pdb)
        b.do("gpsimd",
             (lambda j_, eb_, nf: (lambda eng: eng.dma_start(
                 out=S["expW"][0:c.I - nf * 128,
                               eb_ + 2 * c.DC * c.I + nf * c.D:
                               eb_ + 2 * c.DC * c.I + (nf + 1) * c.D],
                 in_=w2T_in[j_][nf * 128:, :])))(j, eb, nfull),
             writes=[f"expWc2{par}"], dma=pdb)

        for ic in range(c.IC):
            icw = 128 if ic < c.IC - 1 else c.I - 128 * (c.IC - 1)
            for zi in range(2):
                bank = f"g{zi}"

                def h_group(eng, ic_=ic, zi_=zi, bank_=bank, par_=par, icw_=icw, eb_=eb):
                    ii = None
                    for kc in range(c.DC):
                        ii = eng.matmul(
                            P[bank_][0:icw_, :],
                            S["expW"][:, eb_ + zi_ * c.DC * c.I + kc * c.I + ic_ * 128:
                                      eb_ + zi_ * c.DC * c.I + kc * c.I + ic_ * 128 + icw_],
                            S["xbfm"][:, (par_ * c.DC + kc) * c.CAP:
                                      (par_ * c.DC + kc + 1) * c.CAP],
                            start=(kc == 0), stop=(kc == c.DC - 1))
                    return ii

                b.do("tensor", h_group,
                     reads=[f"expWa{par}", f"expWb{par}", f"xbfm{par}"],
                     writes=[bank])
            b.do("scalar",
                 (lambda icw_: (lambda eng: eng.activation(
                     out=S["siltC"][0:icw_, :], in_=P["g0"][0:icw_, :], func=AF.Silu)))(icw),
                 reads=["g0"], writes=["siltC0"])
            b.do("vector",
                 (lambda ic_, icw_: (lambda eng: eng.tensor_mul(
                     S["hh"][0:icw_, ic_ * c.CAP:(ic_ + 1) * c.CAP],
                     S["siltC"][0:icw_, :], P["g1"][0:icw_, :])))(ic, icw),
                 reads=["siltC0", "g1"], writes=[f"hhc{ic}"])

        for sb4 in range(4):
            yh = sb4 % 2
            for nh in range(2):
                bank = f"g{2 + nh}"

                def y_group(eng, sb4_=sb4, nh_=nh, bank_=bank, eb_=eb):
                    ii = None
                    for ic in range(c.IC):
                        icw = 128 if ic < c.IC - 1 else c.I - 128 * (c.IC - 1)
                        ii = eng.matmul(
                            P[bank_][:, :],
                            S["hh"][0:icw, ic * c.CAP + sb4_ * 128: ic * c.CAP + (sb4_ + 1) * 128],
                            S["expW"][0:icw, eb_ + 2 * c.DC * c.I + ic * c.D + nh_ * 512:
                                      eb_ + 2 * c.DC * c.I + ic * c.D + (nh_ + 1) * 512],
                            start=(ic == 0), stop=(ic == c.IC - 1))
                    return ii

                b.do("tensor", y_group,
                     reads=[f"hhc{ic}" for ic in range(c.IC)] +
                           [f"expWc{par}", f"expWc2{par}"],
                     writes=[bank])
                b.do("vector",
                     (lambda bank_, sb4_, nh_, j_, yh_: (lambda eng: eng.tensor_scalar_mul(
                         S["ybt"][:, yh_ * c.D + nh_ * 512: yh_ * c.D + (nh_ + 1) * 512],
                         P[bank_][:, :],
                         S["wTslot"][:, sb4_ * c.EPC + j_: sb4_ * c.EPC + j_ + 1])))(bank, sb4, nh, j, yh),
                     reads=[bank, f"wTslot{sb4}"], writes=[f"ybt{yh}_{nh}"])
            b.do("scalar",
                 (lambda j_, sb4_, yh_: (lambda eng: eng.dma_start(
                     out=agin_yb[j_ * c.CAP + sb4_ * 128: j_ * c.CAP + (sb4_ + 1) * 128, :],
                     in_=S["ybt"][:, yh_ * c.D:(yh_ + 1) * c.D])))(j, sb4, yh),
                 reads=[f"ybt{yh}_0", f"ybt{yh}_1"], writes=[f"agyb{j}_{sb4}"], dma=stw)

        if (j + 1) % c.EPCH == 0:
            k4 = j // c.EPCH
            b.do("gpsimd",
                 (lambda k_: (lambda eng: eng.collective_compute(
                     "AllGather", OP.bypass, replica_groups=RG,
                     ins=[agin_yb[k_ * c.EPCH * c.CAP:(k_ + 1) * c.EPCH * c.CAP, :]],
                     outs=[ybc[k_][:, :]])))(k4),
                 reads=[f"agyb{jj}_{s4}"
                        for jj in range(k4 * c.EPCH, (k4 + 1) * c.EPCH)
                        for s4 in range(4)],
                 writes=[f"ybc{k4}"], coll=cc4s[k4])

    b.fence(exclude=["ln2_full", "tab_full", "cnt_full"] +
            [f"ybc{k}" for k in range(c.NCH)])
    free_scope("C")

    if _cut("D"):
        b.fence()
        b.do("gpsimd", lambda eng: None, wait=[b.gwait])
        b.do("sync", lambda eng: None, wait=[b.gwait])
        b.emit(stack)
        nc.compile()
        return nc
    sbuf("gvb", [128, 4 * 4 * c.D], BF16, scope="D")
    sbuf("gvf", [128, 4 * 4 * c.D], F32, scope="D")
    sbuf("zrow", [128, c.D], F32, scope="D")

    b.do("vector", lambda eng: eng.memset(S["gvb"][:, :], 0),
         writes=[f"gvb{p}" for p in range(4)])
    b.do("vector", lambda eng: eng.memset(S["zrow"][:, :], 0.0), writes=["zrow"])
    b.do("sync",
         lambda eng: eng.dma_start(out=out[c.Tl:c.Tl + 128, :], in_=S["zrow"][:, :]),
         reads=["zrow"], writes=["outPad"], dma=ld)

    def get_nkreg(eng):
        r = getattr(eng, "_nkreg", None)
        if r is None:
            r = eng.alloc_register("nkreg")
            eng._nkreg = r
        return r

    ei = 0
    for k4 in range(c.NCH):
        chunk = [cc_ * c.EPC + k4 * c.EPCH + jj
                 for cc_ in range(c.n_cores) for jj in range(c.EPCH)]
        for e in chunk:
            par = ei % 4
            ei += 1
            gvb = S["gvb"][:, par * 4 * c.D:(par + 1) * 4 * c.D].rearrange(
                "p (a d) -> p a d", d=c.D)
            gvf = S["gvf"][:, par * 4 * c.D:(par + 1) * 4 * c.D].rearrange(
                "p (a d) -> p a d", d=c.D)

            def d_gather(eng, e_=e, gvb_=gvb, k_=k4):
                r = get_nkreg(eng)
                eng.reg_load(r, S["nfs"][0:1, e_:e_ + 1])
                eng.reg_alu(r, r, 1, OP.max)
                return eng.dma_gather(
                    gvb_, ybc[k_][:, :],
                    S["dlist"][:, e_ * 32:(e_ + 1) * 32],
                    c.CAP, r, c.D)

            b.do("gpsimd", d_gather,
                 reads=[f"ybc{k4}", "nfs"] + dlist_all, writes=[f"gvb{par}"], dma=pdc)
            b.do("vector",
                 (lambda par_: (lambda eng: eng.tensor_copy(
                     out=S["gvf"][:, par_ * 4 * c.D:(par_ + 1) * 4 * c.D],
                     in_=S["gvb"][:, par_ * 4 * c.D:(par_ + 1) * 4 * c.D])))(par),
                 reads=[f"gvb{par}"], writes=[f"gvf{par}"])

            def d_scatter(eng, e_=e, gvf_=gvf):
                r = get_nkreg(eng)
                eng.reg_load(r, S["nfs"][0:1, e_:e_ + 1])
                eng.reg_alu(r, r, 1, OP.max)
                return eng.dma_scatter_add(
                    out[:, :], gvf_,
                    S["dlist"][:, (c.E + e_) * 32:(c.E + e_ + 1) * 32],
                    c.CAP, r, c.D)

            b.do("gpsimd", d_scatter,
                 reads=["nfs", f"gvf{par}", "outPad"] + dlist_all + out_all,
                 writes=["outRMW"], dma=pdd)

    if debug_dump:
        dbg_specs = [("dbg_tab", tab_full, [c.E, 2 * c.CAP], F32, ["tab_full"]),
                     ("dbg_ln2", agin_ln2, [c.Tl, c.D], BF16, all_agin),
                     ("dbg_x1", x1_d, [c.Tl, c.D], F32,
                      [f"x1_d{ls}_{tt}" for ls in range(c.segs) for tt in range(4)]),
                     ("dbg_lists", lists_d, [c.E, 2 * c.CAP], I16,
                      ["lists_d1", "lists_d2"]),
                     ("dbg_cnt", cnts_d, [1, c.E], F32, ["cnts_d"])]
        for nm, srcap, shp, dt, rds in dbg_specs:
            od = nc.dram_tensor(nm, shp, dt, kind="ExternalOutput").ap()
            b.do("sync",
                 (lambda od_, s_: (lambda eng: eng.dma_start(out=od_[:, :], in_=s_[:, :])))(od, srcap),
                 reads=rds + ["outRMW"], writes=[nm], dma=ld)
        b.do("sync", lambda eng: None,
             reads=["outRMW"] + out_all + [d[0] for d in dbg_specs])
    b.do("gpsimd", lambda eng: None, reads=["outRMW", "outPad"] + out_all)
    b.do("sync", lambda eng: None, reads=["outRMW", "outPad"] + out_all)
    b.emit(stack)
    nc.compile()
    return nc


def _host_constants(c: Cfg, core):
    o = {}
    Lup = np.zeros((128, 132), np.float32)
    for t in range(128):
        Lup[:t, t] = 1.0
    Lup[:, 128] = 1.0
    o["Lup"] = Lup.astype(BF)
    o["idf32"] = np.eye(128, dtype=np.float32)
    o["idbf"] = np.eye(128, dtype=np.float32).astype(BF)
    ss = np.zeros((128, c.DC * 128), np.float32)
    sb = np.zeros((128, c.DC * 128), np.float32)
    for kc in range(c.DC):
        for hloc in range(2):
            hglob = 2 * kc + hloc
            rc = (hglob % 4) * 32 + hglob // 4
            ss[hloc * 64:(hloc + 1) * 64, kc * 128 + rc] = 0.125
            sb[rc, kc * 128 + hloc * 64: kc * 128 + (hloc + 1) * 64] = 1.0
    o["Ssum"] = ss.astype(BF)
    o["Sb"] = sb.astype(BF)
    o["ones1"] = np.ones((1, 128), np.float32)
    segsel = np.zeros((128, c.segs * c.segs), np.float32)
    for ls in range(c.segs):
        segsel[:, ls * c.segs + ls] = 1.0
    o["segsel"] = segsel.astype(BF)
    ids = core * c.Tl + np.arange(c.Tl, dtype=np.int64)
    o["ids16"] = np.broadcast_to(ids[None, :], (c.E, c.Tl)).astype(np.int16).copy()
    col = np.arange(c.CAP, dtype=np.int64)
    siginv = (col % 32) * 16 + col // 32
    ee = np.arange(c.E, dtype=np.int64)
    ebase = (ee // c.EPC) * (c.EPCH * c.CAP) + ((ee % c.EPC) % c.EPCH) * c.CAP
    sloti = ebase[:, None] + siginv[None, :]
    o["sloti"] = (-(sloti + 1)).astype(np.int16)
    mm = np.zeros((c.nseg_tot, c.segs), np.float32)
    for r in range(c.nseg_tot):
        cc, ls2 = divmod(r, c.segs)
        nglob = cc + c.n_cores * ls2
        for ls in range(c.segs):
            if nglob < core + c.n_cores * ls:
                mm[r, ls] = 1.0
    o["maskmat"] = mm
    rb = np.zeros((128, 2), np.float32)
    rb[:, 0] = core * c.Tl
    rb[:, 1] = (core + 1) * c.Tl
    o["rowbase"] = rb
    o["myexp"] = (core * c.EPC + np.arange(c.EPC, dtype=np.int32)).reshape(c.EPC, 1)
    return o


def host_inputs(c: Cfg, inp):
    x = np.ascontiguousarray(np.asarray(inp["L0_attn_x"], np.float32).reshape(c.T, c.D))
    f32 = lambda a: np.ascontiguousarray(np.asarray(a, np.float32))
    Wq, Wk = f32(inp["L0_attn_Wq"]), f32(inp["L0_attn_Wk"])
    Wv, Wo = f32(inp["L0_attn_Wv"]), f32(inp["L0_attn_Wo"])
    bqkv = np.concatenate(
        [np.asarray(inp[k], np.float32).reshape(c.DC, 128).T
         for k in ("L0_attn_bq", "L0_attn_bk", "L0_attn_bv")], axis=1)
    bqkv = np.ascontiguousarray(bqkv)
    bo_row = np.ascontiguousarray(np.asarray(inp["L0_attn_bo"], np.float32).reshape(1, c.D))
    ws1T = np.ascontiguousarray(np.asarray(inp["L0_ffn_ws1"], np.float32).T)
    ws3T = np.ascontiguousarray(np.asarray(inp["L0_ffn_ws3"], np.float32).T)
    ws2T = np.ascontiguousarray(np.asarray(inp["L0_ffn_ws2"], np.float32).T)
    gwT = np.ascontiguousarray(np.asarray(inp["L0_ffn_gate_w"], np.float32).T)
    gb_b = np.ascontiguousarray(np.broadcast_to(
        np.asarray(inp["L0_ffn_gate_b"], np.float32)[None, :], (128, c.E)))
    w1T = np.ascontiguousarray(np.asarray(inp["L0_ffn_w1"], np.float32).transpose(0, 2, 1))
    w3T = np.ascontiguousarray(np.asarray(inp["L0_ffn_w3"], np.float32).transpose(0, 2, 1))
    w2T = np.ascontiguousarray(np.asarray(inp["L0_ffn_w2"], np.float32).transpose(0, 2, 1))

    in_maps = []
    for core in range(c.n_cores):
        segs = [core + c.n_cores * ls for ls in range(c.segs)]
        xs = np.ascontiguousarray(
            np.concatenate([x[n * c.SEG:(n + 1) * c.SEG] for n in segs], axis=0))
        m = {"x": xs, "Wq": Wq, "Wk": Wk, "Wv": Wv, "Wo": Wo,
             "bqkv": bqkv, "bo_row": bo_row,
             "ws1T": ws1T, "ws3T": ws3T, "ws2T": ws2T,
             "gwT": gwT, "gb_bcast": gb_b,
             "w1T": w1T[core * c.EPC:(core + 1) * c.EPC],
             "w3T": w3T[core * c.EPC:(core + 1) * c.EPC],
             "w2T": w2T[core * c.EPC:(core + 1) * c.EPC]}
        m.update(_host_constants(c, core))
        in_maps.append(m)
    return in_maps


def assemble_out(c: Cfg, results, shape):
    outf = np.empty((c.T, c.D), np.float32)
    for core in range(c.n_cores):
        oc = results[core]["out"]
        for ls in range(c.segs):
            n = core + c.n_cores * ls
            outf[n * c.SEG:(n + 1) * c.SEG] = oc[ls * c.SEG:(ls + 1) * c.SEG]
    return outf.reshape(shape)


_CACHED = {}


def kernel(L0_attn_x, L0_attn_Wq, L0_attn_bq, L0_attn_Wk, L0_attn_bk,
           L0_attn_Wv, L0_attn_bv, L0_attn_Wo, L0_attn_bo,
           L0_ffn_gate_w, L0_ffn_gate_b, L0_ffn_w1, L0_ffn_w2, L0_ffn_w3,
           L0_ffn_ws1, L0_ffn_ws2, L0_ffn_ws3):
    c = Cfg(n_cores=8, segs=8)
    inp = dict(L0_attn_x=L0_attn_x, L0_attn_Wq=L0_attn_Wq, L0_attn_bq=L0_attn_bq,
               L0_attn_Wk=L0_attn_Wk, L0_attn_bk=L0_attn_bk, L0_attn_Wv=L0_attn_Wv,
               L0_attn_bv=L0_attn_bv, L0_attn_Wo=L0_attn_Wo, L0_attn_bo=L0_attn_bo,
               L0_ffn_gate_w=L0_ffn_gate_w, L0_ffn_gate_b=L0_ffn_gate_b,
               L0_ffn_w1=L0_ffn_w1, L0_ffn_w2=L0_ffn_w2, L0_ffn_w3=L0_ffn_w3,
               L0_ffn_ws1=L0_ffn_ws1, L0_ffn_ws2=L0_ffn_ws2, L0_ffn_ws3=L0_ffn_ws3)
    key = (c.n_cores, c.segs)
    if key not in _CACHED:
        stack = ExitStack()
        nc = build_program(c, stack)
        _CACHED[key] = (nc, stack)
    nc = _CACHED[key][0]
    in_maps = host_inputs(c, inp)
    res = run_bass_kernel_spmd(nc, in_maps, core_ids=list(range(c.n_cores)))
    return assemble_out(c, res.results, np.asarray(L0_attn_x).shape)


# version 17 (final)
# speedup vs baseline: 1.6878x; 1.0609x over previous
import numpy as np
import ml_dtypes
from contextlib import ExitStack

import concourse.bass as bass
import concourse.bacc as bacc
import concourse.mybir as mybir
from concourse.bass import IndirectOffsetOnAxis
from concourse.bass_utils import run_bass_kernel_spmd
from concourse import library_config

F32 = mybir.dt.float32
BF16 = mybir.dt.bfloat16
I16 = mybir.dt.int16
I32 = mybir.dt.int32
U32 = mybir.dt.uint32
AF = mybir.ActivationFunctionType
OP = mybir.AluOpType
AX = mybir.AxisListType
BF = ml_dtypes.bfloat16


class Cfg:
    def __init__(self, n_cores=8, segs=8):
        self.n_cores = n_cores
        self.segs = segs
        self.SEG = 512
        self.D = 1024
        self.DC = 8
        self.H = 16
        self.E = 64
        self.CAP = 512
        self.I = 704
        self.IC = 6
        self.SI = 1408
        self.SIC = 11
        self.Tl = segs * self.SEG
        self.tiles = self.Tl // 128
        self.T = self.Tl * n_cores
        self.EPC = self.E // n_cores
        self.nseg_tot = segs * n_cores
        self.NCH = 4
        self.EPCH = self.EPC // self.NCH


class Sem:
    def __init__(self, name, producer=None):
        self.name, self.h, self.n, self.producer = name, None, 0, producer


class Stream:
    def __init__(self, b, name, nsem=4):
        self.sems = [b.sem(f"{name}{i}") for i in range(nsem)]
        self.i = 0

    def next(self):
        s = self.sems[self.i % len(self.sems)]
        self.i += 1
        return s


class Builder:
    ENGINES = ("sync", "scalar", "vector", "tensor", "gpsimd")

    def __init__(self, nc):
        self.nc = nc
        self.steps = []
        self.sems = []
        self.res = {}
        self.gwait = None
        self.eng_sem = {e: self.sem("c_" + e, producer=e) for e in self.ENGINES}

    def sem(self, name, producer=None):
        s = Sem(name, producer)
        self.sems.append(s)
        return s

    def _dep(self, name):
        return self.res.setdefault(name, {"w": None, "r": []})

    def do(self, engine, fn, reads=(), writes=(), wait=(), inc=(), dma=None,
           coll=None):
        waits = {}

        def addw(sv):
            if sv is None:
                return
            s, v = sv
            waits[s] = max(waits.get(s, 0), v)

        for r in reads:
            addw(self._dep(r)["w"])
        for w in writes:
            d = self._dep(w)
            addw(d["w"])
            for sv in d["r"]:
                addw(sv)
        for sv in wait:
            addw(sv)
        addw(self.gwait)

        if coll is not None:
            s = coll
            amt = 1
        elif dma is None:
            s = self.eng_sem[engine]
            amt = 1
        else:
            s = dma.next() if isinstance(dma, Stream) else dma
            amt = 16
            if s.n > 0:
                waits[s] = max(waits.get(s, 0), s.n)
        s.n += amt
        comp = (s, s.n)
        incs = [(s, amt)]
        for (x, a) in inc:
            x.n += a
            incs.append((x, a))

        for w in writes:
            d = self._dep(w)
            d["w"] = comp
            d["r"] = []
        for r in reads:
            self._dep(r)["r"].append(comp)

        self.steps.append((engine, fn,
                           sorted(waits.items(), key=lambda kv: kv[0].name), incs))
        return comp

    def fence(self, exclude=()):
        allr = [k for k in self.res.keys() if k not in exclude]
        h = self.do("vector", lambda eng: None, reads=allr)
        self.gwait = h

    def emit(self, stack):
        for s in self.sems:
            s.h = stack.enter_context(self.nc.semaphore(s.name))
        per = {e: [] for e in self.ENGINES}
        for st in self.steps:
            per[st[0]].append(st)
        with self.nc.Block() as block:
            def mk(engname):
                def body(eng):
                    seen = {}
                    for (_, fn, waits, incs) in per[engname]:
                        for (s, v) in waits:
                            if seen.get(s, 0) >= v:
                                continue
                            eng.wait_ge(s.h, v)
                            seen[s] = v
                        instr = fn(eng)
                        for (s, amt) in incs:
                            if instr is None:
                                eng.sem_inc(s.h, amt)
                            else:
                                instr = instr.then_inc(s.h, amt)
                return body
            for e in self.ENGINES:
                if per[e]:
                    getattr(block, e)(mk(e))


def build_program(c: Cfg, stack: ExitStack, stop_after="D", debug_dump=False):
    nc = bacc.Bacc("TRN2")
    b = Builder(nc)
    _PHASES = ["A1", "AG", "A2a", "Q7", "A2b", "C", "D"]
    _stop = _PHASES.index(stop_after)

    def _cut(phase):
        return _PHASES.index(phase) > _stop

    def din(name, shape, dt):
        return nc.dram_tensor(name, shape, dt, kind="ExternalInput").ap()

    def dint(name, shape, dt, shared=False):
        if shared:
            return nc.dram_tensor(name, shape, dt, addr_space="Shared").ap()
        return nc.dram_tensor(name, shape, dt).ap()

    x_in = din("x", [c.Tl, c.D], F32)
    out = nc.dram_tensor("out", [c.Tl + 128, c.D], F32, kind="ExternalOutput").ap()
    attw_in = [din(w, [c.D, c.D], F32) for w in ("Wq", "Wk", "Wv", "Wo")]
    bqkv_in = din("bqkv", [128, 3 * c.DC], F32)
    bo_in = din("bo_row", [1, c.D], F32)
    ws1T_in = din("ws1T", [c.D, c.SI], F32)
    ws3T_in = din("ws3T", [c.D, c.SI], F32)
    ws2T_in = din("ws2T", [c.SI, c.D], F32)
    gwT_in = din("gwT", [c.D, c.E], F32)
    gb_in = din("gb_bcast", [128, c.E], F32)
    w1T_in = din("w1T", [c.EPC, c.D, c.I], F32)
    w3T_in = din("w3T", [c.EPC, c.D, c.I], F32)
    w2T_in = din("w2T", [c.EPC, c.I, c.D], F32)
    Lup_in = din("Lup", [128, 132], BF16)
    idf_in = din("idf32", [128, 128], F32)
    idb_in = din("idbf", [128, 128], BF16)
    ssum_in = din("Ssum", [128, c.DC * 128], BF16)
    sbsel_in = din("Sb", [128, c.DC * 128], BF16)
    ones1_in = din("ones1", [1, 128], F32)
    segsel_in = din("segsel", [128, c.segs * c.segs], BF16)
    ids16_in = din("ids16", [c.E, c.Tl], I16)
    sloti_in = din("sloti", [c.E, c.CAP], I16)
    maskm_in = din("maskmat", [c.nseg_tot, c.segs], F32)
    rowb_in = din("rowbase", [128, 2], F32)
    myexp_in = din("myexp", [c.EPC, 1], I32)

    ln1_d = dint("ln1_d", [c.Tl, c.D], BF16)
    x1_d = dint("x1_d", [c.Tl, c.D], F32)
    agin_ln2 = dint("agin_ln2", [c.Tl, c.D], BF16)
    ln2_full = dint("ln2_full", [c.n_cores * c.Tl, c.D], BF16, shared=True)
    agin_cnt = dint("agin_cnt", [c.segs, c.E], F32)
    cnt_full = dint("cnt_full", [c.nseg_tot, c.E], F32, shared=True)
    arin_tab = dint("arin_tab", [c.E, 2 * c.CAP], F32)
    tab_full = dint("tab_full", [c.E, 2 * c.CAP], F32, shared=True)
    agin_yb = dint("agin_yb", [c.EPC * c.CAP, c.D], BF16)
    eidx_d = dint("eidx_d", [c.EPC, c.CAP], I16)
    lists_d = dint("lists_d", [c.E, 2 * c.CAP], I16)
    cnts_d = dint("cnts_d", [1, c.E], F32)
    ybc = [dint(f"ybc{k}", [c.n_cores * c.EPCH * c.CAP, c.D], BF16, shared=True)
           for k in range(c.NCH)]

    S = {}
    CMS = {}

    def sbuf(name, shape, dt, scope=None):
        cm = nc.sbuf_tensor("sb_" + name, shape, dt)
        if scope is None:
            S[name] = stack.enter_context(cm)
        else:
            S[name] = cm.__enter__()
            CMS.setdefault(scope, []).append(cm)
        return S[name]

    def free_scope(scope):
        for cm in reversed(CMS.pop(scope, [])):
            cm.__exit__(None, None, None)

    sbuf("bqkv", [128, 3 * c.DC], F32)
    sbuf("bo_row", [1, c.D], F32)
    sbuf("gb", [128, c.E], F32)
    sbuf("Lup", [128, 132], BF16)
    sbuf("idf32", [128, 128], F32)
    sbuf("idbf", [128, 128], BF16)
    sbuf("Ssum", [128, c.DC * 128], BF16)
    sbuf("Sb", [128, c.DC * 128], BF16)
    sbuf("ones1", [1, 128], F32)
    sbuf("segsel", [128, c.segs * c.segs], BF16)
    sbuf("sloti", [c.E, c.CAP], I16)
    sbuf("maskm", [c.nseg_tot, c.segs], F32)
    sbuf("rowb", [128, 2], F32)
    sbuf("myexp", [c.EPC, 1], I32)
    sbuf("wstage", [128, c.SI], F32)
    sbuf("stat", [128, 16], F32)
    sbuf("x1t", [128, 2 * c.D], F32)
    sbuf("gstat", [128, 4], F32)
    sbuf("sbt", [128, c.E], F32)
    sbuf("top8", [128, 8], F32)
    sbuf("ln2fmseg", [128, 2 * c.DC * c.SEG], BF16)
    sbuf("tilec", [c.E, c.tiles], F32)
    sbuf("segcnt", [c.segs, c.E], F32)
    sbuf("cntf", [c.nseg_tot, c.E], F32)
    sbuf("segoff", [c.E, c.segs], F32)
    sbuf("totoff", [c.E, c.tiles], F32)
    sbuf("tabf", [c.E, 2 * c.CAP], F32)
    sbuf("idsG", [c.E, c.CAP], I16)
    sbuf("wG", [c.E, c.CAP], BF16)
    nblk = c.E // 16
    sbuf("minef", [c.E, c.CAP], F32)
    sbuf("mtmp", [c.E, c.CAP], F32)
    sbuf("mtmp2", [c.E, c.CAP], F32)
    sbuf("mtmp3", [c.E, c.CAP], F32)
    sbuf("mtmpI", [c.E, c.CAP], I32)
    sbuf("ltok", [c.E, c.CAP], I16)
    sbuf("onescol", [c.E, 1], F32)
    sbuf("rankD", [c.E, c.CAP], I16)
    sbuf("listSS", [c.E, c.CAP], I16)
    sbuf("listTT", [c.E, c.CAP], I16)
    sbuf("dlist", [128, 2 * c.E * (c.CAP // 16)], I16)
    sbuf("cntrow", [c.E, 1], F32)
    sbuf("nfs", [1, c.E], U32)
    sbuf("nfsf", [1, c.E], F32)
    sbuf("t128", [c.E, 128], F32)

    P = {}
    for n_, sh in (("g0", [128, 512]), ("g1", [128, 512]), ("g2", [128, 512]),
                   ("g3", [128, 512]), ("psB", [128, c.SEG]), ("t0", [c.E, 132]),
                   ("tsm", [128, 128]), ("psC", [c.segs, c.E])):
        P[n_] = stack.enter_context(nc.psum_tensor(n_, sh, F32))

    ld = Stream(b, "ld", 12)
    stw = Stream(b, "st", 8)
    pda = b.sem("pda")
    pdb = Stream(b, "pdb", 8)
    pdc = Stream(b, "pdc", 4)
    pdd = Stream(b, "pdd", 4)
    cc1, cc2, cc3 = b.sem("cc1"), b.sem("cc2"), b.sem("cc3")
    cc4s = [b.sem(f"cc4{k}") for k in range(c.NCH)]
    CST = c.CAP
    RG = [list(range(c.n_cores))]

    for nm, o, i in (("bqkv", S["bqkv"], bqkv_in), ("bo_row", S["bo_row"], bo_in),
                     ("gb", S["gb"], gb_in), ("Lup", S["Lup"], Lup_in),
                     ("idf32", S["idf32"], idf_in), ("idbf", S["idbf"], idb_in),
                     ("Ssum", S["Ssum"], ssum_in), ("Sb", S["Sb"], sbsel_in),
                     ("ones1", S["ones1"], ones1_in), ("segsel", S["segsel"], segsel_in),
                     ("sloti", S["sloti"], sloti_in),
                     ("maskm", S["maskm"], maskm_in), ("rowb", S["rowb"], rowb_in),
                     ("myexp", S["myexp"], myexp_in)):
        b.do("sync", (lambda o_, i_: (lambda eng: eng.dma_start(out=o_[:, :], in_=i_[:, :])))(o, i),
             writes=[nm], dma=ld)

    sbuf("attW", [128, 4 * c.DC * c.D], BF16, scope="A1")
    sbuf("xt2", [128, 2 * c.D], F32, scope="A1")
    sbuf("sqdump", [128, c.D], BF16, scope="A1")
    sbuf("lnt", [128, 2 * c.D], BF16, scope="A1")
    sbuf("ln1fm", [128, 2 * c.DC * c.SEG], BF16, scope="A1")
    sbuf("Qfm", [128, c.DC * c.SEG], BF16, scope="A1")
    sbuf("Kfm", [128, c.DC * c.SEG], BF16, scope="A1")
    sbuf("Vfm", [128, c.DC * c.SEG], BF16, scope="A1")
    sbuf("Pch", [128, 2 * c.SEG], BF16, scope="A1")
    sbuf("praw", [128, c.SEG], F32, scope="A1")
    sbuf("probs", [128, c.SEG], BF16, scope="A1")
    sbuf("pstat", [128, 4], F32, scope="A1")

    for wi in range(4):
        for kc in range(c.DC):
            b.do("sync",
                 (lambda wi_, kc_: (lambda eng: eng.dma_start(
                     out=S["wstage"][:, 0:c.D],
                     in_=attw_in[wi_][kc_ * 128:(kc_ + 1) * 128, :])))(wi, kc),
                 writes=["wstage"], dma=ld)
            b.do("vector",
                 (lambda wi_, kc_: (lambda eng: eng.tensor_copy(
                     out=S["attW"][:, (wi_ * 8 + kc_) * c.D:(wi_ * 8 + kc_ + 1) * c.D],
                     in_=S["wstage"][:, 0:c.D])))(wi, kc),
                 reads=["wstage"], writes=[f"attW{wi}_{kc}"])

    def ln_tile(xt_ap, out_ap, so, out_res, in_res):
        st_ = S["stat"]
        b.do("scalar",
             (lambda xt_, so_: (lambda eng: eng.activation(
                 out=S["sqdump"][:, :], in_=xt_, func=AF.Square,
                 accum_out=st_[:, so_ * 8 + 1: so_ * 8 + 2])))(xt_ap, so),
             reads=list(in_res), writes=[f"stat{so}b", "sqdump"])
        b.do("vector",
             (lambda xt_, so_: (lambda eng: eng.reduce_sum(
                 out=st_[:, so_ * 8 + 0: so_ * 8 + 1], in_=xt_, axis=AX.X)))(xt_ap, so),
             reads=list(in_res), writes=[f"stat{so}a"])
        b.do("vector",
             (lambda so_: (lambda eng: eng.tensor_scalar_mul(
                 st_[:, so_ * 8 + 2: so_ * 8 + 3], st_[:, so_ * 8 + 0: so_ * 8 + 1],
                 1.0 / c.D)))(so),
             reads=[f"stat{so}a"], writes=[f"stat{so}m"])
        b.do("vector",
             (lambda so_: (lambda eng: eng.tensor_mul(
                 st_[:, so_ * 8 + 3: so_ * 8 + 4], st_[:, so_ * 8 + 2: so_ * 8 + 3],
                 st_[:, so_ * 8 + 2: so_ * 8 + 3])))(so),
             reads=[f"stat{so}m"], writes=[f"stat{so}q"])
        b.do("vector",
             (lambda so_: (lambda eng: eng.scalar_tensor_tensor(
                 out=st_[:, so_ * 8 + 4: so_ * 8 + 5], in0=st_[:, so_ * 8 + 1: so_ * 8 + 2],
                 scalar=1.0 / c.D, in1=st_[:, so_ * 8 + 3: so_ * 8 + 4],
                 op0=OP.mult, op1=OP.subtract)))(so),
             reads=[f"stat{so}b", f"stat{so}q"], writes=[f"stat{so}v"])
        b.do("vector",
             (lambda so_: (lambda eng: eng.tensor_scalar_add(
                 st_[:, so_ * 8 + 4: so_ * 8 + 5], st_[:, so_ * 8 + 4: so_ * 8 + 5],
                 1e-5)))(so),
             reads=[f"stat{so}v"], writes=[f"stat{so}v"])
        b.do("scalar",
             (lambda so_: (lambda eng: eng.activation(
                 out=st_[:, so_ * 8 + 6: so_ * 8 + 7], in_=st_[:, so_ * 8 + 4: so_ * 8 + 5],
                 func=AF.Sqrt)))(so),
             reads=[f"stat{so}v"], writes=[f"stat{so}s"])
        b.do("vector",
             (lambda so_: (lambda eng: eng.reciprocal(
                 out=st_[:, so_ * 8 + 5: so_ * 8 + 6],
                 in_=st_[:, so_ * 8 + 6: so_ * 8 + 7])))(so),
             reads=[f"stat{so}s"], writes=[f"stat{so}r"])
        b.do("vector",
             (lambda xt_, so_, o_: (lambda eng: eng.tensor_scalar(
                 out=o_, in0=xt_,
                 scalar1=st_[:, so_ * 8 + 2: so_ * 8 + 3],
                 scalar2=st_[:, so_ * 8 + 5: so_ * 8 + 6],
                 op0=OP.subtract, op1=OP.mult)))(xt_ap, so, out_ap),
             reads=list(in_res) + [f"stat{so}m", f"stat{so}r"], writes=[out_res])

    for ls in range(c.segs):
        q = ls % 2
        for tt in range(4):
            hh = tt % 2
            b.do("sync",
                 (lambda ls_, tt_, h_: (lambda eng: eng.dma_start(
                     out=S["xt2"][:, h_ * c.D:(h_ + 1) * c.D],
                     in_=x_in[ls_ * c.SEG + tt_ * 128: ls_ * c.SEG + (tt_ + 1) * 128, :])))(ls, tt, hh),
                 writes=[f"xt2{hh}"], dma=ld)
            xt = S["xt2"][:, hh * c.D:(hh + 1) * c.D]
            ln_tile(xt, S["lnt"][:, hh * c.D:(hh + 1) * c.D], hh, f"lnt{hh}", [f"xt2{hh}"])
            b.do("scalar",
                 (lambda ls_, tt_, h_: (lambda eng: eng.dma_start(
                     out=ln1_d[ls_ * c.SEG + tt_ * 128: ls_ * c.SEG + (tt_ + 1) * 128, :],
                     in_=S["lnt"][:, h_ * c.D:(h_ + 1) * c.D])))(ls, tt, hh),
                 reads=[f"lnt{hh}"], writes=[f"ln1_d{ls}_{tt}"], dma=stw)
        for kc in range(c.DC):
            b.do("sync",
                 (lambda ls_, kc_, q_: (lambda eng: eng.dma_start(
                     out=S["ln1fm"][:, (q_ * c.DC + kc_) * c.SEG:(q_ * c.DC + kc_ + 1) * c.SEG],
                     in_=ln1_d[ls_ * c.SEG:(ls_ + 1) * c.SEG, kc_ * 128:(kc_ + 1) * 128],
                     transpose=True)))(ls, kc, q),
                 reads=[f"ln1_d{ls}_{t2}" for t2 in range(4)],
                 writes=[f"ln1fm{q}_{kc}"], dma=ld)

        for pi, dst in enumerate(("Qfm", "Kfm", "Vfm")):
            for oc in range(c.DC):
                bank = f"g{(pi * c.DC + oc) % 2}"

                def qkv_group(eng, pi_=pi, oc_=oc, q_=q, bank_=bank):
                    ii = None
                    for kc in range(c.DC):
                        ii = eng.matmul(
                            P[bank_][:, :],
                            S["attW"][:, (pi_ * 8 + kc) * c.D + oc_ * 128:
                                      (pi_ * 8 + kc) * c.D + (oc_ + 1) * 128],
                            S["ln1fm"][:, (q_ * c.DC + kc) * c.SEG:(q_ * c.DC + kc + 1) * c.SEG],
                            start=(kc == 0), stop=(kc == c.DC - 1))
                    return ii

                b.do("tensor", qkv_group,
                     reads=[f"ln1fm{q}_{kc}" for kc in range(c.DC)] +
                           [f"attW{pi}_{kc}" for kc in range(c.DC)],
                     writes=[bank])
                b.do("scalar",
                     (lambda pi_, oc_, dst_, bank_: (lambda eng: eng.activation(
                         out=S[dst_][:, oc_ * c.SEG:(oc_ + 1) * c.SEG], in_=P[bank_][:, :],
                         func=AF.Identity,
                         bias=S["bqkv"][:, pi_ * 8 + oc_: pi_ * 8 + oc_ + 1],
                         scale=1.0)))(pi, oc, dst, bank),
                     reads=[bank, "bqkv"], writes=[f"{dst}_{oc}"])

        for kc in range(c.DC):
            pq = kc % 2
            b.do("vector",
                 (lambda kc_, pq_: (lambda eng: eng.tensor_mul(
                     S["Pch"][:, pq_ * c.SEG:(pq_ + 1) * c.SEG],
                     S["Qfm"][:, kc_ * c.SEG:(kc_ + 1) * c.SEG],
                     S["Kfm"][:, kc_ * c.SEG:(kc_ + 1) * c.SEG])))(kc, pq),
                 reads=[f"Qfm_{kc}", f"Kfm_{kc}"], writes=[f"Pch{pq}"])
            b.do("tensor",
                 (lambda kc_, pq_: (lambda eng: eng.matmul(
                     P["psB"][:, :], S["Ssum"][:, kc_ * 128:(kc_ + 1) * 128],
                     S["Pch"][:, pq_ * c.SEG:(pq_ + 1) * c.SEG],
                     start=(kc_ == 0), stop=(kc_ == c.DC - 1))))(kc, pq),
                 reads=[f"Pch{pq}", "Ssum"], writes=["psB"])

        b.do("vector", lambda eng: eng.memset(S["probs"][:, :], 0.0), writes=["probs"])
        for j in range(4):
            rows = slice(32 * j, 32 * j + 4)
            cols = slice(j, c.SEG, 4)
            b.do("vector",
                 (lambda r_, co_: (lambda eng: eng.tensor_reduce(
                     out=S["pstat"][r_, 0:1], in_=P["psB"][r_, co_],
                     axis=AX.X, op=OP.max, negate=True)))(rows, cols),
                 reads=["psB"], writes=[f"pstat{j}a"])
            b.do("scalar",
                 (lambda r_, co_: (lambda eng: eng.activation(
                     out=S["praw"][r_, co_], in_=P["psB"][r_, co_],
                     func=AF.Exp, bias=S["pstat"][r_, 0:1], scale=1.0,
                     accum_out=S["pstat"][r_, 1:2])))(rows, cols),
                 reads=["psB", f"pstat{j}a"], writes=[f"praw{j}", f"pstat{j}b"])
            b.do("vector",
                 (lambda r_: (lambda eng: eng.reciprocal(
                     out=S["pstat"][r_, 2:3], in_=S["pstat"][r_, 1:2])))(rows),
                 reads=[f"pstat{j}b"], writes=[f"pstat{j}c"])
            b.do("vector",
                 (lambda r_, co_: (lambda eng: eng.tensor_scalar_mul(
                     S["probs"][r_, co_], S["praw"][r_, co_],
                     S["pstat"][r_, 2:3])))(rows, cols),
                 reads=[f"praw{j}", f"pstat{j}c", "probs"], writes=[f"probsw{j}"])

        for kc in range(c.DC):
            bank = f"g{2 + kc % 2}"
            b.do("tensor",
                 (lambda kc_, bank_: (lambda eng: eng.matmul(
                     P[bank_][:, :], S["Sb"][:, kc_ * 128:(kc_ + 1) * 128],
                     S["probs"][:, :], start=True, stop=True)))(kc, bank),
                 reads=["probs", "Sb"] + [f"probsw{j}" for j in range(4)],
                 writes=[bank])
            b.do("vector",
                 (lambda kc_, bank_: (lambda eng: eng.tensor_mul(
                     S["Qfm"][:, kc_ * c.SEG:(kc_ + 1) * c.SEG], P[bank_][:, :],
                     S["Vfm"][:, kc_ * c.SEG:(kc_ + 1) * c.SEG])))(kc, bank),
                 reads=[bank, f"Vfm_{kc}", f"Qfm_{kc}"], writes=[f"attnfm_{kc}"])

        for tt in range(4):
            hh = tt % 2
            b.do("sync",
                 (lambda ls_, tt_, h_: (lambda eng: eng.dma_start(
                     out=S["xt2"][:, h_ * c.D:(h_ + 1) * c.D],
                     in_=x_in[ls_ * c.SEG + tt_ * 128: ls_ * c.SEG + (tt_ + 1) * 128, :])))(ls, tt, hh),
                 writes=[f"xt2{hh}"], dma=ld)
            for nh in range(2):
                bank = f"g{nh}"

                def oproj_group(eng, tt_=tt, nh_=nh, bank_=bank):
                    for kc in range(c.DC):
                        eng.matmul(
                            P[bank_][:, :],
                            S["Qfm"][:, kc * c.SEG + tt_ * 128: kc * c.SEG + (tt_ + 1) * 128],
                            S["attW"][:, (3 * 8 + kc) * c.D + nh_ * 512:
                                      (3 * 8 + kc) * c.D + (nh_ + 1) * 512],
                            start=(kc == 0), stop=False)
                    return eng.matmul(P[bank_][:, :], S["ones1"][:, :],
                                      S["bo_row"][:, nh_ * 512:(nh_ + 1) * 512],
                                      start=False, stop=True)

                b.do("tensor", oproj_group,
                     reads=[f"attnfm_{kc}" for kc in range(c.DC)] +
                           [f"Qfm_{kc}" for kc in range(c.DC)] +
                           [f"attW3_{kc}" for kc in range(c.DC)] + ["ones1", "bo_row"],
                     writes=[bank])
                b.do("vector",
                     (lambda bank_, tt_, nh_, h_: (lambda eng: eng.tensor_add(
                         S["x1t"][:, h_ * c.D + nh_ * 512: h_ * c.D + (nh_ + 1) * 512],
                         P[bank_][:, :],
                         S["xt2"][:, h_ * c.D + nh_ * 512: h_ * c.D + (nh_ + 1) * 512])))(bank, tt, nh, hh),
                     reads=[bank, f"xt2{hh}"], writes=[f"x1t{hh}_{nh}"])
            b.do("sync",
                 (lambda ls_, tt_, h_: (lambda eng: eng.dma_start(
                     out=x1_d[ls_ * c.SEG + tt_ * 128: ls_ * c.SEG + (tt_ + 1) * 128, :],
                     in_=S["x1t"][:, h_ * c.D:(h_ + 1) * c.D])))(ls, tt, hh),
                 reads=[f"x1t{hh}_0", f"x1t{hh}_1"], writes=[f"x1_d{ls}_{tt}"], dma=ld)
            xt = S["x1t"][:, hh * c.D:(hh + 1) * c.D]
            ln_tile(xt, S["lnt"][:, hh * c.D:(hh + 1) * c.D], hh,
                    f"lnt{hh}", [f"x1t{hh}_0", f"x1t{hh}_1"])
            b.do("scalar",
                 (lambda ls_, tt_, h_: (lambda eng: eng.dma_start(
                     out=agin_ln2[ls_ * c.SEG + tt_ * 128: ls_ * c.SEG + (tt_ + 1) * 128, :],
                     in_=S["lnt"][:, h_ * c.D:(h_ + 1) * c.D])))(ls, tt, hh),
                 reads=[f"lnt{hh}"], writes=[f"agin{ls}_{tt}"], dma=stw)

    all_agin = [f"agin{ls}_{tt}" for ls in range(c.segs) for tt in range(4)]

    if _cut("AG"):
        b.fence()
        b.do("gpsimd", lambda eng: None, wait=[b.gwait])
        b.do("sync", lambda eng: None, wait=[b.gwait])
        b.emit(stack)
        nc.compile()
        return nc
    b.do("gpsimd",
         lambda eng: eng.collective_compute(
             "AllGather", OP.bypass, replica_groups=RG,
             ins=[agin_ln2[:, :]], outs=[ln2_full[:, :]]),
         reads=all_agin, writes=["ln2_full"], coll=cc1)

    if _cut("A2a"):
        b.fence()
        b.do("gpsimd", lambda eng: None, wait=[b.gwait])
        b.do("sync", lambda eng: None, wait=[b.gwait])
        b.emit(stack)
        nc.compile()
        return nc
    b.fence(exclude=["ln2_full"])
    free_scope("A1")

    sbuf("gwT", [128, c.DC * c.E], BF16, scope="R")
    sbuf("sc_t", [128, 2 * c.E], F32, scope="R")
    sbuf("M_t", [128, 2 * c.E], BF16, scope="R")
    QW = c.Tl // 8
    sbuf("MT", [c.E, c.Tl], BF16, scope="R")
    sbuf("scoresT", [c.E, c.Tl], BF16, scope="R")
    sbuf("rank16", [c.E, c.Tl], I16, scope="R")
    sbuf("scrF", [c.E, QW], F32, scope="R")
    sbuf("scrI", [c.E, QW], I32, scope="R")
    sbuf("maskh", [c.E, QW], F32, scope="R")
    sbuf("idx16", [c.E, c.Tl], I16, scope="R")
    sbuf("idx16b", [c.E, c.Tl], I16, scope="R")
    sbuf("ids16", [c.E, c.Tl], I16, scope="R")

    b.do("sync", lambda eng: eng.dma_start(out=S["ids16"][:, :], in_=ids16_in[:, :]),
         writes=["ids16"], dma=ld)
    b.do("sync",
         lambda eng: eng.dma_start(
             out=S["wstage"][:, 0:c.DC * c.E].rearrange("p (k e) -> p k e", k=c.DC),
             in_=gwT_in.rearrange("(k p) e -> p k e", p=128)),
         writes=["wstage"], dma=ld)
    b.do("vector",
         lambda eng: eng.tensor_copy(out=S["gwT"][:, :], in_=S["wstage"][:, 0:c.DC * c.E]),
         reads=["wstage"], writes=["gwT"])

    for ls in range(c.segs):
        q = ls % 2
        for kc in range(c.DC):
            b.do("sync",
                 (lambda ls_, kc_, q_: (lambda eng: eng.dma_start(
                     out=S["ln2fmseg"][:, (q_ * c.DC + kc_) * c.SEG:
                                       (q_ * c.DC + kc_ + 1) * c.SEG],
                     in_=agin_ln2[ls_ * c.SEG:(ls_ + 1) * c.SEG, kc_ * 128:(kc_ + 1) * 128],
                     transpose=True)))(ls, kc, q),
                 reads=[f"agin{ls}_{t2}" for t2 in range(4)],
                 writes=[f"l2s{q}_{kc}"], dma=ld)
        for tt in range(4):
            ti = 4 * ls + tt
            tp = ti % 2
            bank = f"g{tp}"

            def gate_group(eng, tt_=tt, q_=q, bank_=bank):
                ii = None
                for kc in range(c.DC):
                    ii = eng.matmul(
                        P[bank_][:, 0:c.E],
                        S["ln2fmseg"][:, (q_ * c.DC + kc) * c.SEG + tt_ * 128:
                                      (q_ * c.DC + kc) * c.SEG + (tt_ + 1) * 128],
                        S["gwT"][:, kc * c.E:(kc + 1) * c.E],
                        start=(kc == 0), stop=(kc == c.DC - 1))
                return ii

            b.do("tensor", gate_group,
                 reads=[f"l2s{q}_{kc}" for kc in range(c.DC)] + ["gwT"], writes=[bank])
            b.do("vector",
                 (lambda bank_: (lambda eng: eng.tensor_reduce(
                     out=S["gstat"][:, 0:1], in_=P[bank_][:, 0:c.E], axis=AX.X,
                     op=OP.max, negate=True)))(bank),
                 reads=[bank], writes=["gstat0"])
            b.do("scalar",
                 (lambda bank_, tp_: (lambda eng: eng.activation(
                     out=S["sc_t"][:, tp_ * c.E:(tp_ + 1) * c.E], in_=P[bank_][:, 0:c.E],
                     func=AF.Exp, bias=S["gstat"][:, 0:1], scale=1.0,
                     accum_out=S["gstat"][:, 1:2])))(bank, tp),
                 reads=[bank, "gstat0"], writes=[f"scraw{tp}", "gstat1"])
            b.do("vector",
                 lambda eng: eng.reciprocal(out=S["gstat"][:, 2:3], in_=S["gstat"][:, 1:2]),
                 reads=["gstat1"], writes=["gstat2"])
            b.do("vector",
                 (lambda tp_: (lambda eng: eng.tensor_scalar_mul(
                     S["sc_t"][:, tp_ * c.E:(tp_ + 1) * c.E],
                     S["sc_t"][:, tp_ * c.E:(tp_ + 1) * c.E], S["gstat"][:, 2:3])))(tp),
                 reads=[f"scraw{tp}", "gstat2"], writes=[f"sct{tp}"])
            b.do("vector",
                 (lambda tp_: (lambda eng: eng.tensor_add(
                     S["sbt"][:, :], S["sc_t"][:, tp_ * c.E:(tp_ + 1) * c.E],
                     S["gb"][:, :])))(tp),
                 reads=[f"sct{tp}", "gb"], writes=["sbt"])
            b.do("vector", lambda eng: eng.max(out=S["top8"][:, :], in_=S["sbt"][:, :]),
                 reads=["sbt"], writes=["top8"])
            b.do("vector",
                 (lambda tp_: (lambda eng: eng.tensor_scalar(
                     out=S["M_t"][:, tp_ * c.E:(tp_ + 1) * c.E], in0=S["sbt"][:, :],
                     scalar1=S["top8"][:, 5:6], scalar2=None, op0=OP.is_ge)))(tp),
                 reads=["sbt", "top8"], writes=[f"Mt{tp}"])
            b.do("tensor",
                 (lambda tp_: (lambda eng: eng.transpose(
                     P["tsm"][0:c.E, 0:128], S["sc_t"][:, tp_ * c.E:(tp_ + 1) * c.E],
                     S["idf32"][:, :])))(tp),
                 reads=[f"sct{tp}", "idf32"], writes=["tsm"])
            b.do("vector",
                 (lambda ti_: (lambda eng: eng.tensor_copy(
                     out=S["scoresT"][:, ti_ * 128:(ti_ + 1) * 128],
                     in_=P["tsm"][0:c.E, 0:128])))(ti),
                 reads=["tsm"], writes=[f"scT{ti}"])
            b.do("tensor",
                 (lambda tp_: (lambda eng: eng.transpose(
                     P["tsm"][0:c.E, 0:64].bitcast(BF16),
                     S["M_t"][:, tp_ * c.E:(tp_ + 1) * c.E],
                     S["idbf"][:, :])))(tp),
                 reads=[f"Mt{tp}", "idbf"], writes=["tsm"])
            b.do("vector",
                 (lambda ti_: (lambda eng: eng.tensor_copy(
                     out=S["MT"][:, ti_ * 128:(ti_ + 1) * 128],
                     in_=P["tsm"][0:c.E, 0:64].bitcast(BF16))))(ti),
                 reads=["tsm"], writes=[f"MTc{ti}"])
            b.do("tensor",
                 (lambda tp_: (lambda eng: eng.matmul(
                     P["t0"][:, 0:129], S["M_t"][:, tp_ * c.E:(tp_ + 1) * c.E],
                     S["Lup"][:, 0:129], start=True, stop=True)))(tp),
                 reads=[f"Mt{tp}", "Lup"], writes=["t0"])
            b.do("vector",
                 (lambda ti_: (lambda eng: eng.tensor_copy(
                     out=S["rank16"][:, ti_ * 128:(ti_ + 1) * 128],
                     in_=P["t0"][:, 0:128])))(ti),
                 reads=["t0"], writes=["rkF"])
            b.do("vector",
                 (lambda ti_: (lambda eng: eng.tensor_copy(
                     out=S["tilec"][:, ti_:ti_ + 1], in_=P["t0"][:, 128:129])))(ti),
                 reads=["t0"], writes=[f"tilec{ti}"])
            b.do("tensor",
                 (lambda tp_, ls_, fst, lst: (lambda eng: eng.matmul(
                     P["psC"][:, :], S["segsel"][:, ls_ * c.segs:(ls_ + 1) * c.segs],
                     S["M_t"][:, tp_ * c.E:(tp_ + 1) * c.E],
                     start=fst, stop=lst)))(tp, ls, ti == 0, ti == c.tiles - 1),
                 reads=[f"Mt{tp}", "segsel"], writes=["psC"])

    b.do("vector", lambda eng: eng.tensor_copy(out=S["segcnt"][:, :], in_=P["psC"][:, :]),
         reads=["psC"], writes=["segcnt"])
    b.do("sync", lambda eng: eng.dma_start(out=agin_cnt[:, :], in_=S["segcnt"][:, :]),
         reads=["segcnt"], writes=["agin_cnt"], dma=ld)
    b.do("gpsimd",
         lambda eng: eng.collective_compute(
             "AllGather", OP.bypass, replica_groups=RG,
             ins=[agin_cnt[:, :]], outs=[cnt_full[:, :]]),
         reads=["agin_cnt"], writes=["cnt_full"], coll=cc2)
    sbuf("zW", [128, 3 * 11264], BF16, scope="Z")
    sbuf("zh", [128, c.SIC * c.SEG], BF16, scope="Z")
    sbuf("silt", [128, c.SEG], F32, scope="Z")

    b.do("gpsimd",
         lambda eng: eng.dma_start(
             out=S["zW"][:, 0:c.DC * c.SI].rearrange("p (k i) -> p k i", k=c.DC),
             in_=ws1T_in.rearrange("(k p) i -> p k i", p=128)),
         writes=[f"zW0_{kc}" for kc in range(c.DC)], dma=pdb)
    b.do("gpsimd",
         lambda eng: eng.dma_start(
             out=S["zW"][:, 11264:11264 + c.DC * c.SI].rearrange(
                 "p (k i) -> p k i", k=c.DC),
             in_=ws3T_in.rearrange("(k p) i -> p k i", p=128)),
         writes=[f"zW1_{kc}" for kc in range(c.DC)], dma=pdb)
    b.do("gpsimd",
         lambda eng: eng.dma_start(
             out=S["zW"][:, 22528:22528 + c.SIC * c.D].rearrange(
                 "p (k i) -> p k i", k=c.SIC),
             in_=ws2T_in.rearrange("(k p) i -> p k i", p=128)),
         writes=[f"zW2_{ic}" for ic in range(c.SIC)], dma=pdb)

    def z_seg(ls):
        q = ls % 2
        for kc in range(c.DC):
            b.do("sync",
                 (lambda ls_, kc_, q_: (lambda eng: eng.dma_start(
                     out=S["ln2fmseg"][:, (q_ * c.DC + kc_) * c.SEG:
                                       (q_ * c.DC + kc_ + 1) * c.SEG],
                     in_=agin_ln2[ls_ * c.SEG:(ls_ + 1) * c.SEG, kc_ * 128:(kc_ + 1) * 128],
                     transpose=True)))(ls, kc, q),
                 reads=[f"agin{ls}_{t2}" for t2 in range(4)],
                 writes=[f"l2s{q}_{kc}"], dma=ld)
        for ic in range(c.SIC):
            for zi in range(2):
                bank = f"g{zi}"

                def z_group(eng, ic_=ic, zi_=zi, q_=q, bank_=bank):
                    ii = None
                    for kc in range(c.DC):
                        ii = eng.matmul(
                            P[bank_][:, :],
                            S["zW"][:, zi_ * 11264 + kc * c.SI + ic_ * 128:
                                    zi_ * 11264 + kc * c.SI + (ic_ + 1) * 128],
                            S["ln2fmseg"][:, (q_ * c.DC + kc) * c.SEG:
                                          (q_ * c.DC + kc + 1) * c.SEG],
                            start=(kc == 0), stop=(kc == c.DC - 1))
                    return ii

                b.do("tensor", z_group,
                     reads=[f"zW{zi}_{kc}" for kc in range(c.DC)] +
                           [f"l2s{q}_{kc}" for kc in range(c.DC)],
                     writes=[bank])
            b.do("scalar",
                 lambda eng: eng.activation(out=S["silt"][:, :], in_=P["g0"][:, :], func=AF.Silu),
                 reads=["g0"], writes=["silt0"])
            b.do("vector",
                 (lambda ic_: (lambda eng: eng.tensor_mul(
                     S["zh"][:, ic_ * c.SEG:(ic_ + 1) * c.SEG], S["silt"][:, :],
                     P["g1"][:, :])))(ic),
                 reads=["silt0", "g1"], writes=[f"zh{ic}"])
        for tt in range(4):
            hh = tt % 2
            b.do("sync",
                 (lambda ls_, tt_, h_: (lambda eng: eng.dma_start(
                     out=S["x1t"][:, h_ * c.D:(h_ + 1) * c.D],
                     in_=x1_d[ls_ * c.SEG + tt_ * 128: ls_ * c.SEG + (tt_ + 1) * 128, :])))(ls, tt, hh),
                 reads=[f"x1_d{ls}_{tt}"], writes=[f"x1t{hh}_0", f"x1t{hh}_1"], dma=ld)
            for nh in range(2):
                bank = f"g{2 + nh}"

                def z2_group(eng, tt_=tt, nh_=nh, bank_=bank):
                    ii = None
                    for ic in range(c.SIC):
                        ii = eng.matmul(
                            P[bank_][:, :],
                            S["zh"][:, ic * c.SEG + tt_ * 128: ic * c.SEG + (tt_ + 1) * 128],
                            S["zW"][:, 22528 + ic * c.D + nh_ * 512:
                                    22528 + ic * c.D + (nh_ + 1) * 512],
                            start=(ic == 0), stop=(ic == c.SIC - 1))
                    return ii

                b.do("tensor", z2_group,
                     reads=[f"zh{ic}" for ic in range(c.SIC)] +
                           [f"zW2_{ic}" for ic in range(c.SIC)],
                     writes=[bank])
                b.do("vector",
                     (lambda bank_, nh_, h_: (lambda eng: eng.tensor_add(
                         S["x1t"][:, h_ * c.D + nh_ * 512: h_ * c.D + (nh_ + 1) * 512],
                         P[bank_][:, :],
                         S["x1t"][:, h_ * c.D + nh_ * 512: h_ * c.D + (nh_ + 1) * 512])))(bank, nh, hh),
                     reads=[bank, f"x1t{hh}_0", f"x1t{hh}_1"], writes=[f"x1t{hh}_{nh}"])
            b.do("scalar",
                 (lambda ls_, tt_, h_: (lambda eng: eng.dma_start(
                     out=out[ls_ * c.SEG + tt_ * 128: ls_ * c.SEG + (tt_ + 1) * 128, :],
                     in_=S["x1t"][:, h_ * c.D:(h_ + 1) * c.D])))(ls, tt, hh),
                 reads=[f"x1t{hh}_0", f"x1t{hh}_1"], writes=[f"out{ls}_{tt}"], dma=stw)

    zsplit = max(1, c.segs - 3)
    for ls in range(zsplit):
        z_seg(ls)

    b.do("sync", lambda eng: eng.dma_start(out=S["cntf"][:, :], in_=cnt_full[:, :]),
         reads=["cnt_full"], writes=["cntf"], dma=ld)

    b.do("tensor",
         lambda eng: eng.matmul(P["tsm"][0:c.E, 0:c.segs], S["cntf"][:, :],
                                S["maskm"][:, :], start=True, stop=True),
         reads=["cntf", "maskm"], writes=["tsm"])
    b.do("vector",
         lambda eng: eng.tensor_copy(out=S["segoff"][:, :], in_=P["tsm"][0:c.E, 0:c.segs]),
         reads=["tsm"], writes=["segoff"])
    for ls in range(c.segs):
        b.do("vector",
             (lambda ls_: (lambda eng: eng.tensor_copy(
                 out=S["totoff"][:, 4 * ls_:4 * ls_ + 1], in_=S["segoff"][:, ls_:ls_ + 1])))(ls),
             reads=["segoff"], writes=[f"totoff{4 * ls}"])
        for j in range(3):
            b.do("vector",
                 (lambda ls_, j_: (lambda eng: eng.tensor_add(
                     S["totoff"][:, 4 * ls_ + j_ + 1:4 * ls_ + j_ + 2],
                     S["totoff"][:, 4 * ls_ + j_:4 * ls_ + j_ + 1],
                     S["tilec"][:, 4 * ls_ + j_:4 * ls_ + j_ + 1])))(ls, j),
                 reads=[f"totoff{4 * ls + j}", f"tilec{4 * ls + j}"],
                 writes=[f"totoff{4 * ls + j + 1}"])
    def V(fn, reads, writes):
        b.do("vector", fn, reads=reads, writes=writes)

    TPQ = QW // 128
    for h4 in range(c.Tl // QW):
        sl = slice(h4 * QW, (h4 + 1) * QW)
        V((lambda sl_: (lambda eng: eng.tensor_copy(
            out=S["scrF"][:, :], in_=S["rank16"][:, sl_])))(sl),
          ["rkF"], ["scrF"])
        for tl in range(TPQ):
            ti = h4 * TPQ + tl
            V((lambda tl_, ti_: (lambda eng: eng.tensor_scalar_add(
                S["scrF"][:, tl_ * 128:(tl_ + 1) * 128],
                S["scrF"][:, tl_ * 128:(tl_ + 1) * 128],
                S["totoff"][:, ti_:ti_ + 1])))(tl, ti),
              ["scrF", f"totoff{ti}"], ["scrF"])
        V(lambda eng: eng.tensor_copy(out=S["scrI"][:, :], in_=S["scrF"][:, :]),
          ["scrF"], ["scrI"])
        V(lambda eng: eng.tensor_scalar(out=S["maskh"][:, :], in0=S["scrF"][:, :],
                                        scalar1=float(c.CAP), scalar2=None,
                                        op0=OP.is_lt),
          ["scrF"], ["maskH"])
        V((lambda sl_: (lambda eng: eng.tensor_mul(
            S["maskh"][:, :], S["maskh"][:, :], S["MT"][:, sl_])))(sl),
          ["maskH"] + [f"MTc{h4 * TPQ + tl}" for tl in range(TPQ)], ["maskH"])
        V(lambda eng: eng.tensor_scalar_add(S["scrF"][:, :], S["scrF"][:, :], 1.0),
          ["scrF"], ["scrF"])
        V(lambda eng: eng.tensor_mul(S["scrF"][:, :], S["scrF"][:, :], S["maskh"][:, :]),
          ["scrF", "maskH"], ["scrF"])
        V((lambda sl_: (lambda eng: eng.tensor_scalar_add(
            S["idx16"][:, sl_], S["scrF"][:, :], -1.0)))(sl),
          ["scrF"], ["idx16F"])
        V(lambda eng: eng.tensor_copy(out=S["scrF"][:, :], in_=S["scrI"][:, :]),
          ["scrI"], ["scrF"])
        V(lambda eng: eng.tensor_scalar(out=S["scrI"][:, :], in0=S["scrI"][:, :],
                                        scalar1=15, scalar2=None, op0=OP.bitwise_and),
          ["scrI"], ["scrI"])
        V(lambda eng: eng.tensor_copy(out=S["maskh"][:, :], in_=S["scrI"][:, :]),
          ["scrI"], ["maskH"])
        V(lambda eng: eng.tensor_sub(S["scrF"][:, :], S["scrF"][:, :], S["maskh"][:, :]),
          ["scrF", "maskH"], ["scrF"])
        V(lambda eng: eng.tensor_scalar_mul(S["scrF"][:, :], S["scrF"][:, :], 0.0625),
          ["scrF"], ["scrF"])
        V(lambda eng: eng.scalar_tensor_tensor(
            out=S["scrF"][:, :], in0=S["maskh"][:, :], scalar=32.0,
            in1=S["scrF"][:, :], op0=OP.mult, op1=OP.add),
          ["scrF", "maskH"], ["scrF"])
        V((lambda sl_: (lambda eng: eng.tensor_copy(
            out=S["maskh"][:, :], in_=S["idx16"][:, sl_])))(sl),
          ["idx16F"], ["maskH"])
        V(lambda eng: eng.tensor_scalar(
            out=S["maskh"][:, :], in0=S["maskh"][:, :],
            scalar1=0.0, scalar2=None, op0=OP.is_ge),
          ["maskH"], ["maskH"])
        V(lambda eng: eng.tensor_scalar_add(S["scrF"][:, :], S["scrF"][:, :], 1.0),
          ["scrF"], ["scrF"])
        V(lambda eng: eng.tensor_mul(S["scrF"][:, :], S["scrF"][:, :], S["maskh"][:, :]),
          ["scrF", "maskH"], ["scrF"])
        V((lambda sl_: (lambda eng: eng.tensor_scalar_add(
            S["idx16b"][:, sl_], S["scrF"][:, :], -1.0)))(sl),
          ["scrF"], ["idx16bF"])
    ix_all = ["idx16F"]
    sct_all = [f"scT{ti}" for ti in range(c.tiles)]

    b.do("gpsimd",
         lambda eng: eng.local_scatter(S["idsG"][:, :], S["ids16"][:, :], S["idx16b"][:, :],
                                       c.E, c.CAP, c.Tl),
         reads=["ids16", "idx16bF"], writes=["idsG"])
    b.do("gpsimd",
         lambda eng: eng.local_scatter(S["wG"][:, :], S["scoresT"][:, :], S["idx16"][:, :],
                                       c.E, c.CAP, c.Tl),
         reads=ix_all + sct_all, writes=["wG"])
    b.do("vector", lambda eng: eng.tensor_copy(out=S["tabf"][:, 0:CST], in_=S["idsG"][:, :]),
         reads=["idsG"], writes=["tabf_a"])
    b.do("vector", lambda eng: eng.tensor_copy(out=S["tabf"][:, CST:2 * CST], in_=S["wG"][:, :]),
         reads=["wG"], writes=["tabf_b"])
    b.do("sync", lambda eng: eng.dma_start(out=arin_tab[:, :], in_=S["tabf"][:, :]),
         reads=["tabf_a", "tabf_b"], writes=["arin_tab"], dma=ld)
    b.do("gpsimd",
         lambda eng: eng.collective_compute(
             "AllReduce", OP.add, replica_groups=RG,
             ins=[arin_tab[:, :]], outs=[tab_full[:, :]]),
         reads=["arin_tab"], writes=["tab_full"], coll=cc3)

    for ls in range(zsplit, c.segs):
        z_seg(ls)
    out_all = [f"out{ls}_{tt}" for ls in range(c.segs) for tt in range(4)]

    b.fence(exclude=["ln2_full", "tab_full", "cnt_full"])
    free_scope("Z")
    free_scope("R")

    if _cut("C"):
        b.fence()
        b.do("gpsimd", lambda eng: None, wait=[b.gwait])
        b.do("sync", lambda eng: None, wait=[b.gwait])
        b.emit(stack)
        nc.compile()
        return nc
    b.do("sync", lambda eng: eng.dma_start(out=S["tabf"][:, :], in_=tab_full[:, :]),
         reads=["tab_full"], writes=["tabf_a", "tabf_b"], dma=ld)
    b.do("vector",
         lambda eng: eng.tensor_scalar(out=S["minef"][:, :], in0=S["tabf"][:, 0:CST],
                                       scalar1=S["rowb"][0:c.E, 0:1], scalar2=None,
                                       op0=OP.is_ge),
         reads=["tabf_a", "rowb"], writes=["minef"])
    b.do("vector",
         lambda eng: eng.tensor_scalar(out=S["mtmp"][:, :], in0=S["tabf"][:, 0:CST],
                                       scalar1=S["rowb"][0:c.E, 1:2], scalar2=None,
                                       op0=OP.is_lt),
         reads=["tabf_a", "rowb"], writes=["mtmp"])
    b.do("vector",
         lambda eng: eng.tensor_mul(S["minef"][:, :], S["minef"][:, :], S["mtmp"][:, :]),
         reads=["mtmp", "minef"], writes=["minef"])
    b.do("vector",
         lambda eng: eng.tensor_scalar(out=S["mtmp"][:, :], in0=S["tabf"][:, CST:2 * CST],
                                       scalar1=0.0, scalar2=None, op0=OP.is_gt),
         reads=["tabf_b"], writes=["mtmp"])
    b.do("vector",
         lambda eng: eng.tensor_mul(S["minef"][:, :], S["minef"][:, :], S["mtmp"][:, :]),
         reads=["mtmp", "minef"], writes=["minef"])
    b.do("vector",
         lambda eng: eng.tensor_scalar(out=S["mtmp"][:, :], in0=S["tabf"][:, 0:CST],
                                       scalar1=S["rowb"][0:c.E, 0:1], scalar2=None,
                                       op0=OP.subtract),
         reads=["tabf_a", "rowb"], writes=["mtmp"])
    b.do("vector",
         lambda eng: eng.tensor_scalar(out=S["mtmp"][:, :], in0=S["mtmp"][:, :],
                                       scalar1=-1.0, scalar2=-1.0, op0=OP.mult,
                                       op1=OP.add),
         reads=["mtmp"], writes=["mtmp"])
    b.do("vector", lambda eng: eng.tensor_copy(out=S["ltok"][:, :], in_=S["mtmp"][:, :]),
         reads=["mtmp"], writes=["ltok"])
    b.do("vector", lambda eng: eng.memset(S["onescol"][:, :], 1.0), writes=["onescol"])
    b.do("vector",
         lambda eng: eng.tensor_tensor_scan(
             out=S["mtmp"][:, :], data0=S["onescol"][:, :].to_broadcast([c.E, c.CAP]),
             data1=S["minef"][:, :], initial=0.0, op0=OP.mult, op1=OP.add),
         reads=["onescol", "minef"], writes=["mtmp"])
    b.do("vector",
         lambda eng: eng.tensor_copy(out=S["cntrow"][:, :], in_=S["mtmp"][:, CST - 1:CST]),
         reads=["mtmp"], writes=["cntrow"])
    b.do("vector",
         lambda eng: eng.tensor_sub(S["mtmp"][:, :], S["mtmp"][:, :], S["minef"][:, :]),
         reads=["mtmp", "minef"], writes=["mtmp"])
    b.do("vector",
         lambda eng: eng.tensor_copy(out=S["mtmpI"][:, :], in_=S["mtmp"][:, :]),
         reads=["mtmp"], writes=["mtmpI"])
    b.do("vector",
         lambda eng: eng.tensor_scalar(out=S["mtmpI"][:, :], in0=S["mtmpI"][:, :],
                                       scalar1=15, scalar2=None, op0=OP.bitwise_and),
         reads=["mtmpI"], writes=["mtmpI"])
    b.do("vector",
         lambda eng: eng.tensor_copy(out=S["mtmp3"][:, :], in_=S["mtmpI"][:, :]),
         reads=["mtmpI"], writes=["mtmp3"])
    b.do("vector",
         lambda eng: eng.tensor_sub(S["mtmp2"][:, :], S["mtmp"][:, :], S["mtmp3"][:, :]),
         reads=["mtmp", "mtmp3"], writes=["mtmp2"])
    b.do("vector",
         lambda eng: eng.tensor_scalar_mul(S["mtmp2"][:, :], S["mtmp2"][:, :], 0.0625),
         reads=["mtmp2"], writes=["mtmp2"])
    b.do("vector",
         lambda eng: eng.scalar_tensor_tensor(out=S["mtmp2"][:, :], in0=S["mtmp3"][:, :],
                                              scalar=32.0, in1=S["mtmp2"][:, :],
                                              op0=OP.mult, op1=OP.add),
         reads=["mtmp3", "mtmp2"], writes=["mtmp2"])
    b.do("vector",
         lambda eng: eng.scalar_tensor_tensor(out=S["mtmp2"][:, :], in0=S["mtmp2"][:, :],
                                              scalar=1.0, in1=S["minef"][:, :],
                                              op0=OP.add, op1=OP.mult),
         reads=["mtmp2", "minef"], writes=["mtmp2"])
    b.do("vector",
         lambda eng: eng.tensor_scalar_add(S["mtmp2"][:, :], S["mtmp2"][:, :], -1.0),
         reads=["mtmp2"], writes=["mtmp2"])
    b.do("vector", lambda eng: eng.tensor_copy(out=S["rankD"][:, :], in_=S["mtmp2"][:, :]),
         reads=["mtmp2"], writes=["rankD"])
    b.do("gpsimd",
         lambda eng: eng.local_scatter(S["listSS"][:, :], S["sloti"][:, :],
                                       S["rankD"][:, :], c.E, c.CAP, c.CAP),
         reads=["sloti", "rankD"], writes=["listSS"])
    b.do("gpsimd",
         lambda eng: eng.local_scatter(S["listTT"][:, :], S["ltok"][:, :],
                                       S["rankD"][:, :], c.E, c.CAP, c.CAP),
         reads=["ltok", "rankD"], writes=["listTT"])
    b.do("vector",
         lambda eng: eng.tensor_scalar(out=S["listSS"][:, :], in0=S["listSS"][:, :],
                                       scalar1=-1.0, scalar2=-1.0, op0=OP.mult, op1=OP.add),
         reads=["listSS"], writes=["listSS"])
    b.do("vector",
         lambda eng: eng.tensor_scalar(out=S["listTT"][:, :], in0=S["listTT"][:, :],
                                       scalar1=-1.0, scalar2=-1.0, op0=OP.mult, op1=OP.add),
         reads=["listTT"], writes=["listTT"])
    b.do("vector",
         lambda eng: eng.tensor_scalar_max(S["listSS"][:, 0:1], S["listSS"][:, 0:1], 0.0),
         reads=["listSS"], writes=["listSS"])
    b.do("vector",
         lambda eng: eng.tensor_copy(out=S["t128"][:, 0:1], in_=S["listTT"][:, 0:1]),
         reads=["listTT"], writes=["t128d0"])
    b.do("vector",
         lambda eng: eng.tensor_scalar(out=S["t128"][:, 1:2], in0=S["t128"][:, 0:1],
                                       scalar1=0.0, scalar2=None, op0=OP.is_lt),
         reads=["t128d0"], writes=["t128d1"])
    b.do("vector",
         lambda eng: eng.scalar_tensor_tensor(out=S["t128"][:, 0:1], in0=S["t128"][:, 1:2],
                                              scalar=float(c.Tl + 1), in1=S["t128"][:, 0:1],
                                              op0=OP.mult, op1=OP.add),
         reads=["t128d1", "t128d0"], writes=["t128d0"])
    b.do("vector",
         lambda eng: eng.tensor_copy(out=S["listTT"][:, 0:1], in_=S["t128"][:, 0:1]),
         reads=["t128d0"], writes=["listTT"])
    b.do("sync", lambda eng: eng.dma_start(out=cnts_d[0:1, :].rearrange("a e -> e a"),
                                           in_=S["cntrow"][:, :]),
         reads=["cntrow"], writes=["cnts_d"], dma=ld)
    b.do("sync", lambda eng: eng.dma_start(out=S["nfsf"][:, :], in_=cnts_d[:, :]),
         reads=["cnts_d"], writes=["nfsf"], dma=ld)
    b.do("vector", lambda eng: eng.tensor_copy(out=S["nfs"][:, :], in_=S["nfsf"][:, :]),
         reads=["nfsf"], writes=["nfs"])
    b.do("sync", lambda eng: eng.dma_start(out=lists_d[:, 0:c.CAP], in_=S["listSS"][:, :]),
         reads=["listSS"], writes=["lists_d1"], dma=ld)
    b.do("sync", lambda eng: eng.dma_start(out=lists_d[:, c.CAP:], in_=S["listTT"][:, :]),
         reads=["listTT"], writes=["lists_d2"], dma=ld)
    b.do("vector", lambda eng: eng.memset(S["dlist"][:, :], 0), writes=["dlist_z"])
    for half, res in ((0, "lists_d1"), (1, "lists_d2")):
        for pp in (0, 16):
            b.do("sync",
                 (lambda half_, pp_: (lambda eng: eng.dma_start(
                     out=S["dlist"][pp_:pp_ + 16,
                                    half_ * c.E * 32:(half_ + 1) * c.E * 32].rearrange(
                         "p (e s) -> p e s", e=c.E),
                     in_=lists_d[:, half_ * c.CAP:(half_ + 1) * c.CAP].rearrange(
                         "e (p s) -> p e s", p=16))))(half, pp),
                 reads=[res, "dlist_z"], writes=[f"dlist{half}_{pp}"], dma=ld)
    dlist_all = [f"dlist{h}_{p}" for h in (0, 1) for p in (0, 16)]

    EWP = 2 * c.DC * c.I + c.IC * c.D
    sbuf("expW", [128, 2 * EWP], BF16, scope="C")
    sbuf("xbfm", [128, 2 * c.DC * c.CAP], BF16, scope="C")
    sbuf("hh", [128, c.IC * c.CAP], BF16, scope="C")
    sbuf("ybt", [128, 2 * c.D], BF16, scope="C")
    sbuf("siltC", [128, c.CAP], F32, scope="C")
    sbuf("myidw", [c.EPC, 2 * c.CAP], F32, scope="C")
    sbuf("myids16", [c.EPC, c.CAP], I16, scope="C")
    sbuf("wTslot", [128, 4 * c.EPC], F32, scope="C")
    sbuf("eidx", [128, c.EPC * (c.CAP // 16)], I16, scope="C")

    b.do("gpsimd",
         lambda eng: eng.indirect_dma_start(
             S["myidw"][:, :], None, tab_full[:, :],
             IndirectOffsetOnAxis(S["myexp"][:, :], 0)),
         reads=["tab_full", "myexp"], writes=["myidw"], dma=pda)
    for sb4 in range(4):
        b.do("tensor",
             (lambda s4: (lambda eng: eng.transpose(
                 P["tsm"][0:128, 0:c.EPC],
                 S["myidw"][:, CST + s4 * 128: CST + (s4 + 1) * 128],
                 S["idf32"][0:c.EPC, 0:c.EPC])))(sb4),
             reads=["myidw", "idf32"], writes=["tsm"])
        b.do("vector",
             (lambda s4: (lambda eng: eng.tensor_copy(
                 out=S["wTslot"][:, s4 * c.EPC:(s4 + 1) * c.EPC],
                 in_=P["tsm"][0:128, 0:c.EPC])))(sb4),
             reads=["tsm"], writes=[f"wTslot{sb4}"])
    b.do("vector",
         lambda eng: eng.tensor_copy(out=S["myids16"][:, :], in_=S["myidw"][:, 0:CST]),
         reads=["myidw"], writes=["myids16"])
    b.do("scalar",
         lambda eng: eng.dma_start(out=eidx_d[:, :], in_=S["myids16"][:, :]),
         reads=["myids16"], writes=["eidx_d"], dma=stw)
    b.do("vector", lambda eng: eng.memset(S["eidx"][:, :], 0), writes=["eidx_z"])
    b.do("sync",
         lambda eng: eng.dma_start(
             out=S["eidx"][0:16, :].rearrange("p (e s) -> p e s", e=c.EPC),
             in_=eidx_d.rearrange("e (p s) -> p e s", p=16)),
         reads=["eidx_d", "eidx_z"], writes=["eidx_a"], dma=ld)
    b.do("sync",
         lambda eng: eng.dma_start(
             out=S["eidx"][16:32, :].rearrange("p (e s) -> p e s", e=c.EPC),
             in_=eidx_d.rearrange("e (p s) -> p e s", p=16)),
         reads=["eidx_d", "eidx_z"], writes=["eidx_b"], dma=ld)

    for j in range(c.EPC):
        par = j % 2
        eb = par * EWP
        b.do("gpsimd",
             (lambda j_, par_: (lambda eng: eng.dma_gather(
                 S["xbfm"][:, par_ * c.DC * c.CAP:(par_ + 1) * c.DC * c.CAP].rearrange(
                     "p (k t) -> p k t", k=c.DC),
                 ln2_full[:, :],
                 S["eidx"][:, j_ * (c.CAP // 16):(j_ + 1) * (c.CAP // 16)],
                 c.CAP, c.CAP, c.D, transpose=True)))(j, par),
             reads=["eidx_a", "eidx_b", "ln2_full"], writes=[f"xbfm{par}"], dma=pdb)
        b.do("gpsimd",
             (lambda j_, eb_: (lambda eng: eng.dma_start(
                 out=S["expW"][:, eb_: eb_ + c.DC * c.I].rearrange(
                     "p (k i) -> p k i", k=c.DC),
                 in_=w1T_in[j_].rearrange("(k p) i -> p k i", p=128))))(j, eb),
             writes=[f"expWa{par}"], dma=pdb)
        b.do("gpsimd",
             (lambda j_, eb_: (lambda eng: eng.dma_start(
                 out=S["expW"][:, eb_ + c.DC * c.I: eb_ + 2 * c.DC * c.I].rearrange(
                     "p (k i) -> p k i", k=c.DC),
                 in_=w3T_in[j_].rearrange("(k p) i -> p k i", p=128))))(j, eb),
             writes=[f"expWb{par}"], dma=pdb)
        nfull = c.I // 128
        b.do("gpsimd",
             (lambda j_, eb_, nf: (lambda eng: eng.dma_start(
                 out=S["expW"][:, eb_ + 2 * c.DC * c.I:
                               eb_ + 2 * c.DC * c.I + nf * c.D].rearrange(
                     "p (k i) -> p k i", k=nf),
                 in_=w2T_in[j_][0:nf * 128, :].rearrange("(k p) i -> p k i", p=128))))(j, eb, nfull),
             writes=[f"expWc{par}"], dma=pdb)
        b.do("gpsimd",
             (lambda j_, eb_, nf: (lambda eng: eng.dma_start(
                 out=S["expW"][0:c.I - nf * 128,
                               eb_ + 2 * c.DC * c.I + nf * c.D:
                               eb_ + 2 * c.DC * c.I + (nf + 1) * c.D],
                 in_=w2T_in[j_][nf * 128:, :])))(j, eb, nfull),
             writes=[f"expWc2{par}"], dma=pdb)

        for ic in range(c.IC):
            icw = 128 if ic < c.IC - 1 else c.I - 128 * (c.IC - 1)
            for zi in range(2):
                bank = f"g{zi}"

                def h_group(eng, ic_=ic, zi_=zi, bank_=bank, par_=par, icw_=icw, eb_=eb):
                    ii = None
                    for kc in range(c.DC):
                        ii = eng.matmul(
                            P[bank_][0:icw_, :],
                            S["expW"][:, eb_ + zi_ * c.DC * c.I + kc * c.I + ic_ * 128:
                                      eb_ + zi_ * c.DC * c.I + kc * c.I + ic_ * 128 + icw_],
                            S["xbfm"][:, (par_ * c.DC + kc) * c.CAP:
                                      (par_ * c.DC + kc + 1) * c.CAP],
                            start=(kc == 0), stop=(kc == c.DC - 1))
                    return ii

                b.do("tensor", h_group,
                     reads=[f"expWa{par}", f"expWb{par}", f"xbfm{par}"],
                     writes=[bank])
            b.do("scalar",
                 (lambda icw_: (lambda eng: eng.activation(
                     out=S["siltC"][0:icw_, :], in_=P["g0"][0:icw_, :], func=AF.Silu)))(icw),
                 reads=["g0"], writes=["siltC0"])
            b.do("vector",
                 (lambda ic_, icw_: (lambda eng: eng.tensor_mul(
                     S["hh"][0:icw_, ic_ * c.CAP:(ic_ + 1) * c.CAP],
                     S["siltC"][0:icw_, :], P["g1"][0:icw_, :])))(ic, icw),
                 reads=["siltC0", "g1"], writes=[f"hhc{ic}"])

        for sb4 in range(4):
            yh = sb4 % 2
            for nh in range(2):
                bank = f"g{2 + nh}"

                def y_group(eng, sb4_=sb4, nh_=nh, bank_=bank, eb_=eb):
                    ii = None
                    for ic in range(c.IC):
                        icw = 128 if ic < c.IC - 1 else c.I - 128 * (c.IC - 1)
                        ii = eng.matmul(
                            P[bank_][:, :],
                            S["hh"][0:icw, ic * c.CAP + sb4_ * 128: ic * c.CAP + (sb4_ + 1) * 128],
                            S["expW"][0:icw, eb_ + 2 * c.DC * c.I + ic * c.D + nh_ * 512:
                                      eb_ + 2 * c.DC * c.I + ic * c.D + (nh_ + 1) * 512],
                            start=(ic == 0), stop=(ic == c.IC - 1))
                    return ii

                b.do("tensor", y_group,
                     reads=[f"hhc{ic}" for ic in range(c.IC)] +
                           [f"expWc{par}", f"expWc2{par}"],
                     writes=[bank])
                b.do("vector",
                     (lambda bank_, sb4_, nh_, j_, yh_: (lambda eng: eng.tensor_scalar_mul(
                         S["ybt"][:, yh_ * c.D + nh_ * 512: yh_ * c.D + (nh_ + 1) * 512],
                         P[bank_][:, :],
                         S["wTslot"][:, sb4_ * c.EPC + j_: sb4_ * c.EPC + j_ + 1])))(bank, sb4, nh, j, yh),
                     reads=[bank, f"wTslot{sb4}"], writes=[f"ybt{yh}_{nh}"])
            b.do("scalar",
                 (lambda j_, sb4_, yh_: (lambda eng: eng.dma_start(
                     out=agin_yb[j_ * c.CAP + sb4_ * 128: j_ * c.CAP + (sb4_ + 1) * 128, :],
                     in_=S["ybt"][:, yh_ * c.D:(yh_ + 1) * c.D])))(j, sb4, yh),
                 reads=[f"ybt{yh}_0", f"ybt{yh}_1"], writes=[f"agyb{j}_{sb4}"], dma=stw)

        if (j + 1) % c.EPCH == 0:
            k4 = j // c.EPCH
            b.do("gpsimd",
                 (lambda k_: (lambda eng: eng.collective_compute(
                     "AllGather", OP.bypass, replica_groups=RG,
                     ins=[agin_yb[k_ * c.EPCH * c.CAP:(k_ + 1) * c.EPCH * c.CAP, :]],
                     outs=[ybc[k_][:, :]])))(k4),
                 reads=[f"agyb{jj}_{s4}"
                        for jj in range(k4 * c.EPCH, (k4 + 1) * c.EPCH)
                        for s4 in range(4)],
                 writes=[f"ybc{k4}"], coll=cc4s[k4])

    b.fence(exclude=["ln2_full", "tab_full", "cnt_full"] +
            [f"ybc{k}" for k in range(c.NCH)])
    free_scope("C")

    if _cut("D"):
        b.fence()
        b.do("gpsimd", lambda eng: None, wait=[b.gwait])
        b.do("sync", lambda eng: None, wait=[b.gwait])
        b.emit(stack)
        nc.compile()
        return nc
    sbuf("gvb", [128, 4 * 4 * c.D], BF16, scope="D")
    sbuf("gvf", [128, 4 * 4 * c.D], F32, scope="D")
    sbuf("zrow", [128, c.D], F32, scope="D")

    b.do("vector", lambda eng: eng.memset(S["gvb"][:, :], 0),
         writes=[f"gvb{p}" for p in range(4)])
    b.do("vector", lambda eng: eng.memset(S["zrow"][:, :], 0.0), writes=["zrow"])
    b.do("sync",
         lambda eng: eng.dma_start(out=out[c.Tl:c.Tl + 128, :], in_=S["zrow"][:, :]),
         reads=["zrow"], writes=["outPad"], dma=ld)

    def get_nkreg(eng):
        r = getattr(eng, "_nkreg", None)
        if r is None:
            r = eng.alloc_register("nkreg")
            eng._nkreg = r
        return r

    ei = 0
    for k4 in range(c.NCH):
        chunk = [cc_ * c.EPC + k4 * c.EPCH + jj
                 for cc_ in range(c.n_cores) for jj in range(c.EPCH)]
        for e in chunk:
            par = ei % 4
            ei += 1
            gvb = S["gvb"][:, par * 4 * c.D:(par + 1) * 4 * c.D].rearrange(
                "p (a d) -> p a d", d=c.D)
            gvf = S["gvf"][:, par * 4 * c.D:(par + 1) * 4 * c.D].rearrange(
                "p (a d) -> p a d", d=c.D)

            def d_gather(eng, e_=e, gvb_=gvb, k_=k4):
                r = get_nkreg(eng)
                eng.reg_load(r, S["nfs"][0:1, e_:e_ + 1])
                eng.reg_alu(r, r, 1, OP.max)
                return eng.dma_gather(
                    gvb_, ybc[k_][:, :],
                    S["dlist"][:, e_ * 32:(e_ + 1) * 32],
                    c.CAP, r, c.D)

            b.do("gpsimd", d_gather,
                 reads=[f"ybc{k4}", "nfs"] + dlist_all, writes=[f"gvb{par}"], dma=pdc)
            b.do("vector",
                 (lambda par_: (lambda eng: eng.tensor_copy(
                     out=S["gvf"][:, par_ * 4 * c.D:(par_ + 1) * 4 * c.D],
                     in_=S["gvb"][:, par_ * 4 * c.D:(par_ + 1) * 4 * c.D])))(par),
                 reads=[f"gvb{par}"], writes=[f"gvf{par}"])

            def d_scatter(eng, e_=e, gvf_=gvf):
                r = get_nkreg(eng)
                eng.reg_load(r, S["nfs"][0:1, e_:e_ + 1])
                eng.reg_alu(r, r, 1, OP.max)
                return eng.dma_scatter_add(
                    out[:, :], gvf_,
                    S["dlist"][:, (c.E + e_) * 32:(c.E + e_ + 1) * 32],
                    c.CAP, r, c.D)

            b.do("gpsimd", d_scatter,
                 reads=["nfs", f"gvf{par}", "outPad"] + dlist_all + out_all,
                 writes=["outRMW"], dma=pdd)

    if debug_dump:
        dbg_specs = [("dbg_tab", tab_full, [c.E, 2 * c.CAP], F32, ["tab_full"]),
                     ("dbg_ln2", agin_ln2, [c.Tl, c.D], BF16, all_agin),
                     ("dbg_x1", x1_d, [c.Tl, c.D], F32,
                      [f"x1_d{ls}_{tt}" for ls in range(c.segs) for tt in range(4)]),
                     ("dbg_lists", lists_d, [c.E, 2 * c.CAP], I16,
                      ["lists_d1", "lists_d2"]),
                     ("dbg_cnt", cnts_d, [1, c.E], F32, ["cnts_d"])]
        for nm, srcap, shp, dt, rds in dbg_specs:
            od = nc.dram_tensor(nm, shp, dt, kind="ExternalOutput").ap()
            b.do("sync",
                 (lambda od_, s_: (lambda eng: eng.dma_start(out=od_[:, :], in_=s_[:, :])))(od, srcap),
                 reads=rds + ["outRMW"], writes=[nm], dma=ld)
        b.do("sync", lambda eng: None,
             reads=["outRMW"] + out_all + [d[0] for d in dbg_specs])
    b.do("gpsimd", lambda eng: None, reads=["outRMW", "outPad"] + out_all)
    b.do("sync", lambda eng: None, reads=["outRMW", "outPad"] + out_all)
    b.emit(stack)
    nc.compile()
    return nc


def _host_constants(c: Cfg, core):
    o = {}
    Lup = np.zeros((128, 132), np.float32)
    for t in range(128):
        Lup[:t, t] = 1.0
    Lup[:, 128] = 1.0
    o["Lup"] = Lup.astype(BF)
    o["idf32"] = np.eye(128, dtype=np.float32)
    o["idbf"] = np.eye(128, dtype=np.float32).astype(BF)
    ss = np.zeros((128, c.DC * 128), np.float32)
    sb = np.zeros((128, c.DC * 128), np.float32)
    for kc in range(c.DC):
        for hloc in range(2):
            hglob = 2 * kc + hloc
            rc = (hglob % 4) * 32 + hglob // 4
            ss[hloc * 64:(hloc + 1) * 64, kc * 128 + rc] = 0.125
            sb[rc, kc * 128 + hloc * 64: kc * 128 + (hloc + 1) * 64] = 1.0
    o["Ssum"] = ss.astype(BF)
    o["Sb"] = sb.astype(BF)
    o["ones1"] = np.ones((1, 128), np.float32)
    segsel = np.zeros((128, c.segs * c.segs), np.float32)
    for ls in range(c.segs):
        segsel[:, ls * c.segs + ls] = 1.0
    o["segsel"] = segsel.astype(BF)
    ids = core * c.Tl + np.arange(c.Tl, dtype=np.int64)
    o["ids16"] = np.broadcast_to(ids[None, :], (c.E, c.Tl)).astype(np.int16).copy()
    col = np.arange(c.CAP, dtype=np.int64)
    siginv = (col % 32) * 16 + col // 32
    ee = np.arange(c.E, dtype=np.int64)
    ebase = (ee // c.EPC) * (c.EPCH * c.CAP) + ((ee % c.EPC) % c.EPCH) * c.CAP
    sloti = ebase[:, None] + siginv[None, :]
    o["sloti"] = (-(sloti + 1)).astype(np.int16)
    mm = np.zeros((c.nseg_tot, c.segs), np.float32)
    for r in range(c.nseg_tot):
        cc, ls2 = divmod(r, c.segs)
        nglob = cc + c.n_cores * ls2
        for ls in range(c.segs):
            if nglob < core + c.n_cores * ls:
                mm[r, ls] = 1.0
    o["maskmat"] = mm
    rb = np.zeros((128, 2), np.float32)
    rb[:, 0] = core * c.Tl
    rb[:, 1] = (core + 1) * c.Tl
    o["rowbase"] = rb
    o["myexp"] = (core * c.EPC + np.arange(c.EPC, dtype=np.int32)).reshape(c.EPC, 1)
    return o


def host_inputs(c: Cfg, inp):
    x = np.ascontiguousarray(np.asarray(inp["L0_attn_x"], np.float32).reshape(c.T, c.D))
    f32 = lambda a: np.ascontiguousarray(np.asarray(a, np.float32))
    Wq, Wk = f32(inp["L0_attn_Wq"]), f32(inp["L0_attn_Wk"])
    Wv, Wo = f32(inp["L0_attn_Wv"]), f32(inp["L0_attn_Wo"])
    bqkv = np.concatenate(
        [np.asarray(inp[k], np.float32).reshape(c.DC, 128).T
         for k in ("L0_attn_bq", "L0_attn_bk", "L0_attn_bv")], axis=1)
    bqkv = np.ascontiguousarray(bqkv)
    bo_row = np.ascontiguousarray(np.asarray(inp["L0_attn_bo"], np.float32).reshape(1, c.D))
    ws1T = np.ascontiguousarray(np.asarray(inp["L0_ffn_ws1"], np.float32).T)
    ws3T = np.ascontiguousarray(np.asarray(inp["L0_ffn_ws3"], np.float32).T)
    ws2T = np.ascontiguousarray(np.asarray(inp["L0_ffn_ws2"], np.float32).T)
    gwT = np.ascontiguousarray(np.asarray(inp["L0_ffn_gate_w"], np.float32).T)
    gb_b = np.ascontiguousarray(np.broadcast_to(
        np.asarray(inp["L0_ffn_gate_b"], np.float32)[None, :], (128, c.E)))
    w1T = np.ascontiguousarray(np.asarray(inp["L0_ffn_w1"], np.float32).transpose(0, 2, 1))
    w3T = np.ascontiguousarray(np.asarray(inp["L0_ffn_w3"], np.float32).transpose(0, 2, 1))
    w2T = np.ascontiguousarray(np.asarray(inp["L0_ffn_w2"], np.float32).transpose(0, 2, 1))

    in_maps = []
    for core in range(c.n_cores):
        segs = [core + c.n_cores * ls for ls in range(c.segs)]
        xs = np.ascontiguousarray(
            np.concatenate([x[n * c.SEG:(n + 1) * c.SEG] for n in segs], axis=0))
        m = {"x": xs, "Wq": Wq, "Wk": Wk, "Wv": Wv, "Wo": Wo,
             "bqkv": bqkv, "bo_row": bo_row,
             "ws1T": ws1T, "ws3T": ws3T, "ws2T": ws2T,
             "gwT": gwT, "gb_bcast": gb_b,
             "w1T": w1T[core * c.EPC:(core + 1) * c.EPC],
             "w3T": w3T[core * c.EPC:(core + 1) * c.EPC],
             "w2T": w2T[core * c.EPC:(core + 1) * c.EPC]}
        m.update(_host_constants(c, core))
        in_maps.append(m)
    return in_maps


def assemble_out(c: Cfg, results, shape):
    outf = np.empty((c.T, c.D), np.float32)
    for core in range(c.n_cores):
        oc = results[core]["out"]
        for ls in range(c.segs):
            n = core + c.n_cores * ls
            outf[n * c.SEG:(n + 1) * c.SEG] = oc[ls * c.SEG:(ls + 1) * c.SEG]
    return outf.reshape(shape)


_CACHED = {}


def kernel(L0_attn_x, L0_attn_Wq, L0_attn_bq, L0_attn_Wk, L0_attn_bk,
           L0_attn_Wv, L0_attn_bv, L0_attn_Wo, L0_attn_bo,
           L0_ffn_gate_w, L0_ffn_gate_b, L0_ffn_w1, L0_ffn_w2, L0_ffn_w3,
           L0_ffn_ws1, L0_ffn_ws2, L0_ffn_ws3):
    c = Cfg(n_cores=8, segs=8)
    inp = dict(L0_attn_x=L0_attn_x, L0_attn_Wq=L0_attn_Wq, L0_attn_bq=L0_attn_bq,
               L0_attn_Wk=L0_attn_Wk, L0_attn_bk=L0_attn_bk, L0_attn_Wv=L0_attn_Wv,
               L0_attn_bv=L0_attn_bv, L0_attn_Wo=L0_attn_Wo, L0_attn_bo=L0_attn_bo,
               L0_ffn_gate_w=L0_ffn_gate_w, L0_ffn_gate_b=L0_ffn_gate_b,
               L0_ffn_w1=L0_ffn_w1, L0_ffn_w2=L0_ffn_w2, L0_ffn_w3=L0_ffn_w3,
               L0_ffn_ws1=L0_ffn_ws1, L0_ffn_ws2=L0_ffn_ws2, L0_ffn_ws3=L0_ffn_ws3)
    key = (c.n_cores, c.segs)
    if key not in _CACHED:
        stack = ExitStack()
        nc = build_program(c, stack)
        _CACHED[key] = (nc, stack)
    nc = _CACHED[key][0]
    in_maps = host_inputs(c, inp)
    res = run_bass_kernel_spmd(nc, in_maps, core_ids=list(range(c.n_cores)))
    return assemble_out(c, res.results, np.asarray(L0_attn_x).shape)


# version 18
# speedup vs baseline: 1.7244x; 1.0217x over previous
import numpy as np
import ml_dtypes
from contextlib import ExitStack

import concourse.bass as bass
import concourse.bacc as bacc
import concourse.mybir as mybir
from concourse.bass import IndirectOffsetOnAxis
from concourse.bass_utils import run_bass_kernel_spmd
from concourse import library_config

F32 = mybir.dt.float32
BF16 = mybir.dt.bfloat16
I16 = mybir.dt.int16
I32 = mybir.dt.int32
U32 = mybir.dt.uint32
AF = mybir.ActivationFunctionType
OP = mybir.AluOpType
AX = mybir.AxisListType
BF = ml_dtypes.bfloat16


class Cfg:
    def __init__(self, n_cores=8, segs=8):
        self.n_cores = n_cores
        self.segs = segs
        self.SEG = 512
        self.D = 1024
        self.DC = 8
        self.H = 16
        self.E = 64
        self.CAP = 512
        self.I = 704
        self.IC = 6
        self.SI = 1408
        self.SIC = 11
        self.Tl = segs * self.SEG
        self.tiles = self.Tl // 128
        self.T = self.Tl * n_cores
        self.EPC = self.E // n_cores
        self.nseg_tot = segs * n_cores
        self.NCH = 4
        self.EPCH = self.EPC // self.NCH


class Sem:
    def __init__(self, name, producer=None):
        self.name, self.h, self.n, self.producer = name, None, 0, producer


class Stream:
    def __init__(self, b, name, nsem=4):
        self.sems = [b.sem(f"{name}{i}") for i in range(nsem)]
        self.i = 0

    def next(self):
        s = self.sems[self.i % len(self.sems)]
        self.i += 1
        return s


class Builder:
    ENGINES = ("sync", "scalar", "vector", "tensor", "gpsimd")

    def __init__(self, nc):
        self.nc = nc
        self.steps = []
        self.sems = []
        self.res = {}
        self.gwait = None
        self.eng_sem = {e: self.sem("c_" + e, producer=e) for e in self.ENGINES}

    def sem(self, name, producer=None):
        s = Sem(name, producer)
        self.sems.append(s)
        return s

    def _dep(self, name):
        return self.res.setdefault(name, {"w": None, "r": []})

    def do(self, engine, fn, reads=(), writes=(), wait=(), inc=(), dma=None,
           coll=None):
        waits = {}

        def addw(sv):
            if sv is None:
                return
            s, v = sv
            waits[s] = max(waits.get(s, 0), v)

        for r in reads:
            addw(self._dep(r)["w"])
        for w in writes:
            d = self._dep(w)
            addw(d["w"])
            for sv in d["r"]:
                addw(sv)
        for sv in wait:
            addw(sv)
        addw(self.gwait)

        if coll is not None:
            s = coll
            amt = 1
        elif dma is None:
            s = self.eng_sem[engine]
            amt = 1
        else:
            s = dma.next() if isinstance(dma, Stream) else dma
            amt = 16
            if s.n > 0:
                waits[s] = max(waits.get(s, 0), s.n)
        s.n += amt
        comp = (s, s.n)
        incs = [(s, amt)]
        for (x, a) in inc:
            x.n += a
            incs.append((x, a))

        for w in writes:
            d = self._dep(w)
            d["w"] = comp
            d["r"] = []
        for r in reads:
            self._dep(r)["r"].append(comp)

        self.steps.append((engine, fn,
                           sorted(waits.items(), key=lambda kv: kv[0].name), incs))
        return comp

    def fence(self, exclude=()):
        allr = [k for k in self.res.keys() if k not in exclude]
        h = self.do("vector", lambda eng: None, reads=allr)
        self.gwait = h

    def emit(self, stack):
        for s in self.sems:
            s.h = stack.enter_context(self.nc.semaphore(s.name))
        per = {e: [] for e in self.ENGINES}
        for st in self.steps:
            per[st[0]].append(st)
        with self.nc.Block() as block:
            def mk(engname):
                def body(eng):
                    seen = {}
                    for (_, fn, waits, incs) in per[engname]:
                        for (s, v) in waits:
                            if seen.get(s, 0) >= v:
                                continue
                            eng.wait_ge(s.h, v)
                            seen[s] = v
                        instr = fn(eng)
                        for (s, amt) in incs:
                            if instr is None:
                                eng.sem_inc(s.h, amt)
                            else:
                                instr = instr.then_inc(s.h, amt)
                return body
            for e in self.ENGINES:
                if per[e]:
                    getattr(block, e)(mk(e))


def build_program(c: Cfg, stack: ExitStack, stop_after="D", debug_dump=False):
    nc = bacc.Bacc("TRN2")
    b = Builder(nc)
    _PHASES = ["A1", "AG", "A2a", "Q7", "A2b", "C", "D"]
    _stop = _PHASES.index(stop_after)

    def _cut(phase):
        return _PHASES.index(phase) > _stop

    def din(name, shape, dt):
        return nc.dram_tensor(name, shape, dt, kind="ExternalInput").ap()

    def dint(name, shape, dt, shared=False):
        if shared:
            return nc.dram_tensor(name, shape, dt, addr_space="Shared").ap()
        return nc.dram_tensor(name, shape, dt).ap()

    x_in = din("x", [c.Tl, c.D], F32)
    out = nc.dram_tensor("out", [c.Tl + 128, c.D], F32, kind="ExternalOutput").ap()
    attw_in = [din(w, [c.D, c.D], F32) for w in ("Wq", "Wk", "Wv", "Wo")]
    bqkv_in = din("bqkv", [128, 3 * c.DC], F32)
    bo_in = din("bo_row", [128, c.D], F32)
    ws1T_in = din("ws1T", [c.D, c.SI], F32)
    ws3T_in = din("ws3T", [c.D, c.SI], F32)
    ws2T_in = din("ws2T", [c.SI, c.D], F32)
    gwT_in = din("gwT", [c.D, c.E], F32)
    gb_in = din("gb_bcast", [128, c.E], F32)
    w1T_in = din("w1T", [c.EPC, c.D, c.I], F32)
    w3T_in = din("w3T", [c.EPC, c.D, c.I], F32)
    w2T_in = din("w2T", [c.EPC, c.I, c.D], F32)
    Lup_in = din("Lup", [128, 132], BF16)
    idf_in = din("idf32", [128, 128], F32)
    idb_in = din("idbf", [128, 128], BF16)
    ssum_in = din("Ssum", [128, c.DC * 128], BF16)
    sbsel_in = din("Sb", [128, c.DC * 128], BF16)
    ones1_in = din("ones1", [1, 128], F32)
    segsel_in = din("segsel", [128, c.segs * c.segs], BF16)
    ids16_in = din("ids16", [c.E, c.Tl], I16)
    sloti_in = din("sloti", [c.E, c.CAP], I16)
    maskm_in = din("maskmat", [c.nseg_tot, c.segs], F32)
    rowb_in = din("rowbase", [128, 2], F32)
    myexp_in = din("myexp", [c.EPC, 1], I32)

    ln1_d = dint("ln1_d", [c.Tl, c.D], BF16)
    x1_d = dint("x1_d", [c.Tl, c.D], F32)
    agin_ln2 = dint("agin_ln2", [c.Tl, c.D], BF16)
    ln2_full = dint("ln2_full", [c.n_cores * c.Tl, c.D], BF16, shared=True)
    agin_cnt = dint("agin_cnt", [c.segs, c.E], F32)
    cnt_full = dint("cnt_full", [c.nseg_tot, c.E], F32, shared=True)
    arin_tab = dint("arin_tab", [c.E, 2 * c.CAP], F32)
    tab_full = dint("tab_full", [c.E, 2 * c.CAP], F32, shared=True)
    agin_yb = dint("agin_yb", [c.EPC * c.CAP, c.D], BF16)
    eidx_d = dint("eidx_d", [c.EPC, c.CAP], I16)
    lists_d = dint("lists_d", [c.E, 2 * c.CAP], I16)
    cnts_d = dint("cnts_d", [1, c.E], F32)
    ybc = [dint(f"ybc{k}", [c.n_cores * c.EPCH * c.CAP, c.D], BF16, shared=True)
           for k in range(c.NCH)]

    S = {}
    CMS = {}

    def sbuf(name, shape, dt, scope=None):
        cm = nc.sbuf_tensor("sb_" + name, shape, dt)
        if scope is None:
            S[name] = stack.enter_context(cm)
        else:
            S[name] = cm.__enter__()
            CMS.setdefault(scope, []).append(cm)
        return S[name]

    def free_scope(scope):
        for cm in reversed(CMS.pop(scope, [])):
            cm.__exit__(None, None, None)

    sbuf("bqkv", [128, 3 * c.DC], F32)
    sbuf("bo_row", [128, c.D], F32)
    sbuf("gb", [128, c.E], F32)
    sbuf("Lup", [128, 132], BF16)
    sbuf("idf32", [128, 128], F32)
    sbuf("idbf", [128, 128], BF16)
    sbuf("Ssum", [128, c.DC * 128], BF16)
    sbuf("Sb", [128, c.DC * 128], BF16)
    sbuf("ones1", [1, 128], F32)
    sbuf("segsel", [128, c.segs * c.segs], BF16)
    sbuf("sloti", [c.E, c.CAP], I16)
    sbuf("maskm", [c.nseg_tot, c.segs], F32)
    sbuf("rowb", [128, 2], F32)
    sbuf("myexp", [c.EPC, 1], I32)
    sbuf("wstage", [128, c.SI], F32)
    sbuf("stat", [128, 16], F32)
    sbuf("x1t", [128, 2 * c.D], F32)
    sbuf("gstat", [128, 4], F32)
    sbuf("sbt", [128, c.E], F32)
    sbuf("top8", [128, 8], F32)
    sbuf("ln2fmseg", [128, 2 * c.DC * c.SEG], BF16)
    sbuf("tilec", [c.E, c.tiles], F32)
    sbuf("segcnt", [c.segs, c.E], F32)
    sbuf("cntf", [c.nseg_tot, c.E], F32)
    sbuf("segoff", [c.E, c.segs], F32)
    sbuf("totoff", [c.E, c.tiles], F32)
    sbuf("tabf", [c.E, 2 * c.CAP], F32)
    sbuf("idsG", [c.E, c.CAP], I16)
    sbuf("wG", [c.E, c.CAP], BF16)
    nblk = c.E // 16
    sbuf("minef", [c.E, c.CAP], F32)
    sbuf("mtmp", [c.E, c.CAP], F32)
    sbuf("mtmp2", [c.E, c.CAP], F32)
    sbuf("mtmp3", [c.E, c.CAP], F32)
    sbuf("mtmpI", [c.E, c.CAP], I32)
    sbuf("ltok", [c.E, c.CAP], I16)
    sbuf("onescol", [c.E, 1], F32)
    sbuf("rankD", [c.E, c.CAP], I16)
    sbuf("listSS", [c.E, c.CAP], I16)
    sbuf("listTT", [c.E, c.CAP], I16)
    sbuf("dlist", [128, 2 * c.E * (c.CAP // 16)], I16)
    sbuf("cntrow", [c.E, 1], F32)
    sbuf("nfs", [1, c.E], U32)
    sbuf("nfsf", [1, c.E], F32)
    sbuf("t128", [c.E, 128], F32)

    P = {}
    for n_, sh in (("g0", [128, 512]), ("g1", [128, 512]), ("g2", [128, 512]),
                   ("g3", [128, 512]), ("psB", [128, c.SEG]), ("t0", [c.E, 132]),
                   ("tsm", [128, 128]), ("psC", [c.segs, c.E])):
        P[n_] = stack.enter_context(nc.psum_tensor(n_, sh, F32))

    ld = Stream(b, "ld", 12)
    stw = Stream(b, "st", 8)
    pda = b.sem("pda")
    pdb = Stream(b, "pdb", 8)
    pdc = Stream(b, "pdc", 4)
    pdd = Stream(b, "pdd", 4)
    cc1, cc2, cc3 = b.sem("cc1"), b.sem("cc2"), b.sem("cc3")
    cc4s = [b.sem(f"cc4{k}") for k in range(c.NCH)]
    CST = c.CAP
    RG = [list(range(c.n_cores))]

    for nm, o, i in (("bqkv", S["bqkv"], bqkv_in), ("bo_row", S["bo_row"], bo_in),
                     ("gb", S["gb"], gb_in), ("Lup", S["Lup"], Lup_in),
                     ("idf32", S["idf32"], idf_in), ("idbf", S["idbf"], idb_in),
                     ("Ssum", S["Ssum"], ssum_in), ("Sb", S["Sb"], sbsel_in),
                     ("ones1", S["ones1"], ones1_in), ("segsel", S["segsel"], segsel_in),
                     ("sloti", S["sloti"], sloti_in),
                     ("maskm", S["maskm"], maskm_in), ("rowb", S["rowb"], rowb_in),
                     ("myexp", S["myexp"], myexp_in)):
        b.do("sync", (lambda o_, i_: (lambda eng: eng.dma_start(out=o_[:, :], in_=i_[:, :])))(o, i),
             writes=[nm], dma=ld)

    sbuf("attW", [128, 4 * c.DC * c.D], BF16, scope="A1")
    sbuf("xt2", [128, 2 * c.D], F32, scope="A1")
    sbuf("sqdump", [128, c.D], BF16, scope="A1")
    sbuf("lnt", [128, 2 * c.D], BF16, scope="A1")
    sbuf("ln1fm", [128, 2 * c.DC * c.SEG], BF16, scope="A1")
    sbuf("Qfm", [128, c.DC * c.SEG], BF16, scope="A1")
    sbuf("Kfm", [128, c.DC * c.SEG], BF16, scope="A1")
    sbuf("Vfm", [128, c.DC * c.SEG], BF16, scope="A1")
    sbuf("Pch", [128, 2 * c.SEG], BF16, scope="A1")
    sbuf("praw", [128, c.SEG], F32, scope="A1")
    sbuf("probs", [128, c.SEG], BF16, scope="A1")
    sbuf("pstat", [128, 4], F32, scope="A1")

    for wi in range(4):
        for kc in range(c.DC):
            b.do("sync",
                 (lambda wi_, kc_: (lambda eng: eng.dma_start(
                     out=S["wstage"][:, 0:c.D],
                     in_=attw_in[wi_][kc_ * 128:(kc_ + 1) * 128, :])))(wi, kc),
                 writes=["wstage"], dma=ld)
            b.do("vector",
                 (lambda wi_, kc_: (lambda eng: eng.tensor_copy(
                     out=S["attW"][:, (wi_ * 8 + kc_) * c.D:(wi_ * 8 + kc_ + 1) * c.D],
                     in_=S["wstage"][:, 0:c.D])))(wi, kc),
                 reads=["wstage"], writes=[f"attW{wi}_{kc}"])

    def ln_tile(xt_ap, out_ap, so, out_res, in_res):
        st_ = S["stat"]
        b.do("scalar",
             (lambda xt_, so_: (lambda eng: eng.activation(
                 out=S["sqdump"][:, :], in_=xt_, func=AF.Square,
                 accum_out=st_[:, so_ * 8 + 1: so_ * 8 + 2])))(xt_ap, so),
             reads=list(in_res), writes=[f"stat{so}b", "sqdump"])
        b.do("vector",
             (lambda xt_, so_: (lambda eng: eng.reduce_sum(
                 out=st_[:, so_ * 8 + 0: so_ * 8 + 1], in_=xt_, axis=AX.X)))(xt_ap, so),
             reads=list(in_res), writes=[f"stat{so}a"])
        b.do("vector",
             (lambda so_: (lambda eng: eng.tensor_scalar_mul(
                 st_[:, so_ * 8 + 2: so_ * 8 + 3], st_[:, so_ * 8 + 0: so_ * 8 + 1],
                 1.0 / c.D)))(so),
             reads=[f"stat{so}a"], writes=[f"stat{so}m"])
        b.do("vector",
             (lambda so_: (lambda eng: eng.tensor_mul(
                 st_[:, so_ * 8 + 3: so_ * 8 + 4], st_[:, so_ * 8 + 2: so_ * 8 + 3],
                 st_[:, so_ * 8 + 2: so_ * 8 + 3])))(so),
             reads=[f"stat{so}m"], writes=[f"stat{so}q"])
        b.do("vector",
             (lambda so_: (lambda eng: eng.scalar_tensor_tensor(
                 out=st_[:, so_ * 8 + 4: so_ * 8 + 5], in0=st_[:, so_ * 8 + 1: so_ * 8 + 2],
                 scalar=1.0 / c.D, in1=st_[:, so_ * 8 + 3: so_ * 8 + 4],
                 op0=OP.mult, op1=OP.subtract)))(so),
             reads=[f"stat{so}b", f"stat{so}q"], writes=[f"stat{so}v"])
        b.do("vector",
             (lambda so_: (lambda eng: eng.tensor_scalar_add(
                 st_[:, so_ * 8 + 4: so_ * 8 + 5], st_[:, so_ * 8 + 4: so_ * 8 + 5],
                 1e-5)))(so),
             reads=[f"stat{so}v"], writes=[f"stat{so}v"])
        b.do("scalar",
             (lambda so_: (lambda eng: eng.activation(
                 out=st_[:, so_ * 8 + 6: so_ * 8 + 7], in_=st_[:, so_ * 8 + 4: so_ * 8 + 5],
                 func=AF.Sqrt)))(so),
             reads=[f"stat{so}v"], writes=[f"stat{so}s"])
        b.do("vector",
             (lambda so_: (lambda eng: eng.reciprocal(
                 out=st_[:, so_ * 8 + 5: so_ * 8 + 6],
                 in_=st_[:, so_ * 8 + 6: so_ * 8 + 7])))(so),
             reads=[f"stat{so}s"], writes=[f"stat{so}r"])
        b.do("vector",
             (lambda xt_, so_, o_: (lambda eng: eng.tensor_scalar(
                 out=o_, in0=xt_,
                 scalar1=st_[:, so_ * 8 + 2: so_ * 8 + 3],
                 scalar2=st_[:, so_ * 8 + 5: so_ * 8 + 6],
                 op0=OP.subtract, op1=OP.mult)))(xt_ap, so, out_ap),
             reads=list(in_res) + [f"stat{so}m", f"stat{so}r"], writes=[out_res])

    for ls in range(c.segs):
        q = ls % 2
        for tt in range(4):
            hh = tt % 2
            b.do("sync",
                 (lambda ls_, tt_, h_: (lambda eng: eng.dma_start(
                     out=S["xt2"][:, h_ * c.D:(h_ + 1) * c.D],
                     in_=x_in[ls_ * c.SEG + tt_ * 128: ls_ * c.SEG + (tt_ + 1) * 128, :])))(ls, tt, hh),
                 writes=[f"xt2{hh}"], dma=ld)
            xt = S["xt2"][:, hh * c.D:(hh + 1) * c.D]
            ln_tile(xt, S["lnt"][:, hh * c.D:(hh + 1) * c.D], hh, f"lnt{hh}", [f"xt2{hh}"])
            b.do("scalar",
                 (lambda ls_, tt_, h_: (lambda eng: eng.dma_start(
                     out=ln1_d[ls_ * c.SEG + tt_ * 128: ls_ * c.SEG + (tt_ + 1) * 128, :],
                     in_=S["lnt"][:, h_ * c.D:(h_ + 1) * c.D])))(ls, tt, hh),
                 reads=[f"lnt{hh}"], writes=[f"ln1_d{ls}_{tt}"], dma=stw)
        for kc in range(c.DC):
            b.do("sync",
                 (lambda ls_, kc_, q_: (lambda eng: eng.dma_start(
                     out=S["ln1fm"][:, (q_ * c.DC + kc_) * c.SEG:(q_ * c.DC + kc_ + 1) * c.SEG],
                     in_=ln1_d[ls_ * c.SEG:(ls_ + 1) * c.SEG, kc_ * 128:(kc_ + 1) * 128],
                     transpose=True)))(ls, kc, q),
                 reads=[f"ln1_d{ls}_{t2}" for t2 in range(4)],
                 writes=[f"ln1fm{q}_{kc}"], dma=ld)

        for pi, dst in enumerate(("Qfm", "Kfm", "Vfm")):
            for oc in range(c.DC):
                bank = f"g{(pi * c.DC + oc) % 2}"

                def qkv_group(eng, pi_=pi, oc_=oc, q_=q, bank_=bank):
                    ii = None
                    for kc in range(c.DC):
                        ii = eng.matmul(
                            P[bank_][:, :],
                            S["attW"][:, (pi_ * 8 + kc) * c.D + oc_ * 128:
                                      (pi_ * 8 + kc) * c.D + (oc_ + 1) * 128],
                            S["ln1fm"][:, (q_ * c.DC + kc) * c.SEG:(q_ * c.DC + kc + 1) * c.SEG],
                            start=(kc == 0), stop=(kc == c.DC - 1))
                    return ii

                b.do("tensor", qkv_group,
                     reads=[f"ln1fm{q}_{kc}" for kc in range(c.DC)] +
                           [f"attW{pi}_{kc}" for kc in range(c.DC)],
                     writes=[bank])
                b.do("scalar",
                     (lambda pi_, oc_, dst_, bank_: (lambda eng: eng.activation(
                         out=S[dst_][:, oc_ * c.SEG:(oc_ + 1) * c.SEG], in_=P[bank_][:, :],
                         func=AF.Identity,
                         bias=S["bqkv"][:, pi_ * 8 + oc_: pi_ * 8 + oc_ + 1],
                         scale=1.0)))(pi, oc, dst, bank),
                     reads=[bank, "bqkv"], writes=[f"{dst}_{oc}"])

        for kc in range(c.DC):
            pq = kc % 2
            b.do("vector",
                 (lambda kc_, pq_: (lambda eng: eng.tensor_mul(
                     S["Pch"][:, pq_ * c.SEG:(pq_ + 1) * c.SEG],
                     S["Qfm"][:, kc_ * c.SEG:(kc_ + 1) * c.SEG],
                     S["Kfm"][:, kc_ * c.SEG:(kc_ + 1) * c.SEG])))(kc, pq),
                 reads=[f"Qfm_{kc}", f"Kfm_{kc}"], writes=[f"Pch{pq}"])
            b.do("tensor",
                 (lambda kc_, pq_: (lambda eng: eng.matmul(
                     P["psB"][:, :], S["Ssum"][:, kc_ * 128:(kc_ + 1) * 128],
                     S["Pch"][:, pq_ * c.SEG:(pq_ + 1) * c.SEG],
                     start=(kc_ == 0), stop=(kc_ == c.DC - 1))))(kc, pq),
                 reads=[f"Pch{pq}", "Ssum"], writes=["psB"])

        b.do("vector", lambda eng: eng.memset(S["probs"][:, :], 0.0), writes=["probs"])
        for j in range(4):
            rows = slice(32 * j, 32 * j + 4)
            cols = slice(j, c.SEG, 4)
            b.do("vector",
                 (lambda r_, co_: (lambda eng: eng.tensor_reduce(
                     out=S["pstat"][r_, 0:1], in_=P["psB"][r_, co_],
                     axis=AX.X, op=OP.max, negate=True)))(rows, cols),
                 reads=["psB"], writes=[f"pstat{j}a"])
            b.do("scalar",
                 (lambda r_, co_: (lambda eng: eng.activation(
                     out=S["praw"][r_, co_], in_=P["psB"][r_, co_],
                     func=AF.Exp, bias=S["pstat"][r_, 0:1], scale=1.0,
                     accum_out=S["pstat"][r_, 1:2])))(rows, cols),
                 reads=["psB", f"pstat{j}a"], writes=[f"praw{j}", f"pstat{j}b"])
            b.do("vector",
                 (lambda r_: (lambda eng: eng.reciprocal(
                     out=S["pstat"][r_, 2:3], in_=S["pstat"][r_, 1:2])))(rows),
                 reads=[f"pstat{j}b"], writes=[f"pstat{j}c"])
            b.do("vector",
                 (lambda r_, co_: (lambda eng: eng.tensor_scalar_mul(
                     S["probs"][r_, co_], S["praw"][r_, co_],
                     S["pstat"][r_, 2:3])))(rows, cols),
                 reads=[f"praw{j}", f"pstat{j}c", "probs"], writes=[f"probsw{j}"])

        for kc in range(c.DC):
            bank = f"g{2 + kc % 2}"
            b.do("tensor",
                 (lambda kc_, bank_: (lambda eng: eng.matmul(
                     P[bank_][:, :], S["Sb"][:, kc_ * 128:(kc_ + 1) * 128],
                     S["probs"][:, :], start=True, stop=True)))(kc, bank),
                 reads=["probs", "Sb"] + [f"probsw{j}" for j in range(4)],
                 writes=[bank])
            b.do("vector",
                 (lambda kc_, bank_: (lambda eng: eng.tensor_mul(
                     S["Qfm"][:, kc_ * c.SEG:(kc_ + 1) * c.SEG], P[bank_][:, :],
                     S["Vfm"][:, kc_ * c.SEG:(kc_ + 1) * c.SEG])))(kc, bank),
                 reads=[bank, f"Vfm_{kc}", f"Qfm_{kc}"], writes=[f"attnfm_{kc}"])

        for tt in range(4):
            hh = tt % 2
            b.do("sync",
                 (lambda ls_, tt_, h_: (lambda eng: eng.dma_start(
                     out=S["xt2"][:, h_ * c.D:(h_ + 1) * c.D],
                     in_=x_in[ls_ * c.SEG + tt_ * 128: ls_ * c.SEG + (tt_ + 1) * 128, :])))(ls, tt, hh),
                 writes=[f"xt2{hh}"], dma=ld)
            for nh in range(2):
                bank = f"g{nh}"

                def oproj_group(eng, tt_=tt, nh_=nh, bank_=bank):
                    ii = None
                    for kc in range(c.DC):
                        ii = eng.matmul(
                            P[bank_][:, :],
                            S["Qfm"][:, kc * c.SEG + tt_ * 128: kc * c.SEG + (tt_ + 1) * 128],
                            S["attW"][:, (3 * 8 + kc) * c.D + nh_ * 512:
                                      (3 * 8 + kc) * c.D + (nh_ + 1) * 512],
                            start=(kc == 0), stop=(kc == c.DC - 1))
                    return ii

                b.do("tensor", oproj_group,
                     reads=[f"attnfm_{kc}" for kc in range(c.DC)] +
                           [f"Qfm_{kc}" for kc in range(c.DC)] +
                           [f"attW3_{kc}" for kc in range(c.DC)],
                     writes=[bank])
                b.do("vector",
                     (lambda bank_, nh_, h_: (lambda eng: eng.tensor_add(
                         S["x1t"][:, h_ * c.D + nh_ * 512: h_ * c.D + (nh_ + 1) * 512],
                         P[bank_][:, :],
                         S["bo_row"][:, nh_ * 512:(nh_ + 1) * 512])))(bank, nh, hh),
                     reads=[bank, "bo_row"], writes=[f"x1t{hh}_{nh}"])
                b.do("vector",
                     (lambda nh_, h_: (lambda eng: eng.tensor_add(
                         S["x1t"][:, h_ * c.D + nh_ * 512: h_ * c.D + (nh_ + 1) * 512],
                         S["x1t"][:, h_ * c.D + nh_ * 512: h_ * c.D + (nh_ + 1) * 512],
                         S["xt2"][:, h_ * c.D + nh_ * 512: h_ * c.D + (nh_ + 1) * 512])))(nh, hh),
                     reads=[f"xt2{hh}", f"x1t{hh}_{nh}"], writes=[f"x1t{hh}_{nh}"])
            b.do("sync",
                 (lambda ls_, tt_, h_: (lambda eng: eng.dma_start(
                     out=x1_d[ls_ * c.SEG + tt_ * 128: ls_ * c.SEG + (tt_ + 1) * 128, :],
                     in_=S["x1t"][:, h_ * c.D:(h_ + 1) * c.D])))(ls, tt, hh),
                 reads=[f"x1t{hh}_0", f"x1t{hh}_1"], writes=[f"x1_d{ls}_{tt}"], dma=ld)
            xt = S["x1t"][:, hh * c.D:(hh + 1) * c.D]
            ln_tile(xt, S["lnt"][:, hh * c.D:(hh + 1) * c.D], hh,
                    f"lnt{hh}", [f"x1t{hh}_0", f"x1t{hh}_1"])
            b.do("scalar",
                 (lambda ls_, tt_, h_: (lambda eng: eng.dma_start(
                     out=agin_ln2[ls_ * c.SEG + tt_ * 128: ls_ * c.SEG + (tt_ + 1) * 128, :],
                     in_=S["lnt"][:, h_ * c.D:(h_ + 1) * c.D])))(ls, tt, hh),
                 reads=[f"lnt{hh}"], writes=[f"agin{ls}_{tt}"], dma=stw)

    all_agin = [f"agin{ls}_{tt}" for ls in range(c.segs) for tt in range(4)]

    if _cut("AG"):
        b.fence()
        b.do("gpsimd", lambda eng: None, wait=[b.gwait])
        b.do("sync", lambda eng: None, wait=[b.gwait])
        b.emit(stack)
        nc.compile()
        return nc
    b.do("gpsimd",
         lambda eng: eng.collective_compute(
             "AllGather", OP.bypass, replica_groups=RG,
             ins=[agin_ln2[:, :]], outs=[ln2_full[:, :]]),
         reads=all_agin, writes=["ln2_full"], coll=cc1)

    if _cut("A2a"):
        b.fence()
        b.do("gpsimd", lambda eng: None, wait=[b.gwait])
        b.do("sync", lambda eng: None, wait=[b.gwait])
        b.emit(stack)
        nc.compile()
        return nc
    b.fence(exclude=["ln2_full"])
    free_scope("A1")

    sbuf("gwT", [128, c.DC * c.E], BF16, scope="R")
    sbuf("sc_t", [128, 2 * c.E], F32, scope="R")
    sbuf("M_t", [128, 2 * c.E], BF16, scope="R")
    QW = c.Tl // 8
    sbuf("MT", [c.E, c.Tl], BF16, scope="R")
    sbuf("scoresT", [c.E, c.Tl], BF16, scope="R")
    sbuf("rank16", [c.E, c.Tl], I16, scope="R")
    sbuf("scrF", [c.E, QW], F32, scope="R")
    sbuf("scrI", [c.E, QW], I32, scope="R")
    sbuf("maskh", [c.E, QW], F32, scope="R")
    sbuf("idx16", [c.E, c.Tl], I16, scope="R")
    sbuf("idx16b", [c.E, c.Tl], I16, scope="R")
    sbuf("ids16", [c.E, c.Tl], I16, scope="R")

    b.do("sync", lambda eng: eng.dma_start(out=S["ids16"][:, :], in_=ids16_in[:, :]),
         writes=["ids16"], dma=ld)
    b.do("sync",
         lambda eng: eng.dma_start(
             out=S["wstage"][:, 0:c.DC * c.E].rearrange("p (k e) -> p k e", k=c.DC),
             in_=gwT_in.rearrange("(k p) e -> p k e", p=128)),
         writes=["wstage"], dma=ld)
    b.do("vector",
         lambda eng: eng.tensor_copy(out=S["gwT"][:, :], in_=S["wstage"][:, 0:c.DC * c.E]),
         reads=["wstage"], writes=["gwT"])

    for ls in range(c.segs):
        q = ls % 2
        for kc in range(c.DC):
            b.do("sync",
                 (lambda ls_, kc_, q_: (lambda eng: eng.dma_start(
                     out=S["ln2fmseg"][:, (q_ * c.DC + kc_) * c.SEG:
                                       (q_ * c.DC + kc_ + 1) * c.SEG],
                     in_=agin_ln2[ls_ * c.SEG:(ls_ + 1) * c.SEG, kc_ * 128:(kc_ + 1) * 128],
                     transpose=True)))(ls, kc, q),
                 reads=[f"agin{ls}_{t2}" for t2 in range(4)],
                 writes=[f"l2s{q}_{kc}"], dma=ld)
        for tt in range(4):
            ti = 4 * ls + tt
            tp = ti % 2
            bank = f"g{tp}"

            def gate_group(eng, tt_=tt, q_=q, bank_=bank):
                ii = None
                for kc in range(c.DC):
                    ii = eng.matmul(
                        P[bank_][:, 0:c.E],
                        S["ln2fmseg"][:, (q_ * c.DC + kc) * c.SEG + tt_ * 128:
                                      (q_ * c.DC + kc) * c.SEG + (tt_ + 1) * 128],
                        S["gwT"][:, kc * c.E:(kc + 1) * c.E],
                        start=(kc == 0), stop=(kc == c.DC - 1))
                return ii

            b.do("tensor", gate_group,
                 reads=[f"l2s{q}_{kc}" for kc in range(c.DC)] + ["gwT"], writes=[bank])
            b.do("vector",
                 (lambda bank_: (lambda eng: eng.tensor_reduce(
                     out=S["gstat"][:, 0:1], in_=P[bank_][:, 0:c.E], axis=AX.X,
                     op=OP.max, negate=True)))(bank),
                 reads=[bank], writes=["gstat0"])
            b.do("scalar",
                 (lambda bank_, tp_: (lambda eng: eng.activation(
                     out=S["sc_t"][:, tp_ * c.E:(tp_ + 1) * c.E], in_=P[bank_][:, 0:c.E],
                     func=AF.Exp, bias=S["gstat"][:, 0:1], scale=1.0,
                     accum_out=S["gstat"][:, 1:2])))(bank, tp),
                 reads=[bank, "gstat0"], writes=[f"scraw{tp}", "gstat1"])
            b.do("vector",
                 lambda eng: eng.reciprocal(out=S["gstat"][:, 2:3], in_=S["gstat"][:, 1:2]),
                 reads=["gstat1"], writes=["gstat2"])
            b.do("vector",
                 (lambda tp_: (lambda eng: eng.tensor_scalar_mul(
                     S["sc_t"][:, tp_ * c.E:(tp_ + 1) * c.E],
                     S["sc_t"][:, tp_ * c.E:(tp_ + 1) * c.E], S["gstat"][:, 2:3])))(tp),
                 reads=[f"scraw{tp}", "gstat2"], writes=[f"sct{tp}"])
            b.do("vector",
                 (lambda tp_: (lambda eng: eng.tensor_add(
                     S["sbt"][:, :], S["sc_t"][:, tp_ * c.E:(tp_ + 1) * c.E],
                     S["gb"][:, :])))(tp),
                 reads=[f"sct{tp}", "gb"], writes=["sbt"])
            b.do("vector", lambda eng: eng.max(out=S["top8"][:, :], in_=S["sbt"][:, :]),
                 reads=["sbt"], writes=["top8"])
            b.do("vector",
                 (lambda tp_: (lambda eng: eng.tensor_scalar(
                     out=S["M_t"][:, tp_ * c.E:(tp_ + 1) * c.E], in0=S["sbt"][:, :],
                     scalar1=S["top8"][:, 5:6], scalar2=None, op0=OP.is_ge)))(tp),
                 reads=["sbt", "top8"], writes=[f"Mt{tp}"])
            b.do("tensor",
                 (lambda tp_: (lambda eng: eng.transpose(
                     P["tsm"][0:c.E, 0:128], S["sc_t"][:, tp_ * c.E:(tp_ + 1) * c.E],
                     S["idf32"][:, :])))(tp),
                 reads=[f"sct{tp}", "idf32"], writes=["tsm"])
            b.do("vector",
                 (lambda ti_: (lambda eng: eng.tensor_copy(
                     out=S["scoresT"][:, ti_ * 128:(ti_ + 1) * 128],
                     in_=P["tsm"][0:c.E, 0:128])))(ti),
                 reads=["tsm"], writes=[f"scT{ti}"])
            b.do("tensor",
                 (lambda tp_: (lambda eng: eng.transpose(
                     P["tsm"][0:c.E, 0:64].bitcast(BF16),
                     S["M_t"][:, tp_ * c.E:(tp_ + 1) * c.E],
                     S["idbf"][:, :])))(tp),
                 reads=[f"Mt{tp}", "idbf"], writes=["tsm"])
            b.do("vector",
                 (lambda ti_: (lambda eng: eng.tensor_copy(
                     out=S["MT"][:, ti_ * 128:(ti_ + 1) * 128],
                     in_=P["tsm"][0:c.E, 0:64].bitcast(BF16))))(ti),
                 reads=["tsm"], writes=[f"MTc{ti}"])
            b.do("tensor",
                 (lambda tp_: (lambda eng: eng.matmul(
                     P["t0"][:, 0:129], S["M_t"][:, tp_ * c.E:(tp_ + 1) * c.E],
                     S["Lup"][:, 0:129], start=True, stop=True)))(tp),
                 reads=[f"Mt{tp}", "Lup"], writes=["t0"])
            b.do("vector",
                 (lambda ti_: (lambda eng: eng.tensor_copy(
                     out=S["rank16"][:, ti_ * 128:(ti_ + 1) * 128],
                     in_=P["t0"][:, 0:128])))(ti),
                 reads=["t0"], writes=["rkF"])
            b.do("vector",
                 (lambda ti_: (lambda eng: eng.tensor_copy(
                     out=S["tilec"][:, ti_:ti_ + 1], in_=P["t0"][:, 128:129])))(ti),
                 reads=["t0"], writes=[f"tilec{ti}"])
            b.do("tensor",
                 (lambda tp_, ls_, fst, lst: (lambda eng: eng.matmul(
                     P["psC"][:, :], S["segsel"][:, ls_ * c.segs:(ls_ + 1) * c.segs],
                     S["M_t"][:, tp_ * c.E:(tp_ + 1) * c.E],
                     start=fst, stop=lst)))(tp, ls, ti == 0, ti == c.tiles - 1),
                 reads=[f"Mt{tp}", "segsel"], writes=["psC"])

    b.do("vector", lambda eng: eng.tensor_copy(out=S["segcnt"][:, :], in_=P["psC"][:, :]),
         reads=["psC"], writes=["segcnt"])
    b.do("sync", lambda eng: eng.dma_start(out=agin_cnt[:, :], in_=S["segcnt"][:, :]),
         reads=["segcnt"], writes=["agin_cnt"], dma=ld)
    b.do("gpsimd",
         lambda eng: eng.collective_compute(
             "AllGather", OP.bypass, replica_groups=RG,
             ins=[agin_cnt[:, :]], outs=[cnt_full[:, :]]),
         reads=["agin_cnt"], writes=["cnt_full"], coll=cc2)
    sbuf("zW", [128, 3 * 11264], BF16, scope="Z")
    sbuf("zh", [128, c.SIC * c.SEG], BF16, scope="Z")
    sbuf("silt", [128, c.SEG], F32, scope="Z")

    b.do("gpsimd",
         lambda eng: eng.dma_start(
             out=S["zW"][:, 0:c.DC * c.SI].rearrange("p (k i) -> p k i", k=c.DC),
             in_=ws1T_in.rearrange("(k p) i -> p k i", p=128)),
         writes=[f"zW0_{kc}" for kc in range(c.DC)], dma=pdb)
    b.do("gpsimd",
         lambda eng: eng.dma_start(
             out=S["zW"][:, 11264:11264 + c.DC * c.SI].rearrange(
                 "p (k i) -> p k i", k=c.DC),
             in_=ws3T_in.rearrange("(k p) i -> p k i", p=128)),
         writes=[f"zW1_{kc}" for kc in range(c.DC)], dma=pdb)
    b.do("gpsimd",
         lambda eng: eng.dma_start(
             out=S["zW"][:, 22528:22528 + c.SIC * c.D].rearrange(
                 "p (k i) -> p k i", k=c.SIC),
             in_=ws2T_in.rearrange("(k p) i -> p k i", p=128)),
         writes=[f"zW2_{ic}" for ic in range(c.SIC)], dma=pdb)

    def z_seg(ls):
        q = ls % 2
        for kc in range(c.DC):
            b.do("sync",
                 (lambda ls_, kc_, q_: (lambda eng: eng.dma_start(
                     out=S["ln2fmseg"][:, (q_ * c.DC + kc_) * c.SEG:
                                       (q_ * c.DC + kc_ + 1) * c.SEG],
                     in_=agin_ln2[ls_ * c.SEG:(ls_ + 1) * c.SEG, kc_ * 128:(kc_ + 1) * 128],
                     transpose=True)))(ls, kc, q),
                 reads=[f"agin{ls}_{t2}" for t2 in range(4)],
                 writes=[f"l2s{q}_{kc}"], dma=ld)
        for ic in range(c.SIC):
            for zi in range(2):
                bank = f"g{zi}"

                def z_group(eng, ic_=ic, zi_=zi, q_=q, bank_=bank):
                    ii = None
                    for kc in range(c.DC):
                        ii = eng.matmul(
                            P[bank_][:, :],
                            S["zW"][:, zi_ * 11264 + kc * c.SI + ic_ * 128:
                                    zi_ * 11264 + kc * c.SI + (ic_ + 1) * 128],
                            S["ln2fmseg"][:, (q_ * c.DC + kc) * c.SEG:
                                          (q_ * c.DC + kc + 1) * c.SEG],
                            start=(kc == 0), stop=(kc == c.DC - 1))
                    return ii

                b.do("tensor", z_group,
                     reads=[f"zW{zi}_{kc}" for kc in range(c.DC)] +
                           [f"l2s{q}_{kc}" for kc in range(c.DC)],
                     writes=[bank])
            b.do("scalar",
                 lambda eng: eng.activation(out=S["silt"][:, :], in_=P["g0"][:, :], func=AF.Silu),
                 reads=["g0"], writes=["silt0"])
            b.do("vector",
                 (lambda ic_: (lambda eng: eng.tensor_mul(
                     S["zh"][:, ic_ * c.SEG:(ic_ + 1) * c.SEG], S["silt"][:, :],
                     P["g1"][:, :])))(ic),
                 reads=["silt0", "g1"], writes=[f"zh{ic}"])
        for tt in range(4):
            hh = tt % 2
            b.do("sync",
                 (lambda ls_, tt_, h_: (lambda eng: eng.dma_start(
                     out=S["x1t"][:, h_ * c.D:(h_ + 1) * c.D],
                     in_=x1_d[ls_ * c.SEG + tt_ * 128: ls_ * c.SEG + (tt_ + 1) * 128, :])))(ls, tt, hh),
                 reads=[f"x1_d{ls}_{tt}"], writes=[f"x1t{hh}_0", f"x1t{hh}_1"], dma=ld)
            for nh in range(2):
                bank = f"g{2 + nh}"

                def z2_group(eng, tt_=tt, nh_=nh, bank_=bank):
                    ii = None
                    for ic in range(c.SIC):
                        ii = eng.matmul(
                            P[bank_][:, :],
                            S["zh"][:, ic * c.SEG + tt_ * 128: ic * c.SEG + (tt_ + 1) * 128],
                            S["zW"][:, 22528 + ic * c.D + nh_ * 512:
                                    22528 + ic * c.D + (nh_ + 1) * 512],
                            start=(ic == 0), stop=(ic == c.SIC - 1))
                    return ii

                b.do("tensor", z2_group,
                     reads=[f"zh{ic}" for ic in range(c.SIC)] +
                           [f"zW2_{ic}" for ic in range(c.SIC)],
                     writes=[bank])
                b.do("vector",
                     (lambda bank_, nh_, h_: (lambda eng: eng.tensor_add(
                         S["x1t"][:, h_ * c.D + nh_ * 512: h_ * c.D + (nh_ + 1) * 512],
                         P[bank_][:, :],
                         S["x1t"][:, h_ * c.D + nh_ * 512: h_ * c.D + (nh_ + 1) * 512])))(bank, nh, hh),
                     reads=[bank, f"x1t{hh}_0", f"x1t{hh}_1"], writes=[f"x1t{hh}_{nh}"])
            b.do("scalar",
                 (lambda ls_, tt_, h_: (lambda eng: eng.dma_start(
                     out=out[ls_ * c.SEG + tt_ * 128: ls_ * c.SEG + (tt_ + 1) * 128, :],
                     in_=S["x1t"][:, h_ * c.D:(h_ + 1) * c.D])))(ls, tt, hh),
                 reads=[f"x1t{hh}_0", f"x1t{hh}_1"], writes=[f"out{ls}_{tt}"], dma=stw)

    zsplit = max(1, c.segs - 3)
    for ls in range(zsplit):
        z_seg(ls)

    b.do("sync", lambda eng: eng.dma_start(out=S["cntf"][:, :], in_=cnt_full[:, :]),
         reads=["cnt_full"], writes=["cntf"], dma=ld)

    b.do("tensor",
         lambda eng: eng.matmul(P["tsm"][0:c.E, 0:c.segs], S["cntf"][:, :],
                                S["maskm"][:, :], start=True, stop=True),
         reads=["cntf", "maskm"], writes=["tsm"])
    b.do("vector",
         lambda eng: eng.tensor_copy(out=S["segoff"][:, :], in_=P["tsm"][0:c.E, 0:c.segs]),
         reads=["tsm"], writes=["segoff"])
    for ls in range(c.segs):
        b.do("vector",
             (lambda ls_: (lambda eng: eng.tensor_copy(
                 out=S["totoff"][:, 4 * ls_:4 * ls_ + 1], in_=S["segoff"][:, ls_:ls_ + 1])))(ls),
             reads=["segoff"], writes=[f"totoff{4 * ls}"])
        for j in range(3):
            b.do("vector",
                 (lambda ls_, j_: (lambda eng: eng.tensor_add(
                     S["totoff"][:, 4 * ls_ + j_ + 1:4 * ls_ + j_ + 2],
                     S["totoff"][:, 4 * ls_ + j_:4 * ls_ + j_ + 1],
                     S["tilec"][:, 4 * ls_ + j_:4 * ls_ + j_ + 1])))(ls, j),
                 reads=[f"totoff{4 * ls + j}", f"tilec{4 * ls + j}"],
                 writes=[f"totoff{4 * ls + j + 1}"])
    def V(fn, reads, writes):
        b.do("vector", fn, reads=reads, writes=writes)

    TPQ = QW // 128
    for h4 in range(c.Tl // QW):
        sl = slice(h4 * QW, (h4 + 1) * QW)
        V((lambda sl_: (lambda eng: eng.tensor_copy(
            out=S["scrF"][:, :], in_=S["rank16"][:, sl_])))(sl),
          ["rkF"], ["scrF"])
        for tl in range(TPQ):
            ti = h4 * TPQ + tl
            V((lambda tl_, ti_: (lambda eng: eng.tensor_scalar_add(
                S["scrF"][:, tl_ * 128:(tl_ + 1) * 128],
                S["scrF"][:, tl_ * 128:(tl_ + 1) * 128],
                S["totoff"][:, ti_:ti_ + 1])))(tl, ti),
              ["scrF", f"totoff{ti}"], ["scrF"])
        V(lambda eng: eng.tensor_copy(out=S["scrI"][:, :], in_=S["scrF"][:, :]),
          ["scrF"], ["scrI"])
        V(lambda eng: eng.tensor_scalar(out=S["maskh"][:, :], in0=S["scrF"][:, :],
                                        scalar1=float(c.CAP), scalar2=None,
                                        op0=OP.is_lt),
          ["scrF"], ["maskH"])
        V((lambda sl_: (lambda eng: eng.tensor_mul(
            S["maskh"][:, :], S["maskh"][:, :], S["MT"][:, sl_])))(sl),
          ["maskH"] + [f"MTc{h4 * TPQ + tl}" for tl in range(TPQ)], ["maskH"])
        V(lambda eng: eng.tensor_scalar_add(S["scrF"][:, :], S["scrF"][:, :], 1.0),
          ["scrF"], ["scrF"])
        V(lambda eng: eng.tensor_mul(S["scrF"][:, :], S["scrF"][:, :], S["maskh"][:, :]),
          ["scrF", "maskH"], ["scrF"])
        V((lambda sl_: (lambda eng: eng.tensor_scalar_add(
            S["idx16"][:, sl_], S["scrF"][:, :], -1.0)))(sl),
          ["scrF"], ["idx16F"])
        V(lambda eng: eng.tensor_copy(out=S["scrF"][:, :], in_=S["scrI"][:, :]),
          ["scrI"], ["scrF"])
        V(lambda eng: eng.tensor_scalar(out=S["scrI"][:, :], in0=S["scrI"][:, :],
                                        scalar1=15, scalar2=None, op0=OP.bitwise_and),
          ["scrI"], ["scrI"])
        V(lambda eng: eng.tensor_copy(out=S["maskh"][:, :], in_=S["scrI"][:, :]),
          ["scrI"], ["maskH"])
        V(lambda eng: eng.tensor_sub(S["scrF"][:, :], S["scrF"][:, :], S["maskh"][:, :]),
          ["scrF", "maskH"], ["scrF"])
        V(lambda eng: eng.tensor_scalar_mul(S["scrF"][:, :], S["scrF"][:, :], 0.0625),
          ["scrF"], ["scrF"])
        V(lambda eng: eng.scalar_tensor_tensor(
            out=S["scrF"][:, :], in0=S["maskh"][:, :], scalar=32.0,
            in1=S["scrF"][:, :], op0=OP.mult, op1=OP.add),
          ["scrF", "maskH"], ["scrF"])
        V((lambda sl_: (lambda eng: eng.tensor_copy(
            out=S["maskh"][:, :], in_=S["idx16"][:, sl_])))(sl),
          ["idx16F"], ["maskH"])
        V(lambda eng: eng.tensor_scalar(
            out=S["maskh"][:, :], in0=S["maskh"][:, :],
            scalar1=0.0, scalar2=None, op0=OP.is_ge),
          ["maskH"], ["maskH"])
        V(lambda eng: eng.tensor_scalar_add(S["scrF"][:, :], S["scrF"][:, :], 1.0),
          ["scrF"], ["scrF"])
        V(lambda eng: eng.tensor_mul(S["scrF"][:, :], S["scrF"][:, :], S["maskh"][:, :]),
          ["scrF", "maskH"], ["scrF"])
        V((lambda sl_: (lambda eng: eng.tensor_scalar_add(
            S["idx16b"][:, sl_], S["scrF"][:, :], -1.0)))(sl),
          ["scrF"], ["idx16bF"])
    ix_all = ["idx16F"]
    sct_all = [f"scT{ti}" for ti in range(c.tiles)]

    b.do("gpsimd",
         lambda eng: eng.local_scatter(S["idsG"][:, :], S["ids16"][:, :], S["idx16b"][:, :],
                                       c.E, c.CAP, c.Tl),
         reads=["ids16", "idx16bF"], writes=["idsG"])
    b.do("gpsimd",
         lambda eng: eng.local_scatter(S["wG"][:, :], S["scoresT"][:, :], S["idx16"][:, :],
                                       c.E, c.CAP, c.Tl),
         reads=ix_all + sct_all, writes=["wG"])
    b.do("vector", lambda eng: eng.tensor_copy(out=S["tabf"][:, 0:CST], in_=S["idsG"][:, :]),
         reads=["idsG"], writes=["tabf_a"])
    b.do("vector", lambda eng: eng.tensor_copy(out=S["tabf"][:, CST:2 * CST], in_=S["wG"][:, :]),
         reads=["wG"], writes=["tabf_b"])
    b.do("sync", lambda eng: eng.dma_start(out=arin_tab[:, :], in_=S["tabf"][:, :]),
         reads=["tabf_a", "tabf_b"], writes=["arin_tab"], dma=ld)
    b.do("gpsimd",
         lambda eng: eng.collective_compute(
             "AllReduce", OP.add, replica_groups=RG,
             ins=[arin_tab[:, :]], outs=[tab_full[:, :]]),
         reads=["arin_tab"], writes=["tab_full"], coll=cc3)

    for ls in range(zsplit, c.segs):
        z_seg(ls)
    out_all = [f"out{ls}_{tt}" for ls in range(c.segs) for tt in range(4)]

    b.fence(exclude=["ln2_full", "tab_full", "cnt_full"])
    free_scope("Z")
    free_scope("R")

    if _cut("C"):
        b.fence()
        b.do("gpsimd", lambda eng: None, wait=[b.gwait])
        b.do("sync", lambda eng: None, wait=[b.gwait])
        b.emit(stack)
        nc.compile()
        return nc
    b.do("sync", lambda eng: eng.dma_start(out=S["tabf"][:, :], in_=tab_full[:, :]),
         reads=["tab_full"], writes=["tabf_a", "tabf_b"], dma=ld)
    b.do("vector",
         lambda eng: eng.tensor_scalar(out=S["minef"][:, :], in0=S["tabf"][:, 0:CST],
                                       scalar1=S["rowb"][0:c.E, 0:1], scalar2=None,
                                       op0=OP.is_ge),
         reads=["tabf_a", "rowb"], writes=["minef"])
    b.do("vector",
         lambda eng: eng.tensor_scalar(out=S["mtmp"][:, :], in0=S["tabf"][:, 0:CST],
                                       scalar1=S["rowb"][0:c.E, 1:2], scalar2=None,
                                       op0=OP.is_lt),
         reads=["tabf_a", "rowb"], writes=["mtmp"])
    b.do("vector",
         lambda eng: eng.tensor_mul(S["minef"][:, :], S["minef"][:, :], S["mtmp"][:, :]),
         reads=["mtmp", "minef"], writes=["minef"])
    b.do("vector",
         lambda eng: eng.tensor_scalar(out=S["mtmp"][:, :], in0=S["tabf"][:, CST:2 * CST],
                                       scalar1=0.0, scalar2=None, op0=OP.is_gt),
         reads=["tabf_b"], writes=["mtmp"])
    b.do("vector",
         lambda eng: eng.tensor_mul(S["minef"][:, :], S["minef"][:, :], S["mtmp"][:, :]),
         reads=["mtmp", "minef"], writes=["minef"])
    b.do("vector",
         lambda eng: eng.tensor_scalar(out=S["mtmp"][:, :], in0=S["tabf"][:, 0:CST],
                                       scalar1=S["rowb"][0:c.E, 0:1], scalar2=None,
                                       op0=OP.subtract),
         reads=["tabf_a", "rowb"], writes=["mtmp"])
    b.do("vector",
         lambda eng: eng.tensor_scalar(out=S["mtmp"][:, :], in0=S["mtmp"][:, :],
                                       scalar1=-1.0, scalar2=-1.0, op0=OP.mult,
                                       op1=OP.add),
         reads=["mtmp"], writes=["mtmp"])
    b.do("vector", lambda eng: eng.tensor_copy(out=S["ltok"][:, :], in_=S["mtmp"][:, :]),
         reads=["mtmp"], writes=["ltok"])
    b.do("vector", lambda eng: eng.memset(S["onescol"][:, :], 1.0), writes=["onescol"])
    b.do("vector",
         lambda eng: eng.tensor_tensor_scan(
             out=S["mtmp"][:, :], data0=S["onescol"][:, :].to_broadcast([c.E, c.CAP]),
             data1=S["minef"][:, :], initial=0.0, op0=OP.mult, op1=OP.add),
         reads=["onescol", "minef"], writes=["mtmp"])
    b.do("vector",
         lambda eng: eng.tensor_copy(out=S["cntrow"][:, :], in_=S["mtmp"][:, CST - 1:CST]),
         reads=["mtmp"], writes=["cntrow"])
    b.do("vector",
         lambda eng: eng.tensor_sub(S["mtmp"][:, :], S["mtmp"][:, :], S["minef"][:, :]),
         reads=["mtmp", "minef"], writes=["mtmp"])
    b.do("vector",
         lambda eng: eng.tensor_copy(out=S["mtmpI"][:, :], in_=S["mtmp"][:, :]),
         reads=["mtmp"], writes=["mtmpI"])
    b.do("vector",
         lambda eng: eng.tensor_scalar(out=S["mtmpI"][:, :], in0=S["mtmpI"][:, :],
                                       scalar1=15, scalar2=None, op0=OP.bitwise_and),
         reads=["mtmpI"], writes=["mtmpI"])
    b.do("vector",
         lambda eng: eng.tensor_copy(out=S["mtmp3"][:, :], in_=S["mtmpI"][:, :]),
         reads=["mtmpI"], writes=["mtmp3"])
    b.do("vector",
         lambda eng: eng.tensor_sub(S["mtmp2"][:, :], S["mtmp"][:, :], S["mtmp3"][:, :]),
         reads=["mtmp", "mtmp3"], writes=["mtmp2"])
    b.do("vector",
         lambda eng: eng.tensor_scalar_mul(S["mtmp2"][:, :], S["mtmp2"][:, :], 0.0625),
         reads=["mtmp2"], writes=["mtmp2"])
    b.do("vector",
         lambda eng: eng.scalar_tensor_tensor(out=S["mtmp2"][:, :], in0=S["mtmp3"][:, :],
                                              scalar=32.0, in1=S["mtmp2"][:, :],
                                              op0=OP.mult, op1=OP.add),
         reads=["mtmp3", "mtmp2"], writes=["mtmp2"])
    b.do("vector",
         lambda eng: eng.scalar_tensor_tensor(out=S["mtmp2"][:, :], in0=S["mtmp2"][:, :],
                                              scalar=1.0, in1=S["minef"][:, :],
                                              op0=OP.add, op1=OP.mult),
         reads=["mtmp2", "minef"], writes=["mtmp2"])
    b.do("vector",
         lambda eng: eng.tensor_scalar_add(S["mtmp2"][:, :], S["mtmp2"][:, :], -1.0),
         reads=["mtmp2"], writes=["mtmp2"])
    b.do("vector", lambda eng: eng.tensor_copy(out=S["rankD"][:, :], in_=S["mtmp2"][:, :]),
         reads=["mtmp2"], writes=["rankD"])
    b.do("gpsimd",
         lambda eng: eng.local_scatter(S["listSS"][:, :], S["sloti"][:, :],
                                       S["rankD"][:, :], c.E, c.CAP, c.CAP),
         reads=["sloti", "rankD"], writes=["listSS"])
    b.do("gpsimd",
         lambda eng: eng.local_scatter(S["listTT"][:, :], S["ltok"][:, :],
                                       S["rankD"][:, :], c.E, c.CAP, c.CAP),
         reads=["ltok", "rankD"], writes=["listTT"])
    b.do("vector",
         lambda eng: eng.tensor_scalar(out=S["listSS"][:, :], in0=S["listSS"][:, :],
                                       scalar1=-1.0, scalar2=-1.0, op0=OP.mult, op1=OP.add),
         reads=["listSS"], writes=["listSS"])
    b.do("vector",
         lambda eng: eng.tensor_scalar(out=S["listTT"][:, :], in0=S["listTT"][:, :],
                                       scalar1=-1.0, scalar2=-1.0, op0=OP.mult, op1=OP.add),
         reads=["listTT"], writes=["listTT"])
    b.do("vector",
         lambda eng: eng.tensor_scalar_max(S["listSS"][:, 0:1], S["listSS"][:, 0:1], 0.0),
         reads=["listSS"], writes=["listSS"])
    b.do("vector",
         lambda eng: eng.tensor_copy(out=S["t128"][:, 0:1], in_=S["listTT"][:, 0:1]),
         reads=["listTT"], writes=["t128d0"])
    b.do("vector",
         lambda eng: eng.tensor_scalar(out=S["t128"][:, 1:2], in0=S["t128"][:, 0:1],
                                       scalar1=0.0, scalar2=None, op0=OP.is_lt),
         reads=["t128d0"], writes=["t128d1"])
    b.do("vector",
         lambda eng: eng.scalar_tensor_tensor(out=S["t128"][:, 0:1], in0=S["t128"][:, 1:2],
                                              scalar=float(c.Tl + 1), in1=S["t128"][:, 0:1],
                                              op0=OP.mult, op1=OP.add),
         reads=["t128d1", "t128d0"], writes=["t128d0"])
    b.do("vector",
         lambda eng: eng.tensor_copy(out=S["listTT"][:, 0:1], in_=S["t128"][:, 0:1]),
         reads=["t128d0"], writes=["listTT"])
    b.do("sync", lambda eng: eng.dma_start(out=cnts_d[0:1, :].rearrange("a e -> e a"),
                                           in_=S["cntrow"][:, :]),
         reads=["cntrow"], writes=["cnts_d"], dma=ld)
    b.do("sync", lambda eng: eng.dma_start(out=S["nfsf"][:, :], in_=cnts_d[:, :]),
         reads=["cnts_d"], writes=["nfsf"], dma=ld)
    b.do("vector", lambda eng: eng.tensor_copy(out=S["nfs"][:, :], in_=S["nfsf"][:, :]),
         reads=["nfsf"], writes=["nfs"])
    b.do("sync", lambda eng: eng.dma_start(out=lists_d[:, 0:c.CAP], in_=S["listSS"][:, :]),
         reads=["listSS"], writes=["lists_d1"], dma=ld)
    b.do("sync", lambda eng: eng.dma_start(out=lists_d[:, c.CAP:], in_=S["listTT"][:, :]),
         reads=["listTT"], writes=["lists_d2"], dma=ld)
    b.do("vector", lambda eng: eng.memset(S["dlist"][:, :], 0), writes=["dlist_z"])
    for half, res in ((0, "lists_d1"), (1, "lists_d2")):
        for pp in (0, 16):
            b.do("sync",
                 (lambda half_, pp_: (lambda eng: eng.dma_start(
                     out=S["dlist"][pp_:pp_ + 16,
                                    half_ * c.E * 32:(half_ + 1) * c.E * 32].rearrange(
                         "p (e s) -> p e s", e=c.E),
                     in_=lists_d[:, half_ * c.CAP:(half_ + 1) * c.CAP].rearrange(
                         "e (p s) -> p e s", p=16))))(half, pp),
                 reads=[res, "dlist_z"], writes=[f"dlist{half}_{pp}"], dma=ld)
    dlist_all = [f"dlist{h}_{p}" for h in (0, 1) for p in (0, 16)]

    EWP = 2 * c.DC * c.I + c.IC * c.D
    sbuf("expW", [128, 2 * EWP], BF16, scope="C")
    sbuf("xbfm", [128, 2 * c.DC * c.CAP], BF16, scope="C")
    sbuf("hh", [128, c.IC * c.CAP], BF16, scope="C")
    sbuf("ybt", [128, 2 * c.D], BF16, scope="C")
    sbuf("siltC", [128, c.CAP], F32, scope="C")
    sbuf("myidw", [c.EPC, 2 * c.CAP], F32, scope="C")
    sbuf("myids16", [c.EPC, c.CAP], I16, scope="C")
    sbuf("wTslot", [128, 4 * c.EPC], F32, scope="C")
    sbuf("eidx", [128, c.EPC * (c.CAP // 16)], I16, scope="C")

    b.do("gpsimd",
         lambda eng: eng.indirect_dma_start(
             S["myidw"][:, :], None, tab_full[:, :],
             IndirectOffsetOnAxis(S["myexp"][:, :], 0)),
         reads=["tab_full", "myexp"], writes=["myidw"], dma=pda)
    for sb4 in range(4):
        b.do("tensor",
             (lambda s4: (lambda eng: eng.transpose(
                 P["tsm"][0:128, 0:c.EPC],
                 S["myidw"][:, CST + s4 * 128: CST + (s4 + 1) * 128],
                 S["idf32"][0:c.EPC, 0:c.EPC])))(sb4),
             reads=["myidw", "idf32"], writes=["tsm"])
        b.do("vector",
             (lambda s4: (lambda eng: eng.tensor_copy(
                 out=S["wTslot"][:, s4 * c.EPC:(s4 + 1) * c.EPC],
                 in_=P["tsm"][0:128, 0:c.EPC])))(sb4),
             reads=["tsm"], writes=[f"wTslot{sb4}"])
    b.do("vector",
         lambda eng: eng.tensor_copy(out=S["myids16"][:, :], in_=S["myidw"][:, 0:CST]),
         reads=["myidw"], writes=["myids16"])
    b.do("scalar",
         lambda eng: eng.dma_start(out=eidx_d[:, :], in_=S["myids16"][:, :]),
         reads=["myids16"], writes=["eidx_d"], dma=stw)
    b.do("vector", lambda eng: eng.memset(S["eidx"][:, :], 0), writes=["eidx_z"])
    b.do("sync",
         lambda eng: eng.dma_start(
             out=S["eidx"][0:16, :].rearrange("p (e s) -> p e s", e=c.EPC),
             in_=eidx_d.rearrange("e (p s) -> p e s", p=16)),
         reads=["eidx_d", "eidx_z"], writes=["eidx_a"], dma=ld)
    b.do("sync",
         lambda eng: eng.dma_start(
             out=S["eidx"][16:32, :].rearrange("p (e s) -> p e s", e=c.EPC),
             in_=eidx_d.rearrange("e (p s) -> p e s", p=16)),
         reads=["eidx_d", "eidx_z"], writes=["eidx_b"], dma=ld)

    for j in range(c.EPC):
        par = j % 2
        eb = par * EWP
        b.do("gpsimd",
             (lambda j_, par_: (lambda eng: eng.dma_gather(
                 S["xbfm"][:, par_ * c.DC * c.CAP:(par_ + 1) * c.DC * c.CAP].rearrange(
                     "p (k t) -> p k t", k=c.DC),
                 ln2_full[:, :],
                 S["eidx"][:, j_ * (c.CAP // 16):(j_ + 1) * (c.CAP // 16)],
                 c.CAP, c.CAP, c.D, transpose=True)))(j, par),
             reads=["eidx_a", "eidx_b", "ln2_full"], writes=[f"xbfm{par}"], dma=pdb)
        b.do("gpsimd",
             (lambda j_, eb_: (lambda eng: eng.dma_start(
                 out=S["expW"][:, eb_: eb_ + c.DC * c.I].rearrange(
                     "p (k i) -> p k i", k=c.DC),
                 in_=w1T_in[j_].rearrange("(k p) i -> p k i", p=128))))(j, eb),
             writes=[f"expWa{par}"], dma=pdb)
        b.do("gpsimd",
             (lambda j_, eb_: (lambda eng: eng.dma_start(
                 out=S["expW"][:, eb_ + c.DC * c.I: eb_ + 2 * c.DC * c.I].rearrange(
                     "p (k i) -> p k i", k=c.DC),
                 in_=w3T_in[j_].rearrange("(k p) i -> p k i", p=128))))(j, eb),
             writes=[f"expWb{par}"], dma=pdb)
        nfull = c.I // 128
        b.do("gpsimd",
             (lambda j_, eb_, nf: (lambda eng: eng.dma_start(
                 out=S["expW"][:, eb_ + 2 * c.DC * c.I:
                               eb_ + 2 * c.DC * c.I + nf * c.D].rearrange(
                     "p (k i) -> p k i", k=nf),
                 in_=w2T_in[j_][0:nf * 128, :].rearrange("(k p) i -> p k i", p=128))))(j, eb, nfull),
             writes=[f"expWc{par}"], dma=pdb)
        b.do("gpsimd",
             (lambda j_, eb_, nf: (lambda eng: eng.dma_start(
                 out=S["expW"][0:c.I - nf * 128,
                               eb_ + 2 * c.DC * c.I + nf * c.D:
                               eb_ + 2 * c.DC * c.I + (nf + 1) * c.D],
                 in_=w2T_in[j_][nf * 128:, :])))(j, eb, nfull),
             writes=[f"expWc2{par}"], dma=pdb)

        for ic in range(c.IC):
            icw = 128 if ic < c.IC - 1 else c.I - 128 * (c.IC - 1)
            for zi in range(2):
                bank = f"g{zi}"

                def h_group(eng, ic_=ic, zi_=zi, bank_=bank, par_=par, icw_=icw, eb_=eb):
                    ii = None
                    for kc in range(c.DC):
                        ii = eng.matmul(
                            P[bank_][0:icw_, :],
                            S["expW"][:, eb_ + zi_ * c.DC * c.I + kc * c.I + ic_ * 128:
                                      eb_ + zi_ * c.DC * c.I + kc * c.I + ic_ * 128 + icw_],
                            S["xbfm"][:, (par_ * c.DC + kc) * c.CAP:
                                      (par_ * c.DC + kc + 1) * c.CAP],
                            start=(kc == 0), stop=(kc == c.DC - 1))
                    return ii

                b.do("tensor", h_group,
                     reads=[f"expWa{par}", f"expWb{par}", f"xbfm{par}"],
                     writes=[bank])
            b.do("scalar",
                 (lambda icw_: (lambda eng: eng.activation(
                     out=S["siltC"][0:icw_, :], in_=P["g0"][0:icw_, :], func=AF.Silu)))(icw),
                 reads=["g0"], writes=["siltC0"])
            b.do("vector",
                 (lambda ic_, icw_: (lambda eng: eng.tensor_mul(
                     S["hh"][0:icw_, ic_ * c.CAP:(ic_ + 1) * c.CAP],
                     S["siltC"][0:icw_, :], P["g1"][0:icw_, :])))(ic, icw),
                 reads=["siltC0", "g1"], writes=[f"hhc{ic}"])

        for sb4 in range(4):
            yh = sb4 % 2
            for nh in range(2):
                bank = f"g{2 + nh}"

                def y_group(eng, sb4_=sb4, nh_=nh, bank_=bank, eb_=eb):
                    ii = None
                    for ic in range(c.IC):
                        icw = 128 if ic < c.IC - 1 else c.I - 128 * (c.IC - 1)
                        ii = eng.matmul(
                            P[bank_][:, :],
                            S["hh"][0:icw, ic * c.CAP + sb4_ * 128: ic * c.CAP + (sb4_ + 1) * 128],
                            S["expW"][0:icw, eb_ + 2 * c.DC * c.I + ic * c.D + nh_ * 512:
                                      eb_ + 2 * c.DC * c.I + ic * c.D + (nh_ + 1) * 512],
                            start=(ic == 0), stop=(ic == c.IC - 1))
                    return ii

                b.do("tensor", y_group,
                     reads=[f"hhc{ic}" for ic in range(c.IC)] +
                           [f"expWc{par}", f"expWc2{par}"],
                     writes=[bank])
                b.do("vector",
                     (lambda bank_, sb4_, nh_, j_, yh_: (lambda eng: eng.tensor_scalar_mul(
                         S["ybt"][:, yh_ * c.D + nh_ * 512: yh_ * c.D + (nh_ + 1) * 512],
                         P[bank_][:, :],
                         S["wTslot"][:, sb4_ * c.EPC + j_: sb4_ * c.EPC + j_ + 1])))(bank, sb4, nh, j, yh),
                     reads=[bank, f"wTslot{sb4}"], writes=[f"ybt{yh}_{nh}"])
            b.do("scalar",
                 (lambda j_, sb4_, yh_: (lambda eng: eng.dma_start(
                     out=agin_yb[j_ * c.CAP + sb4_ * 128: j_ * c.CAP + (sb4_ + 1) * 128, :],
                     in_=S["ybt"][:, yh_ * c.D:(yh_ + 1) * c.D])))(j, sb4, yh),
                 reads=[f"ybt{yh}_0", f"ybt{yh}_1"], writes=[f"agyb{j}_{sb4}"], dma=stw)

        if (j + 1) % c.EPCH == 0:
            k4 = j // c.EPCH
            b.do("gpsimd",
                 (lambda k_: (lambda eng: eng.collective_compute(
                     "AllGather", OP.bypass, replica_groups=RG,
                     ins=[agin_yb[k_ * c.EPCH * c.CAP:(k_ + 1) * c.EPCH * c.CAP, :]],
                     outs=[ybc[k_][:, :]])))(k4),
                 reads=[f"agyb{jj}_{s4}"
                        for jj in range(k4 * c.EPCH, (k4 + 1) * c.EPCH)
                        for s4 in range(4)],
                 writes=[f"ybc{k4}"], coll=cc4s[k4])

    b.fence(exclude=["ln2_full", "tab_full", "cnt_full"] +
            [f"ybc{k}" for k in range(c.NCH)])
    free_scope("C")

    if _cut("D"):
        b.fence()
        b.do("gpsimd", lambda eng: None, wait=[b.gwait])
        b.do("sync", lambda eng: None, wait=[b.gwait])
        b.emit(stack)
        nc.compile()
        return nc
    sbuf("gvb", [128, 4 * 4 * c.D], BF16, scope="D")
    sbuf("gvf", [128, 4 * 4 * c.D], F32, scope="D")
    sbuf("zrow", [128, c.D], F32, scope="D")

    b.do("vector", lambda eng: eng.memset(S["gvb"][:, :], 0),
         writes=[f"gvb{p}" for p in range(4)])
    b.do("vector", lambda eng: eng.memset(S["zrow"][:, :], 0.0), writes=["zrow"])
    b.do("sync",
         lambda eng: eng.dma_start(out=out[c.Tl:c.Tl + 128, :], in_=S["zrow"][:, :]),
         reads=["zrow"], writes=["outPad"], dma=ld)

    def get_nkreg(eng):
        r = getattr(eng, "_nkreg", None)
        if r is None:
            r = eng.alloc_register("nkreg")
            eng._nkreg = r
        return r

    ei = 0
    for k4 in range(c.NCH):
        chunk = [cc_ * c.EPC + k4 * c.EPCH + jj
                 for cc_ in range(c.n_cores) for jj in range(c.EPCH)]
        for e in chunk:
            par = ei % 4
            ei += 1
            gvb = S["gvb"][:, par * 4 * c.D:(par + 1) * 4 * c.D].rearrange(
                "p (a d) -> p a d", d=c.D)
            gvf = S["gvf"][:, par * 4 * c.D:(par + 1) * 4 * c.D].rearrange(
                "p (a d) -> p a d", d=c.D)

            def d_gather(eng, e_=e, gvb_=gvb, k_=k4):
                r = get_nkreg(eng)
                eng.reg_load(r, S["nfs"][0:1, e_:e_ + 1])
                eng.reg_alu(r, r, 1, OP.max)
                return eng.dma_gather(
                    gvb_, ybc[k_][:, :],
                    S["dlist"][:, e_ * 32:(e_ + 1) * 32],
                    c.CAP, r, c.D)

            b.do("gpsimd", d_gather,
                 reads=[f"ybc{k4}", "nfs"] + dlist_all, writes=[f"gvb{par}"], dma=pdc)
            b.do("vector",
                 (lambda par_: (lambda eng: eng.tensor_copy(
                     out=S["gvf"][:, par_ * 4 * c.D:(par_ + 1) * 4 * c.D],
                     in_=S["gvb"][:, par_ * 4 * c.D:(par_ + 1) * 4 * c.D])))(par),
                 reads=[f"gvb{par}"], writes=[f"gvf{par}"])

            def d_scatter(eng, e_=e, gvf_=gvf):
                r = get_nkreg(eng)
                eng.reg_load(r, S["nfs"][0:1, e_:e_ + 1])
                eng.reg_alu(r, r, 1, OP.max)
                return eng.dma_scatter_add(
                    out[:, :], gvf_,
                    S["dlist"][:, (c.E + e_) * 32:(c.E + e_ + 1) * 32],
                    c.CAP, r, c.D)

            b.do("gpsimd", d_scatter,
                 reads=["nfs", f"gvf{par}", "outPad"] + dlist_all + out_all,
                 writes=["outRMW"], dma=pdd)

    if debug_dump:
        dbg_specs = [("dbg_tab", tab_full, [c.E, 2 * c.CAP], F32, ["tab_full"]),
                     ("dbg_ln2", agin_ln2, [c.Tl, c.D], BF16, all_agin),
                     ("dbg_x1", x1_d, [c.Tl, c.D], F32,
                      [f"x1_d{ls}_{tt}" for ls in range(c.segs) for tt in range(4)]),
                     ("dbg_lists", lists_d, [c.E, 2 * c.CAP], I16,
                      ["lists_d1", "lists_d2"]),
                     ("dbg_cnt", cnts_d, [1, c.E], F32, ["cnts_d"])]
        for nm, srcap, shp, dt, rds in dbg_specs:
            od = nc.dram_tensor(nm, shp, dt, kind="ExternalOutput").ap()
            b.do("sync",
                 (lambda od_, s_: (lambda eng: eng.dma_start(out=od_[:, :], in_=s_[:, :])))(od, srcap),
                 reads=rds + ["outRMW"], writes=[nm], dma=ld)
        b.do("sync", lambda eng: None,
             reads=["outRMW"] + out_all + [d[0] for d in dbg_specs])
    b.do("gpsimd", lambda eng: None, reads=["outRMW", "outPad"] + out_all)
    b.do("sync", lambda eng: None, reads=["outRMW", "outPad"] + out_all)
    b.emit(stack)
    nc.compile()
    return nc


def _host_constants(c: Cfg, core):
    o = {}
    Lup = np.zeros((128, 132), np.float32)
    for t in range(128):
        Lup[:t, t] = 1.0
    Lup[:, 128] = 1.0
    o["Lup"] = Lup.astype(BF)
    o["idf32"] = np.eye(128, dtype=np.float32)
    o["idbf"] = np.eye(128, dtype=np.float32).astype(BF)
    ss = np.zeros((128, c.DC * 128), np.float32)
    sb = np.zeros((128, c.DC * 128), np.float32)
    for kc in range(c.DC):
        for hloc in range(2):
            hglob = 2 * kc + hloc
            rc = (hglob % 4) * 32 + hglob // 4
            ss[hloc * 64:(hloc + 1) * 64, kc * 128 + rc] = 0.125
            sb[rc, kc * 128 + hloc * 64: kc * 128 + (hloc + 1) * 64] = 1.0
    o["Ssum"] = ss.astype(BF)
    o["Sb"] = sb.astype(BF)
    o["ones1"] = np.ones((1, 128), np.float32)
    segsel = np.zeros((128, c.segs * c.segs), np.float32)
    for ls in range(c.segs):
        segsel[:, ls * c.segs + ls] = 1.0
    o["segsel"] = segsel.astype(BF)
    ids = core * c.Tl + np.arange(c.Tl, dtype=np.int64)
    o["ids16"] = np.broadcast_to(ids[None, :], (c.E, c.Tl)).astype(np.int16).copy()
    col = np.arange(c.CAP, dtype=np.int64)
    siginv = (col % 32) * 16 + col // 32
    ee = np.arange(c.E, dtype=np.int64)
    ebase = (ee // c.EPC) * (c.EPCH * c.CAP) + ((ee % c.EPC) % c.EPCH) * c.CAP
    sloti = ebase[:, None] + siginv[None, :]
    o["sloti"] = (-(sloti + 1)).astype(np.int16)
    mm = np.zeros((c.nseg_tot, c.segs), np.float32)
    for r in range(c.nseg_tot):
        cc, ls2 = divmod(r, c.segs)
        nglob = cc + c.n_cores * ls2
        for ls in range(c.segs):
            if nglob < core + c.n_cores * ls:
                mm[r, ls] = 1.0
    o["maskmat"] = mm
    rb = np.zeros((128, 2), np.float32)
    rb[:, 0] = core * c.Tl
    rb[:, 1] = (core + 1) * c.Tl
    o["rowbase"] = rb
    o["myexp"] = (core * c.EPC + np.arange(c.EPC, dtype=np.int32)).reshape(c.EPC, 1)
    return o


def host_inputs(c: Cfg, inp):
    x = np.ascontiguousarray(np.asarray(inp["L0_attn_x"], np.float32).reshape(c.T, c.D))
    f32 = lambda a: np.ascontiguousarray(np.asarray(a, np.float32))
    Wq, Wk = f32(inp["L0_attn_Wq"]), f32(inp["L0_attn_Wk"])
    Wv, Wo = f32(inp["L0_attn_Wv"]), f32(inp["L0_attn_Wo"])
    bqkv = np.concatenate(
        [np.asarray(inp[k], np.float32).reshape(c.DC, 128).T
         for k in ("L0_attn_bq", "L0_attn_bk", "L0_attn_bv")], axis=1)
    bqkv = np.ascontiguousarray(bqkv)
    bo_row = np.ascontiguousarray(np.broadcast_to(
        np.asarray(inp["L0_attn_bo"], np.float32).reshape(1, c.D), (128, c.D)))
    ws1T = np.ascontiguousarray(np.asarray(inp["L0_ffn_ws1"], np.float32).T)
    ws3T = np.ascontiguousarray(np.asarray(inp["L0_ffn_ws3"], np.float32).T)
    ws2T = np.ascontiguousarray(np.asarray(inp["L0_ffn_ws2"], np.float32).T)
    gwT = np.ascontiguousarray(np.asarray(inp["L0_ffn_gate_w"], np.float32).T)
    gb_b = np.ascontiguousarray(np.broadcast_to(
        np.asarray(inp["L0_ffn_gate_b"], np.float32)[None, :], (128, c.E)))
    w1T = np.ascontiguousarray(np.asarray(inp["L0_ffn_w1"], np.float32).transpose(0, 2, 1))
    w3T = np.ascontiguousarray(np.asarray(inp["L0_ffn_w3"], np.float32).transpose(0, 2, 1))
    w2T = np.ascontiguousarray(np.asarray(inp["L0_ffn_w2"], np.float32).transpose(0, 2, 1))

    in_maps = []
    for core in range(c.n_cores):
        segs = [core + c.n_cores * ls for ls in range(c.segs)]
        xs = np.ascontiguousarray(
            np.concatenate([x[n * c.SEG:(n + 1) * c.SEG] for n in segs], axis=0))
        m = {"x": xs, "Wq": Wq, "Wk": Wk, "Wv": Wv, "Wo": Wo,
             "bqkv": bqkv, "bo_row": bo_row,
             "ws1T": ws1T, "ws3T": ws3T, "ws2T": ws2T,
             "gwT": gwT, "gb_bcast": gb_b,
             "w1T": w1T[core * c.EPC:(core + 1) * c.EPC],
             "w3T": w3T[core * c.EPC:(core + 1) * c.EPC],
             "w2T": w2T[core * c.EPC:(core + 1) * c.EPC]}
        m.update(_host_constants(c, core))
        in_maps.append(m)
    return in_maps


def assemble_out(c: Cfg, results, shape):
    outf = np.empty((c.T, c.D), np.float32)
    for core in range(c.n_cores):
        oc = results[core]["out"]
        for ls in range(c.segs):
            n = core + c.n_cores * ls
            outf[n * c.SEG:(n + 1) * c.SEG] = oc[ls * c.SEG:(ls + 1) * c.SEG]
    return outf.reshape(shape)


_CACHED = {}


def kernel(L0_attn_x, L0_attn_Wq, L0_attn_bq, L0_attn_Wk, L0_attn_bk,
           L0_attn_Wv, L0_attn_bv, L0_attn_Wo, L0_attn_bo,
           L0_ffn_gate_w, L0_ffn_gate_b, L0_ffn_w1, L0_ffn_w2, L0_ffn_w3,
           L0_ffn_ws1, L0_ffn_ws2, L0_ffn_ws3):
    c = Cfg(n_cores=8, segs=8)
    inp = dict(L0_attn_x=L0_attn_x, L0_attn_Wq=L0_attn_Wq, L0_attn_bq=L0_attn_bq,
               L0_attn_Wk=L0_attn_Wk, L0_attn_bk=L0_attn_bk, L0_attn_Wv=L0_attn_Wv,
               L0_attn_bv=L0_attn_bv, L0_attn_Wo=L0_attn_Wo, L0_attn_bo=L0_attn_bo,
               L0_ffn_gate_w=L0_ffn_gate_w, L0_ffn_gate_b=L0_ffn_gate_b,
               L0_ffn_w1=L0_ffn_w1, L0_ffn_w2=L0_ffn_w2, L0_ffn_w3=L0_ffn_w3,
               L0_ffn_ws1=L0_ffn_ws1, L0_ffn_ws2=L0_ffn_ws2, L0_ffn_ws3=L0_ffn_ws3)
    key = (c.n_cores, c.segs)
    if key not in _CACHED:
        stack = ExitStack()
        nc = build_program(c, stack)
        _CACHED[key] = (nc, stack)
    nc = _CACHED[key][0]
    in_maps = host_inputs(c, inp)
    res = run_bass_kernel_spmd(nc, in_maps, core_ids=list(range(c.n_cores)))
    return assemble_out(c, res.results, np.asarray(L0_attn_x).shape)
